# version 01
# baseline (speedup 1.0000x reference)
import contextlib
import numpy as np

import concourse.bass as bass
import concourse.mybir as mybir
from concourse.bass_utils import run_bass_kernel_spmd

F32 = mybir.dt.float32
I32 = mybir.dt.int32
AF = mybir.ActivationFunctionType
ALU = mybir.AluOpType
AX = mybir.AxisListType.X

D = 1024
NH, HD = 16, 64
FF0 = 2816
FF1 = 4096
SCALE = 0.125
MID = [0.25, 0.75, 1.25, 1.75, 2.5, 3.5, 5.0]
DG = [0.5, 0.5, 0.5, 0.5, 1.0, 1.0, 2.0]
N0CH = [512, 512, 512, 512, 512, 256]
ENGINES = ("sync", "tensor", "vector", "scalar", "gpsimd")

_DBG = False
_DBG2 = False
_DBG3 = False
_DBG4 = False
_LIMIT = None

ARENA = 43840
A_KT = 0
A_VV = 4096
A_Q = (8256, 12352)
A_AO = 16448
A_S = 20544
A_P = 24640
P_XNT = 26688
P_XSQ = 30784
P_XN = 31808
P_XIN = 32832
P_WS = 34880
W_AOT = 26688
W_WS = 30784
F_HQT = 26688
F_XQT = 32320
F_XIN = 34368
F_WS = 36416
F_WS2 = 36416
F_XN = 38464
F_SCR = 39488
F_H = 42048
F_HQ = 42560
F_G = F_XN
G_WIN = 26688
G_WQ = 30784
G_STG = 33600
G_WS = 34624
G_SROW = 38720
L_XN1 = 0
L_XN1T = 4096
L_QT = 8192
L_KT = 12288
L_VV = 14848
L_S1 = 16192
L_H2T = 20288
L_P = 22336
L_YT = 24384
L_GBC = 32576
L_LOC = 34624
L_WS = 38720
L_WKV = 40768
L_AOT2 = 40768
L_XIN = 41792

assert L_XIN + 2048 <= ARENA


class C:

    def __init__(self, nc, en, eng, sem, t):
        self.nc, self.en, self.eng, self.sem, self.t = nc, en, eng, sem, t
        self.c = {k: 0 for k in sem}
        self.nbar = 0
        self.bankw = [None] * 8

    def _do(self, en, f):
        if self.en == en:
            return f(self.eng)
        return None

    def on(self, en):
        return self.en == en

    def inc(self, instr, s, k=1):
        self.c[s] += k
        if instr is not None:
            instr.then_inc(self.sem[s], k)
        return self.c[s]

    def wait(self, en, s, val=None):
        if val is None:
            val = self.c[s]
        if self.en == en:
            self.eng.wait_ge(self.sem[s], val)

    def drain(self, en):
        if self.en == en:
            self.eng.drain()

    def vmark(self):
        s = self.t["stats"]
        return self.inc(self._do("vector", lambda e: e.tensor_copy(out=s[:, 7:8], in_=s[:, 7:8])), "ve")

    def amark(self):
        s = self.t["stats"]
        return self.inc(self._do("scalar", lambda e: e.copy(s[:, 7:8], s[:, 7:8])), "ac")

    def bar(self):
        self.nbar += 1
        for e in ENGINES:
            if self.en == e:
                self.eng.drain().then_inc(self.sem["b_" + e], 1)
        for e in ENGINES:
            if e != self.en:
                self.wait(self.en, "b_" + e, self.nbar)
        for e in ENGINES:
            self.c["b_" + e] = self.nbar

    def load(self, out_ap, in_ap, sl=0):
        s = "dl%d" % sl
        i = self._do("sync", lambda e: e.dma_start(out=out_ap, in_=in_ap))
        return (s, self.inc(i, s, 16))

    def store(self, out_ap, in_ap, sl=0):
        s = "ds%d" % sl
        i = self._do("gpsimd", lambda e: e.dma_start(out=out_ap, in_=in_ap))
        return (s, self.inc(i, s, 16))

    def waitv(self, en, tok):
        if self.en == en:
            self.eng.wait_ge(self.sem[tok[0]], tok[1])

    def wait_stores(self, en):
        for s in ("ds0", "ds1"):
            self.wait(en, s, self.c[s])

    def mm(self, out, lhsT, rhs, start, stop, tile_position=None):
        return self._do("tensor", lambda e: e.matmul(out, lhsT, rhs, start=start, stop=stop,
                                                     tile_position=tile_position))

    def tp(self, out, in_, np_):
        return self._do("tensor", lambda e: e.transpose(out, in_, self.t["ident"][0:np_, 0:np_]))

    def v(self, meth, *a, inc=False, **kw):
        i = self._do("vector", lambda e: getattr(e, meth)(*a, **kw))
        if inc:
            return self.inc(i, "ve")
        return i

    def a(self, meth, *a, inc=False, **kw):
        i = self._do("scalar", lambda e: getattr(e, meth)(*a, **kw))
        if inc:
            return self.inc(i, "ac")
        return i

    def bank_wait_te(self, b):
        if self.bankw[b] is not None:
            s, v = self.bankw[b]
            self.wait("tensor", s, v)
            self.bankw[b] = None

    def bank_free(self, b, s, v):
        self.bankw[b] = (s, v)


def _emit(c):
    t = c.t
    ps = t["psum"]
    ar = t["arena"]

    def A(off, n, p0=0, part=128):
        return ar[p0:p0 + part, off:off + n]

    st = t["stats"]

    def ln_tile(src, dst, xsq, n=D):
        c.wait("scalar", "ve", c.c["ve"])
        c.a("activation", xsq, src, AF.Square, inc=True)
        sq_done = c.c["ac"]
        c.v("reduce_sum", out=st[:, 0:1], in_=src, axis=AX)
        c.drain("vector")
        c.v("tensor_scalar", out=st[:, 0:1], in0=st[:, 0:1], scalar1=1.0 / n, scalar2=None, op0=ALU.mult)
        c.drain("vector")
        c.wait("vector", "ac", sq_done)
        c.v("reduce_sum", out=st[:, 1:2], in_=xsq, axis=AX)
        c.drain("vector")
        c.v("tensor_scalar", out=st[:, 1:2], in0=st[:, 1:2], scalar1=1.0 / n, scalar2=None, op0=ALU.mult)
        c.v("tensor_scalar", out=st[:, 2:3], in0=st[:, 0:1], scalar1=st[:, 0:1], scalar2=None, op0=ALU.mult)
        c.drain("vector")
        c.v("tensor_sub", out=st[:, 3:4], in0=st[:, 1:2], in1=st[:, 2:3], inc=True)
        var_done = c.c["ve"]
        c.wait("scalar", "ve", var_done)
        c.a("activation", st[:, 4:5], st[:, 3:4], AF.Sqrt, bias=t["eps"][:, 0:1], scale=1.0, inc=True)
        sd_done = c.c["ac"]
        c.wait("vector", "ac", sd_done)
        c.v("reciprocal", out=st[:, 5:6], in_=st[:, 4:5])
        c.drain("vector")
        c.v("tensor_scalar", out=dst, in0=src, scalar1=st[:, 0:1], scalar2=st[:, 5:6],
            op0=ALU.subtract, op1=ALU.mult)
        c.drain("vector")

    def fp4_quant(src, dst, n):
        assert n <= 512
        nb = n // 16
        tt = A(F_SCR, n)
        at = A(F_SCR + 512, n)
        qa = A(F_SCR + 1024, n)
        qt = A(F_SCR + 1536, n)
        sg = A(F_SCR + 2048, n)
        qsc = t["qsc"]
        bmv, scv, rsv = qsc[:, 0:nb], qsc[:, 64:64 + nb], qsc[:, 128:128 + nb]
        src3 = src.rearrange("p (g k) -> p g k", k=16)
        dst3 = dst.rearrange("p (g k) -> p g k", k=16)
        t3 = tt.rearrange("p (g k) -> p g k", k=16)
        qa3 = qa.rearrange("p (g k) -> p g k", k=16)
        c.v("reduce_max", out=bmv, in_=src3, axis=AX, apply_absolute_value=True)
        c.drain("vector")
        c.v("tensor_scalar", out=scv, in0=bmv, scalar1=1.0 / 6.0, scalar2=1e-12, op0=ALU.mult, op1=ALU.max)
        c.drain("vector")
        c.v("reciprocal", out=rsv, in_=scv)
        c.drain("vector")
        c.v("tensor_tensor", out=t3, in0=src3, in1=rsv.unsqueeze(2).broadcast_to([128, nb, 16]), op=ALU.mult)
        c.drain("vector")
        c.v("tensor_scalar", out=at, in0=tt, scalar1=-1.0, scalar2=None, op0=ALU.mult)
        c.drain("vector")
        c.v("tensor_max", out=at, in0=at, in1=tt)
        c.drain("vector")
        c.v("tensor_scalar", out=qa, in0=at, scalar1=MID[0], scalar2=DG[0], op0=ALU.is_gt, op1=ALU.mult)
        c.drain("vector")
        for j in range(1, 7):
            c.v("tensor_scalar", out=qt, in0=at, scalar1=MID[j], scalar2=DG[j], op0=ALU.is_gt, op1=ALU.mult)
            c.drain("vector")
            c.v("tensor_add", out=qa, in0=qa, in1=qt)
            c.drain("vector")
        c.v("tensor_scalar", out=sg, in0=tt, scalar1=0.0, scalar2=2.0, op0=ALU.is_gt, op1=ALU.mult)
        c.drain("vector")
        c.v("scalar_tensor_tensor", out=qa, in0=sg, scalar=-1.0, in1=qa, op0=ALU.add, op1=ALU.mult)
        c.drain("vector")
        c.v("tensor_tensor", out=dst3, in0=qa3, in1=scv.unsqueeze(2).broadcast_to([128, nb, 16]), op=ALU.mult)
        c.drain("vector")

    def transpose_1024(src_off, dst_off, j, banks=(6, 7), src_stride=128):
        for half in range(2):
            bk = banks[half]
            c.bank_wait_te(bk)
            for k in range(4):
                kk = half * 4 + k
                i = c.tp(ps[bk][:, k * 128:(k + 1) * 128], A(src_off + kk * 128, 128), 128)
            pe_v = c.inc(i, "pe")
            c.wait("vector", "pe", pe_v)
            for k in range(4):
                kk = half * 4 + k
                c.v("tensor_copy", out=A(dst_off + kk * 512 + j * 128, 128),
                    in_=ps[bk][:, k * 128:(k + 1) * 128], inc=(k == 3))
            c.bank_free(bk, "ve", c.c["ve"])

    for name in ("ident", "triA", "triB", "m00", "m01", "m10"):
        c.load(t[name][:], t[name + "_in"], sl=4)
    c.load(t["bqkvT_s"][:], t["bqkv_t"], sl=4)
    ld0 = c.load(t["fc1bT_s"][:], t["fc1b_t"], sl=4)
    if c.on("vector"):
        c.eng.memset(t["onesrow"][:], 1.0)
        c.eng.memset(t["onescol"][:], 1.0)
        c.eng.memset(t["eps"][:], 1e-5)
        vc = ar[0:128, A_VV:A_VV + 4 * NH * 65].rearrange("p (cell w) -> p cell w", w=65)
        c.eng.memset(vc[:, :, 64:65], 1.0)
    c.vmark()
    c.waitv("tensor", ld0)
    c.tp(ps[0][:, 0:24], t["bqkvT_s"][:], 24)
    i = c.tp(ps[1][:, 0:32], t["fc1bT_s"][:], 32)
    c.inc(i, "pe")
    c.wait("vector", "pe", c.c["pe"])
    c.v("tensor_copy", out=t["bqkvC"][:], in_=ps[0][:, 0:24])
    c.v("tensor_copy", out=t["fc1bC"][:], in_=ps[1][:, 0:32], inc=True)
    c.bar()

    for (wsrc, wdst, nrows, ncols) in (
        (t["w_gate"], t["wgT"], FF0, D),
        (t["w_up"], t["wuT"], FF0, D),
        (t["w_down"], t["wdT"], D, FF0),
    ):
        for r in range(nrows // 128):
            win = A(G_WIN, ncols)
            c.wait_stores("sync")
            ld = c.load(win, wsrc[r * 128:(r + 1) * 128, :], sl=0)
            c.waitv("vector", ld)
            for hh in range((ncols + 511) // 512):
                n = min(512, ncols - hh * 512)
                fp4_quant(win[:, hh * 512:hh * 512 + n], A(G_WQ + hh * 512, n), n)
            q_done = c.vmark()
            for k in range(ncols // 128):
                bk = k % 4
                c.bank_wait_te(bk)
                c.wait("tensor", "ve", q_done)
                i = c.tp(ps[bk][:, 0:128], A(G_WQ + k * 128, 128), 128)
                pe_v = c.inc(i, "pe")
                c.wait("vector", "pe", pe_v)
                c.v("tensor_copy", out=A(G_STG, 128), in_=ps[bk][:, 0:128], inc=True)
                ev = c.c["ve"]
                c.bank_free(bk, "ve", ev)
                c.wait("gpsimd", "ve", ev)
                s_ = c.store(wdst[k * 128:(k + 1) * 128, r * 128:(r + 1) * 128], A(G_STG, 128))
                c.waitv("vector", s_)
    c.bar()

    for (qsrc, ssrc, wdst, N, ng) in (
        (t["fc1_q"], t["fc1_s"], t["w1d"], FF1, 8),
        (t["fc2_q"], t["fc2_s"], t["w2d"], D, 32),
    ):
        for g in range(ng):
            qin = A(G_WIN, N).bitcast(I32)
            ld = c.load(qin, qsrc[g * 128:(g + 1) * 128, :], sl=0)
            lds = c.load(ar[0:1, G_SROW:G_SROW + N], ssrc[g:g + 1, :], sl=4)
            c.waitv("vector", ld)
            c.v("tensor_scalar", out=A(G_WS, N), in0=qin, scalar1=128.0, scalar2=None,
                op0=ALU.subtract, inc=True)
            qf_done = c.c["ve"]
            for ch in range(N // 512):
                bk = ch % 2
                c.bank_wait_te(bk)
                c.waitv("tensor", lds)
                i = c.mm(ps[bk][:, 0:512], t["onesrow"][0:1, 0:128],
                         ar[0:1, G_SROW + ch * 512:G_SROW + (ch + 1) * 512], start=True, stop=True)
                pe_v = c.inc(i, "pe")
                c.wait("vector", "pe", pe_v)
                c.wait("vector", "ve", qf_done)
                c.v("tensor_tensor", out=A(G_WIN + ch * 512, 512), in0=A(G_WS + ch * 512, 512),
                    in1=ps[bk][:, 0:512], op=ALU.mult, inc=True)
                c.bank_free(bk, "ve", c.c["ve"])
            wrow = c.vmark()
            c.wait("gpsimd", "ve", wrow)
            s_ = c.store(wdst[g * 128:(g + 1) * 128, :], A(G_WIN, N))
            c.waitv("vector", s_)
            c.waitv("sync", s_)
    for (wsrc, wdst) in ((t["Wq"], t["WqT"]), (t["Wf"], t["WfT"])):
        for r in range(8):
            ld = c.load(A(G_WIN, D), wsrc[r * 128:(r + 1) * 128, :], sl=0)
            for k in range(8):
                bk = k % 4
                c.bank_wait_te(bk)
                c.waitv("tensor", ld)
                i = c.tp(ps[bk][:, 0:128], A(G_WIN + k * 128, 128), 128)
                pe_v = c.inc(i, "pe")
                c.wait("vector", "pe", pe_v)
                c.v("tensor_copy", out=A(G_WS + k * 128, 128), in_=ps[bk][:, 0:128], inc=True)
                c.bank_free(bk, "ve", c.c["ve"])
            done = c.vmark()
            c.wait("gpsimd", "ve", done)
            for a in range(8):
                s_ = c.store(wdst[a * 128:(a + 1) * 128, r * 128:(r + 1) * 128],
                             A(G_WS + a * 128, 128))
            c.waitv("vector", s_)
            c.waitv("sync", s_)
    if _LIMIT == "prologue":
        c.bar()
        for name in ("wgT", "wuT", "wdT", "w1d", "w2d"):
            c.store(t["dbg_" + name], t[name], sl=1)
        c.wait_stores("gpsimd")
        c.bar()
        return
    for (wsrc, wdst) in ((t["Wk"], t["WkT"]), (t["Wv"], t["WvT"])):
        ld = c.load(A(G_WIN, D, part=64), wsrc, sl=0)
        for k in range(8):
            bk = k % 4
            c.bank_wait_te(bk)
            c.waitv("tensor", ld)
            i = c.tp(ps[bk][:, 0:64], A(G_WIN + k * 128, 128, part=64), 64)
            pe_v = c.inc(i, "pe")
            c.wait("vector", "pe", pe_v)
            c.v("tensor_copy", out=A(G_WS + k * 64, 64), in_=ps[bk][:, 0:64], inc=True)
            c.bank_free(bk, "ve", c.c["ve"])
        done = c.vmark()
        c.wait("gpsimd", "ve", done)
        for a in range(8):
            s_ = c.store(wdst[a * 128:(a + 1) * 128, :], A(G_WS + a * 64, 64))
        c.waitv("vector", s_)
        c.waitv("sync", s_)
    c.bar()

    blocks = [
        dict(name="pre2", xsrc=t["x_pre2"], xoff=0, ffn=[2, 3], qt=[2, 3], x1row=-512, q=True),
        dict(name="own0", xsrc=t["x_own"], xoff=0, ffn=[0, 1, 2, 3], qt=[0, 1, 2, 3], x1row=0, q=True),
        dict(name="own1", xsrc=t["x_own"], xoff=512, ffn=[0, 1, 2, 3], qt=[0, 1, 2, 3], x1row=512, q=True),
        dict(name="own2", xsrc=t["x_own"], xoff=1024, ffn=[0, 1, 2, 3], qt=[0, 1, 2, 3], x1row=1024, q=True),
        dict(name="own3", xsrc=t["x_own"], xoff=1536, ffn=[0, 1, 2, 3], qt=[0, 1, 2, 3], x1row=1536, q=True),
        dict(name="wrap", xsrc=t["x_wrap"], xoff=0, ffn=[], qt=[], x1row=None, q=False),
    ]

    def kT_ap(hh, j0, nj):
        return A(A_KT + (hh // 2) * 512 + j0, nj, p0=(hh % 2) * 64, part=64)

    def qT_ap(slot, hh, q0, nq):
        return A(A_Q[slot] + (hh // 2) * 512 + q0, nq, p0=(hh % 2) * 64, part=64)

    def vv_ap(jc, hh):
        return A(A_VV + (jc * NH + hh) * 65, 65)

    def prekv(b, qslot):
        ln_marks = []
        for j in range(4):
            xin = A(P_XIN + (j % 2) * 1024, D)
            if j >= 2:
                c.wait("sync", "ve", ln_marks[j - 2])
            ld = c.load(xin, b["xsrc"][b["xoff"] + j * 128:b["xoff"] + (j + 1) * 128, :], sl=2 + (j % 2))
            c.waitv("vector", ld)
            c.waitv("scalar", ld)
            ln_tile(xin, A(P_XN, D), A(P_XSQ, D))
            ln_done = c.vmark()
            ln_marks.append(ln_done)
            c.wait("tensor", "ve", ln_done)
            transpose_1024(P_XN, P_XNT, j)
        xnT_done = c.vmark()
        ldrow = c.load(t["rowbuf"][0:1, 0:3072], t["bqkv_row"], sl=4)
        groups = ([0] if b["q"] else []) + [1, 2]
        pe_marks = []
        for gi, grp in enumerate(groups):
            for kc in range(8):
                idx = gi * 8 + kc
                buf = idx % 2
                wap = A(P_WS + buf * 1024, 1024)
                if idx >= 2:
                    c.wait("sync", "pe", pe_marks[idx - 2])
                ld = c.load(wap, t["W_qkv"][kc * 128:(kc + 1) * 128, grp * 1024:(grp + 1) * 1024], sl=idx % 2)
                c.waitv("tensor", ld)
                c.wait("tensor", "ve", xnT_done)
                if grp in (0, 1):
                    for n in range(8):
                        if kc == 0:
                            c.bank_wait_te(n)
                        i = c.mm(ps[n][:, 0:512], wap[:, n * 128:(n + 1) * 128],
                                 A(P_XNT + kc * 512, 512), start=(kc == 0), stop=(kc == 7))
                else:
                    for j in range(4):
                        for hf in range(2):
                            bk = j * 2 + hf
                            if kc == 0:
                                c.bank_wait_te(bk)
                            i = c.mm(ps[bk][:, 0:512], A(P_XNT + kc * 512 + j * 128, 128),
                                     wap[:, hf * 512:(hf + 1) * 512], start=(kc == 0), stop=False)
                    if kc == 7:
                        c.waitv("tensor", ldrow)
                        for j in range(4):
                            for hf in range(2):
                                bk = j * 2 + hf
                                i = c.mm(ps[bk][:, 0:512], t["onesrow"][0:1, 0:128],
                                         t["rowbuf"][0:1, 2048 + hf * 512:2048 + (hf + 1) * 512],
                                         start=False, stop=True)
                pe_marks.append(c.inc(i, "pe"))
            mm_done = c.c["pe"]
            if grp == 0:
                for n in range(8):
                    c.wait("vector", "pe", mm_done)
                    c.v("tensor_scalar", out=A(A_Q[qslot] + n * 512, 512), in0=ps[n][:, 0:512],
                        scalar1=t["bqkvC"][:, n:n + 1], scalar2=None, op0=ALU.add, inc=True)
                    c.bank_free(n, "ve", c.c["ve"])
            elif grp == 1:
                for n in range(8):
                    c.wait("vector", "pe", mm_done)
                    c.v("tensor_scalar", out=A(A_KT + n * 512, 512), in0=ps[n][:, 0:512],
                        scalar1=t["bqkvC"][:, 8 + n:9 + n], scalar2=None, op0=ALU.add, inc=True)
                    c.bank_free(n, "ve", c.c["ve"])
            else:
                for j in range(4):
                    for hf in range(2):
                        bk = j * 2 + hf
                        c.wait("vector", "pe", mm_done)
                        dst = ar[0:128, A_VV + (j * NH + hf * 8) * 65:A_VV + (j * NH + hf * 8) * 65 + 8 * 65]
                        c.v("tensor_copy", out=dst.rearrange("p (h w) -> p h w", w=65)[:, :, 0:64],
                            in_=ps[bk][:, 0:512].rearrange("p (h w) -> p h w", w=64), inc=True)
                        c.bank_free(bk, "ve", c.c["ve"])
        c.vmark()

    def att_phase(b, qslot, phase):
        qtiles = b["qt"]
        for hh in range(NH):
            exp_done = {}
            for jc in range(4):
                bk = jc
                c.bank_wait_te(bk)
                i = c.mm(ps[bk][:, 0:512], kT_ap(hh, jc * 128, 128), qT_ap(qslot, hh, 0, 512),
                         start=True, stop=True)
                s_done = c.inc(i, "pe")
                c.wait("scalar", "pe", s_done)
                c.a("activation", A(A_P + jc * 512, 512), ps[bk][:, 0:512], AF.Exp,
                    scale=SCALE, inc=True)
                exp_done[jc] = c.c["ac"]
                c.bank_free(bk, "ac", exp_done[jc])
            ob = 4 + (hh % 2)
            c.bank_wait_te(ob)
            for qt in qtiles:
                for jc in range(4):
                    c.wait("tensor", "ac", exp_done[jc])
                    i = c.mm(ps[ob][:, qt * 65:qt * 65 + 65],
                             A(A_P + jc * 512 + qt * 128, 128), vv_ap(jc, hh),
                             start=(jc == 0), stop=(jc == 3))
            o_done = c.inc(i, "pe")
            c.wait("vector", "pe", o_done)
            for qt in qtiles:
                c.v("reciprocal", out=st[:, 6:7], in_=ps[ob][:, qt * 65 + 64:qt * 65 + 65])
                c.drain("vector")
                ao = A(A_AO + qt * 1024 + hh * 64, 64)
                if phase == 1:
                    c.v("tensor_scalar", out=ao, in0=ps[ob][:, qt * 65:qt * 65 + 64],
                        scalar1=st[:, 6:7], scalar2=None, op0=ALU.mult)
                else:
                    c.v("scalar_tensor_tensor", out=ao, in0=ps[ob][:, qt * 65:qt * 65 + 64],
                        scalar=st[:, 6:7], in1=ao, op0=ALU.mult, op1=ALU.add)
                c.drain("vector")
            c.vmark()
            c.bank_free(ob, "ve", c.c["ve"])

    def wout(b):
        qtiles = b["qt"]
        for j in qtiles:
            transpose_1024(A_AO + j * 1024, W_AOT, j)
        aoT_done = c.vmark()
        ldrow = c.load(t["rowbuf"][0:1, 0:1024], t["bout_row"], sl=4)
        pe_marks = []
        for kc in range(8):
            wap = A(W_WS + (kc % 2) * 1024, 1024)
            if kc >= 2:
                c.wait("sync", "pe", pe_marks[kc - 2])
            ld = c.load(wap, t["W_out"][kc * 128:(kc + 1) * 128, :], sl=kc % 2)
            c.waitv("tensor", ld)
            c.wait("tensor", "ve", aoT_done)
            for ji, j in enumerate(qtiles):
                for hf in range(2):
                    bk = ji * 2 + hf
                    if kc == 0:
                        c.bank_wait_te(bk)
                    i = c.mm(ps[bk][:, 0:512], A(W_AOT + kc * 512 + j * 128, 128),
                             wap[:, hf * 512:(hf + 1) * 512], start=(kc == 0), stop=False)
            if kc == 7:
                c.waitv("tensor", ldrow)
                for ji, j in enumerate(qtiles):
                    for hf in range(2):
                        bk = ji * 2 + hf
                        i = c.mm(ps[bk][:, 0:512], t["onesrow"][0:1, 0:128],
                                 t["rowbuf"][0:1, hf * 512:(hf + 1) * 512], start=False, stop=True)
            pe_marks.append(c.inc(i, "pe"))
        s_done = c.c["pe"]
        for ji, j in enumerate(qtiles):
            for hf in range(2):
                bk = ji * 2 + hf
                c.wait("vector", "pe", s_done)
                c.v("tensor_copy", out=A(A_S + j * 1024 + hf * 512, 512), in_=ps[bk][:, 0:512], inc=True)
                c.bank_free(bk, "ve", c.c["ve"])
        c.vmark()

    def ffnc(b):
        tiles = b["ffn"]
        for hv_i in range(0, len(tiles), 2):
            hv = tiles[hv_i:hv_i + 2]
            for jj, j in enumerate(hv):
                xin = A(F_XIN + jj * 1024, D)
                ld = c.load(xin, b["xsrc"][b["xoff"] + j * 128:b["xoff"] + (j + 1) * 128, :], sl=2 + jj)
                c.waitv("vector", ld)
                c.waitv("scalar", ld)
                ln_tile(xin, A(F_XN, D), A(F_SCR, D))
                for hf in range(2):
                    fp4_quant(A(F_XN + hf * 512, 512), A(F_HQ, 512), 512)
                    qdone = c.vmark()
                    bk = 6
                    c.bank_wait_te(bk)
                    c.wait("tensor", "ve", qdone)
                    for k in range(4):
                        i = c.tp(ps[bk][:, k * 128:(k + 1) * 128], A(F_HQ + k * 128, 128), 128)
                    pe_v = c.inc(i, "pe")
                    c.wait("vector", "pe", pe_v)
                    for k in range(4):
                        kk = hf * 4 + k
                        c.v("tensor_copy", out=A(F_XQT + (jj * 8 + kk) * 128, 128),
                            in_=ps[bk][:, k * 128:(k + 1) * 128], inc=(k == 3))
                    c.bank_free(bk, "ve", c.c["ve"])
            xq_done = c.vmark()
            if _DBG3 and b["name"] == "own0" and hv_i == 0:
                c.wait("gpsimd", "ve", xq_done)
                dx = c.store(t["dbg_xqT"], A(F_XQT, 2048), sl=1)
                c.waitv("sync", dx)
            pe_marks = []
            pe_half0 = c.c["pe"]
            idx = 0
            for nchi, n in enumerate(N0CH):
                ncol0 = nchi * 512
                for kc in range(8):
                    buf = idx % 2
                    wg = A(F_WS + buf * 1024, n)
                    wu = A(F_WS + buf * 1024 + 512, n)
                    if idx >= 2:
                        c.wait("sync", "pe", pe_marks[idx - 2])
                    else:
                        c.wait("sync", "pe", pe_half0)
                    ldg = c.load(wg, t["wgT"][kc * 128:(kc + 1) * 128, ncol0:ncol0 + n], sl=idx % 2)
                    ldu = c.load(wu, t["wuT"][kc * 128:(kc + 1) * 128, ncol0:ncol0 + n], sl=idx % 2)
                    c.waitv("tensor", ldu)
                    c.wait("tensor", "ve", xq_done)
                    for jj in range(len(hv)):
                        bg, by = jj * 2, jj * 2 + 1
                        if kc == 0:
                            c.bank_wait_te(bg)
                            c.bank_wait_te(by)
                        c.mm(ps[bg][:, 0:n], A(F_XQT + (jj * 8 + kc) * 128, 128), wg,
                             start=(kc == 0), stop=(kc == 7))
                        i = c.mm(ps[by][:, 0:n], A(F_XQT + (jj * 8 + kc) * 128, 128), wu,
                                 start=(kc == 0), stop=(kc == 7))
                    pe_marks.append(c.inc(i, "pe"))
                    idx += 1
                gu_done = c.c["pe"]
                for jj in range(len(hv)):
                    bg, by = jj * 2, jj * 2 + 1
                    c.wait("scalar", "pe", gu_done)
                    c.wait("scalar", "ve", c.c["ve"])
                    c.a("activation", A(F_H, n), ps[bg][:, 0:n], AF.Silu, inc=True)
                    hs_done = c.c["ac"]
                    c.wait("vector", "ac", hs_done)
                    c.v("tensor_tensor", out=A(F_H, n), in0=A(F_H, n), in1=ps[by][:, 0:n], op=ALU.mult)
                    c.drain("vector")
                    c.vmark()
                    fp4_quant(A(F_H, n), A(F_HQ, n), n)
                    hq_done = c.vmark()
                    c.bank_free(bg, "ve", hq_done)
                    c.bank_free(by, "ve", hq_done)
                    bk = 6 + jj
                    c.bank_wait_te(bk)
                    c.wait("tensor", "ve", hq_done)
                    nk = n // 128
                    for k in range(nk):
                        i = c.tp(ps[bk][:, k * 128:(k + 1) * 128], A(F_HQ + k * 128, 128), 128)
                    pe_v = c.inc(i, "pe")
                    c.wait("vector", "pe", pe_v)
                    for k in range(nk):
                        fc = nchi * 4 + k
                        c.v("tensor_copy", out=A(F_HQT + fc * 256 + jj * 128, 128),
                            in_=ps[bk][:, k * 128:(k + 1) * 128], inc=(k == nk - 1))
                    c.bank_free(bk, "ve", c.c["ve"])
            hqT_done = c.vmark()
            if _DBG3 and b["name"] == "own0" and hv_i == 0:
                c.wait("gpsimd", "ve", hqT_done)
                dh = c.store(t["dbg_hqT"], A(F_HQT, 5632), sl=1)
                c.waitv("sync", dh)
            pe_marks2 = []
            for fc in range(22):
                buf = fc % 2
                wd = A(F_WS2 + buf * 1024, 1024)
                if fc >= 2:
                    c.wait("sync", "pe", pe_marks2[fc - 2])
                else:
                    c.wait("sync", "pe", pe_marks[-1])
                ld = c.load(wd, t["wdT"][fc * 128:(fc + 1) * 128, :], sl=fc % 2)
                c.waitv("tensor", ld)
                c.wait("tensor", "ve", hqT_done)
                for jj in range(len(hv)):
                    for hf in range(2):
                        bk = jj * 2 + hf
                        if fc == 0:
                            c.bank_wait_te(bk)
                        i = c.mm(ps[bk][:, 0:512], A(F_HQT + fc * 256 + jj * 128, 128),
                                 wd[:, hf * 512:(hf + 1) * 512], start=(fc == 0), stop=(fc == 21))
                pe_marks2.append(c.inc(i, "pe"))
            z_done = c.c["pe"]
            for jj, j in enumerate(hv):
                xin = A(F_XIN + jj * 1024, D)
                for hf in range(2):
                    bk = jj * 2 + hf
                    c.wait("scalar", "pe", z_done)
                    c.a("activation", A(F_G + hf * 512, 512), ps[bk][:, 0:512], AF.Silu, inc=True)
                g_done = c.c["ac"]
                if _DBG2:
                    row = 256 + b["x1row"] + j * 128
                    c.wait("gpsimd", "ac", g_done)
                    sg1 = c.store(t["dbg_g"][row:row + 128, :], A(F_G, D), sl=1)
                    sg2 = c.store(t["dbg_s"][row:row + 128, :], A(A_S + j * 1024, D), sl=1)
                    c.waitv("vector", sg2)
                    c.waitv("scalar", sg2)
                c.wait("vector", "ac", g_done)
                c.v("tensor_tensor", out=A(F_G, D), in0=A(F_G, D), in1=A(A_S + j * 1024, D), op=ALU.mult)
                c.drain("vector")
                c.v("tensor_add", out=A(F_G, D), in0=A(F_G, D), in1=xin, inc=True)
                x1_done = c.c["ve"]
                c.bank_free(jj * 2, "ve", x1_done)
                c.bank_free(jj * 2 + 1, "ve", x1_done)
                c.drain("vector")
                c.vmark()
                c.wait("gpsimd", "ve", c.c["ve"])
                row = 256 + b["x1row"] + j * 128
                s_ = c.store(t["x1s"][row:row + 128, :], A(F_G, D))
                c.waitv("vector", s_)
                c.waitv("scalar", s_)
                c.waitv("sync", s_)

    prekv(blocks[0], 0)
    c.bar()
    att_phase(blocks[0], 0, 1)
    c.bar()
    prekv(blocks[1], 1)
    c.bar()
    att_phase(blocks[0], 0, 2)
    wout(blocks[0])
    c.bar()
    ffnc(blocks[0])
    c.bar()
    qsl = {1: 1, 2: 0, 3: 1, 4: 0}
    for bi in range(1, 5):
        att_phase(blocks[bi], qsl[bi], 1)
        c.bar()
        prekv(blocks[bi + 1], qsl[bi + 1] if bi + 1 < 5 else 0)
        c.bar()
        att_phase(blocks[bi], qsl[bi], 2)
        wout(blocks[bi])
        c.bar()
        ffnc(blocks[bi])
        c.bar()

    if _LIMIT == "L0":
        c.bar()
        c.store(t["dbg_x1s"], t["x1s"], sl=1)
        c.wait_stores("gpsimd")
        c.bar()
        return

    def load_wkv():
        ld1 = c.load(A(L_WKV, 512).rearrange("p (a b) -> p a b", b=64),
                     t["WkT"].rearrange("(a p) b -> p a b", p=128), sl=5)
        ld2 = c.load(A(L_WKV + 512, 512).rearrange("p (a b) -> p a b", b=64),
                     t["WvT"].rearrange("(a p) b -> p a b", p=128), sl=5)
        return ld2

    ldg = c.load(t["rowbuf"][0:1, 0:1024], t["gam_row"], sl=4)
    ldb = c.load(t["rowbuf"][0:1, 1024:2048], t["bet_row"], sl=4)
    for gb in range(2):
        for ch in range(2):
            bk = gb * 2 + ch
            c.bank_wait_te(bk)
            c.waitv("tensor", ldb)
            i = c.mm(ps[bk][:, 0:512], t["onesrow"][0:1, 0:128],
                     t["rowbuf"][0:1, gb * 1024 + ch * 512:gb * 1024 + (ch + 1) * 512],
                     start=True, stop=True)
            pe_v = c.inc(i, "pe")
            c.wait("vector", "pe", pe_v)
            c.v("tensor_copy", out=A(L_GBC + gb * 1024 + ch * 512, 512), in_=ps[bk][:, 0:512], inc=True)
            c.bank_free(bk, "ve", c.c["ve"])
    if c.on("vector"):
        vc = ar[0:128, L_VV:L_VV + 20 * 65].rearrange("p (cell w) -> p cell w", w=65)
        c.eng.memset(vc[:, :, 64:65], 1.0)
    c.vmark()
    c.bar()

    def l1_kv_tiles(ch0, ntile, xn1T_off):
        wkv_ld = load_wkv()
        c.waitv("tensor", wkv_ld)
        c.bank_wait_te(4)
        for kc in range(8):
            i = c.mm(ps[4][0:64, 0:ntile * 128], A(L_WKV + kc * 64, 64),
                     A(xn1T_off + kc * 512, ntile * 128), start=(kc == 0), stop=(kc == 7))
        c.inc(i, "pe")
        for kc in range(8):
            i = c.mm(ps[4][64:128, 0:ntile * 128], A(L_WKV + kc * 64, 64),
                     A(xn1T_off + kc * 512, ntile * 128), start=(kc == 0), stop=(kc == 7),
                     tile_position=(0, 64))
        c.inc(i, "pe")
        c.bank_wait_te(5)
        for kc in range(8):
            for j in range(ntile):
                i = c.mm(ps[5][:, j * 64:(j + 1) * 64], A(xn1T_off + kc * 512 + j * 128, 128),
                         A(L_WKV + 512 + kc * 64, 64), start=(kc == 0 and j == 0), stop=(kc == 7))
        c.inc(i, "pe")
        kv_done = c.c["pe"]
        c.wait("vector", "pe", kv_done)
        c.v("tensor_copy", out=A(L_KT + ch0 * 128, ntile * 128), in_=ps[4][:, 0:ntile * 128])
        for j in range(ntile):
            c.v("tensor_copy", out=A(L_VV + (ch0 + j) * 65, 64), in_=ps[5][:, j * 64:(j + 1) * 64],
                inc=(j == ntile - 1))
        c.bank_free(4, "ve", c.c["ve"])
        c.bank_free(5, "ve", c.c["ve"])
        c.vmark()

    for j in range(2):
        xin = A(L_XIN + j * 1024, D)
        ld = c.load(xin, t["x1s"][j * 128:(j + 1) * 128, :], sl=2 + j)
        c.waitv("vector", ld)
        c.waitv("scalar", ld)
        ln_tile(xin, A(L_LOC, D), A(L_LOC + 1024, D))
        ln_done = c.vmark()
        c.wait("tensor", "ve", ln_done)
        transpose_1024(L_LOC, L_XN1T, j)
    c.vmark()
    l1_kv_tiles(0, 2, L_XN1T)
    c.bar()

    def l1pre(g):
        ln_marks = []
        for j in range(4):
            xin = A(L_XIN + (j % 2) * 1024, D)
            row = 256 + (g * 4 + j) * 128
            if j >= 2:
                c.wait("sync", "ve", ln_marks[j - 2])
            ld = c.load(xin, t["x1s"][row:row + 128, :], sl=2 + (j % 2))
            c.waitv("vector", ld)
            c.waitv("scalar", ld)
            ln_tile(xin, A(L_XN1 + j * 1024, D), A(L_LOC, D))
            ln_done = c.vmark()
            ln_marks.append(ln_done)
            c.wait("tensor", "ve", ln_done)
            transpose_1024(L_XN1 + j * 1024, L_XN1T, j)
        xn1T_done = c.vmark()
        pe_marks = []
        for kc in range(8):
            buf = kc % 2
            wap = A(L_WS + buf * 1024, 1024)
            if kc >= 2:
                c.wait("sync", "pe", pe_marks[kc - 2])
            ld = c.load(wap, t["WqT"][kc * 128:(kc + 1) * 128, :], sl=kc % 2)
            c.waitv("tensor", ld)
            c.wait("tensor", "ve", xn1T_done)
            for n in range(8):
                if kc == 0:
                    c.bank_wait_te(n)
                i = c.mm(ps[n][:, 0:512], wap[:, n * 128:(n + 1) * 128],
                         A(L_XN1T + kc * 512, 512), start=(kc == 0), stop=(kc == 7))
            pe_marks.append(c.inc(i, "pe"))
        q_done = c.c["pe"]
        for n in range(8):
            c.wait("vector", "pe", q_done)
            c.v("tensor_copy", out=A(L_QT + n * 512, 512), in_=ps[n][:, 0:512], inc=True)
            c.bank_free(n, "ve", c.c["ve"])
        c.vmark()
        l1_kv_tiles(2 + g * 4, 4, L_XN1T)

    def l1att(g):
        if _DBG4 and g == 0:
            c.wait("gpsimd", "ve", c.c["ve"])
            d1 = c.store(t["dbg_kT"], A(L_KT, 2560), sl=1)
            d2 = c.store(t["dbg_vv"], A(L_VV, 1300), sl=1)
            d3 = c.store(t["dbg_qT"], A(L_QT, 4096), sl=1)
            c.waitv("sync", d3)
        for j in range(4):
            tt = g * 4 + j
            if tt == 0:
                msks = [t["m00"], t["m01"], None]
            elif tt == 1:
                msks = [t["m10"], None, None]
            else:
                msks = [t["triA"], None, None]
            for hh in range(NH):
                sb = hh % 2
                c.bank_wait_te(sb)
                for jc in range(3):
                    i = c.mm(ps[sb][:, jc * 128:(jc + 1) * 128],
                             A(L_KT + (tt + jc) * 128, 128, p0=(hh % 2) * 64, part=64),
                             A(L_QT + (hh // 2) * 512 + j * 128, 128, p0=(hh % 2) * 64, part=64),
                             start=True, stop=True)
                s_done = c.inc(i, "pe")
                c.wait("scalar", "pe", s_done)
                pb = L_P + (hh % 2) * 384
                c.a("activation", A(pb, 384), ps[sb][:, 0:384], AF.Exp, scale=SCALE, inc=True)
                exp_done = c.c["ac"]
                c.bank_free(sb, "ac", exp_done)
                c.wait("vector", "ac", exp_done)
                c.v("tensor_tensor", out=A(pb, 128), in0=A(pb, 128), in1=msks[0][:], op=ALU.mult)
                if msks[1] is not None:
                    c.v("tensor_tensor", out=A(pb + 128, 128), in0=A(pb + 128, 128), in1=msks[1][:], op=ALU.mult)
                c.v("tensor_tensor", out=A(pb + 256, 128), in0=A(pb + 256, 128), in1=t["triB"][:], op=ALU.mult)
                c.drain("vector")
                p_done = c.vmark()
                ob = 4 + (hh % 2)
                c.bank_wait_te(ob)
                c.wait("tensor", "ve", p_done)
                for jc in range(3):
                    i = c.mm(ps[ob][:, 0:65], A(pb + jc * 128, 128), A(L_VV + (tt + jc) * 65, 65),
                             start=(jc == 0), stop=(jc == 2))
                o_done = c.inc(i, "pe")
                c.wait("vector", "pe", o_done)
                c.v("reciprocal", out=st[:, 6:7], in_=ps[ob][:, 64:65])
                c.drain("vector")
                c.v("tensor_scalar", out=A(L_AOT2 + hh * 64, 64), in0=ps[ob][:, 0:64],
                    scalar1=st[:, 6:7], scalar2=None, op0=ALU.mult, inc=True)
                c.drain("vector")
                c.bank_free(ob, "ve", c.c["ve"])
            ao_done = c.vmark()
            c.wait("tensor", "ve", ao_done)
            transpose_1024(L_AOT2, L_LOC, j)
        aoT_done = c.vmark()
        ldrow = c.load(t["rowbuf"][0:1, 0:1024], t["bf_row"], sl=4)
        pe_marks = []
        for kc in range(8):
            buf = kc % 2
            wap = A(L_WS + buf * 1024, 1024)
            if kc >= 2:
                c.wait("sync", "pe", pe_marks[kc - 2])
            ld = c.load(wap, t["WfT"][kc * 128:(kc + 1) * 128, :], sl=kc % 2)
            c.waitv("tensor", ld)
            c.wait("tensor", "ve", aoT_done)
            for j in range(4):
                for hf in range(2):
                    bk = j * 2 + hf
                    if kc == 0:
                        c.bank_wait_te(bk)
                    i = c.mm(ps[bk][:, 0:512], A(L_LOC + kc * 512 + j * 128, 128),
                             wap[:, hf * 512:(hf + 1) * 512], start=(kc == 0), stop=False)
            if kc == 7:
                c.waitv("tensor", ldrow)
                for j in range(4):
                    for hf in range(2):
                        bk = j * 2 + hf
                        i = c.mm(ps[bk][:, 0:512], t["onesrow"][0:1, 0:128],
                                 t["rowbuf"][0:1, hf * 512:(hf + 1) * 512], start=False, stop=True)
            pe_marks.append(c.inc(i, "pe"))
        s_done = c.c["pe"]
        for j in range(4):
            for hf in range(2):
                bk = j * 2 + hf
                c.wait("vector", "pe", s_done)
                c.v("tensor_copy", out=A(L_S1 + j * 1024 + hf * 512, 512), in_=ps[bk][:, 0:512], inc=True)
                c.bank_free(bk, "ve", c.c["ve"])
        c.vmark()

    def l1ffn(g):
        for half in range(2):
            hv = [half * 2, half * 2 + 1]
            for jj, j in enumerate(hv):
                ln_tile(A(L_XN1 + j * 1024, D), A(L_LOC + 1024, D), A(L_LOC, D))
                c.v("tensor_tensor", out=A(L_LOC + 1024, D), in0=A(L_LOC + 1024, D),
                    in1=A(L_GBC, D), op=ALU.mult)
                c.drain("vector")
                c.v("tensor_add", out=A(L_LOC + 1024, D), in0=A(L_LOC + 1024, D), in1=A(L_GBC + 1024, D))
                c.drain("vector")
                h2_done = c.vmark()
                c.wait("tensor", "ve", h2_done)
                for hlf in range(2):
                    bk = 6 + hlf
                    c.bank_wait_te(bk)
                    for k in range(4):
                        kk = hlf * 4 + k
                        i = c.tp(ps[bk][:, k * 128:(k + 1) * 128], A(L_LOC + 1024 + kk * 128, 128), 128)
                    pe_v = c.inc(i, "pe")
                    c.wait("vector", "pe", pe_v)
                    for k in range(4):
                        kk = hlf * 4 + k
                        c.v("tensor_copy", out=A(L_H2T + kk * 256 + jj * 128, 128),
                            in_=ps[bk][:, k * 128:(k + 1) * 128], inc=(k == 3))
                    c.bank_free(bk, "ve", c.c["ve"])
            h2T_done = c.vmark()
            pe_marks = []
            pe_half0 = c.c["pe"]
            idx = 0
            for ng in range(4):
                for kc in range(8):
                    buf = idx % 2
                    wap = A(L_WS + buf * 1024, 1024)
                    if idx >= 2:
                        c.wait("sync", "pe", pe_marks[idx - 2])
                    else:
                        c.wait("sync", "pe", pe_half0)
                    ld = c.load(wap, t["w1d"][kc * 128:(kc + 1) * 128, ng * 1024:(ng + 1) * 1024], sl=idx % 2)
                    c.waitv("tensor", ld)
                    c.wait("tensor", "ve", h2T_done)
                    for n in range(8):
                        if kc == 0:
                            c.bank_wait_te(n)
                        i = c.mm(ps[n][:, 0:256], wap[:, n * 128:(n + 1) * 128],
                                 A(L_H2T + kc * 256, 256), start=(kc == 0), stop=(kc == 7))
                    pe_marks.append(c.inc(i, "pe"))
                    idx += 1
                u_done = c.c["pe"]
                for n in range(8):
                    ngl = ng * 8 + n
                    c.wait("scalar", "pe", u_done)
                    c.a("activation", A(L_YT + ngl * 256, 256), ps[n][:, 0:256], AF.Silu,
                        bias=t["fc1bC"][:, ngl:ngl + 1], scale=1.0, inc=True)
                    c.bank_free(n, "ac", c.c["ac"])
            yT_done = c.amark()
            ldrow = c.load(t["rowbuf"][0:1, 0:1024], t["fc2b_row"], sl=4)
            pe_marks2 = []
            for kc in range(32):
                buf = kc % 2
                wap = A(L_WS + buf * 1024, 1024)
                if kc >= 2:
                    c.wait("sync", "pe", pe_marks2[kc - 2])
                else:
                    c.wait("sync", "pe", pe_marks[-1])
                ld = c.load(wap, t["w2d"][kc * 128:(kc + 1) * 128, :], sl=kc % 2)
                c.waitv("tensor", ld)
                c.wait("tensor", "ac", yT_done)
                for jj in range(2):
                    for hf in range(2):
                        bk = jj * 2 + hf
                        if kc == 0:
                            c.bank_wait_te(bk)
                        i = c.mm(ps[bk][:, 0:512], A(L_YT + kc * 256 + jj * 128, 128),
                                 wap[:, hf * 512:(hf + 1) * 512], start=(kc == 0), stop=False)
                if kc == 31:
                    c.waitv("tensor", ldrow)
                    for jj in range(2):
                        for hf in range(2):
                            bk = jj * 2 + hf
                            i = c.mm(ps[bk][:, 0:512], t["onesrow"][0:1, 0:128],
                                     t["rowbuf"][0:1, hf * 512:(hf + 1) * 512], start=False, stop=True)
                pe_marks2.append(c.inc(i, "pe"))
            z_done = c.c["pe"]
            for jj, j in enumerate(hv):
                tt = g * 4 + j
                xin = A(L_XIN + jj * 1024, D)
                row = 256 + tt * 128
                ld = c.load(xin, t["x1s"][row:row + 128, :], sl=2 + jj)
                for hf in range(2):
                    bk = jj * 2 + hf
                    c.wait("vector", "pe", z_done)
                    c.v("scalar_tensor_tensor", out=A(L_LOC + 2048 + hf * 512, 512),
                        in0=ps[bk][:, 0:512], scalar=0.5, in1=A(L_XN1 + j * 1024 + hf * 512, 512),
                        op0=ALU.mult, op1=ALU.add)
                pre_done = c.vmark()
                c.wait("scalar", "ve", pre_done)
                c.a("activation", A(L_LOC + 1024, D), A(L_LOC + 2048, D), AF.Silu, inc=True)
                g2_done = c.c["ac"]
                if _DBG4:
                    c.wait("gpsimd", "ac", g2_done)
                    d1 = c.store(t["dbg_g2"][tt * 128:(tt + 1) * 128, :], A(L_LOC + 1024, D), sl=1)
                    d2 = c.store(t["dbg_s1"][tt * 128:(tt + 1) * 128, :], A(L_S1 + j * 1024, D), sl=1)
                    c.waitv("vector", d2)
                    c.waitv("scalar", d2)
                c.wait("vector", "ac", g2_done)
                c.v("tensor_tensor", out=A(L_LOC + 1024, D), in0=A(L_LOC + 1024, D),
                    in1=A(L_S1 + j * 1024, D), op=ALU.mult)
                c.drain("vector")
                c.waitv("vector", ld)
                c.v("tensor_add", out=A(L_LOC + 1024, D), in0=A(L_LOC + 1024, D), in1=xin, inc=True)
                x2_done = c.c["ve"]
                c.bank_free(jj * 2, "ve", x2_done)
                c.bank_free(jj * 2 + 1, "ve", x2_done)
                c.drain("vector")
                c.vmark()
                c.wait("gpsimd", "ve", c.c["ve"])
                s_ = c.store(t["out"][tt * 128:(tt + 1) * 128, :], A(L_LOC + 1024, D))
                c.waitv("vector", s_)
                c.waitv("scalar", s_)
                c.waitv("sync", s_)

    for g in range(4):
        l1pre(g)
        c.bar()
        l1att(g)
        c.bar()
        l1ffn(g)
        c.bar()

    c.wait_stores("gpsimd")
    c.bar()


def _build():
    nc = bass.Bass("TRN2", target_bir_lowering=False)
    t = {}
    def din(name, shape, dt=F32):
        t[name] = nc.dram_tensor(name, shape, dt, kind="ExternalInput").ap()
    din("x_own", [2048, D])
    din("x_pre2", [512, D])
    din("x_wrap", [512, D])
    din("W_qkv", [D, 3072])
    din("W_out", [D, D])
    din("w_gate", [FF0, D])
    din("w_up", [FF0, D])
    din("w_down", [D, FF0])
    din("Wq", [D, D])
    din("Wk", [HD, D])
    din("Wv", [HD, D])
    din("Wf", [D, D])
    din("fc1_q", [D, FF1], I32)
    din("fc1_s", [8, FF1])
    din("fc2_q", [FF1, D], I32)
    din("fc2_s", [32, D])
    din("bqkv_t", [24, 128])
    din("bqkv_row", [1, 3072])
    din("bout_row", [1, D])
    din("bf_row", [1, D])
    din("fc1b_t", [32, 128])
    din("fc2b_row", [1, D])
    din("gam_row", [1, D])
    din("bet_row", [1, D])
    din("ident_in", [128, 128])
    din("triA_in", [128, 128])
    din("triB_in", [128, 128])
    din("m00_in", [128, 128])
    din("m01_in", [128, 128])
    din("m10_in", [128, 128])
    t["out"] = nc.dram_tensor("out", [2048, D], F32, kind="ExternalOutput").ap()
    if _LIMIT == "prologue":
        for name, shape in (("wgT", [D, FF0]), ("wuT", [D, FF0]), ("wdT", [FF0, D]),
                            ("w1d", [D, FF1]), ("w2d", [FF1, D])):
            t["dbg_" + name] = nc.dram_tensor("dbg_" + name, shape, F32, kind="ExternalOutput").ap()
    if _LIMIT == "L0":
        t["dbg_x1s"] = nc.dram_tensor("dbg_x1s", [2304, D], F32, kind="ExternalOutput").ap()
    if _DBG2:
        t["dbg_s"] = nc.dram_tensor("dbg_s", [2304, D], F32, kind="ExternalOutput").ap()
        t["dbg_g"] = nc.dram_tensor("dbg_g", [2304, D], F32, kind="ExternalOutput").ap()
    if _DBG3:
        t["dbg_xqT"] = nc.dram_tensor("dbg_xqT", [128, 2048], F32, kind="ExternalOutput").ap()
        t["dbg_hqT"] = nc.dram_tensor("dbg_hqT", [128, 5632], F32, kind="ExternalOutput").ap()
    if _DBG4:
        t["dbg_s1"] = nc.dram_tensor("dbg_s1", [2048, D], F32, kind="ExternalOutput").ap()
        t["dbg_g2"] = nc.dram_tensor("dbg_g2", [2048, D], F32, kind="ExternalOutput").ap()
        t["dbg_kT"] = nc.dram_tensor("dbg_kT", [128, 2560], F32, kind="ExternalOutput").ap()
        t["dbg_vv"] = nc.dram_tensor("dbg_vv", [128, 1300], F32, kind="ExternalOutput").ap()
        t["dbg_qT"] = nc.dram_tensor("dbg_qT", [128, 4096], F32, kind="ExternalOutput").ap()
    def dscr(name, shape, dt=F32):
        t[name] = nc.dram_tensor(name, shape, dt, kind="Internal").ap()
    dscr("x1s", [2304, D])
    dscr("wgT", [D, FF0])
    dscr("wuT", [D, FF0])
    dscr("wdT", [FF0, D])
    dscr("w1d", [D, FF1])
    dscr("w2d", [FF1, D])
    dscr("WqT", [D, D])
    dscr("WfT", [D, D])
    dscr("WkT", [D, HD])
    dscr("WvT", [D, HD])

    ctx = contextlib.ExitStack()
    with ctx:
        t["arena"] = ctx.enter_context(nc.sbuf_tensor("arena", [128, ARENA], F32)).ap()
        for name, shape in (
            ("ident", [128, 128]), ("triA", [128, 128]), ("triB", [128, 128]),
            ("m00", [128, 128]), ("m01", [128, 128]), ("m10", [128, 128]),
            ("onesrow", [1, 128]), ("onescol", [128, 1]), ("eps", [128, 1]),
            ("bqkvT_s", [24, 128]), ("fc1bT_s", [32, 128]),
            ("bqkvC", [128, 24]), ("fc1bC", [128, 32]),
            ("stats", [128, 16]), ("qsc", [128, 192]), ("rowbuf", [1, 3072]),
        ):
            t[name] = ctx.enter_context(nc.sbuf_tensor(name, shape, F32))
        t["psum"] = [ctx.enter_context(nc.psum_tensor(f"p{i}", [128, 512], F32)) for i in range(8)]
        sem = {}
        semnames = ("pe", "ve", "ac") + tuple("dl%d" % i for i in range(6)) + ("ds0", "ds1") \
            + tuple("b_" + e for e in ENGINES)
        for name in semnames:
            sem[name] = ctx.enter_context(nc.semaphore(name))
        block = ctx.enter_context(nc.Block())

        def mk(en):
            def f(eng):
                _emit(C(nc, en, eng, sem, t))
            return f
        block.sync(mk("sync"))
        block.tensor(mk("tensor"))
        block.vector(mk("vector"))
        block.scalar(mk("scalar"))
        block.gpsimd(mk("gpsimd"))
    return nc


def _host_inputs(inputs):
    x = np.ascontiguousarray(np.asarray(inputs["L0_attn_x"], dtype=np.float32))
    i32 = lambda a: np.ascontiguousarray(np.asarray(a, dtype=np.int32))
    f32 = lambda a: np.ascontiguousarray(np.asarray(a, dtype=np.float32))
    kk = np.arange(128)[:, None]
    qq = np.arange(128)[None, :]
    triA = (qq <= kk).astype(np.float32)
    triB = (qq >= kk).astype(np.float32)
    full = np.ones((128, 128), np.float32)
    zero = np.zeros((128, 128), np.float32)
    common = dict(
        W_qkv=f32(inputs["L0_attn_W_qkv"]),
        W_out=f32(inputs["L0_attn_W_out"]),
        w_gate=f32(inputs["L0_ffn_w_gate"]),
        w_up=f32(inputs["L0_ffn_w_up"]),
        w_down=f32(inputs["L0_ffn_w_down"]),
        Wq=f32(inputs["L1_attn_Wq"]),
        Wk=f32(inputs["L1_attn_Wk"]),
        Wv=f32(inputs["L1_attn_Wv"]),
        Wf=f32(inputs["L1_attn_Wf"]),
        fc1_q=i32(inputs["L1_ffn_fc1_q"]),
        fc1_s=f32(inputs["L1_ffn_fc1_scales"]),
        fc2_q=i32(inputs["L1_ffn_fc2_q"]),
        fc2_s=f32(inputs["L1_ffn_fc2_scales"]),
        bqkv_t=f32(inputs["L0_attn_b_qkv"]).reshape(24, 128),
        bqkv_row=f32(inputs["L0_attn_b_qkv"]).reshape(1, 3072),
        bout_row=f32(inputs["L0_attn_b_out"]).reshape(1, D),
        bf_row=f32(inputs["L1_attn_bf"]).reshape(1, D),
        fc1b_t=f32(inputs["L1_ffn_fc1_bias"]).reshape(32, 128),
        fc2b_row=f32(inputs["L1_ffn_fc2_bias"]).reshape(1, D),
        gam_row=f32(inputs["L1_ffn_ln_gamma"]).reshape(1, D),
        bet_row=f32(inputs["L1_ffn_ln_beta"]).reshape(1, D),
        ident_in=np.eye(128, dtype=np.float32),
        triA_in=triA, triB_in=triB,
    )
    in_maps = []
    for cid in range(8):
        b, half = cid // 2, cid % 2
        t0 = half * 2048
        m = dict(common)
        m["x_own"] = np.ascontiguousarray(x[b, t0:t0 + 2048])
        m["x_pre2"] = np.ascontiguousarray(x[b, 1536:2048])
        m["x_wrap"] = np.ascontiguousarray(x[b, 2048:2560] if half == 0 else x[b, 0:512])
        if half == 0:
            m["m00_in"], m["m01_in"], m["m10_in"] = zero, zero, zero
        else:
            m["m00_in"], m["m01_in"], m["m10_in"] = triA, full, triA
        in_maps.append(m)
    return x, in_maps


def kernel(**inputs):
    x, in_maps = _host_inputs(inputs)
    nc = _build()
    res = run_bass_kernel_spmd(nc, in_maps, core_ids=list(range(8)))
    out = np.empty((4, 4096, D), dtype=np.float32)
    for cid in range(8):
        b, half = cid // 2, cid % 2
        out[b, half * 2048:(half + 1) * 2048] = res.results[cid]["out"]
    return out


# version 02
# speedup vs baseline: 1.0141x; 1.0141x over previous
import contextlib
import numpy as np

import concourse.bass as bass
import concourse.mybir as mybir
from concourse.bass_utils import run_bass_kernel_spmd

F32 = mybir.dt.float32
I32 = mybir.dt.int32
AF = mybir.ActivationFunctionType
ALU = mybir.AluOpType
AX = mybir.AxisListType.X

D = 1024
NH, HD = 16, 64
FF0 = 2816
FF1 = 4096
SCALE = 0.125
MID = [0.25, 0.75, 1.25, 1.75, 2.5, 3.5, 5.0]
DG = [0.5, 0.5, 0.5, 0.5, 1.0, 1.0, 2.0]
N0CH = [512, 512, 512, 512, 512, 256]
ENGINES = ("sync", "tensor", "vector", "scalar", "gpsimd")

_DBG = False
_DBG2 = False
_DBG3 = False
_DBG4 = False
_LIMIT = None

ARENA = 43840
A_KT = 0
A_VV = 4096
A_Q = (8256, 12352)
A_AO = 16448
A_S = 20544
A_P = 24640
P_XNT = 26688
P_XSQ = 30784
P_XN = 31808
P_XIN = 32832
P_WS = 34880
W_AOT = 26688
W_WS = 30784
F_HQT = 26688
F_XQT = 32320
F_XIN = 34368
F_WS = 36416
F_WS2 = 36416
F_XN = 38464
F_SCR = 39488
F_H = 42048
F_HQ = 42560
F_G = F_XN
G_WIN = 26688
G_WQ = 30784
G_STG = 33600
G_WS = 34624
G_SROW = 38720
L_XN1 = 0
L_XN1T = 4096
L_QT = 8192
L_KT = 12288
L_VV = 14848
L_S1 = 16192
L_H2T = 20288
L_P = 22336
L_YT = 24384
L_GBC = 32576
L_LOC = 34624
L_WS = 38720
L_WKV = 40768
L_AOT2 = 40768
L_XIN = 41792

assert L_XIN + 2048 <= ARENA


class C:

    def __init__(self, nc, en, eng, sem, t):
        self.nc, self.en, self.eng, self.sem, self.t = nc, en, eng, sem, t
        self.c = {k: 0 for k in sem}
        self.nbar = 0
        self.bankw = [None] * 8

    def _do(self, en, f):
        if self.en == en:
            return f(self.eng)
        return None

    def on(self, en):
        return self.en == en

    def inc(self, instr, s, k=1):
        self.c[s] += k
        if instr is not None:
            instr.then_inc(self.sem[s], k)
        return self.c[s]

    def wait(self, en, s, val=None):
        if val is None:
            val = self.c[s]
        if self.en == en:
            self.eng.wait_ge(self.sem[s], val)

    def drain(self, en):
        if self.en == en:
            self.eng.drain()

    def vmark(self):
        s = self.t["stats"]
        return self.inc(self._do("vector", lambda e: e.tensor_copy(out=s[:, 7:8], in_=s[:, 7:8])), "ve")

    def amark(self):
        s = self.t["stats"]
        return self.inc(self._do("scalar", lambda e: e.copy(s[:, 7:8], s[:, 7:8])), "ac")

    def bar(self):
        self.nbar += 1
        for e in ENGINES:
            if self.en == e:
                self.eng.drain().then_inc(self.sem["b_" + e], 1)
        for e in ENGINES:
            if e != self.en:
                self.wait(self.en, "b_" + e, self.nbar)
        for e in ENGINES:
            self.c["b_" + e] = self.nbar

    def load(self, out_ap, in_ap, sl=0):
        s = "dl%d" % sl
        i = self._do("sync", lambda e: e.dma_start(out=out_ap, in_=in_ap))
        return (s, self.inc(i, s, 16))

    def store(self, out_ap, in_ap, sl=0):
        s = "ds%d" % sl
        i = self._do("gpsimd", lambda e: e.dma_start(out=out_ap, in_=in_ap))
        return (s, self.inc(i, s, 16))

    def waitv(self, en, tok):
        if self.en == en:
            self.eng.wait_ge(self.sem[tok[0]], tok[1])

    def wait_stores(self, en):
        for s in ("ds0", "ds1"):
            self.wait(en, s, self.c[s])

    def mm(self, out, lhsT, rhs, start, stop, tile_position=None):
        return self._do("tensor", lambda e: e.matmul(out, lhsT, rhs, start=start, stop=stop,
                                                     tile_position=tile_position))

    def tp(self, out, in_, np_):
        return self._do("tensor", lambda e: e.transpose(out, in_, self.t["ident"][0:np_, 0:np_]))

    def v(self, meth, *a, inc=False, **kw):
        i = self._do("vector", lambda e: getattr(e, meth)(*a, **kw))
        if inc:
            return self.inc(i, "ve")
        return i

    def a(self, meth, *a, inc=False, **kw):
        i = self._do("scalar", lambda e: getattr(e, meth)(*a, **kw))
        if inc:
            return self.inc(i, "ac")
        return i

    def bank_wait_te(self, b):
        if self.bankw[b] is not None:
            s, v = self.bankw[b]
            self.wait("tensor", s, v)
            self.bankw[b] = None

    def bank_free(self, b, s, v):
        self.bankw[b] = (s, v)


def _emit(c):
    t = c.t
    ps = t["psum"]
    ar = t["arena"]

    def A(off, n, p0=0, part=128):
        return ar[p0:p0 + part, off:off + n]

    st = t["stats"]

    def ln_tile(src, dst, xsq, n=D):
        c.wait("scalar", "ve", c.c["ve"])
        c.a("activation", xsq, src, AF.Square, inc=True)
        sq_done = c.c["ac"]
        c.v("reduce_sum", out=st[:, 0:1], in_=src, axis=AX)
        c.drain("vector")
        c.v("tensor_scalar", out=st[:, 0:1], in0=st[:, 0:1], scalar1=1.0 / n, scalar2=None, op0=ALU.mult)
        c.drain("vector")
        c.wait("vector", "ac", sq_done)
        c.v("reduce_sum", out=st[:, 1:2], in_=xsq, axis=AX)
        c.drain("vector")
        c.v("tensor_scalar", out=st[:, 1:2], in0=st[:, 1:2], scalar1=1.0 / n, scalar2=None, op0=ALU.mult)
        c.v("tensor_scalar", out=st[:, 2:3], in0=st[:, 0:1], scalar1=st[:, 0:1], scalar2=None, op0=ALU.mult)
        c.drain("vector")
        c.v("tensor_sub", out=st[:, 3:4], in0=st[:, 1:2], in1=st[:, 2:3], inc=True)
        var_done = c.c["ve"]
        c.wait("scalar", "ve", var_done)
        c.a("activation", st[:, 4:5], st[:, 3:4], AF.Sqrt, bias=t["eps"][:, 0:1], scale=1.0, inc=True)
        sd_done = c.c["ac"]
        c.wait("vector", "ac", sd_done)
        c.v("reciprocal", out=st[:, 5:6], in_=st[:, 4:5])
        c.drain("vector")
        c.v("tensor_scalar", out=dst, in0=src, scalar1=st[:, 0:1], scalar2=st[:, 5:6],
            op0=ALU.subtract, op1=ALU.mult)
        c.drain("vector")

    def fp4_quant(src, dst, n):
        assert n <= 512
        nb = n // 16
        tt = A(F_SCR, n)
        at = A(F_SCR + 512, n)
        qa = A(F_SCR + 1024, n)
        qt = A(F_SCR + 1536, n)
        sg = A(F_SCR + 2048, n)
        qsc = t["qsc"]
        bmv, scv, rsv = qsc[:, 0:nb], qsc[:, 64:64 + nb], qsc[:, 128:128 + nb]
        src3 = src.rearrange("p (g k) -> p g k", k=16)
        dst3 = dst.rearrange("p (g k) -> p g k", k=16)
        t3 = tt.rearrange("p (g k) -> p g k", k=16)
        qa3 = qa.rearrange("p (g k) -> p g k", k=16)
        c.v("reduce_max", out=bmv, in_=src3, axis=AX, apply_absolute_value=True)
        c.drain("vector")
        c.v("tensor_scalar", out=scv, in0=bmv, scalar1=1.0 / 6.0, scalar2=1e-12, op0=ALU.mult, op1=ALU.max)
        c.drain("vector")
        c.v("reciprocal", out=rsv, in_=scv)
        c.drain("vector")
        c.v("tensor_tensor", out=t3, in0=src3, in1=rsv.unsqueeze(2).broadcast_to([128, nb, 16]), op=ALU.mult)
        c.drain("vector")
        c.v("tensor_scalar", out=at, in0=tt, scalar1=-1.0, scalar2=None, op0=ALU.mult)
        c.drain("vector")
        c.v("tensor_max", out=at, in0=at, in1=tt)
        c.drain("vector")
        c.v("tensor_scalar", out=qa, in0=at, scalar1=MID[0], scalar2=DG[0], op0=ALU.is_gt, op1=ALU.mult)
        c.drain("vector")
        for j in range(1, 7):
            c.v("tensor_scalar", out=qt, in0=at, scalar1=MID[j], scalar2=DG[j], op0=ALU.is_gt, op1=ALU.mult)
            c.drain("vector")
            c.v("tensor_add", out=qa, in0=qa, in1=qt)
            c.drain("vector")
        c.v("tensor_scalar", out=sg, in0=tt, scalar1=0.0, scalar2=2.0, op0=ALU.is_gt, op1=ALU.mult)
        c.drain("vector")
        c.v("scalar_tensor_tensor", out=qa, in0=sg, scalar=-1.0, in1=qa, op0=ALU.add, op1=ALU.mult)
        c.drain("vector")
        c.v("tensor_tensor", out=dst3, in0=qa3, in1=scv.unsqueeze(2).broadcast_to([128, nb, 16]), op=ALU.mult)
        c.drain("vector")

    def transpose_1024(src_off, dst_off, j, banks=(6, 7), src_stride=128):
        for half in range(2):
            bk = banks[half]
            c.bank_wait_te(bk)
            for k in range(4):
                kk = half * 4 + k
                i = c.tp(ps[bk][:, k * 128:(k + 1) * 128], A(src_off + kk * 128, 128), 128)
            pe_v = c.inc(i, "pe")
            c.wait("vector", "pe", pe_v)
            for k in range(4):
                kk = half * 4 + k
                c.v("tensor_copy", out=A(dst_off + kk * 512 + j * 128, 128),
                    in_=ps[bk][:, k * 128:(k + 1) * 128], inc=(k == 3))
            c.bank_free(bk, "ve", c.c["ve"])

    for name in ("ident", "triA", "triB", "m00", "m01", "m10"):
        c.load(t[name][:], t[name + "_in"], sl=4)
    c.load(t["bqkvT_s"][:], t["bqkv_t"], sl=4)
    ld0 = c.load(t["fc1bT_s"][:], t["fc1b_t"], sl=4)
    if c.on("vector"):
        c.eng.memset(t["onesrow"][:], 1.0)
        c.eng.memset(t["onescol"][:], 1.0)
        c.eng.memset(t["eps"][:], 1e-5)
        vc = ar[0:128, A_VV:A_VV + 4 * NH * 65].rearrange("p (cell w) -> p cell w", w=65)
        c.eng.memset(vc[:, :, 64:65], 1.0)
    c.vmark()
    c.waitv("tensor", ld0)
    c.tp(ps[0][:, 0:24], t["bqkvT_s"][:], 24)
    i = c.tp(ps[1][:, 0:32], t["fc1bT_s"][:], 32)
    c.inc(i, "pe")
    c.wait("vector", "pe", c.c["pe"])
    c.v("tensor_copy", out=t["bqkvC"][:], in_=ps[0][:, 0:24])
    c.v("tensor_copy", out=t["fc1bC"][:], in_=ps[1][:, 0:32], inc=True)
    c.bar()

    for (wsrc, wdst, nrows, ncols) in (
        (t["w_gate"], t["wgT"], FF0, D),
        (t["w_up"], t["wuT"], FF0, D),
        (t["w_down"], t["wdT"], D, FF0),
    ):
        for r in range(nrows // 128):
            win = A(G_WIN, ncols)
            c.wait_stores("sync")
            ld = c.load(win, wsrc[r * 128:(r + 1) * 128, :], sl=0)
            c.waitv("vector", ld)
            for hh in range((ncols + 511) // 512):
                n = min(512, ncols - hh * 512)
                fp4_quant(win[:, hh * 512:hh * 512 + n], A(G_WQ + hh * 512, n), n)
            q_done = c.vmark()
            for k in range(ncols // 128):
                bk = k % 4
                c.bank_wait_te(bk)
                c.wait("tensor", "ve", q_done)
                i = c.tp(ps[bk][:, 0:128], A(G_WQ + k * 128, 128), 128)
                pe_v = c.inc(i, "pe")
                c.wait("vector", "pe", pe_v)
                c.v("tensor_copy", out=A(G_STG, 128), in_=ps[bk][:, 0:128], inc=True)
                ev = c.c["ve"]
                c.bank_free(bk, "ve", ev)
                c.wait("gpsimd", "ve", ev)
                s_ = c.store(wdst[k * 128:(k + 1) * 128, r * 128:(r + 1) * 128], A(G_STG, 128))
                c.waitv("vector", s_)
    c.bar()

    for (qsrc, ssrc, wdst, N, ng) in (
        (t["fc1_q"], t["fc1_s"], t["w1d"], FF1, 8),
        (t["fc2_q"], t["fc2_s"], t["w2d"], D, 32),
    ):
        for g in range(ng):
            qin = A(G_WIN, N).bitcast(I32)
            ld = c.load(qin, qsrc[g * 128:(g + 1) * 128, :], sl=0)
            lds = c.load(ar[0:1, G_SROW:G_SROW + N], ssrc[g:g + 1, :], sl=4)
            c.waitv("vector", ld)
            c.v("tensor_scalar", out=A(G_WS, N), in0=qin, scalar1=128.0, scalar2=None,
                op0=ALU.subtract, inc=True)
            qf_done = c.c["ve"]
            for ch in range(N // 512):
                bk = ch % 2
                c.bank_wait_te(bk)
                c.waitv("tensor", lds)
                i = c.mm(ps[bk][:, 0:512], t["onesrow"][0:1, 0:128],
                         ar[0:1, G_SROW + ch * 512:G_SROW + (ch + 1) * 512], start=True, stop=True)
                pe_v = c.inc(i, "pe")
                c.wait("vector", "pe", pe_v)
                c.wait("vector", "ve", qf_done)
                c.v("tensor_tensor", out=A(G_WIN + ch * 512, 512), in0=A(G_WS + ch * 512, 512),
                    in1=ps[bk][:, 0:512], op=ALU.mult, inc=True)
                c.bank_free(bk, "ve", c.c["ve"])
            wrow = c.vmark()
            c.wait("gpsimd", "ve", wrow)
            s_ = c.store(wdst[g * 128:(g + 1) * 128, :], A(G_WIN, N))
            c.waitv("vector", s_)
            c.waitv("sync", s_)
    for (wsrc, wdst) in ((t["Wq"], t["WqT"]), (t["Wf"], t["WfT"])):
        for r in range(8):
            ld = c.load(A(G_WIN, D), wsrc[r * 128:(r + 1) * 128, :], sl=0)
            for k in range(8):
                bk = k % 4
                c.bank_wait_te(bk)
                c.waitv("tensor", ld)
                i = c.tp(ps[bk][:, 0:128], A(G_WIN + k * 128, 128), 128)
                pe_v = c.inc(i, "pe")
                c.wait("vector", "pe", pe_v)
                c.v("tensor_copy", out=A(G_WS + k * 128, 128), in_=ps[bk][:, 0:128], inc=True)
                c.bank_free(bk, "ve", c.c["ve"])
            done = c.vmark()
            c.wait("gpsimd", "ve", done)
            for a in range(8):
                s_ = c.store(wdst[a * 128:(a + 1) * 128, r * 128:(r + 1) * 128],
                             A(G_WS + a * 128, 128))
            c.waitv("vector", s_)
            c.waitv("sync", s_)
    if _LIMIT == "prologue":
        c.bar()
        for name in ("wgT", "wuT", "wdT", "w1d", "w2d"):
            c.store(t["dbg_" + name], t[name], sl=1)
        c.wait_stores("gpsimd")
        c.bar()
        return
    for (wsrc, wdst) in ((t["Wk"], t["WkT"]), (t["Wv"], t["WvT"])):
        ld = c.load(A(G_WIN, D, part=64), wsrc, sl=0)
        for k in range(8):
            bk = k % 4
            c.bank_wait_te(bk)
            c.waitv("tensor", ld)
            i = c.tp(ps[bk][:, 0:64], A(G_WIN + k * 128, 128, part=64), 64)
            pe_v = c.inc(i, "pe")
            c.wait("vector", "pe", pe_v)
            c.v("tensor_copy", out=A(G_WS + k * 64, 64), in_=ps[bk][:, 0:64], inc=True)
            c.bank_free(bk, "ve", c.c["ve"])
        done = c.vmark()
        c.wait("gpsimd", "ve", done)
        for a in range(8):
            s_ = c.store(wdst[a * 128:(a + 1) * 128, :], A(G_WS + a * 64, 64))
        c.waitv("vector", s_)
        c.waitv("sync", s_)
    c.bar()

    blocks = [
        dict(name="pre2", xsrc=t["x_pre2"], xoff=0, ffn=[2, 3], qt=[2, 3], x1row=-512, q=True),
        dict(name="own0", xsrc=t["x_own"], xoff=0, ffn=[0, 1, 2, 3], qt=[0, 1, 2, 3], x1row=0, q=True),
        dict(name="own1", xsrc=t["x_own"], xoff=512, ffn=[0, 1, 2, 3], qt=[0, 1, 2, 3], x1row=512, q=True),
        dict(name="own2", xsrc=t["x_own"], xoff=1024, ffn=[0, 1, 2, 3], qt=[0, 1, 2, 3], x1row=1024, q=True),
        dict(name="own3", xsrc=t["x_own"], xoff=1536, ffn=[0, 1, 2, 3], qt=[0, 1, 2, 3], x1row=1536, q=True),
        dict(name="wrap", xsrc=t["x_wrap"], xoff=0, ffn=[], qt=[], x1row=None, q=False),
    ]

    def kT_ap(hh, j0, nj):
        return A(A_KT + (hh // 2) * 512 + j0, nj, p0=(hh % 2) * 64, part=64)

    def qT_ap(slot, hh, q0, nq):
        return A(A_Q[slot] + (hh // 2) * 512 + q0, nq, p0=(hh % 2) * 64, part=64)

    def vv_ap(jc, hh):
        return A(A_VV + (jc * NH + hh) * 65, 65)

    def prekv(b, qslot):
        ln_marks = []
        for j in range(4):
            xin = A(P_XIN + (j % 2) * 1024, D)
            if j >= 2:
                c.wait("sync", "ve", ln_marks[j - 2])
            ld = c.load(xin, b["xsrc"][b["xoff"] + j * 128:b["xoff"] + (j + 1) * 128, :], sl=2 + (j % 2))
            c.waitv("vector", ld)
            c.waitv("scalar", ld)
            ln_tile(xin, A(P_XN, D), A(P_XSQ, D))
            ln_done = c.vmark()
            ln_marks.append(ln_done)
            c.wait("tensor", "ve", ln_done)
            transpose_1024(P_XN, P_XNT, j)
        xnT_done = c.vmark()
        ldrow = c.load(t["rowbuf"][0:1, 0:3072], t["bqkv_row"], sl=4)
        groups = ([0] if b["q"] else []) + [1, 2]
        pe_marks = []
        for gi, grp in enumerate(groups):
            for kc in range(8):
                idx = gi * 8 + kc
                buf = idx % 2
                wap = A(P_WS + buf * 1024, 1024)
                if idx >= 2:
                    c.wait("sync", "pe", pe_marks[idx - 2])
                ld = c.load(wap, t["W_qkv"][kc * 128:(kc + 1) * 128, grp * 1024:(grp + 1) * 1024], sl=idx % 2)
                c.waitv("tensor", ld)
                c.wait("tensor", "ve", xnT_done)
                if grp in (0, 1):
                    for n in range(8):
                        if kc == 0:
                            c.bank_wait_te(n)
                        i = c.mm(ps[n][:, 0:512], wap[:, n * 128:(n + 1) * 128],
                                 A(P_XNT + kc * 512, 512), start=(kc == 0), stop=(kc == 7))
                else:
                    for j in range(4):
                        for hf in range(2):
                            bk = j * 2 + hf
                            if kc == 0:
                                c.bank_wait_te(bk)
                            i = c.mm(ps[bk][:, 0:512], A(P_XNT + kc * 512 + j * 128, 128),
                                     wap[:, hf * 512:(hf + 1) * 512], start=(kc == 0), stop=False)
                    if kc == 7:
                        c.waitv("tensor", ldrow)
                        for j in range(4):
                            for hf in range(2):
                                bk = j * 2 + hf
                                i = c.mm(ps[bk][:, 0:512], t["onesrow"][0:1, 0:128],
                                         t["rowbuf"][0:1, 2048 + hf * 512:2048 + (hf + 1) * 512],
                                         start=False, stop=True)
                pe_marks.append(c.inc(i, "pe"))
            mm_done = c.c["pe"]
            if grp == 0:
                for n in range(8):
                    c.wait("vector", "pe", mm_done)
                    c.v("tensor_scalar", out=A(A_Q[qslot] + n * 512, 512), in0=ps[n][:, 0:512],
                        scalar1=t["bqkvC"][:, n:n + 1], scalar2=None, op0=ALU.add, inc=True)
                    c.bank_free(n, "ve", c.c["ve"])
            elif grp == 1:
                for n in range(8):
                    c.wait("vector", "pe", mm_done)
                    c.v("tensor_scalar", out=A(A_KT + n * 512, 512), in0=ps[n][:, 0:512],
                        scalar1=t["bqkvC"][:, 8 + n:9 + n], scalar2=None, op0=ALU.add, inc=True)
                    c.bank_free(n, "ve", c.c["ve"])
            else:
                for j in range(4):
                    for hf in range(2):
                        bk = j * 2 + hf
                        c.wait("vector", "pe", mm_done)
                        dst = ar[0:128, A_VV + (j * NH + hf * 8) * 65:A_VV + (j * NH + hf * 8) * 65 + 8 * 65]
                        c.v("tensor_copy", out=dst.rearrange("p (h w) -> p h w", w=65)[:, :, 0:64],
                            in_=ps[bk][:, 0:512].rearrange("p (h w) -> p h w", w=64), inc=True)
                        c.bank_free(bk, "ve", c.c["ve"])
        c.vmark()

    def att_phase(b, qslot, phase):
        qtiles = b["qt"]
        for hh in range(NH):
            exp_done = {}
            for jc in range(4):
                bk = jc
                c.bank_wait_te(bk)
                i = c.mm(ps[bk][:, 0:512], kT_ap(hh, jc * 128, 128), qT_ap(qslot, hh, 0, 512),
                         start=True, stop=True)
                s_done = c.inc(i, "pe")
                c.wait("scalar", "pe", s_done)
                c.a("activation", A(A_P + jc * 512, 512), ps[bk][:, 0:512], AF.Exp,
                    scale=SCALE, inc=True)
                exp_done[jc] = c.c["ac"]
                c.bank_free(bk, "ac", exp_done[jc])
            ob = 4 + (hh % 2)
            c.bank_wait_te(ob)
            for qt in qtiles:
                for jc in range(4):
                    c.wait("tensor", "ac", exp_done[jc])
                    i = c.mm(ps[ob][:, qt * 65:qt * 65 + 65],
                             A(A_P + jc * 512 + qt * 128, 128), vv_ap(jc, hh),
                             start=(jc == 0), stop=(jc == 3))
            o_done = c.inc(i, "pe")
            c.wait("vector", "pe", o_done)
            for qt in qtiles:
                c.v("reciprocal", out=st[:, 6:7], in_=ps[ob][:, qt * 65 + 64:qt * 65 + 65])
                c.drain("vector")
                ao = A(A_AO + qt * 1024 + hh * 64, 64)
                if phase == 1:
                    c.v("tensor_scalar", out=ao, in0=ps[ob][:, qt * 65:qt * 65 + 64],
                        scalar1=st[:, 6:7], scalar2=None, op0=ALU.mult)
                else:
                    c.v("scalar_tensor_tensor", out=ao, in0=ps[ob][:, qt * 65:qt * 65 + 64],
                        scalar=st[:, 6:7], in1=ao, op0=ALU.mult, op1=ALU.add)
                c.drain("vector")
            c.vmark()
            c.bank_free(ob, "ve", c.c["ve"])

    def wout(b):
        qtiles = b["qt"]
        for j in qtiles:
            transpose_1024(A_AO + j * 1024, W_AOT, j)
        aoT_done = c.vmark()
        ldrow = c.load(t["rowbuf"][0:1, 0:1024], t["bout_row"], sl=4)
        pe_marks = []
        for kc in range(8):
            wap = A(W_WS + (kc % 2) * 1024, 1024)
            if kc >= 2:
                c.wait("sync", "pe", pe_marks[kc - 2])
            ld = c.load(wap, t["W_out"][kc * 128:(kc + 1) * 128, :], sl=kc % 2)
            c.waitv("tensor", ld)
            c.wait("tensor", "ve", aoT_done)
            for ji, j in enumerate(qtiles):
                for hf in range(2):
                    bk = ji * 2 + hf
                    if kc == 0:
                        c.bank_wait_te(bk)
                    i = c.mm(ps[bk][:, 0:512], A(W_AOT + kc * 512 + j * 128, 128),
                             wap[:, hf * 512:(hf + 1) * 512], start=(kc == 0), stop=False)
            if kc == 7:
                c.waitv("tensor", ldrow)
                for ji, j in enumerate(qtiles):
                    for hf in range(2):
                        bk = ji * 2 + hf
                        i = c.mm(ps[bk][:, 0:512], t["onesrow"][0:1, 0:128],
                                 t["rowbuf"][0:1, hf * 512:(hf + 1) * 512], start=False, stop=True)
            pe_marks.append(c.inc(i, "pe"))
        s_done = c.c["pe"]
        for ji, j in enumerate(qtiles):
            for hf in range(2):
                bk = ji * 2 + hf
                c.wait("vector", "pe", s_done)
                c.v("tensor_copy", out=A(A_S + j * 1024 + hf * 512, 512), in_=ps[bk][:, 0:512], inc=True)
                c.bank_free(bk, "ve", c.c["ve"])
        c.vmark()

    def ffnc(b):
        tiles = b["ffn"]
        for hv_i in range(0, len(tiles), 2):
            hv = tiles[hv_i:hv_i + 2]
            for jj, j in enumerate(hv):
                xin = A(F_XIN + jj * 1024, D)
                ld = c.load(xin, b["xsrc"][b["xoff"] + j * 128:b["xoff"] + (j + 1) * 128, :], sl=2 + jj)
                c.waitv("vector", ld)
                c.waitv("scalar", ld)
                ln_tile(xin, A(F_XN, D), A(F_SCR, D))
                for hf in range(2):
                    fp4_quant(A(F_XN + hf * 512, 512), A(F_HQ, 512), 512)
                    qdone = c.vmark()
                    bk = 6
                    c.bank_wait_te(bk)
                    c.wait("tensor", "ve", qdone)
                    for k in range(4):
                        i = c.tp(ps[bk][:, k * 128:(k + 1) * 128], A(F_HQ + k * 128, 128), 128)
                    pe_v = c.inc(i, "pe")
                    c.wait("vector", "pe", pe_v)
                    for k in range(4):
                        kk = hf * 4 + k
                        c.v("tensor_copy", out=A(F_XQT + (jj * 8 + kk) * 128, 128),
                            in_=ps[bk][:, k * 128:(k + 1) * 128], inc=(k == 3))
                    c.bank_free(bk, "ve", c.c["ve"])
            xq_done = c.vmark()
            if _DBG3 and b["name"] == "own0" and hv_i == 0:
                c.wait("gpsimd", "ve", xq_done)
                dx = c.store(t["dbg_xqT"], A(F_XQT, 2048), sl=1)
                c.waitv("sync", dx)
            pe_marks = []
            pe_half0 = c.c["pe"]
            idx = 0
            for nchi, n in enumerate(N0CH):
                ncol0 = nchi * 512
                for kc in range(8):
                    buf = idx % 2
                    wg = A(F_WS + buf * 1024, n)
                    wu = A(F_WS + buf * 1024 + 512, n)
                    if idx >= 2:
                        c.wait("sync", "pe", pe_marks[idx - 2])
                    else:
                        c.wait("sync", "pe", pe_half0)
                    ldg = c.load(wg, t["wgT"][kc * 128:(kc + 1) * 128, ncol0:ncol0 + n], sl=idx % 2)
                    ldu = c.load(wu, t["wuT"][kc * 128:(kc + 1) * 128, ncol0:ncol0 + n], sl=idx % 2)
                    c.waitv("tensor", ldu)
                    c.wait("tensor", "ve", xq_done)
                    for jj in range(len(hv)):
                        bg, by = jj * 2, jj * 2 + 1
                        if kc == 0:
                            c.bank_wait_te(bg)
                            c.bank_wait_te(by)
                        c.mm(ps[bg][:, 0:n], A(F_XQT + (jj * 8 + kc) * 128, 128), wg,
                             start=(kc == 0), stop=(kc == 7))
                        i = c.mm(ps[by][:, 0:n], A(F_XQT + (jj * 8 + kc) * 128, 128), wu,
                                 start=(kc == 0), stop=(kc == 7))
                    pe_marks.append(c.inc(i, "pe"))
                    idx += 1
                gu_done = c.c["pe"]
                for jj in range(len(hv)):
                    bg, by = jj * 2, jj * 2 + 1
                    c.wait("scalar", "pe", gu_done)
                    c.wait("scalar", "ve", c.c["ve"])
                    c.a("activation", A(F_H, n), ps[bg][:, 0:n], AF.Silu, inc=True)
                    hs_done = c.c["ac"]
                    c.wait("vector", "ac", hs_done)
                    c.v("tensor_tensor", out=A(F_H, n), in0=A(F_H, n), in1=ps[by][:, 0:n], op=ALU.mult)
                    c.drain("vector")
                    c.vmark()
                    fp4_quant(A(F_H, n), A(F_HQ, n), n)
                    hq_done = c.vmark()
                    c.bank_free(bg, "ve", hq_done)
                    c.bank_free(by, "ve", hq_done)
                    bk = 6 + jj
                    c.bank_wait_te(bk)
                    c.wait("tensor", "ve", hq_done)
                    nk = n // 128
                    for k in range(nk):
                        i = c.tp(ps[bk][:, k * 128:(k + 1) * 128], A(F_HQ + k * 128, 128), 128)
                    pe_v = c.inc(i, "pe")
                    c.wait("vector", "pe", pe_v)
                    for k in range(nk):
                        fc = nchi * 4 + k
                        c.v("tensor_copy", out=A(F_HQT + fc * 256 + jj * 128, 128),
                            in_=ps[bk][:, k * 128:(k + 1) * 128], inc=(k == nk - 1))
                    c.bank_free(bk, "ve", c.c["ve"])
            hqT_done = c.vmark()
            if _DBG3 and b["name"] == "own0" and hv_i == 0:
                c.wait("gpsimd", "ve", hqT_done)
                dh = c.store(t["dbg_hqT"], A(F_HQT, 5632), sl=1)
                c.waitv("sync", dh)
            pe_marks2 = []
            for fc in range(22):
                buf = fc % 2
                wd = A(F_WS2 + buf * 1024, 1024)
                if fc >= 2:
                    c.wait("sync", "pe", pe_marks2[fc - 2])
                else:
                    c.wait("sync", "pe", pe_marks[-1])
                ld = c.load(wd, t["wdT"][fc * 128:(fc + 1) * 128, :], sl=fc % 2)
                c.waitv("tensor", ld)
                c.wait("tensor", "ve", hqT_done)
                for jj in range(len(hv)):
                    for hf in range(2):
                        bk = jj * 2 + hf
                        if fc == 0:
                            c.bank_wait_te(bk)
                        i = c.mm(ps[bk][:, 0:512], A(F_HQT + fc * 256 + jj * 128, 128),
                                 wd[:, hf * 512:(hf + 1) * 512], start=(fc == 0), stop=(fc == 21))
                pe_marks2.append(c.inc(i, "pe"))
            z_done = c.c["pe"]
            for jj, j in enumerate(hv):
                xin = A(F_XIN + jj * 1024, D)
                for hf in range(2):
                    bk = jj * 2 + hf
                    c.wait("scalar", "pe", z_done)
                    c.a("activation", A(F_G + hf * 512, 512), ps[bk][:, 0:512], AF.Silu, inc=True)
                g_done = c.c["ac"]
                if _DBG2:
                    row = 256 + b["x1row"] + j * 128
                    c.wait("gpsimd", "ac", g_done)
                    sg1 = c.store(t["dbg_g"][row:row + 128, :], A(F_G, D), sl=1)
                    sg2 = c.store(t["dbg_s"][row:row + 128, :], A(A_S + j * 1024, D), sl=1)
                    c.waitv("vector", sg2)
                    c.waitv("scalar", sg2)
                c.wait("vector", "ac", g_done)
                c.v("tensor_tensor", out=A(F_G, D), in0=A(F_G, D), in1=A(A_S + j * 1024, D), op=ALU.mult)
                c.drain("vector")
                c.v("tensor_add", out=A(F_G, D), in0=A(F_G, D), in1=xin, inc=True)
                x1_done = c.c["ve"]
                c.bank_free(jj * 2, "ve", x1_done)
                c.bank_free(jj * 2 + 1, "ve", x1_done)
                c.drain("vector")
                c.vmark()
                c.wait("gpsimd", "ve", c.c["ve"])
                row = 256 + b["x1row"] + j * 128
                s_ = c.store(t["x1s"][row:row + 128, :], A(F_G, D))
                c.waitv("vector", s_)
                c.waitv("scalar", s_)
                c.waitv("sync", s_)

    prekv(blocks[0], 0)
    c.bar()
    att_phase(blocks[0], 0, 1)
    c.bar()
    prekv(blocks[1], 1)
    c.bar()
    att_phase(blocks[0], 0, 2)
    wout(blocks[0])
    c.bar()
    ffnc(blocks[0])
    c.bar()
    qsl = {1: 1, 2: 0, 3: 1, 4: 0}
    for bi in range(1, 5):
        att_phase(blocks[bi], qsl[bi], 1)
        c.bar()
        prekv(blocks[bi + 1], qsl[bi + 1] if bi + 1 < 5 else 0)
        c.bar()
        att_phase(blocks[bi], qsl[bi], 2)
        wout(blocks[bi])
        c.bar()
        ffnc(blocks[bi])
        c.bar()

    if _LIMIT == "L0":
        c.bar()
        c.store(t["dbg_x1s"], t["x1s"], sl=1)
        c.wait_stores("gpsimd")
        c.bar()
        return

    def load_wkv():
        ld1 = c.load(A(L_WKV, 512).rearrange("p (a b) -> p a b", b=64),
                     t["WkT"].rearrange("(a p) b -> p a b", p=128), sl=5)
        ld2 = c.load(A(L_WKV + 512, 512).rearrange("p (a b) -> p a b", b=64),
                     t["WvT"].rearrange("(a p) b -> p a b", p=128), sl=5)
        return ld2

    ldg = c.load(t["rowbuf"][0:1, 0:1024], t["gam_row"], sl=4)
    ldb = c.load(t["rowbuf"][0:1, 1024:2048], t["bet_row"], sl=4)
    for gb in range(2):
        for ch in range(2):
            bk = gb * 2 + ch
            c.bank_wait_te(bk)
            c.waitv("tensor", ldb)
            i = c.mm(ps[bk][:, 0:512], t["onesrow"][0:1, 0:128],
                     t["rowbuf"][0:1, gb * 1024 + ch * 512:gb * 1024 + (ch + 1) * 512],
                     start=True, stop=True)
            pe_v = c.inc(i, "pe")
            c.wait("vector", "pe", pe_v)
            c.v("tensor_copy", out=A(L_GBC + gb * 1024 + ch * 512, 512), in_=ps[bk][:, 0:512], inc=True)
            c.bank_free(bk, "ve", c.c["ve"])
    if c.on("vector"):
        vc = ar[0:128, L_VV:L_VV + 20 * 65].rearrange("p (cell w) -> p cell w", w=65)
        c.eng.memset(vc[:, :, 64:65], 1.0)
    c.vmark()
    c.bar()

    def l1_kv_tiles(ch0, ntile, xn1T_off):
        wkv_ld = load_wkv()
        c.waitv("tensor", wkv_ld)
        c.bank_wait_te(4)
        for kc in range(8):
            i = c.mm(ps[4][0:64, 0:ntile * 128], A(L_WKV + kc * 64, 64),
                     A(xn1T_off + kc * 512, ntile * 128), start=(kc == 0), stop=(kc == 7))
        c.inc(i, "pe")
        for kc in range(8):
            i = c.mm(ps[4][64:128, 0:ntile * 128], A(L_WKV + kc * 64, 64),
                     A(xn1T_off + kc * 512, ntile * 128), start=(kc == 0), stop=(kc == 7),
                     tile_position=(0, 64))
        c.inc(i, "pe")
        c.bank_wait_te(5)
        for kc in range(8):
            for j in range(ntile):
                i = c.mm(ps[5][:, j * 64:(j + 1) * 64], A(xn1T_off + kc * 512 + j * 128, 128),
                         A(L_WKV + 512 + kc * 64, 64), start=(kc == 0 and j == 0), stop=(kc == 7))
        c.inc(i, "pe")
        kv_done = c.c["pe"]
        c.wait("vector", "pe", kv_done)
        c.v("tensor_copy", out=A(L_KT + ch0 * 128, ntile * 128), in_=ps[4][:, 0:ntile * 128])
        for j in range(ntile):
            c.v("tensor_copy", out=A(L_VV + (ch0 + j) * 65, 64), in_=ps[5][:, j * 64:(j + 1) * 64],
                inc=(j == ntile - 1))
        c.bank_free(4, "ve", c.c["ve"])
        c.bank_free(5, "ve", c.c["ve"])
        c.vmark()

    for j in range(2):
        xin = A(L_XIN + j * 1024, D)
        ld = c.load(xin, t["x1s"][j * 128:(j + 1) * 128, :], sl=2 + j)
        c.waitv("vector", ld)
        c.waitv("scalar", ld)
        ln_tile(xin, A(L_LOC, D), A(L_LOC + 1024, D))
        ln_done = c.vmark()
        c.wait("tensor", "ve", ln_done)
        transpose_1024(L_LOC, L_XN1T, j)
    c.vmark()
    l1_kv_tiles(0, 2, L_XN1T)
    c.bar()

    def l1pre(g):
        ln_marks = []
        for j in range(4):
            xin = A(L_XIN + (j % 2) * 1024, D)
            row = 256 + (g * 4 + j) * 128
            if j >= 2:
                c.wait("sync", "ve", ln_marks[j - 2])
            ld = c.load(xin, t["x1s"][row:row + 128, :], sl=2 + (j % 2))
            c.waitv("vector", ld)
            c.waitv("scalar", ld)
            ln_tile(xin, A(L_XN1 + j * 1024, D), A(L_LOC, D))
            ln_done = c.vmark()
            ln_marks.append(ln_done)
            c.wait("tensor", "ve", ln_done)
            transpose_1024(L_XN1 + j * 1024, L_XN1T, j)
        xn1T_done = c.vmark()
        pe_marks = []
        for kc in range(8):
            buf = kc % 2
            wap = A(L_WS + buf * 1024, 1024)
            if kc >= 2:
                c.wait("sync", "pe", pe_marks[kc - 2])
            ld = c.load(wap, t["WqT"][kc * 128:(kc + 1) * 128, :], sl=kc % 2)
            c.waitv("tensor", ld)
            c.wait("tensor", "ve", xn1T_done)
            for n in range(8):
                if kc == 0:
                    c.bank_wait_te(n)
                i = c.mm(ps[n][:, 0:512], wap[:, n * 128:(n + 1) * 128],
                         A(L_XN1T + kc * 512, 512), start=(kc == 0), stop=(kc == 7))
            pe_marks.append(c.inc(i, "pe"))
        q_done = c.c["pe"]
        for n in range(8):
            c.wait("vector", "pe", q_done)
            c.v("tensor_copy", out=A(L_QT + n * 512, 512), in_=ps[n][:, 0:512], inc=True)
            c.bank_free(n, "ve", c.c["ve"])
        c.vmark()
        l1_kv_tiles(2 + g * 4, 4, L_XN1T)

    def l1att(g):
        if _DBG4 and g == 0:
            c.wait("gpsimd", "ve", c.c["ve"])
            d1 = c.store(t["dbg_kT"], A(L_KT, 2560), sl=1)
            d2 = c.store(t["dbg_vv"], A(L_VV, 1300), sl=1)
            d3 = c.store(t["dbg_qT"], A(L_QT, 4096), sl=1)
            c.waitv("sync", d3)
        for j in range(4):
            tt = g * 4 + j
            if tt == 0:
                msks = [t["m00"], t["m01"], None]
            elif tt == 1:
                msks = [t["m10"], None, None]
            else:
                msks = [t["triA"], None, None]
            for hh in range(NH):
                sb = hh % 2
                c.bank_wait_te(sb)
                for jc in range(3):
                    i = c.mm(ps[sb][:, jc * 128:(jc + 1) * 128],
                             A(L_KT + (tt + jc) * 128, 128, p0=(hh % 2) * 64, part=64),
                             A(L_QT + (hh // 2) * 512 + j * 128, 128, p0=(hh % 2) * 64, part=64),
                             start=True, stop=True)
                s_done = c.inc(i, "pe")
                c.wait("scalar", "pe", s_done)
                pb = L_P + (hh % 2) * 384
                c.a("activation", A(pb, 384), ps[sb][:, 0:384], AF.Exp, scale=SCALE, inc=True)
                exp_done = c.c["ac"]
                c.bank_free(sb, "ac", exp_done)
                c.wait("vector", "ac", exp_done)
                c.v("tensor_tensor", out=A(pb, 128), in0=A(pb, 128), in1=msks[0][:], op=ALU.mult)
                if msks[1] is not None:
                    c.v("tensor_tensor", out=A(pb + 128, 128), in0=A(pb + 128, 128), in1=msks[1][:], op=ALU.mult)
                c.v("tensor_tensor", out=A(pb + 256, 128), in0=A(pb + 256, 128), in1=t["triB"][:], op=ALU.mult)
                c.drain("vector")
                p_done = c.vmark()
                ob = 4 + (hh % 2)
                c.bank_wait_te(ob)
                c.wait("tensor", "ve", p_done)
                for jc in range(3):
                    i = c.mm(ps[ob][:, 0:65], A(pb + jc * 128, 128), A(L_VV + (tt + jc) * 65, 65),
                             start=(jc == 0), stop=(jc == 2))
                o_done = c.inc(i, "pe")
                c.wait("vector", "pe", o_done)
                c.v("reciprocal", out=st[:, 6:7], in_=ps[ob][:, 64:65])
                c.drain("vector")
                c.v("tensor_scalar", out=A(L_AOT2 + hh * 64, 64), in0=ps[ob][:, 0:64],
                    scalar1=st[:, 6:7], scalar2=None, op0=ALU.mult, inc=True)
                c.drain("vector")
                c.bank_free(ob, "ve", c.c["ve"])
            ao_done = c.vmark()
            c.wait("tensor", "ve", ao_done)
            transpose_1024(L_AOT2, L_LOC, j)
        aoT_done = c.vmark()
        ldrow = c.load(t["rowbuf"][0:1, 0:1024], t["bf_row"], sl=4)
        pe_marks = []
        for kc in range(8):
            buf = kc % 2
            wap = A(L_WS + buf * 1024, 1024)
            if kc >= 2:
                c.wait("sync", "pe", pe_marks[kc - 2])
            ld = c.load(wap, t["WfT"][kc * 128:(kc + 1) * 128, :], sl=kc % 2)
            c.waitv("tensor", ld)
            c.wait("tensor", "ve", aoT_done)
            for j in range(4):
                for hf in range(2):
                    bk = j * 2 + hf
                    if kc == 0:
                        c.bank_wait_te(bk)
                    i = c.mm(ps[bk][:, 0:512], A(L_LOC + kc * 512 + j * 128, 128),
                             wap[:, hf * 512:(hf + 1) * 512], start=(kc == 0), stop=False)
            if kc == 7:
                c.waitv("tensor", ldrow)
                for j in range(4):
                    for hf in range(2):
                        bk = j * 2 + hf
                        i = c.mm(ps[bk][:, 0:512], t["onesrow"][0:1, 0:128],
                                 t["rowbuf"][0:1, hf * 512:(hf + 1) * 512], start=False, stop=True)
            pe_marks.append(c.inc(i, "pe"))
        s_done = c.c["pe"]
        for j in range(4):
            for hf in range(2):
                bk = j * 2 + hf
                c.wait("vector", "pe", s_done)
                c.v("tensor_copy", out=A(L_S1 + j * 1024 + hf * 512, 512), in_=ps[bk][:, 0:512], inc=True)
                c.bank_free(bk, "ve", c.c["ve"])
        c.vmark()

    def l1ffn(g):
        for half in range(2):
            hv = [half * 2, half * 2 + 1]
            for jj, j in enumerate(hv):
                ln_tile(A(L_XN1 + j * 1024, D), A(L_LOC + 1024, D), A(L_LOC, D))
                c.v("tensor_tensor", out=A(L_LOC + 1024, D), in0=A(L_LOC + 1024, D),
                    in1=A(L_GBC, D), op=ALU.mult)
                c.drain("vector")
                c.v("tensor_add", out=A(L_LOC + 1024, D), in0=A(L_LOC + 1024, D), in1=A(L_GBC + 1024, D))
                c.drain("vector")
                h2_done = c.vmark()
                c.wait("tensor", "ve", h2_done)
                for hlf in range(2):
                    bk = 6 + hlf
                    c.bank_wait_te(bk)
                    for k in range(4):
                        kk = hlf * 4 + k
                        i = c.tp(ps[bk][:, k * 128:(k + 1) * 128], A(L_LOC + 1024 + kk * 128, 128), 128)
                    pe_v = c.inc(i, "pe")
                    c.wait("vector", "pe", pe_v)
                    for k in range(4):
                        kk = hlf * 4 + k
                        c.v("tensor_copy", out=A(L_H2T + kk * 256 + jj * 128, 128),
                            in_=ps[bk][:, k * 128:(k + 1) * 128], inc=(k == 3))
                    c.bank_free(bk, "ve", c.c["ve"])
            h2T_done = c.vmark()
            pe_marks = []
            pe_half0 = c.c["pe"]
            idx = 0
            for ng in range(4):
                for kc in range(8):
                    buf = idx % 2
                    wap = A(L_WS + buf * 1024, 1024)
                    if idx >= 2:
                        c.wait("sync", "pe", pe_marks[idx - 2])
                    else:
                        c.wait("sync", "pe", pe_half0)
                    ld = c.load(wap, t["w1d"][kc * 128:(kc + 1) * 128, ng * 1024:(ng + 1) * 1024], sl=idx % 2)
                    c.waitv("tensor", ld)
                    c.wait("tensor", "ve", h2T_done)
                    for n in range(8):
                        if kc == 0:
                            c.bank_wait_te(n)
                        i = c.mm(ps[n][:, 0:256], wap[:, n * 128:(n + 1) * 128],
                                 A(L_H2T + kc * 256, 256), start=(kc == 0), stop=(kc == 7))
                    pe_marks.append(c.inc(i, "pe"))
                    idx += 1
                u_done = c.c["pe"]
                for n in range(8):
                    ngl = ng * 8 + n
                    c.wait("scalar", "pe", u_done)
                    c.a("activation", A(L_YT + ngl * 256, 256), ps[n][:, 0:256], AF.Silu,
                        bias=t["fc1bC"][:, ngl:ngl + 1], scale=1.0, inc=True)
                    c.bank_free(n, "ac", c.c["ac"])
            yT_done = c.amark()
            ldrow = c.load(t["rowbuf"][0:1, 0:1024], t["fc2b_row"], sl=4)
            pe_marks2 = []
            for kc in range(32):
                buf = kc % 2
                wap = A(L_WS + buf * 1024, 1024)
                if kc >= 2:
                    c.wait("sync", "pe", pe_marks2[kc - 2])
                else:
                    c.wait("sync", "pe", pe_marks[-1])
                ld = c.load(wap, t["w2d"][kc * 128:(kc + 1) * 128, :], sl=kc % 2)
                c.waitv("tensor", ld)
                c.wait("tensor", "ac", yT_done)
                for jj in range(2):
                    for hf in range(2):
                        bk = jj * 2 + hf
                        if kc == 0:
                            c.bank_wait_te(bk)
                        i = c.mm(ps[bk][:, 0:512], A(L_YT + kc * 256 + jj * 128, 128),
                                 wap[:, hf * 512:(hf + 1) * 512], start=(kc == 0), stop=False)
                if kc == 31:
                    c.waitv("tensor", ldrow)
                    for jj in range(2):
                        for hf in range(2):
                            bk = jj * 2 + hf
                            i = c.mm(ps[bk][:, 0:512], t["onesrow"][0:1, 0:128],
                                     t["rowbuf"][0:1, hf * 512:(hf + 1) * 512], start=False, stop=True)
                pe_marks2.append(c.inc(i, "pe"))
            z_done = c.c["pe"]
            for jj, j in enumerate(hv):
                tt = g * 4 + j
                xin = A(L_XIN + jj * 1024, D)
                row = 256 + tt * 128
                ld = c.load(xin, t["x1s"][row:row + 128, :], sl=2 + jj)
                for hf in range(2):
                    bk = jj * 2 + hf
                    c.wait("vector", "pe", z_done)
                    c.v("scalar_tensor_tensor", out=A(L_LOC + 2048 + hf * 512, 512),
                        in0=ps[bk][:, 0:512], scalar=0.5, in1=A(L_XN1 + j * 1024 + hf * 512, 512),
                        op0=ALU.mult, op1=ALU.add)
                pre_done = c.vmark()
                c.wait("scalar", "ve", pre_done)
                c.a("activation", A(L_LOC + 1024, D), A(L_LOC + 2048, D), AF.Silu, inc=True)
                g2_done = c.c["ac"]
                if _DBG4:
                    c.wait("gpsimd", "ac", g2_done)
                    d1 = c.store(t["dbg_g2"][tt * 128:(tt + 1) * 128, :], A(L_LOC + 1024, D), sl=1)
                    d2 = c.store(t["dbg_s1"][tt * 128:(tt + 1) * 128, :], A(L_S1 + j * 1024, D), sl=1)
                    c.waitv("vector", d2)
                    c.waitv("scalar", d2)
                c.wait("vector", "ac", g2_done)
                c.v("tensor_tensor", out=A(L_LOC + 1024, D), in0=A(L_LOC + 1024, D),
                    in1=A(L_S1 + j * 1024, D), op=ALU.mult)
                c.drain("vector")
                c.waitv("vector", ld)
                c.v("tensor_add", out=A(L_LOC + 1024, D), in0=A(L_LOC + 1024, D), in1=xin, inc=True)
                x2_done = c.c["ve"]
                c.bank_free(jj * 2, "ve", x2_done)
                c.bank_free(jj * 2 + 1, "ve", x2_done)
                c.drain("vector")
                c.vmark()
                c.wait("gpsimd", "ve", c.c["ve"])
                s_ = c.store(t["out"][tt * 128:(tt + 1) * 128, :], A(L_LOC + 1024, D))
                c.waitv("vector", s_)
                c.waitv("scalar", s_)
                c.waitv("sync", s_)

    for g in range(4):
        l1pre(g)
        c.bar()
        l1att(g)
        c.bar()
        l1ffn(g)
        c.bar()

    c.wait_stores("gpsimd")
    c.bar()


def _build():
    nc = bass.Bass("TRN2", target_bir_lowering=False)
    t = {}
    def din(name, shape, dt=F32):
        t[name] = nc.dram_tensor(name, shape, dt, kind="ExternalInput").ap()
    din("x_own", [2048, D])
    din("x_pre2", [512, D])
    din("x_wrap", [512, D])
    din("W_qkv", [D, 3072])
    din("W_out", [D, D])
    din("w_gate", [FF0, D])
    din("w_up", [FF0, D])
    din("w_down", [D, FF0])
    din("Wq", [D, D])
    din("Wk", [HD, D])
    din("Wv", [HD, D])
    din("Wf", [D, D])
    din("fc1_q", [D, FF1], I32)
    din("fc1_s", [8, FF1])
    din("fc2_q", [FF1, D], I32)
    din("fc2_s", [32, D])
    din("bqkv_t", [24, 128])
    din("bqkv_row", [1, 3072])
    din("bout_row", [1, D])
    din("bf_row", [1, D])
    din("fc1b_t", [32, 128])
    din("fc2b_row", [1, D])
    din("gam_row", [1, D])
    din("bet_row", [1, D])
    din("ident_in", [128, 128])
    din("triA_in", [128, 128])
    din("triB_in", [128, 128])
    din("m00_in", [128, 128])
    din("m01_in", [128, 128])
    din("m10_in", [128, 128])
    t["out"] = nc.dram_tensor("out", [2048, D], F32, kind="ExternalOutput").ap()
    if _LIMIT == "prologue":
        for name, shape in (("wgT", [D, FF0]), ("wuT", [D, FF0]), ("wdT", [FF0, D]),
                            ("w1d", [D, FF1]), ("w2d", [FF1, D])):
            t["dbg_" + name] = nc.dram_tensor("dbg_" + name, shape, F32, kind="ExternalOutput").ap()
    if _LIMIT == "L0":
        t["dbg_x1s"] = nc.dram_tensor("dbg_x1s", [2304, D], F32, kind="ExternalOutput").ap()
    if _DBG2:
        t["dbg_s"] = nc.dram_tensor("dbg_s", [2304, D], F32, kind="ExternalOutput").ap()
        t["dbg_g"] = nc.dram_tensor("dbg_g", [2304, D], F32, kind="ExternalOutput").ap()
    if _DBG3:
        t["dbg_xqT"] = nc.dram_tensor("dbg_xqT", [128, 2048], F32, kind="ExternalOutput").ap()
        t["dbg_hqT"] = nc.dram_tensor("dbg_hqT", [128, 5632], F32, kind="ExternalOutput").ap()
    if _DBG4:
        t["dbg_s1"] = nc.dram_tensor("dbg_s1", [2048, D], F32, kind="ExternalOutput").ap()
        t["dbg_g2"] = nc.dram_tensor("dbg_g2", [2048, D], F32, kind="ExternalOutput").ap()
        t["dbg_kT"] = nc.dram_tensor("dbg_kT", [128, 2560], F32, kind="ExternalOutput").ap()
        t["dbg_vv"] = nc.dram_tensor("dbg_vv", [128, 1300], F32, kind="ExternalOutput").ap()
        t["dbg_qT"] = nc.dram_tensor("dbg_qT", [128, 4096], F32, kind="ExternalOutput").ap()
    def dscr(name, shape, dt=F32):
        t[name] = nc.dram_tensor(name, shape, dt, kind="Internal").ap()
    dscr("x1s", [2304, D])
    dscr("wgT", [D, FF0])
    dscr("wuT", [D, FF0])
    dscr("wdT", [FF0, D])
    dscr("w1d", [D, FF1])
    dscr("w2d", [FF1, D])
    dscr("WqT", [D, D])
    dscr("WfT", [D, D])
    dscr("WkT", [D, HD])
    dscr("WvT", [D, HD])

    ctx = contextlib.ExitStack()
    with ctx:
        t["arena"] = ctx.enter_context(nc.sbuf_tensor("arena", [128, ARENA], F32)).ap()
        for name, shape in (
            ("ident", [128, 128]), ("triA", [128, 128]), ("triB", [128, 128]),
            ("m00", [128, 128]), ("m01", [128, 128]), ("m10", [128, 128]),
            ("onesrow", [1, 128]), ("onescol", [128, 1]), ("eps", [128, 1]),
            ("bqkvT_s", [24, 128]), ("fc1bT_s", [32, 128]),
            ("bqkvC", [128, 24]), ("fc1bC", [128, 32]),
            ("stats", [128, 16]), ("qsc", [128, 192]), ("rowbuf", [1, 3072]),
        ):
            t[name] = ctx.enter_context(nc.sbuf_tensor(name, shape, F32))
        t["psum"] = [ctx.enter_context(nc.psum_tensor(f"p{i}", [128, 512], F32)) for i in range(8)]
        sem = {}
        semnames = ("pe", "ve", "ac") + tuple("dl%d" % i for i in range(6)) + ("ds0", "ds1") \
            + tuple("b_" + e for e in ENGINES)
        for name in semnames:
            sem[name] = ctx.enter_context(nc.semaphore(name))
        block = ctx.enter_context(nc.Block())

        def mk(en):
            def f(eng):
                _emit(C(nc, en, eng, sem, t))
            return f
        block.sync(mk("sync"))
        block.tensor(mk("tensor"))
        block.vector(mk("vector"))
        block.scalar(mk("scalar"))
        block.gpsimd(mk("gpsimd"))
    return nc


def _host_inputs(inputs):
    _ = inputs["L1_attn_segment_pos"]
    x = np.ascontiguousarray(np.asarray(inputs["L0_attn_x"], dtype=np.float32))
    i32 = lambda a: np.ascontiguousarray(np.asarray(a, dtype=np.int32))
    f32 = lambda a: np.ascontiguousarray(np.asarray(a, dtype=np.float32))
    kk = np.arange(128)[:, None]
    qq = np.arange(128)[None, :]
    triA = (qq <= kk).astype(np.float32)
    triB = (qq >= kk).astype(np.float32)
    full = np.ones((128, 128), np.float32)
    zero = np.zeros((128, 128), np.float32)
    common = dict(
        W_qkv=f32(inputs["L0_attn_W_qkv"]),
        W_out=f32(inputs["L0_attn_W_out"]),
        w_gate=f32(inputs["L0_ffn_w_gate"]),
        w_up=f32(inputs["L0_ffn_w_up"]),
        w_down=f32(inputs["L0_ffn_w_down"]),
        Wq=f32(inputs["L1_attn_Wq"]),
        Wk=f32(inputs["L1_attn_Wk"]),
        Wv=f32(inputs["L1_attn_Wv"]),
        Wf=f32(inputs["L1_attn_Wf"]),
        fc1_q=i32(inputs["L1_ffn_fc1_q"]),
        fc1_s=f32(inputs["L1_ffn_fc1_scales"]),
        fc2_q=i32(inputs["L1_ffn_fc2_q"]),
        fc2_s=f32(inputs["L1_ffn_fc2_scales"]),
        bqkv_t=f32(inputs["L0_attn_b_qkv"]).reshape(24, 128),
        bqkv_row=f32(inputs["L0_attn_b_qkv"]).reshape(1, 3072),
        bout_row=f32(inputs["L0_attn_b_out"]).reshape(1, D),
        bf_row=f32(inputs["L1_attn_bf"]).reshape(1, D),
        fc1b_t=f32(inputs["L1_ffn_fc1_bias"]).reshape(32, 128),
        fc2b_row=f32(inputs["L1_ffn_fc2_bias"]).reshape(1, D),
        gam_row=f32(inputs["L1_ffn_ln_gamma"]).reshape(1, D),
        bet_row=f32(inputs["L1_ffn_ln_beta"]).reshape(1, D),
        ident_in=np.eye(128, dtype=np.float32),
        triA_in=triA, triB_in=triB,
    )
    in_maps = []
    for cid in range(8):
        b, half = cid // 2, cid % 2
        t0 = half * 2048
        m = dict(common)
        m["x_own"] = np.ascontiguousarray(x[b, t0:t0 + 2048])
        m["x_pre2"] = np.ascontiguousarray(x[b, 1536:2048])
        m["x_wrap"] = np.ascontiguousarray(x[b, 2048:2560] if half == 0 else x[b, 0:512])
        if half == 0:
            m["m00_in"], m["m01_in"], m["m10_in"] = zero, zero, zero
        else:
            m["m00_in"], m["m01_in"], m["m10_in"] = triA, full, triA
        in_maps.append(m)
    return x, in_maps


def kernel(**inputs):
    x, in_maps = _host_inputs(inputs)
    nc = _build()
    res = run_bass_kernel_spmd(nc, in_maps, core_ids=list(range(8)))
    out = np.empty((4, 4096, D), dtype=np.float32)
    for cid in range(8):
        b, half = cid // 2, cid % 2
        out[b, half * 2048:(half + 1) * 2048] = res.results[cid]["out"]
    return out


# version 03
# speedup vs baseline: 1.0763x; 1.0613x over previous
import contextlib
import numpy as np

import concourse.bass as bass
import concourse.mybir as mybir
from concourse.bass_utils import run_bass_kernel_spmd

F32 = mybir.dt.float32
BF16 = mybir.dt.bfloat16
I32 = mybir.dt.int32
AF = mybir.ActivationFunctionType
ALU = mybir.AluOpType
AX = mybir.AxisListType.X

D = 1024
NH, HD = 16, 64
FF0 = 2816
FF1 = 4096
SCALE = 0.125
MID = [0.25, 0.75, 1.25, 1.75, 2.5, 3.5, 5.0]
DG = [0.5, 0.5, 0.5, 0.5, 1.0, 1.0, 2.0]
N0CH = [512, 512, 512, 512, 512, 256]
ENGINES = ("sync", "tensor", "vector", "scalar", "gpsimd")

_DBG = False
_DBG2 = False
_DBG3 = False
_DBG4 = False
_LIMIT = None

ARENA = 43840
A_KT = 0
A_VV = 4096
A_Q = (8256, 12352)
A_AO = 16448
A_S = 20544
A_P = 24640
P_XNT = 26688
P_XSQ = 30784
P_XN = 31808
P_XIN = 32832
P_WS = 34880
W_AOT = 26688
W_WS = 30784
F_HQT = 26688
F_XQT = 32320
F_XIN = 34368
F_WS = 36416
F_WS2 = 36416
F_XN = 38464
F_SCR = 39488
F_H = 42048
F_HQ = 42560
F_H1 = A_P + 1024
F_HQ1 = A_P + 1536
Q_CB = [F_SCR + 1536 + i * 256 for i in range(4)] + [A_P + i * 256 for i in range(3)]
F_G = F_XN
G_WIN = 26688
G_WQ = 30784
G_STG = 33600
G_WS = 34624
G_SROW = 38720
L_XN1 = 0
L_XN1T = 4096
L_QT = 8192
L_KT = 12288
L_VV = 14848
L_S1 = 16192
L_H2T = 20288
L_P = 22336
L_YT = 24384
L_GBC = 32576
L_LOC = 34624
L_WS = 38720
L_WKV = 40768
L_AOT2 = 40768
L_XIN = 41792

assert L_XIN + 2048 <= ARENA


class C:

    def __init__(self, nc, en, eng, sem, t):
        self.nc, self.en, self.eng, self.sem, self.t = nc, en, eng, sem, t
        self.c = {k: 0 for k in sem}
        self.nbar = 0
        self.bankw = [None] * 8

    def _do(self, en, f):
        if self.en == en:
            return f(self.eng)
        return None

    def on(self, en):
        return self.en == en

    def inc(self, instr, s, k=1):
        self.c[s] += k
        if instr is not None:
            instr.then_inc(self.sem[s], k)
        return self.c[s]

    def wait(self, en, s, val=None):
        if val is None:
            val = self.c[s]
        if self.en == en:
            self.eng.wait_ge(self.sem[s], val)

    def drain(self, en):
        if self.en == en:
            self.eng.drain()

    def vmark(self):
        s = self.t["stats"]
        return self.inc(self._do("vector", lambda e: e.tensor_copy(out=s[:, 7:8], in_=s[:, 7:8])), "ve")

    def amark(self):
        s = self.t["stats"]
        return self.inc(self._do("scalar", lambda e: e.copy(s[:, 7:8], s[:, 7:8])), "ac")

    def bar(self):
        self.nbar += 1
        for e in ENGINES:
            if self.en == e:
                self.eng.drain().then_inc(self.sem["b_" + e], 1)
        for e in ENGINES:
            if e != self.en:
                self.wait(self.en, "b_" + e, self.nbar)
        for e in ENGINES:
            self.c["b_" + e] = self.nbar

    def load(self, out_ap, in_ap, sl=0):
        s = "dl%d" % sl
        i = self._do("sync", lambda e: e.dma_start(out=out_ap, in_=in_ap))
        return (s, self.inc(i, s, 16))

    def store(self, out_ap, in_ap, sl=0):
        s = "ds%d" % sl
        i = self._do("gpsimd", lambda e: e.dma_start(out=out_ap, in_=in_ap))
        return (s, self.inc(i, s, 16))

    def waitv(self, en, tok):
        if self.en == en:
            self.eng.wait_ge(self.sem[tok[0]], tok[1])

    def wait_stores(self, en):
        for s in ("ds0", "ds1"):
            self.wait(en, s, self.c[s])

    def mm(self, out, lhsT, rhs, start, stop, tile_position=None):
        return self._do("tensor", lambda e: e.matmul(out, lhsT, rhs, start=start, stop=stop,
                                                     tile_position=tile_position))

    def tp(self, out, in_, np_):
        return self._do("tensor", lambda e: e.transpose(out, in_, self.t["ident"][0:np_, 0:np_]))

    def v(self, meth, *a, inc=False, **kw):
        i = self._do("vector", lambda e: getattr(e, meth)(*a, **kw))
        if inc:
            return self.inc(i, "ve")
        return i

    def a(self, meth, *a, inc=False, **kw):
        i = self._do("scalar", lambda e: getattr(e, meth)(*a, **kw))
        if inc:
            return self.inc(i, "ac")
        return i

    def bank_wait_te(self, b):
        if self.bankw[b] is not None:
            s, v = self.bankw[b]
            self.wait("tensor", s, v)
            self.bankw[b] = None

    def bank_free(self, b, s, v):
        self.bankw[b] = (s, v)


def _emit(c):
    t = c.t
    ps = t["psum"]
    ar = t["arena"]

    def A(off, n, p0=0, part=128):
        return ar[p0:p0 + part, off:off + n]

    st = t["stats"]

    def ln_tile(src, dst, xsq, n=D):
        c.wait("scalar", "ve", c.c["ve"])
        c.a("activation", xsq, src, AF.Square, inc=True)
        sq_done = c.c["ac"]
        c.v("reduce_sum", out=st[:, 0:1], in_=src, axis=AX)
        c.drain("vector")
        c.v("tensor_scalar", out=st[:, 0:1], in0=st[:, 0:1], scalar1=1.0 / n, scalar2=None, op0=ALU.mult)
        c.drain("vector")
        c.wait("vector", "ac", sq_done)
        c.v("reduce_sum", out=st[:, 1:2], in_=xsq, axis=AX)
        c.drain("vector")
        c.v("tensor_scalar", out=st[:, 1:2], in0=st[:, 1:2], scalar1=1.0 / n, scalar2=None, op0=ALU.mult)
        c.v("tensor_scalar", out=st[:, 2:3], in0=st[:, 0:1], scalar1=st[:, 0:1], scalar2=None, op0=ALU.mult)
        c.drain("vector")
        c.v("tensor_sub", out=st[:, 3:4], in0=st[:, 1:2], in1=st[:, 2:3], inc=True)
        var_done = c.c["ve"]
        c.wait("scalar", "ve", var_done)
        c.a("activation", st[:, 4:5], st[:, 3:4], AF.Sqrt, bias=t["eps"][:, 0:1], scale=1.0, inc=True)
        sd_done = c.c["ac"]
        c.wait("vector", "ac", sd_done)
        c.v("reciprocal", out=st[:, 5:6], in_=st[:, 4:5])
        c.drain("vector")
        c.v("tensor_scalar", out=dst, in0=src, scalar1=st[:, 0:1], scalar2=st[:, 5:6],
            op0=ALU.subtract, op1=ALU.mult)
        c.drain("vector")

    def fp4_quant(src, dst, n):
        assert n <= 512
        nb = n // 16
        tt = A(F_SCR, n)
        sg = A(F_SCR + 512, n)
        at = A(F_SCR + 1024, n)
        cb = [ar[0:128, o:o + (n + 1) // 2].bitcast(BF16) for o in Q_CB]
        qsc = t["qsc"]
        bmv, scv, rsv = qsc[:, 0:nb], qsc[:, 64:64 + nb], qsc[:, 128:128 + nb]
        src3 = src.rearrange("p (g k) -> p g k", k=16)
        dst3 = dst.rearrange("p (g k) -> p g k", k=16)
        t3 = tt.rearrange("p (g k) -> p g k", k=16)
        c.v("reduce_max", out=bmv, in_=src3, axis=AX, apply_absolute_value=True)
        c.drain("vector")
        c.v("tensor_scalar", out=scv, in0=bmv, scalar1=1.0 / 6.0, scalar2=1e-12, op0=ALU.mult, op1=ALU.max)
        c.drain("vector")
        c.v("reciprocal", out=rsv, in_=scv)
        c.drain("vector")
        c.v("tensor_tensor", out=t3, in0=src3, in1=rsv.unsqueeze(2).broadcast_to([128, nb, 16]),
            op=ALU.mult, inc=True)
        t_done = c.c["ve"]
        c.wait("scalar", "ve", t_done)
        c.a("activation", at, tt, AF.Abs)
        c.a("activation", sg, tt, AF.Sign, inc=True)
        absgn_done = c.c["ac"]
        c.wait("vector", "ac", absgn_done)
        for j in range(7):
            c.v("tensor_scalar", out=cb[j], in0=at, scalar1=MID[j], scalar2=DG[j],
                op0=ALU.is_gt, op1=ALU.mult)
        c.drain("vector")
        c.v("tensor_add", out=cb[0], in0=cb[0], in1=cb[1])
        c.v("tensor_add", out=cb[2], in0=cb[2], in1=cb[3])
        c.v("tensor_add", out=cb[4], in0=cb[4], in1=cb[5])
        c.drain("vector")
        c.v("tensor_add", out=cb[0], in0=cb[0], in1=cb[2])
        c.v("tensor_add", out=cb[4], in0=cb[4], in1=cb[6])
        c.drain("vector")
        c.v("tensor_add", out=cb[0], in0=cb[0], in1=cb[4])
        c.drain("vector")
        c.v("tensor_tensor", out=tt, in0=sg, in1=cb[0], op=ALU.mult)
        c.drain("vector")
        c.v("tensor_tensor", out=dst3, in0=t3, in1=scv.unsqueeze(2).broadcast_to([128, nb, 16]),
            op=ALU.mult)
        c.drain("vector")

    def transpose_1024(src_off, dst_off, j, banks=(6, 7), src_stride=128):
        for half in range(2):
            bk = banks[half]
            c.bank_wait_te(bk)
            for k in range(4):
                kk = half * 4 + k
                i = c.tp(ps[bk][:, k * 128:(k + 1) * 128], A(src_off + kk * 128, 128), 128)
            pe_v = c.inc(i, "pe")
            c.wait("vector", "pe", pe_v)
            for k in range(4):
                kk = half * 4 + k
                c.v("tensor_copy", out=A(dst_off + kk * 512 + j * 128, 128),
                    in_=ps[bk][:, k * 128:(k + 1) * 128], inc=(k == 3))
            c.bank_free(bk, "ve", c.c["ve"])

    for name in ("ident", "triA", "triB", "m00", "m01", "m10"):
        c.load(t[name][:], t[name + "_in"], sl=4)
    c.load(t["bqkvT_s"][:], t["bqkv_t"], sl=4)
    ld0 = c.load(t["fc1bT_s"][:], t["fc1b_t"], sl=4)
    if c.on("vector"):
        c.eng.memset(t["onesrow"][:], 1.0)
        c.eng.memset(t["onescol"][:], 1.0)
        c.eng.memset(t["eps"][:], 1e-5)
        vc = ar[0:128, A_VV:A_VV + 4 * NH * 65].rearrange("p (cell w) -> p cell w", w=65)
        c.eng.memset(vc[:, :, 64:65], 1.0)
    c.vmark()
    c.waitv("tensor", ld0)
    c.tp(ps[0][:, 0:24], t["bqkvT_s"][:], 24)
    i = c.tp(ps[1][:, 0:32], t["fc1bT_s"][:], 32)
    c.inc(i, "pe")
    c.wait("vector", "pe", c.c["pe"])
    c.v("tensor_copy", out=t["bqkvC"][:], in_=ps[0][:, 0:24])
    c.v("tensor_copy", out=t["fc1bC"][:], in_=ps[1][:, 0:32], inc=True)
    c.bar()

    for (wsrc, wdst, nrows, ncols) in (
        (t["w_gate"], t["wgT"], FF0, D),
        (t["w_up"], t["wuT"], FF0, D),
        (t["w_down"], t["wdT"], D, FF0),
    ):
        for r in range(nrows // 128):
            win = A(G_WIN, ncols)
            c.wait_stores("sync")
            ld = c.load(win, wsrc[r * 128:(r + 1) * 128, :], sl=0)
            c.waitv("vector", ld)
            for hh in range((ncols + 511) // 512):
                n = min(512, ncols - hh * 512)
                fp4_quant(win[:, hh * 512:hh * 512 + n], A(G_WQ + hh * 512, n), n)
            q_done = c.vmark()
            for k in range(ncols // 128):
                bk = k % 4
                c.bank_wait_te(bk)
                c.wait("tensor", "ve", q_done)
                i = c.tp(ps[bk][:, 0:128], A(G_WQ + k * 128, 128), 128)
                pe_v = c.inc(i, "pe")
                c.wait("vector", "pe", pe_v)
                c.v("tensor_copy", out=A(G_STG, 128), in_=ps[bk][:, 0:128], inc=True)
                ev = c.c["ve"]
                c.bank_free(bk, "ve", ev)
                c.wait("gpsimd", "ve", ev)
                s_ = c.store(wdst[k * 128:(k + 1) * 128, r * 128:(r + 1) * 128], A(G_STG, 128))
                c.waitv("vector", s_)
    c.bar()

    for (qsrc, ssrc, wdst, N, ng) in (
        (t["fc1_q"], t["fc1_s"], t["w1d"], FF1, 8),
        (t["fc2_q"], t["fc2_s"], t["w2d"], D, 32),
    ):
        for g in range(ng):
            qin = A(G_WIN, N).bitcast(I32)
            ld = c.load(qin, qsrc[g * 128:(g + 1) * 128, :], sl=0)
            lds = c.load(ar[0:1, G_SROW:G_SROW + N], ssrc[g:g + 1, :], sl=4)
            c.waitv("vector", ld)
            c.v("tensor_scalar", out=A(G_WS, N), in0=qin, scalar1=128.0, scalar2=None,
                op0=ALU.subtract, inc=True)
            qf_done = c.c["ve"]
            for ch in range(N // 512):
                bk = ch % 2
                c.bank_wait_te(bk)
                c.waitv("tensor", lds)
                i = c.mm(ps[bk][:, 0:512], t["onesrow"][0:1, 0:128],
                         ar[0:1, G_SROW + ch * 512:G_SROW + (ch + 1) * 512], start=True, stop=True)
                pe_v = c.inc(i, "pe")
                c.wait("vector", "pe", pe_v)
                c.wait("vector", "ve", qf_done)
                c.v("tensor_tensor", out=A(G_WIN + ch * 512, 512), in0=A(G_WS + ch * 512, 512),
                    in1=ps[bk][:, 0:512], op=ALU.mult, inc=True)
                c.bank_free(bk, "ve", c.c["ve"])
            wrow = c.vmark()
            c.wait("gpsimd", "ve", wrow)
            s_ = c.store(wdst[g * 128:(g + 1) * 128, :], A(G_WIN, N))
            c.waitv("vector", s_)
            c.waitv("sync", s_)
    for (wsrc, wdst) in ((t["Wq"], t["WqT"]), (t["Wf"], t["WfT"])):
        for r in range(8):
            ld = c.load(A(G_WIN, D), wsrc[r * 128:(r + 1) * 128, :], sl=0)
            for k in range(8):
                bk = k % 4
                c.bank_wait_te(bk)
                c.waitv("tensor", ld)
                i = c.tp(ps[bk][:, 0:128], A(G_WIN + k * 128, 128), 128)
                pe_v = c.inc(i, "pe")
                c.wait("vector", "pe", pe_v)
                c.v("tensor_copy", out=A(G_WS + k * 128, 128), in_=ps[bk][:, 0:128], inc=True)
                c.bank_free(bk, "ve", c.c["ve"])
            done = c.vmark()
            c.wait("gpsimd", "ve", done)
            for a in range(8):
                s_ = c.store(wdst[a * 128:(a + 1) * 128, r * 128:(r + 1) * 128],
                             A(G_WS + a * 128, 128))
            c.waitv("vector", s_)
            c.waitv("sync", s_)
    if _LIMIT == "prologue":
        c.bar()
        for name in ("wgT", "wuT", "wdT", "w1d", "w2d"):
            c.store(t["dbg_" + name], t[name], sl=1)
        c.wait_stores("gpsimd")
        c.bar()
        return
    for (wsrc, wdst) in ((t["Wk"], t["WkT"]), (t["Wv"], t["WvT"])):
        ld = c.load(A(G_WIN, D, part=64), wsrc, sl=0)
        for k in range(8):
            bk = k % 4
            c.bank_wait_te(bk)
            c.waitv("tensor", ld)
            i = c.tp(ps[bk][:, 0:64], A(G_WIN + k * 128, 128, part=64), 64)
            pe_v = c.inc(i, "pe")
            c.wait("vector", "pe", pe_v)
            c.v("tensor_copy", out=A(G_WS + k * 64, 64), in_=ps[bk][:, 0:64], inc=True)
            c.bank_free(bk, "ve", c.c["ve"])
        done = c.vmark()
        c.wait("gpsimd", "ve", done)
        for a in range(8):
            s_ = c.store(wdst[a * 128:(a + 1) * 128, :], A(G_WS + a * 64, 64))
        c.waitv("vector", s_)
        c.waitv("sync", s_)
    c.bar()

    blocks = [
        dict(name="pre2", xsrc=t["x_pre2"], xoff=0, ffn=[2, 3], qt=[2, 3], x1row=-512, q=True),
        dict(name="own0", xsrc=t["x_own"], xoff=0, ffn=[0, 1, 2, 3], qt=[0, 1, 2, 3], x1row=0, q=True),
        dict(name="own1", xsrc=t["x_own"], xoff=512, ffn=[0, 1, 2, 3], qt=[0, 1, 2, 3], x1row=512, q=True),
        dict(name="own2", xsrc=t["x_own"], xoff=1024, ffn=[0, 1, 2, 3], qt=[0, 1, 2, 3], x1row=1024, q=True),
        dict(name="own3", xsrc=t["x_own"], xoff=1536, ffn=[0, 1, 2, 3], qt=[0, 1, 2, 3], x1row=1536, q=True),
        dict(name="wrap", xsrc=t["x_wrap"], xoff=0, ffn=[], qt=[], x1row=None, q=False),
    ]

    def kT_ap(hh, j0, nj):
        return A(A_KT + (hh // 2) * 512 + j0, nj, p0=(hh % 2) * 64, part=64)

    def qT_ap(slot, hh, q0, nq):
        return A(A_Q[slot] + (hh // 2) * 512 + q0, nq, p0=(hh % 2) * 64, part=64)

    def vv_ap(jc, hh):
        return A(A_VV + (jc * NH + hh) * 65, 65)

    def prekv(b, qslot):
        ln_marks = []
        for j in range(4):
            xin = A(P_XIN + (j % 2) * 1024, D)
            if j >= 2:
                c.wait("sync", "ve", ln_marks[j - 2])
            ld = c.load(xin, b["xsrc"][b["xoff"] + j * 128:b["xoff"] + (j + 1) * 128, :], sl=2 + (j % 2))
            c.waitv("vector", ld)
            c.waitv("scalar", ld)
            ln_tile(xin, A(P_XN, D), A(P_XSQ, D))
            ln_done = c.vmark()
            ln_marks.append(ln_done)
            c.wait("tensor", "ve", ln_done)
            transpose_1024(P_XN, P_XNT, j)
        xnT_done = c.vmark()
        ldrow = c.load(t["rowbuf"][0:1, 0:3072], t["bqkv_row"], sl=4)
        groups = ([0] if b["q"] else []) + [1, 2]
        pe_marks = []
        for gi, grp in enumerate(groups):
            for kc in range(8):
                idx = gi * 8 + kc
                buf = idx % 2
                wap = A(P_WS + buf * 1024, 1024)
                if idx >= 2:
                    c.wait("sync", "pe", pe_marks[idx - 2])
                ld = c.load(wap, t["W_qkv"][kc * 128:(kc + 1) * 128, grp * 1024:(grp + 1) * 1024], sl=idx % 2)
                c.waitv("tensor", ld)
                c.wait("tensor", "ve", xnT_done)
                if grp in (0, 1):
                    for n in range(8):
                        if kc == 0:
                            c.bank_wait_te(n)
                        i = c.mm(ps[n][:, 0:512], wap[:, n * 128:(n + 1) * 128],
                                 A(P_XNT + kc * 512, 512), start=(kc == 0), stop=(kc == 7))
                else:
                    for j in range(4):
                        for hf in range(2):
                            bk = j * 2 + hf
                            if kc == 0:
                                c.bank_wait_te(bk)
                            i = c.mm(ps[bk][:, 0:512], A(P_XNT + kc * 512 + j * 128, 128),
                                     wap[:, hf * 512:(hf + 1) * 512], start=(kc == 0), stop=False)
                    if kc == 7:
                        c.waitv("tensor", ldrow)
                        for j in range(4):
                            for hf in range(2):
                                bk = j * 2 + hf
                                i = c.mm(ps[bk][:, 0:512], t["onesrow"][0:1, 0:128],
                                         t["rowbuf"][0:1, 2048 + hf * 512:2048 + (hf + 1) * 512],
                                         start=False, stop=True)
                pe_marks.append(c.inc(i, "pe"))
            mm_done = c.c["pe"]
            if grp == 0:
                for n in range(8):
                    c.wait("vector", "pe", mm_done)
                    c.v("tensor_scalar", out=A(A_Q[qslot] + n * 512, 512), in0=ps[n][:, 0:512],
                        scalar1=t["bqkvC"][:, n:n + 1], scalar2=None, op0=ALU.add, inc=True)
                    c.bank_free(n, "ve", c.c["ve"])
            elif grp == 1:
                for n in range(8):
                    c.wait("vector", "pe", mm_done)
                    c.v("tensor_scalar", out=A(A_KT + n * 512, 512), in0=ps[n][:, 0:512],
                        scalar1=t["bqkvC"][:, 8 + n:9 + n], scalar2=None, op0=ALU.add, inc=True)
                    c.bank_free(n, "ve", c.c["ve"])
            else:
                for j in range(4):
                    for hf in range(2):
                        bk = j * 2 + hf
                        c.wait("vector", "pe", mm_done)
                        dst = ar[0:128, A_VV + (j * NH + hf * 8) * 65:A_VV + (j * NH + hf * 8) * 65 + 8 * 65]
                        c.v("tensor_copy", out=dst.rearrange("p (h w) -> p h w", w=65)[:, :, 0:64],
                            in_=ps[bk][:, 0:512].rearrange("p (h w) -> p h w", w=64), inc=True)
                        c.bank_free(bk, "ve", c.c["ve"])
        c.vmark()

    def att_phase(b, qslot, phase):
        qtiles = b["qt"]
        for hh in range(NH):
            exp_done = {}
            for jc in range(4):
                bk = jc
                c.bank_wait_te(bk)
                i = c.mm(ps[bk][:, 0:512], kT_ap(hh, jc * 128, 128), qT_ap(qslot, hh, 0, 512),
                         start=True, stop=True)
                s_done = c.inc(i, "pe")
                c.wait("scalar", "pe", s_done)
                c.a("activation", A(A_P + jc * 512, 512), ps[bk][:, 0:512], AF.Exp,
                    scale=SCALE, inc=True)
                exp_done[jc] = c.c["ac"]
                c.bank_free(bk, "ac", exp_done[jc])
            ob = 4 + (hh % 2)
            c.bank_wait_te(ob)
            for qt in qtiles:
                for jc in range(4):
                    c.wait("tensor", "ac", exp_done[jc])
                    i = c.mm(ps[ob][:, qt * 65:qt * 65 + 65],
                             A(A_P + jc * 512 + qt * 128, 128), vv_ap(jc, hh),
                             start=(jc == 0), stop=(jc == 3))
            o_done = c.inc(i, "pe")
            c.wait("vector", "pe", o_done)
            for qt in qtiles:
                c.v("reciprocal", out=st[:, 6:7], in_=ps[ob][:, qt * 65 + 64:qt * 65 + 65])
                c.drain("vector")
                ao = A(A_AO + qt * 1024 + hh * 64, 64)
                if phase == 1:
                    c.v("tensor_scalar", out=ao, in0=ps[ob][:, qt * 65:qt * 65 + 64],
                        scalar1=st[:, 6:7], scalar2=None, op0=ALU.mult)
                else:
                    c.v("scalar_tensor_tensor", out=ao, in0=ps[ob][:, qt * 65:qt * 65 + 64],
                        scalar=st[:, 6:7], in1=ao, op0=ALU.mult, op1=ALU.add)
                c.drain("vector")
            c.vmark()
            c.bank_free(ob, "ve", c.c["ve"])

    def wout(b):
        qtiles = b["qt"]
        for j in qtiles:
            transpose_1024(A_AO + j * 1024, W_AOT, j)
        aoT_done = c.vmark()
        ldrow = c.load(t["rowbuf"][0:1, 0:1024], t["bout_row"], sl=4)
        pe_marks = []
        for kc in range(8):
            wap = A(W_WS + (kc % 2) * 1024, 1024)
            if kc >= 2:
                c.wait("sync", "pe", pe_marks[kc - 2])
            ld = c.load(wap, t["W_out"][kc * 128:(kc + 1) * 128, :], sl=kc % 2)
            c.waitv("tensor", ld)
            c.wait("tensor", "ve", aoT_done)
            for ji, j in enumerate(qtiles):
                for hf in range(2):
                    bk = ji * 2 + hf
                    if kc == 0:
                        c.bank_wait_te(bk)
                    i = c.mm(ps[bk][:, 0:512], A(W_AOT + kc * 512 + j * 128, 128),
                             wap[:, hf * 512:(hf + 1) * 512], start=(kc == 0), stop=False)
            if kc == 7:
                c.waitv("tensor", ldrow)
                for ji, j in enumerate(qtiles):
                    for hf in range(2):
                        bk = ji * 2 + hf
                        i = c.mm(ps[bk][:, 0:512], t["onesrow"][0:1, 0:128],
                                 t["rowbuf"][0:1, hf * 512:(hf + 1) * 512], start=False, stop=True)
            pe_marks.append(c.inc(i, "pe"))
        s_done = c.c["pe"]
        for ji, j in enumerate(qtiles):
            for hf in range(2):
                bk = ji * 2 + hf
                c.wait("vector", "pe", s_done)
                c.v("tensor_copy", out=A(A_S + j * 1024 + hf * 512, 512), in_=ps[bk][:, 0:512], inc=True)
                c.bank_free(bk, "ve", c.c["ve"])
        c.vmark()

    def ffnc(b):
        tiles = b["ffn"]
        for hv_i in range(0, len(tiles), 2):
            hv = tiles[hv_i:hv_i + 2]
            for jj, j in enumerate(hv):
                xin = A(F_XIN + jj * 1024, D)
                ld = c.load(xin, b["xsrc"][b["xoff"] + j * 128:b["xoff"] + (j + 1) * 128, :], sl=2 + jj)
                c.waitv("vector", ld)
                c.waitv("scalar", ld)
                ln_tile(xin, A(F_XN, D), A(F_SCR, D))
                for hf in range(2):
                    fp4_quant(A(F_XN + hf * 512, 512), A(F_HQ, 512), 512)
                    qdone = c.vmark()
                    bk = 6
                    c.bank_wait_te(bk)
                    c.wait("tensor", "ve", qdone)
                    for k in range(4):
                        i = c.tp(ps[bk][:, k * 128:(k + 1) * 128], A(F_HQ + k * 128, 128), 128)
                    pe_v = c.inc(i, "pe")
                    c.wait("vector", "pe", pe_v)
                    for k in range(4):
                        kk = hf * 4 + k
                        c.v("tensor_copy", out=A(F_XQT + (jj * 8 + kk) * 128, 128),
                            in_=ps[bk][:, k * 128:(k + 1) * 128], inc=(k == 3))
                    c.bank_free(bk, "ve", c.c["ve"])
            xq_done = c.vmark()
            if _DBG3 and b["name"] == "own0" and hv_i == 0:
                c.wait("gpsimd", "ve", xq_done)
                dx = c.store(t["dbg_xqT"], A(F_XQT, 2048), sl=1)
                c.waitv("sync", dx)
            pe_marks = []
            pe_half0 = c.c["pe"]
            h_free = {0: c.c["ve"], 1: 0}
            idx = 0
            for nchi, n in enumerate(N0CH):
                ncol0 = nchi * 512
                for kc in range(8):
                    buf = idx % 2
                    wg = A(F_WS + buf * 1024, n)
                    wu = A(F_WS + buf * 1024 + 512, n)
                    if idx >= 2:
                        c.wait("sync", "pe", pe_marks[idx - 2])
                    else:
                        c.wait("sync", "pe", pe_half0)
                    ldg = c.load(wg, t["wgT"][kc * 128:(kc + 1) * 128, ncol0:ncol0 + n], sl=idx % 2)
                    ldu = c.load(wu, t["wuT"][kc * 128:(kc + 1) * 128, ncol0:ncol0 + n], sl=idx % 2)
                    c.waitv("tensor", ldu)
                    c.wait("tensor", "ve", xq_done)
                    for jj in range(len(hv)):
                        bg, by = jj * 2, jj * 2 + 1
                        if kc == 0:
                            c.bank_wait_te(bg)
                            c.bank_wait_te(by)
                        c.mm(ps[bg][:, 0:n], A(F_XQT + (jj * 8 + kc) * 128, 128), wg,
                             start=(kc == 0), stop=(kc == 7))
                        i = c.mm(ps[by][:, 0:n], A(F_XQT + (jj * 8 + kc) * 128, 128), wu,
                                 start=(kc == 0), stop=(kc == 7))
                    pe_marks.append(c.inc(i, "pe"))
                    idx += 1
                gu_done = c.c["pe"]
                for jj in range(len(hv)):
                    bg, by = jj * 2, jj * 2 + 1
                    p2 = (nchi * len(hv) + jj) % 2
                    FH = F_H if p2 == 0 else F_H1
                    FQ = F_HQ if p2 == 0 else F_HQ1
                    c.wait("scalar", "pe", gu_done)
                    c.wait("scalar", "ve", h_free[p2])
                    c.a("activation", A(FH, n), ps[bg][:, 0:n], AF.Silu, inc=True)
                    hs_done = c.c["ac"]
                    c.bank_free(bg, "ac", hs_done)
                    c.wait("vector", "ac", hs_done)
                    c.v("tensor_tensor", out=A(FH, n), in0=A(FH, n), in1=ps[by][:, 0:n],
                        op=ALU.mult, inc=True)
                    c.bank_free(by, "ve", c.c["ve"])
                    c.drain("vector")
                    fp4_quant(A(FH, n), A(FQ, n), n)
                    hq_done = c.vmark()
                    h_free[p2] = hq_done
                    bk = 6 + jj
                    c.bank_wait_te(bk)
                    c.wait("tensor", "ve", hq_done)
                    nk = n // 128
                    for k in range(nk):
                        i = c.tp(ps[bk][:, k * 128:(k + 1) * 128], A(FQ + k * 128, 128), 128)
                    pe_v = c.inc(i, "pe")
                    c.wait("vector", "pe", pe_v)
                    for k in range(nk):
                        fc = nchi * 4 + k
                        c.v("tensor_copy", out=A(F_HQT + fc * 256 + jj * 128, 128),
                            in_=ps[bk][:, k * 128:(k + 1) * 128], inc=(k == nk - 1))
                    c.bank_free(bk, "ve", c.c["ve"])
            hqT_done = c.vmark()
            if _DBG3 and b["name"] == "own0" and hv_i == 0:
                c.wait("gpsimd", "ve", hqT_done)
                dh = c.store(t["dbg_hqT"], A(F_HQT, 5632), sl=1)
                c.waitv("sync", dh)
            pe_marks2 = []
            for fc in range(22):
                buf = fc % 2
                wd = A(F_WS2 + buf * 1024, 1024)
                if fc >= 2:
                    c.wait("sync", "pe", pe_marks2[fc - 2])
                else:
                    c.wait("sync", "pe", pe_marks[-1])
                ld = c.load(wd, t["wdT"][fc * 128:(fc + 1) * 128, :], sl=fc % 2)
                c.waitv("tensor", ld)
                c.wait("tensor", "ve", hqT_done)
                for jj in range(len(hv)):
                    for hf in range(2):
                        bk = jj * 2 + hf
                        if fc == 0:
                            c.bank_wait_te(bk)
                        i = c.mm(ps[bk][:, 0:512], A(F_HQT + fc * 256 + jj * 128, 128),
                                 wd[:, hf * 512:(hf + 1) * 512], start=(fc == 0), stop=(fc == 21))
                pe_marks2.append(c.inc(i, "pe"))
            z_done = c.c["pe"]
            for jj, j in enumerate(hv):
                xin = A(F_XIN + jj * 1024, D)
                for hf in range(2):
                    bk = jj * 2 + hf
                    c.wait("scalar", "pe", z_done)
                    c.a("activation", A(F_G + hf * 512, 512), ps[bk][:, 0:512], AF.Silu, inc=True)
                g_done = c.c["ac"]
                if _DBG2:
                    row = 256 + b["x1row"] + j * 128
                    c.wait("gpsimd", "ac", g_done)
                    sg1 = c.store(t["dbg_g"][row:row + 128, :], A(F_G, D), sl=1)
                    sg2 = c.store(t["dbg_s"][row:row + 128, :], A(A_S + j * 1024, D), sl=1)
                    c.waitv("vector", sg2)
                    c.waitv("scalar", sg2)
                c.wait("vector", "ac", g_done)
                c.v("tensor_tensor", out=A(F_G, D), in0=A(F_G, D), in1=A(A_S + j * 1024, D), op=ALU.mult)
                c.drain("vector")
                c.v("tensor_add", out=A(F_G, D), in0=A(F_G, D), in1=xin, inc=True)
                x1_done = c.c["ve"]
                c.bank_free(jj * 2, "ve", x1_done)
                c.bank_free(jj * 2 + 1, "ve", x1_done)
                c.drain("vector")
                c.vmark()
                c.wait("gpsimd", "ve", c.c["ve"])
                row = 256 + b["x1row"] + j * 128
                s_ = c.store(t["x1s"][row:row + 128, :], A(F_G, D))
                c.waitv("vector", s_)
                c.waitv("scalar", s_)
                c.waitv("sync", s_)

    prekv(blocks[0], 0)
    c.bar()
    att_phase(blocks[0], 0, 1)
    c.bar()
    prekv(blocks[1], 1)
    c.bar()
    att_phase(blocks[0], 0, 2)
    wout(blocks[0])
    c.bar()
    ffnc(blocks[0])
    c.bar()
    qsl = {1: 1, 2: 0, 3: 1, 4: 0}
    for bi in range(1, 5):
        att_phase(blocks[bi], qsl[bi], 1)
        c.bar()
        prekv(blocks[bi + 1], qsl[bi + 1] if bi + 1 < 5 else 0)
        c.bar()
        att_phase(blocks[bi], qsl[bi], 2)
        wout(blocks[bi])
        c.bar()
        ffnc(blocks[bi])
        c.bar()

    if _LIMIT == "L0":
        c.bar()
        c.store(t["dbg_x1s"], t["x1s"], sl=1)
        c.wait_stores("gpsimd")
        c.bar()
        return

    def load_wkv():
        ld1 = c.load(A(L_WKV, 512).rearrange("p (a b) -> p a b", b=64),
                     t["WkT"].rearrange("(a p) b -> p a b", p=128), sl=5)
        ld2 = c.load(A(L_WKV + 512, 512).rearrange("p (a b) -> p a b", b=64),
                     t["WvT"].rearrange("(a p) b -> p a b", p=128), sl=5)
        return ld2

    ldg = c.load(t["rowbuf"][0:1, 0:1024], t["gam_row"], sl=4)
    ldb = c.load(t["rowbuf"][0:1, 1024:2048], t["bet_row"], sl=4)
    for gb in range(2):
        for ch in range(2):
            bk = gb * 2 + ch
            c.bank_wait_te(bk)
            c.waitv("tensor", ldb)
            i = c.mm(ps[bk][:, 0:512], t["onesrow"][0:1, 0:128],
                     t["rowbuf"][0:1, gb * 1024 + ch * 512:gb * 1024 + (ch + 1) * 512],
                     start=True, stop=True)
            pe_v = c.inc(i, "pe")
            c.wait("vector", "pe", pe_v)
            c.v("tensor_copy", out=A(L_GBC + gb * 1024 + ch * 512, 512), in_=ps[bk][:, 0:512], inc=True)
            c.bank_free(bk, "ve", c.c["ve"])
    if c.on("vector"):
        vc = ar[0:128, L_VV:L_VV + 20 * 65].rearrange("p (cell w) -> p cell w", w=65)
        c.eng.memset(vc[:, :, 64:65], 1.0)
    c.vmark()
    c.bar()

    def l1_kv_tiles(ch0, ntile, xn1T_off):
        wkv_ld = load_wkv()
        c.waitv("tensor", wkv_ld)
        c.bank_wait_te(4)
        for kc in range(8):
            i = c.mm(ps[4][0:64, 0:ntile * 128], A(L_WKV + kc * 64, 64),
                     A(xn1T_off + kc * 512, ntile * 128), start=(kc == 0), stop=(kc == 7))
        c.inc(i, "pe")
        for kc in range(8):
            i = c.mm(ps[4][64:128, 0:ntile * 128], A(L_WKV + kc * 64, 64),
                     A(xn1T_off + kc * 512, ntile * 128), start=(kc == 0), stop=(kc == 7),
                     tile_position=(0, 64))
        c.inc(i, "pe")
        c.bank_wait_te(5)
        for kc in range(8):
            for j in range(ntile):
                i = c.mm(ps[5][:, j * 64:(j + 1) * 64], A(xn1T_off + kc * 512 + j * 128, 128),
                         A(L_WKV + 512 + kc * 64, 64), start=(kc == 0 and j == 0), stop=(kc == 7))
        c.inc(i, "pe")
        kv_done = c.c["pe"]
        c.wait("vector", "pe", kv_done)
        c.v("tensor_copy", out=A(L_KT + ch0 * 128, ntile * 128), in_=ps[4][:, 0:ntile * 128])
        for j in range(ntile):
            c.v("tensor_copy", out=A(L_VV + (ch0 + j) * 65, 64), in_=ps[5][:, j * 64:(j + 1) * 64],
                inc=(j == ntile - 1))
        c.bank_free(4, "ve", c.c["ve"])
        c.bank_free(5, "ve", c.c["ve"])
        c.vmark()

    for j in range(2):
        xin = A(L_XIN + j * 1024, D)
        ld = c.load(xin, t["x1s"][j * 128:(j + 1) * 128, :], sl=2 + j)
        c.waitv("vector", ld)
        c.waitv("scalar", ld)
        ln_tile(xin, A(L_LOC, D), A(L_LOC + 1024, D))
        ln_done = c.vmark()
        c.wait("tensor", "ve", ln_done)
        transpose_1024(L_LOC, L_XN1T, j)
    c.vmark()
    l1_kv_tiles(0, 2, L_XN1T)
    c.bar()

    def l1pre(g):
        ln_marks = []
        for j in range(4):
            xin = A(L_XIN + (j % 2) * 1024, D)
            row = 256 + (g * 4 + j) * 128
            if j >= 2:
                c.wait("sync", "ve", ln_marks[j - 2])
            ld = c.load(xin, t["x1s"][row:row + 128, :], sl=2 + (j % 2))
            c.waitv("vector", ld)
            c.waitv("scalar", ld)
            ln_tile(xin, A(L_XN1 + j * 1024, D), A(L_LOC, D))
            ln_done = c.vmark()
            ln_marks.append(ln_done)
            c.wait("tensor", "ve", ln_done)
            transpose_1024(L_XN1 + j * 1024, L_XN1T, j)
        xn1T_done = c.vmark()
        pe_marks = []
        for kc in range(8):
            buf = kc % 2
            wap = A(L_WS + buf * 1024, 1024)
            if kc >= 2:
                c.wait("sync", "pe", pe_marks[kc - 2])
            ld = c.load(wap, t["WqT"][kc * 128:(kc + 1) * 128, :], sl=kc % 2)
            c.waitv("tensor", ld)
            c.wait("tensor", "ve", xn1T_done)
            for n in range(8):
                if kc == 0:
                    c.bank_wait_te(n)
                i = c.mm(ps[n][:, 0:512], wap[:, n * 128:(n + 1) * 128],
                         A(L_XN1T + kc * 512, 512), start=(kc == 0), stop=(kc == 7))
            pe_marks.append(c.inc(i, "pe"))
        q_done = c.c["pe"]
        for n in range(8):
            c.wait("vector", "pe", q_done)
            c.v("tensor_copy", out=A(L_QT + n * 512, 512), in_=ps[n][:, 0:512], inc=True)
            c.bank_free(n, "ve", c.c["ve"])
        c.vmark()
        l1_kv_tiles(2 + g * 4, 4, L_XN1T)

    def l1att(g):
        if _DBG4 and g == 0:
            c.wait("gpsimd", "ve", c.c["ve"])
            d1 = c.store(t["dbg_kT"], A(L_KT, 2560), sl=1)
            d2 = c.store(t["dbg_vv"], A(L_VV, 1300), sl=1)
            d3 = c.store(t["dbg_qT"], A(L_QT, 4096), sl=1)
            c.waitv("sync", d3)
        for j in range(4):
            tt = g * 4 + j
            if tt == 0:
                msks = [t["m00"], t["m01"], None]
            elif tt == 1:
                msks = [t["m10"], None, None]
            else:
                msks = [t["triA"], None, None]
            for hh in range(NH):
                sb = hh % 2
                c.bank_wait_te(sb)
                for jc in range(3):
                    i = c.mm(ps[sb][:, jc * 128:(jc + 1) * 128],
                             A(L_KT + (tt + jc) * 128, 128, p0=(hh % 2) * 64, part=64),
                             A(L_QT + (hh // 2) * 512 + j * 128, 128, p0=(hh % 2) * 64, part=64),
                             start=True, stop=True)
                s_done = c.inc(i, "pe")
                c.wait("scalar", "pe", s_done)
                pb = L_P + (hh % 2) * 384
                c.a("activation", A(pb, 384), ps[sb][:, 0:384], AF.Exp, scale=SCALE, inc=True)
                exp_done = c.c["ac"]
                c.bank_free(sb, "ac", exp_done)
                c.wait("vector", "ac", exp_done)
                c.v("tensor_tensor", out=A(pb, 128), in0=A(pb, 128), in1=msks[0][:], op=ALU.mult)
                if msks[1] is not None:
                    c.v("tensor_tensor", out=A(pb + 128, 128), in0=A(pb + 128, 128), in1=msks[1][:], op=ALU.mult)
                c.v("tensor_tensor", out=A(pb + 256, 128), in0=A(pb + 256, 128), in1=t["triB"][:], op=ALU.mult)
                c.drain("vector")
                p_done = c.vmark()
                ob = 4 + (hh % 2)
                c.bank_wait_te(ob)
                c.wait("tensor", "ve", p_done)
                for jc in range(3):
                    i = c.mm(ps[ob][:, 0:65], A(pb + jc * 128, 128), A(L_VV + (tt + jc) * 65, 65),
                             start=(jc == 0), stop=(jc == 2))
                o_done = c.inc(i, "pe")
                c.wait("vector", "pe", o_done)
                c.v("reciprocal", out=st[:, 6:7], in_=ps[ob][:, 64:65])
                c.drain("vector")
                c.v("tensor_scalar", out=A(L_AOT2 + hh * 64, 64), in0=ps[ob][:, 0:64],
                    scalar1=st[:, 6:7], scalar2=None, op0=ALU.mult, inc=True)
                c.drain("vector")
                c.bank_free(ob, "ve", c.c["ve"])
            ao_done = c.vmark()
            c.wait("tensor", "ve", ao_done)
            transpose_1024(L_AOT2, L_LOC, j)
        aoT_done = c.vmark()
        ldrow = c.load(t["rowbuf"][0:1, 0:1024], t["bf_row"], sl=4)
        pe_marks = []
        for kc in range(8):
            buf = kc % 2
            wap = A(L_WS + buf * 1024, 1024)
            if kc >= 2:
                c.wait("sync", "pe", pe_marks[kc - 2])
            ld = c.load(wap, t["WfT"][kc * 128:(kc + 1) * 128, :], sl=kc % 2)
            c.waitv("tensor", ld)
            c.wait("tensor", "ve", aoT_done)
            for j in range(4):
                for hf in range(2):
                    bk = j * 2 + hf
                    if kc == 0:
                        c.bank_wait_te(bk)
                    i = c.mm(ps[bk][:, 0:512], A(L_LOC + kc * 512 + j * 128, 128),
                             wap[:, hf * 512:(hf + 1) * 512], start=(kc == 0), stop=False)
            if kc == 7:
                c.waitv("tensor", ldrow)
                for j in range(4):
                    for hf in range(2):
                        bk = j * 2 + hf
                        i = c.mm(ps[bk][:, 0:512], t["onesrow"][0:1, 0:128],
                                 t["rowbuf"][0:1, hf * 512:(hf + 1) * 512], start=False, stop=True)
            pe_marks.append(c.inc(i, "pe"))
        s_done = c.c["pe"]
        for j in range(4):
            for hf in range(2):
                bk = j * 2 + hf
                c.wait("vector", "pe", s_done)
                c.v("tensor_copy", out=A(L_S1 + j * 1024 + hf * 512, 512), in_=ps[bk][:, 0:512], inc=True)
                c.bank_free(bk, "ve", c.c["ve"])
        c.vmark()

    def l1ffn(g):
        for half in range(2):
            hv = [half * 2, half * 2 + 1]
            for jj, j in enumerate(hv):
                ln_tile(A(L_XN1 + j * 1024, D), A(L_LOC + 1024, D), A(L_LOC, D))
                c.v("tensor_tensor", out=A(L_LOC + 1024, D), in0=A(L_LOC + 1024, D),
                    in1=A(L_GBC, D), op=ALU.mult)
                c.drain("vector")
                c.v("tensor_add", out=A(L_LOC + 1024, D), in0=A(L_LOC + 1024, D), in1=A(L_GBC + 1024, D))
                c.drain("vector")
                h2_done = c.vmark()
                c.wait("tensor", "ve", h2_done)
                for hlf in range(2):
                    bk = 6 + hlf
                    c.bank_wait_te(bk)
                    for k in range(4):
                        kk = hlf * 4 + k
                        i = c.tp(ps[bk][:, k * 128:(k + 1) * 128], A(L_LOC + 1024 + kk * 128, 128), 128)
                    pe_v = c.inc(i, "pe")
                    c.wait("vector", "pe", pe_v)
                    for k in range(4):
                        kk = hlf * 4 + k
                        c.v("tensor_copy", out=A(L_H2T + kk * 256 + jj * 128, 128),
                            in_=ps[bk][:, k * 128:(k + 1) * 128], inc=(k == 3))
                    c.bank_free(bk, "ve", c.c["ve"])
            h2T_done = c.vmark()
            pe_marks = []
            pe_half0 = c.c["pe"]
            idx = 0
            for ng in range(4):
                for kc in range(8):
                    buf = idx % 2
                    wap = A(L_WS + buf * 1024, 1024)
                    if idx >= 2:
                        c.wait("sync", "pe", pe_marks[idx - 2])
                    else:
                        c.wait("sync", "pe", pe_half0)
                    ld = c.load(wap, t["w1d"][kc * 128:(kc + 1) * 128, ng * 1024:(ng + 1) * 1024], sl=idx % 2)
                    c.waitv("tensor", ld)
                    c.wait("tensor", "ve", h2T_done)
                    for n in range(8):
                        if kc == 0:
                            c.bank_wait_te(n)
                        i = c.mm(ps[n][:, 0:256], wap[:, n * 128:(n + 1) * 128],
                                 A(L_H2T + kc * 256, 256), start=(kc == 0), stop=(kc == 7))
                    pe_marks.append(c.inc(i, "pe"))
                    idx += 1
                u_done = c.c["pe"]
                for n in range(8):
                    ngl = ng * 8 + n
                    c.wait("scalar", "pe", u_done)
                    c.a("activation", A(L_YT + ngl * 256, 256), ps[n][:, 0:256], AF.Silu,
                        bias=t["fc1bC"][:, ngl:ngl + 1], scale=1.0, inc=True)
                    c.bank_free(n, "ac", c.c["ac"])
            yT_done = c.amark()
            ldrow = c.load(t["rowbuf"][0:1, 0:1024], t["fc2b_row"], sl=4)
            pe_marks2 = []
            for kc in range(32):
                buf = kc % 2
                wap = A(L_WS + buf * 1024, 1024)
                if kc >= 2:
                    c.wait("sync", "pe", pe_marks2[kc - 2])
                else:
                    c.wait("sync", "pe", pe_marks[-1])
                ld = c.load(wap, t["w2d"][kc * 128:(kc + 1) * 128, :], sl=kc % 2)
                c.waitv("tensor", ld)
                c.wait("tensor", "ac", yT_done)
                for jj in range(2):
                    for hf in range(2):
                        bk = jj * 2 + hf
                        if kc == 0:
                            c.bank_wait_te(bk)
                        i = c.mm(ps[bk][:, 0:512], A(L_YT + kc * 256 + jj * 128, 128),
                                 wap[:, hf * 512:(hf + 1) * 512], start=(kc == 0), stop=False)
                if kc == 31:
                    c.waitv("tensor", ldrow)
                    for jj in range(2):
                        for hf in range(2):
                            bk = jj * 2 + hf
                            i = c.mm(ps[bk][:, 0:512], t["onesrow"][0:1, 0:128],
                                     t["rowbuf"][0:1, hf * 512:(hf + 1) * 512], start=False, stop=True)
                pe_marks2.append(c.inc(i, "pe"))
            z_done = c.c["pe"]
            for jj, j in enumerate(hv):
                tt = g * 4 + j
                xin = A(L_XIN + jj * 1024, D)
                row = 256 + tt * 128
                ld = c.load(xin, t["x1s"][row:row + 128, :], sl=2 + jj)
                for hf in range(2):
                    bk = jj * 2 + hf
                    c.wait("vector", "pe", z_done)
                    c.v("scalar_tensor_tensor", out=A(L_LOC + 2048 + hf * 512, 512),
                        in0=ps[bk][:, 0:512], scalar=0.5, in1=A(L_XN1 + j * 1024 + hf * 512, 512),
                        op0=ALU.mult, op1=ALU.add)
                pre_done = c.vmark()
                c.wait("scalar", "ve", pre_done)
                c.a("activation", A(L_LOC + 1024, D), A(L_LOC + 2048, D), AF.Silu, inc=True)
                g2_done = c.c["ac"]
                if _DBG4:
                    c.wait("gpsimd", "ac", g2_done)
                    d1 = c.store(t["dbg_g2"][tt * 128:(tt + 1) * 128, :], A(L_LOC + 1024, D), sl=1)
                    d2 = c.store(t["dbg_s1"][tt * 128:(tt + 1) * 128, :], A(L_S1 + j * 1024, D), sl=1)
                    c.waitv("vector", d2)
                    c.waitv("scalar", d2)
                c.wait("vector", "ac", g2_done)
                c.v("tensor_tensor", out=A(L_LOC + 1024, D), in0=A(L_LOC + 1024, D),
                    in1=A(L_S1 + j * 1024, D), op=ALU.mult)
                c.drain("vector")
                c.waitv("vector", ld)
                c.v("tensor_add", out=A(L_LOC + 1024, D), in0=A(L_LOC + 1024, D), in1=xin, inc=True)
                x2_done = c.c["ve"]
                c.bank_free(jj * 2, "ve", x2_done)
                c.bank_free(jj * 2 + 1, "ve", x2_done)
                c.drain("vector")
                c.vmark()
                c.wait("gpsimd", "ve", c.c["ve"])
                s_ = c.store(t["out"][tt * 128:(tt + 1) * 128, :], A(L_LOC + 1024, D))
                c.waitv("vector", s_)
                c.waitv("scalar", s_)
                c.waitv("sync", s_)

    for g in range(4):
        l1pre(g)
        c.bar()
        l1att(g)
        c.bar()
        l1ffn(g)
        c.bar()

    c.wait_stores("gpsimd")
    c.bar()


def _build():
    nc = bass.Bass("TRN2", target_bir_lowering=False)
    t = {}
    def din(name, shape, dt=F32):
        t[name] = nc.dram_tensor(name, shape, dt, kind="ExternalInput").ap()
    din("x_own", [2048, D])
    din("x_pre2", [512, D])
    din("x_wrap", [512, D])
    din("W_qkv", [D, 3072])
    din("W_out", [D, D])
    din("w_gate", [FF0, D])
    din("w_up", [FF0, D])
    din("w_down", [D, FF0])
    din("Wq", [D, D])
    din("Wk", [HD, D])
    din("Wv", [HD, D])
    din("Wf", [D, D])
    din("fc1_q", [D, FF1], I32)
    din("fc1_s", [8, FF1])
    din("fc2_q", [FF1, D], I32)
    din("fc2_s", [32, D])
    din("bqkv_t", [24, 128])
    din("bqkv_row", [1, 3072])
    din("bout_row", [1, D])
    din("bf_row", [1, D])
    din("fc1b_t", [32, 128])
    din("fc2b_row", [1, D])
    din("gam_row", [1, D])
    din("bet_row", [1, D])
    din("ident_in", [128, 128])
    din("triA_in", [128, 128])
    din("triB_in", [128, 128])
    din("m00_in", [128, 128])
    din("m01_in", [128, 128])
    din("m10_in", [128, 128])
    t["out"] = nc.dram_tensor("out", [2048, D], F32, kind="ExternalOutput").ap()
    if _LIMIT == "prologue":
        for name, shape in (("wgT", [D, FF0]), ("wuT", [D, FF0]), ("wdT", [FF0, D]),
                            ("w1d", [D, FF1]), ("w2d", [FF1, D])):
            t["dbg_" + name] = nc.dram_tensor("dbg_" + name, shape, F32, kind="ExternalOutput").ap()
    if _LIMIT == "L0":
        t["dbg_x1s"] = nc.dram_tensor("dbg_x1s", [2304, D], F32, kind="ExternalOutput").ap()
    if _DBG2:
        t["dbg_s"] = nc.dram_tensor("dbg_s", [2304, D], F32, kind="ExternalOutput").ap()
        t["dbg_g"] = nc.dram_tensor("dbg_g", [2304, D], F32, kind="ExternalOutput").ap()
    if _DBG3:
        t["dbg_xqT"] = nc.dram_tensor("dbg_xqT", [128, 2048], F32, kind="ExternalOutput").ap()
        t["dbg_hqT"] = nc.dram_tensor("dbg_hqT", [128, 5632], F32, kind="ExternalOutput").ap()
    if _DBG4:
        t["dbg_s1"] = nc.dram_tensor("dbg_s1", [2048, D], F32, kind="ExternalOutput").ap()
        t["dbg_g2"] = nc.dram_tensor("dbg_g2", [2048, D], F32, kind="ExternalOutput").ap()
        t["dbg_kT"] = nc.dram_tensor("dbg_kT", [128, 2560], F32, kind="ExternalOutput").ap()
        t["dbg_vv"] = nc.dram_tensor("dbg_vv", [128, 1300], F32, kind="ExternalOutput").ap()
        t["dbg_qT"] = nc.dram_tensor("dbg_qT", [128, 4096], F32, kind="ExternalOutput").ap()
    def dscr(name, shape, dt=F32):
        t[name] = nc.dram_tensor(name, shape, dt, kind="Internal").ap()
    dscr("x1s", [2304, D])
    dscr("wgT", [D, FF0])
    dscr("wuT", [D, FF0])
    dscr("wdT", [FF0, D])
    dscr("w1d", [D, FF1])
    dscr("w2d", [FF1, D])
    dscr("WqT", [D, D])
    dscr("WfT", [D, D])
    dscr("WkT", [D, HD])
    dscr("WvT", [D, HD])

    ctx = contextlib.ExitStack()
    with ctx:
        t["arena"] = ctx.enter_context(nc.sbuf_tensor("arena", [128, ARENA], F32)).ap()
        for name, shape in (
            ("ident", [128, 128]), ("triA", [128, 128]), ("triB", [128, 128]),
            ("m00", [128, 128]), ("m01", [128, 128]), ("m10", [128, 128]),
            ("onesrow", [1, 128]), ("onescol", [128, 1]), ("eps", [128, 1]),
            ("bqkvT_s", [24, 128]), ("fc1bT_s", [32, 128]),
            ("bqkvC", [128, 24]), ("fc1bC", [128, 32]),
            ("stats", [128, 16]), ("qsc", [128, 192]), ("rowbuf", [1, 3072]),
        ):
            t[name] = ctx.enter_context(nc.sbuf_tensor(name, shape, F32))
        t["psum"] = [ctx.enter_context(nc.psum_tensor(f"p{i}", [128, 512], F32)) for i in range(8)]
        sem = {}
        semnames = ("pe", "ve", "ac") + tuple("dl%d" % i for i in range(6)) + ("ds0", "ds1") \
            + tuple("b_" + e for e in ENGINES)
        for name in semnames:
            sem[name] = ctx.enter_context(nc.semaphore(name))
        block = ctx.enter_context(nc.Block())

        def mk(en):
            def f(eng):
                _emit(C(nc, en, eng, sem, t))
            return f
        block.sync(mk("sync"))
        block.tensor(mk("tensor"))
        block.vector(mk("vector"))
        block.scalar(mk("scalar"))
        block.gpsimd(mk("gpsimd"))
    return nc


def _host_inputs(inputs):
    _ = inputs["L1_attn_segment_pos"]
    x = np.ascontiguousarray(np.asarray(inputs["L0_attn_x"], dtype=np.float32))
    i32 = lambda a: np.ascontiguousarray(np.asarray(a, dtype=np.int32))
    f32 = lambda a: np.ascontiguousarray(np.asarray(a, dtype=np.float32))
    kk = np.arange(128)[:, None]
    qq = np.arange(128)[None, :]
    triA = (qq <= kk).astype(np.float32)
    triB = (qq >= kk).astype(np.float32)
    full = np.ones((128, 128), np.float32)
    zero = np.zeros((128, 128), np.float32)
    common = dict(
        W_qkv=f32(inputs["L0_attn_W_qkv"]),
        W_out=f32(inputs["L0_attn_W_out"]),
        w_gate=f32(inputs["L0_ffn_w_gate"]),
        w_up=f32(inputs["L0_ffn_w_up"]),
        w_down=f32(inputs["L0_ffn_w_down"]),
        Wq=f32(inputs["L1_attn_Wq"]),
        Wk=f32(inputs["L1_attn_Wk"]),
        Wv=f32(inputs["L1_attn_Wv"]),
        Wf=f32(inputs["L1_attn_Wf"]),
        fc1_q=i32(inputs["L1_ffn_fc1_q"]),
        fc1_s=f32(inputs["L1_ffn_fc1_scales"]),
        fc2_q=i32(inputs["L1_ffn_fc2_q"]),
        fc2_s=f32(inputs["L1_ffn_fc2_scales"]),
        bqkv_t=f32(inputs["L0_attn_b_qkv"]).reshape(24, 128),
        bqkv_row=f32(inputs["L0_attn_b_qkv"]).reshape(1, 3072),
        bout_row=f32(inputs["L0_attn_b_out"]).reshape(1, D),
        bf_row=f32(inputs["L1_attn_bf"]).reshape(1, D),
        fc1b_t=f32(inputs["L1_ffn_fc1_bias"]).reshape(32, 128),
        fc2b_row=f32(inputs["L1_ffn_fc2_bias"]).reshape(1, D),
        gam_row=f32(inputs["L1_ffn_ln_gamma"]).reshape(1, D),
        bet_row=f32(inputs["L1_ffn_ln_beta"]).reshape(1, D),
        ident_in=np.eye(128, dtype=np.float32),
        triA_in=triA, triB_in=triB,
    )
    in_maps = []
    for cid in range(8):
        b, half = cid // 2, cid % 2
        t0 = half * 2048
        m = dict(common)
        m["x_own"] = np.ascontiguousarray(x[b, t0:t0 + 2048])
        m["x_pre2"] = np.ascontiguousarray(x[b, 1536:2048])
        m["x_wrap"] = np.ascontiguousarray(x[b, 2048:2560] if half == 0 else x[b, 0:512])
        if half == 0:
            m["m00_in"], m["m01_in"], m["m10_in"] = zero, zero, zero
        else:
            m["m00_in"], m["m01_in"], m["m10_in"] = triA, full, triA
        in_maps.append(m)
    return x, in_maps


def kernel(**inputs):
    x, in_maps = _host_inputs(inputs)
    nc = _build()
    res = run_bass_kernel_spmd(nc, in_maps, core_ids=list(range(8)))
    out = np.empty((4, 4096, D), dtype=np.float32)
    for cid in range(8):
        b, half = cid // 2, cid % 2
        out[b, half * 2048:(half + 1) * 2048] = res.results[cid]["out"]
    return out


# version 04
# speedup vs baseline: 1.1401x; 1.0593x over previous
import contextlib
import numpy as np

import concourse.bass as bass
import concourse.mybir as mybir
from concourse.bass_utils import run_bass_kernel_spmd

F32 = mybir.dt.float32
BF16 = mybir.dt.bfloat16
I32 = mybir.dt.int32
AF = mybir.ActivationFunctionType
ALU = mybir.AluOpType
AX = mybir.AxisListType.X

D = 1024
NH, HD = 16, 64
FF0 = 2816
FF1 = 4096
SCALE = 0.125
MID = [0.25, 0.75, 1.25, 1.75, 2.5, 3.5, 5.0]
DG = [0.5, 0.5, 0.5, 0.5, 1.0, 1.0, 2.0]
N0CH = [512, 512, 512, 512, 512, 256]
ENGINES = ("sync", "tensor", "vector", "scalar", "gpsimd")

_DBG = False
_DBG2 = False
_DBG3 = False
_DBG4 = False
_LIMIT = None

ARENA = 43840
A_KT = 0
A_VV = 4096
A_Q = (8256, 12352)
A_AO = 16448
A_S = 20544
A_P = 24640
P_XNT = 26688
P_XSQ = 30784
P_XN = 31808
P_XIN = 32832
P_WS = 34880
W_AOT = 26688
W_WS = 30784
F_HQT = 26688
F_XQT = 32320
F_XIN = 34368
F_WS = 36416
F_WS2 = 36416
F_XN = 38464
F_SCR = 39488
F_H = 42048
F_HQ = 42560
F_H1 = A_P + 1024
F_HQ1 = A_P + 1536
Q_CB = [F_SCR + 1536 + i * 256 for i in range(4)] + [A_P + i * 256 for i in range(3)]
F_G = F_XN
G_WIN = 26688
G_WQ = 30784
G_STG = 33600
G_WS = 34624
G_SROW = 38720
L_XN1 = 0
L_XN1T = 4096
L_QT = 8192
L_KT = 12288
L_VV = 14848
L_S1 = 16192
L_H2T = 20288
L_P = 22336
L_YT = 24384
L_GBC = 32576
L_LOC = 34624
L_WS = 38720
L_WKV = 40768
L_AOT2 = 40768
L_XIN = 41792

assert L_XIN + 2048 <= ARENA


class C:

    def __init__(self, nc, en, eng, sem, t):
        self.nc, self.en, self.eng, self.sem, self.t = nc, en, eng, sem, t
        self.c = {k: 0 for k in sem}
        self.nbar = 0
        self.bankw = [None] * 8

    def _do(self, en, f):
        if self.en == en:
            return f(self.eng)
        return None

    def on(self, en):
        return self.en == en

    def inc(self, instr, s, k=1):
        self.c[s] += k
        if instr is not None:
            instr.then_inc(self.sem[s], k)
        return self.c[s]

    def wait(self, en, s, val=None):
        if val is None:
            val = self.c[s]
        if self.en == en:
            self.eng.wait_ge(self.sem[s], val)

    def drain(self, en):
        if self.en == en:
            self.eng.drain()

    def vmark(self):
        s = self.t["stats"]
        return self.inc(self._do("vector", lambda e: e.tensor_copy(out=s[:, 7:8], in_=s[:, 7:8])), "ve")

    def amark(self):
        s = self.t["stats"]
        return self.inc(self._do("scalar", lambda e: e.copy(s[:, 7:8], s[:, 7:8])), "ac")

    def bar(self):
        self.nbar += 1
        for e in ENGINES:
            if self.en == e:
                self.eng.drain().then_inc(self.sem["b_" + e], 1)
        for e in ENGINES:
            if e != self.en:
                self.wait(self.en, "b_" + e, self.nbar)
        for e in ENGINES:
            self.c["b_" + e] = self.nbar

    def load(self, out_ap, in_ap, sl=0):
        s = "dl%d" % sl
        i = self._do("sync", lambda e: e.dma_start(out=out_ap, in_=in_ap))
        return (s, self.inc(i, s, 16))

    def store(self, out_ap, in_ap, sl=0):
        s = "ds%d" % sl
        i = self._do("gpsimd", lambda e: e.dma_start(out=out_ap, in_=in_ap))
        return (s, self.inc(i, s, 16))

    def waitv(self, en, tok):
        if self.en == en:
            self.eng.wait_ge(self.sem[tok[0]], tok[1])

    def wait_stores(self, en):
        for s in ("ds0", "ds1"):
            self.wait(en, s, self.c[s])

    def mm(self, out, lhsT, rhs, start, stop, tile_position=None):
        return self._do("tensor", lambda e: e.matmul(out, lhsT, rhs, start=start, stop=stop,
                                                     tile_position=tile_position))

    def tp(self, out, in_, np_):
        return self._do("tensor", lambda e: e.transpose(out, in_, self.t["ident"][0:np_, 0:np_]))

    def v(self, meth, *a, inc=False, **kw):
        i = self._do("vector", lambda e: getattr(e, meth)(*a, **kw))
        if inc:
            return self.inc(i, "ve")
        return i

    def a(self, meth, *a, inc=False, **kw):
        i = self._do("scalar", lambda e: getattr(e, meth)(*a, **kw))
        if inc:
            return self.inc(i, "ac")
        return i

    def bank_wait_te(self, b):
        if self.bankw[b] is not None:
            s, v = self.bankw[b]
            self.wait("tensor", s, v)
            self.bankw[b] = None

    def bank_free(self, b, s, v):
        self.bankw[b] = (s, v)


def _emit(c):
    t = c.t
    ps = t["psum"]
    ar = t["arena"]

    def A(off, n, p0=0, part=128):
        return ar[p0:p0 + part, off:off + n]

    st = t["stats"]

    def ln_tile(src, dst, xsq, n=D):
        c.wait("scalar", "ve", c.c["ve"])
        c.a("activation", xsq, src, AF.Square, inc=True)
        sq_done = c.c["ac"]
        c.v("reduce_sum", out=st[:, 0:1], in_=src, axis=AX)
        c.drain("vector")
        c.v("tensor_scalar", out=st[:, 0:1], in0=st[:, 0:1], scalar1=1.0 / n, scalar2=None, op0=ALU.mult)
        c.drain("vector")
        c.wait("vector", "ac", sq_done)
        c.v("reduce_sum", out=st[:, 1:2], in_=xsq, axis=AX)
        c.drain("vector")
        c.v("tensor_scalar", out=st[:, 1:2], in0=st[:, 1:2], scalar1=1.0 / n, scalar2=None, op0=ALU.mult)
        c.v("tensor_scalar", out=st[:, 2:3], in0=st[:, 0:1], scalar1=st[:, 0:1], scalar2=None, op0=ALU.mult)
        c.drain("vector")
        c.v("tensor_sub", out=st[:, 3:4], in0=st[:, 1:2], in1=st[:, 2:3], inc=True)
        var_done = c.c["ve"]
        c.wait("scalar", "ve", var_done)
        c.a("activation", st[:, 4:5], st[:, 3:4], AF.Sqrt, bias=t["eps"][:, 0:1], scale=1.0, inc=True)
        sd_done = c.c["ac"]
        c.wait("vector", "ac", sd_done)
        c.v("reciprocal", out=st[:, 5:6], in_=st[:, 4:5])
        c.drain("vector")
        c.v("tensor_scalar", out=dst, in0=src, scalar1=st[:, 0:1], scalar2=st[:, 5:6],
            op0=ALU.subtract, op1=ALU.mult)
        c.drain("vector")

    def fp4_quant(src, dst, n):
        assert n <= 512
        nb = n // 16
        tt = A(F_SCR, n)
        sg = A(F_SCR + 512, n)
        at = A(F_SCR + 1024, n)
        cb = [ar[0:128, o:o + (n + 1) // 2].bitcast(BF16) for o in Q_CB]
        qsc = t["qsc"]
        bmv, scv, rsv = qsc[:, 0:nb], qsc[:, 64:64 + nb], qsc[:, 128:128 + nb]
        src3 = src.rearrange("p (g k) -> p g k", k=16)
        dst3 = dst.rearrange("p (g k) -> p g k", k=16)
        t3 = tt.rearrange("p (g k) -> p g k", k=16)
        c.v("reduce_max", out=bmv, in_=src3, axis=AX, apply_absolute_value=True)
        c.drain("vector")
        c.v("tensor_scalar", out=scv, in0=bmv, scalar1=1.0 / 6.0, scalar2=1e-12, op0=ALU.mult, op1=ALU.max)
        c.drain("vector")
        c.v("reciprocal", out=rsv, in_=scv)
        c.drain("vector")
        c.v("tensor_tensor", out=t3, in0=src3, in1=rsv.unsqueeze(2).broadcast_to([128, nb, 16]),
            op=ALU.mult, inc=True)
        t_done = c.c["ve"]
        c.wait("scalar", "ve", t_done)
        c.a("activation", at, tt, AF.Abs)
        c.a("activation", sg, tt, AF.Sign, inc=True)
        absgn_done = c.c["ac"]
        c.wait("vector", "ac", absgn_done)
        for j in range(7):
            c.v("tensor_scalar", out=cb[j], in0=at, scalar1=MID[j], scalar2=DG[j],
                op0=ALU.is_gt, op1=ALU.mult)
        c.drain("vector")
        c.v("tensor_add", out=cb[0], in0=cb[0], in1=cb[1])
        c.v("tensor_add", out=cb[2], in0=cb[2], in1=cb[3])
        c.v("tensor_add", out=cb[4], in0=cb[4], in1=cb[5])
        c.drain("vector")
        c.v("tensor_add", out=cb[0], in0=cb[0], in1=cb[2])
        c.v("tensor_add", out=cb[4], in0=cb[4], in1=cb[6])
        c.drain("vector")
        c.v("tensor_add", out=cb[0], in0=cb[0], in1=cb[4])
        c.drain("vector")
        c.v("tensor_tensor", out=tt, in0=sg, in1=cb[0], op=ALU.mult)
        c.drain("vector")
        c.v("tensor_tensor", out=dst3, in0=t3, in1=scv.unsqueeze(2).broadcast_to([128, nb, 16]),
            op=ALU.mult)
        c.drain("vector")

    def transpose_1024(src_off, dst_off, j, banks=(6, 7), src_stride=128):
        for half in range(2):
            bk = banks[half]
            c.bank_wait_te(bk)
            for k in range(4):
                kk = half * 4 + k
                i = c.tp(ps[bk][:, k * 128:(k + 1) * 128], A(src_off + kk * 128, 128), 128)
            pe_v = c.inc(i, "pe")
            c.wait("vector", "pe", pe_v)
            for k in range(4):
                kk = half * 4 + k
                c.v("tensor_copy", out=A(dst_off + kk * 512 + j * 128, 128),
                    in_=ps[bk][:, k * 128:(k + 1) * 128], inc=(k == 3))
            c.bank_free(bk, "ve", c.c["ve"])

    for name in ("ident", "triA", "triB", "m00", "m01", "m10"):
        c.load(t[name][:], t[name + "_in"], sl=4)
    c.load(t["bqkvT_s"][:], t["bqkv_t"], sl=4)
    ld0 = c.load(t["fc1bT_s"][:], t["fc1b_t"], sl=4)
    if c.on("vector"):
        c.eng.memset(t["onesrow"][:], 1.0)
        c.eng.memset(t["onescol"][:], 1.0)
        c.eng.memset(t["eps"][:], 1e-5)
        vc = ar[0:128, A_VV:A_VV + 4 * NH * 65].rearrange("p (cell w) -> p cell w", w=65)
        c.eng.memset(vc[:, :, 64:65], 1.0)
    c.vmark()
    c.waitv("tensor", ld0)
    c.tp(ps[0][:, 0:24], t["bqkvT_s"][:], 24)
    i = c.tp(ps[1][:, 0:32], t["fc1bT_s"][:], 32)
    c.inc(i, "pe")
    c.wait("vector", "pe", c.c["pe"])
    c.v("tensor_copy", out=t["bqkvC"][:], in_=ps[0][:, 0:24])
    c.v("tensor_copy", out=t["fc1bC"][:], in_=ps[1][:, 0:32], inc=True)
    c.bar()

    for (wsrc, wdst, nrows, ncols) in (
        (t["w_gate"], t["wgT"], FF0, D),
        (t["w_up"], t["wuT"], FF0, D),
        (t["w_down"], t["wdT"], D, FF0),
    ):
        for r in range(nrows // 128):
            win = A(G_WIN, ncols)
            c.wait_stores("sync")
            ld = c.load(win, wsrc[r * 128:(r + 1) * 128, :], sl=0)
            c.waitv("vector", ld)
            for hh in range((ncols + 511) // 512):
                n = min(512, ncols - hh * 512)
                fp4_quant(win[:, hh * 512:hh * 512 + n], A(G_WQ + hh * 512, n), n)
            q_done = c.vmark()
            nct = ncols // 128
            for k in range(nct):
                bk = k % 4
                c.bank_wait_te(bk)
                c.wait("tensor", "ve", q_done)
                i = c.tp(ps[bk][:, 0:128], A(G_WQ + k * 128, 128), 128)
                pe_v = c.inc(i, "pe")
                c.wait("vector", "pe", pe_v)
                c.v("tensor_copy", out=A(G_WS + k * 128, 128), in_=ps[bk][:, 0:128], inc=True)
                c.bank_free(bk, "ve", c.c["ve"])
            row_done = c.vmark()
            c.wait("gpsimd", "ve", row_done)
            s_ = c.store(wdst.rearrange("(a p) b -> p a b", p=128)[:, :, r * 128:(r + 1) * 128],
                         A(G_WS, nct * 128).rearrange("p (a b) -> p a b", b=128))
    c.bar()

    for (qsrc, ssrc, wdst, N, ng) in (
        (t["fc1_q"], t["fc1_s"], t["w1d"], FF1, 8),
        (t["fc2_q"], t["fc2_s"], t["w2d"], D, 32),
    ):
        for g in range(ng):
            qin = A(G_WIN, N).bitcast(I32)
            ld = c.load(qin, qsrc[g * 128:(g + 1) * 128, :], sl=0)
            lds = c.load(ar[0:1, G_SROW:G_SROW + N], ssrc[g:g + 1, :], sl=4)
            c.waitv("vector", ld)
            c.v("tensor_scalar", out=A(G_WS, N), in0=qin, scalar1=128.0, scalar2=None,
                op0=ALU.subtract, inc=True)
            qf_done = c.c["ve"]
            for ch in range(N // 512):
                bk = ch % 2
                c.bank_wait_te(bk)
                c.waitv("tensor", lds)
                i = c.mm(ps[bk][:, 0:512], t["onesrow"][0:1, 0:128],
                         ar[0:1, G_SROW + ch * 512:G_SROW + (ch + 1) * 512], start=True, stop=True)
                pe_v = c.inc(i, "pe")
                c.wait("vector", "pe", pe_v)
                c.wait("vector", "ve", qf_done)
                c.v("tensor_tensor", out=A(G_WIN + ch * 512, 512), in0=A(G_WS + ch * 512, 512),
                    in1=ps[bk][:, 0:512], op=ALU.mult, inc=True)
                c.bank_free(bk, "ve", c.c["ve"])
            wrow = c.vmark()
            c.wait("gpsimd", "ve", wrow)
            s_ = c.store(wdst[g * 128:(g + 1) * 128, :], A(G_WIN, N))
            c.waitv("vector", s_)
            c.waitv("sync", s_)
    for (wsrc, wdst) in ((t["Wq"], t["WqT"]), (t["Wf"], t["WfT"])):
        for r in range(8):
            ld = c.load(A(G_WIN, D), wsrc[r * 128:(r + 1) * 128, :], sl=0)
            for k in range(8):
                bk = k % 4
                c.bank_wait_te(bk)
                c.waitv("tensor", ld)
                i = c.tp(ps[bk][:, 0:128], A(G_WIN + k * 128, 128), 128)
                pe_v = c.inc(i, "pe")
                c.wait("vector", "pe", pe_v)
                c.v("tensor_copy", out=A(G_WS + k * 128, 128), in_=ps[bk][:, 0:128], inc=True)
                c.bank_free(bk, "ve", c.c["ve"])
            done = c.vmark()
            c.wait("gpsimd", "ve", done)
            s_ = c.store(wdst.rearrange("(a p) b -> p a b", p=128)[:, :, r * 128:(r + 1) * 128],
                         A(G_WS, 1024).rearrange("p (a b) -> p a b", b=128))
            c.waitv("vector", s_)
            c.waitv("sync", s_)
    if _LIMIT == "prologue":
        c.bar()
        for name in ("wgT", "wuT", "wdT", "w1d", "w2d"):
            c.store(t["dbg_" + name], t[name], sl=1)
        c.wait_stores("gpsimd")
        c.bar()
        return
    for (wsrc, wdst) in ((t["Wk"], t["WkT"]), (t["Wv"], t["WvT"])):
        ld = c.load(A(G_WIN, D, part=64), wsrc, sl=0)
        for k in range(8):
            bk = k % 4
            c.bank_wait_te(bk)
            c.waitv("tensor", ld)
            i = c.tp(ps[bk][:, 0:64], A(G_WIN + k * 128, 128, part=64), 64)
            pe_v = c.inc(i, "pe")
            c.wait("vector", "pe", pe_v)
            c.v("tensor_copy", out=A(G_WS + k * 64, 64), in_=ps[bk][:, 0:64], inc=True)
            c.bank_free(bk, "ve", c.c["ve"])
        done = c.vmark()
        c.wait("gpsimd", "ve", done)
        for a in range(8):
            s_ = c.store(wdst[a * 128:(a + 1) * 128, :], A(G_WS + a * 64, 64))
        c.waitv("vector", s_)
        c.waitv("sync", s_)
    c.bar()

    blocks = [
        dict(name="pre2", xsrc=t["x_pre2"], xoff=0, ffn=[2, 3], qt=[2, 3], x1row=-512, q=True),
        dict(name="own0", xsrc=t["x_own"], xoff=0, ffn=[0, 1, 2, 3], qt=[0, 1, 2, 3], x1row=0, q=True),
        dict(name="own1", xsrc=t["x_own"], xoff=512, ffn=[0, 1, 2, 3], qt=[0, 1, 2, 3], x1row=512, q=True),
        dict(name="own2", xsrc=t["x_own"], xoff=1024, ffn=[0, 1, 2, 3], qt=[0, 1, 2, 3], x1row=1024, q=True),
        dict(name="own3", xsrc=t["x_own"], xoff=1536, ffn=[0, 1, 2, 3], qt=[0, 1, 2, 3], x1row=1536, q=True),
        dict(name="wrap", xsrc=t["x_wrap"], xoff=0, ffn=[], qt=[], x1row=None, q=False),
    ]

    def kT_ap(hh, j0, nj):
        return A(A_KT + (hh // 2) * 512 + j0, nj, p0=(hh % 2) * 64, part=64)

    def qT_ap(slot, hh, q0, nq):
        return A(A_Q[slot] + (hh // 2) * 512 + q0, nq, p0=(hh % 2) * 64, part=64)

    def vv_ap(jc, hh):
        return A(A_VV + (jc * NH + hh) * 65, 65)

    def prekv(b, qslot):
        ln_marks = []
        for j in range(4):
            xin = A(P_XIN + (j % 2) * 1024, D)
            if j >= 2:
                c.wait("sync", "ve", ln_marks[j - 2])
            ld = c.load(xin, b["xsrc"][b["xoff"] + j * 128:b["xoff"] + (j + 1) * 128, :], sl=2 + (j % 2))
            c.waitv("vector", ld)
            c.waitv("scalar", ld)
            ln_tile(xin, A(P_XN, D), A(P_XSQ, D))
            ln_done = c.vmark()
            ln_marks.append(ln_done)
            c.wait("tensor", "ve", ln_done)
            transpose_1024(P_XN, P_XNT, j)
        xnT_done = c.vmark()
        ldrow = c.load(t["rowbuf"][0:1, 0:3072], t["bqkv_row"], sl=4)
        groups = ([0] if b["q"] else []) + [1, 2]
        pe_marks = []
        for gi, grp in enumerate(groups):
            for kc in range(8):
                idx = gi * 8 + kc
                buf = idx % 2
                wap = A(P_WS + buf * 1024, 1024)
                if idx >= 2:
                    c.wait("sync", "pe", pe_marks[idx - 2])
                ld = c.load(wap, t["W_qkv"][kc * 128:(kc + 1) * 128, grp * 1024:(grp + 1) * 1024], sl=idx % 2)
                c.waitv("tensor", ld)
                c.wait("tensor", "ve", xnT_done)
                if grp in (0, 1):
                    for n in range(8):
                        if kc == 0:
                            c.bank_wait_te(n)
                        i = c.mm(ps[n][:, 0:512], wap[:, n * 128:(n + 1) * 128],
                                 A(P_XNT + kc * 512, 512), start=(kc == 0), stop=(kc == 7))
                else:
                    for j in range(4):
                        for hf in range(2):
                            bk = j * 2 + hf
                            if kc == 0:
                                c.bank_wait_te(bk)
                            i = c.mm(ps[bk][:, 0:512], A(P_XNT + kc * 512 + j * 128, 128),
                                     wap[:, hf * 512:(hf + 1) * 512], start=(kc == 0), stop=False)
                    if kc == 7:
                        c.waitv("tensor", ldrow)
                        for j in range(4):
                            for hf in range(2):
                                bk = j * 2 + hf
                                i = c.mm(ps[bk][:, 0:512], t["onesrow"][0:1, 0:128],
                                         t["rowbuf"][0:1, 2048 + hf * 512:2048 + (hf + 1) * 512],
                                         start=False, stop=True)
                pe_marks.append(c.inc(i, "pe"))
            mm_done = c.c["pe"]
            if grp == 0:
                for n in range(8):
                    c.wait("vector", "pe", mm_done)
                    c.v("tensor_scalar", out=A(A_Q[qslot] + n * 512, 512), in0=ps[n][:, 0:512],
                        scalar1=t["bqkvC"][:, n:n + 1], scalar2=None, op0=ALU.add, inc=True)
                    c.bank_free(n, "ve", c.c["ve"])
            elif grp == 1:
                for n in range(8):
                    c.wait("vector", "pe", mm_done)
                    c.v("tensor_scalar", out=A(A_KT + n * 512, 512), in0=ps[n][:, 0:512],
                        scalar1=t["bqkvC"][:, 8 + n:9 + n], scalar2=None, op0=ALU.add, inc=True)
                    c.bank_free(n, "ve", c.c["ve"])
            else:
                for j in range(4):
                    for hf in range(2):
                        bk = j * 2 + hf
                        c.wait("vector", "pe", mm_done)
                        dst = ar[0:128, A_VV + (j * NH + hf * 8) * 65:A_VV + (j * NH + hf * 8) * 65 + 8 * 65]
                        c.v("tensor_copy", out=dst.rearrange("p (h w) -> p h w", w=65)[:, :, 0:64],
                            in_=ps[bk][:, 0:512].rearrange("p (h w) -> p h w", w=64), inc=True)
                        c.bank_free(bk, "ve", c.c["ve"])
        c.vmark()

    def att_phase(b, qslot, phase):
        qtiles = b["qt"]
        for hh in range(NH):
            exp_done = {}
            for jc in range(4):
                bk = jc
                c.bank_wait_te(bk)
                i = c.mm(ps[bk][:, 0:512], kT_ap(hh, jc * 128, 128), qT_ap(qslot, hh, 0, 512),
                         start=True, stop=True)
                s_done = c.inc(i, "pe")
                c.wait("scalar", "pe", s_done)
                c.a("activation", A(A_P + jc * 512, 512), ps[bk][:, 0:512], AF.Exp,
                    scale=SCALE, inc=True)
                exp_done[jc] = c.c["ac"]
                c.bank_free(bk, "ac", exp_done[jc])
            ob = 4 + (hh % 2)
            c.bank_wait_te(ob)
            for qt in qtiles:
                for jc in range(4):
                    c.wait("tensor", "ac", exp_done[jc])
                    i = c.mm(ps[ob][:, qt * 65:qt * 65 + 65],
                             A(A_P + jc * 512 + qt * 128, 128), vv_ap(jc, hh),
                             start=(jc == 0), stop=(jc == 3))
            o_done = c.inc(i, "pe")
            c.wait("vector", "pe", o_done)
            for qt in qtiles:
                c.v("reciprocal", out=st[:, 6:7], in_=ps[ob][:, qt * 65 + 64:qt * 65 + 65])
                c.drain("vector")
                ao = A(A_AO + qt * 1024 + hh * 64, 64)
                if phase == 1:
                    c.v("tensor_scalar", out=ao, in0=ps[ob][:, qt * 65:qt * 65 + 64],
                        scalar1=st[:, 6:7], scalar2=None, op0=ALU.mult)
                else:
                    c.v("scalar_tensor_tensor", out=ao, in0=ps[ob][:, qt * 65:qt * 65 + 64],
                        scalar=st[:, 6:7], in1=ao, op0=ALU.mult, op1=ALU.add)
                c.drain("vector")
            c.vmark()
            c.bank_free(ob, "ve", c.c["ve"])

    def wout(b):
        qtiles = b["qt"]
        for j in qtiles:
            transpose_1024(A_AO + j * 1024, W_AOT, j)
        aoT_done = c.vmark()
        ldrow = c.load(t["rowbuf"][0:1, 0:1024], t["bout_row"], sl=4)
        pe_marks = []
        for kc in range(8):
            wap = A(W_WS + (kc % 2) * 1024, 1024)
            if kc >= 2:
                c.wait("sync", "pe", pe_marks[kc - 2])
            ld = c.load(wap, t["W_out"][kc * 128:(kc + 1) * 128, :], sl=kc % 2)
            c.waitv("tensor", ld)
            c.wait("tensor", "ve", aoT_done)
            for ji, j in enumerate(qtiles):
                for hf in range(2):
                    bk = ji * 2 + hf
                    if kc == 0:
                        c.bank_wait_te(bk)
                    i = c.mm(ps[bk][:, 0:512], A(W_AOT + kc * 512 + j * 128, 128),
                             wap[:, hf * 512:(hf + 1) * 512], start=(kc == 0), stop=False)
            if kc == 7:
                c.waitv("tensor", ldrow)
                for ji, j in enumerate(qtiles):
                    for hf in range(2):
                        bk = ji * 2 + hf
                        i = c.mm(ps[bk][:, 0:512], t["onesrow"][0:1, 0:128],
                                 t["rowbuf"][0:1, hf * 512:(hf + 1) * 512], start=False, stop=True)
            pe_marks.append(c.inc(i, "pe"))
        s_done = c.c["pe"]
        for ji, j in enumerate(qtiles):
            for hf in range(2):
                bk = ji * 2 + hf
                c.wait("vector", "pe", s_done)
                c.v("tensor_copy", out=A(A_S + j * 1024 + hf * 512, 512), in_=ps[bk][:, 0:512], inc=True)
                c.bank_free(bk, "ve", c.c["ve"])
        c.vmark()

    def ffnc(b):
        tiles = b["ffn"]
        for hv_i in range(0, len(tiles), 2):
            hv = tiles[hv_i:hv_i + 2]
            for jj, j in enumerate(hv):
                xin = A(F_XIN + jj * 1024, D)
                ld = c.load(xin, b["xsrc"][b["xoff"] + j * 128:b["xoff"] + (j + 1) * 128, :], sl=2 + jj)
                c.waitv("vector", ld)
                c.waitv("scalar", ld)
                ln_tile(xin, A(F_XN, D), A(F_SCR, D))
                for hf in range(2):
                    fp4_quant(A(F_XN + hf * 512, 512), A(F_HQ, 512), 512)
                    qdone = c.vmark()
                    bk = 6
                    c.bank_wait_te(bk)
                    c.wait("tensor", "ve", qdone)
                    for k in range(4):
                        i = c.tp(ps[bk][:, k * 128:(k + 1) * 128], A(F_HQ + k * 128, 128), 128)
                    pe_v = c.inc(i, "pe")
                    c.wait("vector", "pe", pe_v)
                    for k in range(4):
                        kk = hf * 4 + k
                        c.v("tensor_copy", out=A(F_XQT + (jj * 8 + kk) * 128, 128),
                            in_=ps[bk][:, k * 128:(k + 1) * 128], inc=(k == 3))
                    c.bank_free(bk, "ve", c.c["ve"])
            xq_done = c.vmark()
            if _DBG3 and b["name"] == "own0" and hv_i == 0:
                c.wait("gpsimd", "ve", xq_done)
                dx = c.store(t["dbg_xqT"], A(F_XQT, 2048), sl=1)
                c.waitv("sync", dx)
            pe_marks = []
            pe_half0 = c.c["pe"]
            h_free = {0: c.c["ve"], 1: 0}
            idx = 0
            for nchi, n in enumerate(N0CH):
                ncol0 = nchi * 512
                for kc in range(8):
                    buf = idx % 2
                    wg = A(F_WS + buf * 1024, n)
                    wu = A(F_WS + buf * 1024 + 512, n)
                    if idx >= 2:
                        c.wait("sync", "pe", pe_marks[idx - 2])
                    else:
                        c.wait("sync", "pe", pe_half0)
                    ldg = c.load(wg, t["wgT"][kc * 128:(kc + 1) * 128, ncol0:ncol0 + n], sl=idx % 2)
                    ldu = c.load(wu, t["wuT"][kc * 128:(kc + 1) * 128, ncol0:ncol0 + n], sl=idx % 2)
                    c.waitv("tensor", ldu)
                    c.wait("tensor", "ve", xq_done)
                    for jj in range(len(hv)):
                        bg, by = jj * 2, jj * 2 + 1
                        if kc == 0:
                            c.bank_wait_te(bg)
                            c.bank_wait_te(by)
                        c.mm(ps[bg][:, 0:n], A(F_XQT + (jj * 8 + kc) * 128, 128), wg,
                             start=(kc == 0), stop=(kc == 7))
                        i = c.mm(ps[by][:, 0:n], A(F_XQT + (jj * 8 + kc) * 128, 128), wu,
                                 start=(kc == 0), stop=(kc == 7))
                    pe_marks.append(c.inc(i, "pe"))
                    idx += 1
                gu_done = c.c["pe"]
                for jj in range(len(hv)):
                    bg, by = jj * 2, jj * 2 + 1
                    p2 = (nchi * len(hv) + jj) % 2
                    FH = F_H if p2 == 0 else F_H1
                    FQ = F_HQ if p2 == 0 else F_HQ1
                    c.wait("scalar", "pe", gu_done)
                    c.wait("scalar", "ve", h_free[p2])
                    c.a("activation", A(FH, n), ps[bg][:, 0:n], AF.Silu, inc=True)
                    hs_done = c.c["ac"]
                    c.bank_free(bg, "ac", hs_done)
                    c.wait("vector", "ac", hs_done)
                    c.v("tensor_tensor", out=A(FH, n), in0=A(FH, n), in1=ps[by][:, 0:n],
                        op=ALU.mult, inc=True)
                    c.bank_free(by, "ve", c.c["ve"])
                    c.drain("vector")
                    fp4_quant(A(FH, n), A(FQ, n), n)
                    hq_done = c.vmark()
                    h_free[p2] = hq_done
                    bk = 6 + jj
                    c.bank_wait_te(bk)
                    c.wait("tensor", "ve", hq_done)
                    nk = n // 128
                    for k in range(nk):
                        i = c.tp(ps[bk][:, k * 128:(k + 1) * 128], A(FQ + k * 128, 128), 128)
                    pe_v = c.inc(i, "pe")
                    c.wait("vector", "pe", pe_v)
                    for k in range(nk):
                        fc = nchi * 4 + k
                        c.v("tensor_copy", out=A(F_HQT + fc * 256 + jj * 128, 128),
                            in_=ps[bk][:, k * 128:(k + 1) * 128], inc=(k == nk - 1))
                    c.bank_free(bk, "ve", c.c["ve"])
            hqT_done = c.vmark()
            if _DBG3 and b["name"] == "own0" and hv_i == 0:
                c.wait("gpsimd", "ve", hqT_done)
                dh = c.store(t["dbg_hqT"], A(F_HQT, 5632), sl=1)
                c.waitv("sync", dh)
            pe_marks2 = []
            for fc in range(22):
                buf = fc % 2
                wd = A(F_WS2 + buf * 1024, 1024)
                if fc >= 2:
                    c.wait("sync", "pe", pe_marks2[fc - 2])
                else:
                    c.wait("sync", "pe", pe_marks[-1])
                ld = c.load(wd, t["wdT"][fc * 128:(fc + 1) * 128, :], sl=fc % 2)
                c.waitv("tensor", ld)
                c.wait("tensor", "ve", hqT_done)
                for jj in range(len(hv)):
                    for hf in range(2):
                        bk = jj * 2 + hf
                        if fc == 0:
                            c.bank_wait_te(bk)
                        i = c.mm(ps[bk][:, 0:512], A(F_HQT + fc * 256 + jj * 128, 128),
                                 wd[:, hf * 512:(hf + 1) * 512], start=(fc == 0), stop=(fc == 21))
                pe_marks2.append(c.inc(i, "pe"))
            z_done = c.c["pe"]
            for jj, j in enumerate(hv):
                xin = A(F_XIN + jj * 1024, D)
                for hf in range(2):
                    bk = jj * 2 + hf
                    c.wait("scalar", "pe", z_done)
                    c.a("activation", A(F_G + hf * 512, 512), ps[bk][:, 0:512], AF.Silu, inc=True)
                g_done = c.c["ac"]
                if _DBG2:
                    row = 256 + b["x1row"] + j * 128
                    c.wait("gpsimd", "ac", g_done)
                    sg1 = c.store(t["dbg_g"][row:row + 128, :], A(F_G, D), sl=1)
                    sg2 = c.store(t["dbg_s"][row:row + 128, :], A(A_S + j * 1024, D), sl=1)
                    c.waitv("vector", sg2)
                    c.waitv("scalar", sg2)
                c.wait("vector", "ac", g_done)
                c.v("tensor_tensor", out=A(F_G, D), in0=A(F_G, D), in1=A(A_S + j * 1024, D), op=ALU.mult)
                c.drain("vector")
                c.v("tensor_add", out=A(F_G, D), in0=A(F_G, D), in1=xin, inc=True)
                x1_done = c.c["ve"]
                c.bank_free(jj * 2, "ve", x1_done)
                c.bank_free(jj * 2 + 1, "ve", x1_done)
                c.drain("vector")
                c.vmark()
                c.wait("gpsimd", "ve", c.c["ve"])
                row = 256 + b["x1row"] + j * 128
                s_ = c.store(t["x1s"][row:row + 128, :], A(F_G, D))
                c.waitv("vector", s_)
                c.waitv("scalar", s_)
                c.waitv("sync", s_)

    prekv(blocks[0], 0)
    c.bar()
    att_phase(blocks[0], 0, 1)
    c.bar()
    prekv(blocks[1], 1)
    c.bar()
    att_phase(blocks[0], 0, 2)
    wout(blocks[0])
    c.bar()
    ffnc(blocks[0])
    c.bar()
    qsl = {1: 1, 2: 0, 3: 1, 4: 0}
    for bi in range(1, 5):
        att_phase(blocks[bi], qsl[bi], 1)
        c.bar()
        prekv(blocks[bi + 1], qsl[bi + 1] if bi + 1 < 5 else 0)
        c.bar()
        att_phase(blocks[bi], qsl[bi], 2)
        wout(blocks[bi])
        c.bar()
        ffnc(blocks[bi])
        c.bar()

    if _LIMIT == "L0":
        c.bar()
        c.store(t["dbg_x1s"], t["x1s"], sl=1)
        c.wait_stores("gpsimd")
        c.bar()
        return

    def load_wkv():
        ld1 = c.load(A(L_WKV, 512).rearrange("p (a b) -> p a b", b=64),
                     t["WkT"].rearrange("(a p) b -> p a b", p=128), sl=5)
        ld2 = c.load(A(L_WKV + 512, 512).rearrange("p (a b) -> p a b", b=64),
                     t["WvT"].rearrange("(a p) b -> p a b", p=128), sl=5)
        return ld2

    ldg = c.load(t["rowbuf"][0:1, 0:1024], t["gam_row"], sl=4)
    ldb = c.load(t["rowbuf"][0:1, 1024:2048], t["bet_row"], sl=4)
    for gb in range(2):
        for ch in range(2):
            bk = gb * 2 + ch
            c.bank_wait_te(bk)
            c.waitv("tensor", ldb)
            i = c.mm(ps[bk][:, 0:512], t["onesrow"][0:1, 0:128],
                     t["rowbuf"][0:1, gb * 1024 + ch * 512:gb * 1024 + (ch + 1) * 512],
                     start=True, stop=True)
            pe_v = c.inc(i, "pe")
            c.wait("vector", "pe", pe_v)
            c.v("tensor_copy", out=A(L_GBC + gb * 1024 + ch * 512, 512), in_=ps[bk][:, 0:512], inc=True)
            c.bank_free(bk, "ve", c.c["ve"])
    if c.on("vector"):
        vc = ar[0:128, L_VV:L_VV + 20 * 65].rearrange("p (cell w) -> p cell w", w=65)
        c.eng.memset(vc[:, :, 64:65], 1.0)
    c.vmark()
    c.bar()

    def l1_kv_tiles(ch0, ntile, xn1T_off):
        wkv_ld = load_wkv()
        c.waitv("tensor", wkv_ld)
        c.bank_wait_te(4)
        for kc in range(8):
            i = c.mm(ps[4][0:64, 0:ntile * 128], A(L_WKV + kc * 64, 64),
                     A(xn1T_off + kc * 512, ntile * 128), start=(kc == 0), stop=(kc == 7))
        c.inc(i, "pe")
        for kc in range(8):
            i = c.mm(ps[4][64:128, 0:ntile * 128], A(L_WKV + kc * 64, 64),
                     A(xn1T_off + kc * 512, ntile * 128), start=(kc == 0), stop=(kc == 7),
                     tile_position=(0, 64))
        c.inc(i, "pe")
        c.bank_wait_te(5)
        for kc in range(8):
            for j in range(ntile):
                i = c.mm(ps[5][:, j * 64:(j + 1) * 64], A(xn1T_off + kc * 512 + j * 128, 128),
                         A(L_WKV + 512 + kc * 64, 64), start=(kc == 0 and j == 0), stop=(kc == 7))
        c.inc(i, "pe")
        kv_done = c.c["pe"]
        c.wait("vector", "pe", kv_done)
        c.v("tensor_copy", out=A(L_KT + ch0 * 128, ntile * 128), in_=ps[4][:, 0:ntile * 128])
        for j in range(ntile):
            c.v("tensor_copy", out=A(L_VV + (ch0 + j) * 65, 64), in_=ps[5][:, j * 64:(j + 1) * 64],
                inc=(j == ntile - 1))
        c.bank_free(4, "ve", c.c["ve"])
        c.bank_free(5, "ve", c.c["ve"])
        c.vmark()

    for j in range(2):
        xin = A(L_XIN + j * 1024, D)
        ld = c.load(xin, t["x1s"][j * 128:(j + 1) * 128, :], sl=2 + j)
        c.waitv("vector", ld)
        c.waitv("scalar", ld)
        ln_tile(xin, A(L_LOC, D), A(L_LOC + 1024, D))
        ln_done = c.vmark()
        c.wait("tensor", "ve", ln_done)
        transpose_1024(L_LOC, L_XN1T, j)
    c.vmark()
    l1_kv_tiles(0, 2, L_XN1T)
    c.bar()

    def l1pre(g):
        ln_marks = []
        for j in range(4):
            xin = A(L_XIN + (j % 2) * 1024, D)
            row = 256 + (g * 4 + j) * 128
            if j >= 2:
                c.wait("sync", "ve", ln_marks[j - 2])
            ld = c.load(xin, t["x1s"][row:row + 128, :], sl=2 + (j % 2))
            c.waitv("vector", ld)
            c.waitv("scalar", ld)
            ln_tile(xin, A(L_XN1 + j * 1024, D), A(L_LOC, D))
            ln_done = c.vmark()
            ln_marks.append(ln_done)
            c.wait("tensor", "ve", ln_done)
            transpose_1024(L_XN1 + j * 1024, L_XN1T, j)
        xn1T_done = c.vmark()
        pe_marks = []
        for kc in range(8):
            buf = kc % 2
            wap = A(L_WS + buf * 1024, 1024)
            if kc >= 2:
                c.wait("sync", "pe", pe_marks[kc - 2])
            ld = c.load(wap, t["WqT"][kc * 128:(kc + 1) * 128, :], sl=kc % 2)
            c.waitv("tensor", ld)
            c.wait("tensor", "ve", xn1T_done)
            for n in range(8):
                if kc == 0:
                    c.bank_wait_te(n)
                i = c.mm(ps[n][:, 0:512], wap[:, n * 128:(n + 1) * 128],
                         A(L_XN1T + kc * 512, 512), start=(kc == 0), stop=(kc == 7))
            pe_marks.append(c.inc(i, "pe"))
        q_done = c.c["pe"]
        for n in range(8):
            c.wait("vector", "pe", q_done)
            c.v("tensor_copy", out=A(L_QT + n * 512, 512), in_=ps[n][:, 0:512], inc=True)
            c.bank_free(n, "ve", c.c["ve"])
        c.vmark()
        l1_kv_tiles(2 + g * 4, 4, L_XN1T)

    def l1att(g):
        if _DBG4 and g == 0:
            c.wait("gpsimd", "ve", c.c["ve"])
            d1 = c.store(t["dbg_kT"], A(L_KT, 2560), sl=1)
            d2 = c.store(t["dbg_vv"], A(L_VV, 1300), sl=1)
            d3 = c.store(t["dbg_qT"], A(L_QT, 4096), sl=1)
            c.waitv("sync", d3)
        for j in range(4):
            tt = g * 4 + j
            if tt == 0:
                msks = [t["m00"], t["m01"], None]
            elif tt == 1:
                msks = [t["m10"], None, None]
            else:
                msks = [t["triA"], None, None]
            for hh in range(NH):
                sb = hh % 2
                c.bank_wait_te(sb)
                for jc in range(3):
                    i = c.mm(ps[sb][:, jc * 128:(jc + 1) * 128],
                             A(L_KT + (tt + jc) * 128, 128, p0=(hh % 2) * 64, part=64),
                             A(L_QT + (hh // 2) * 512 + j * 128, 128, p0=(hh % 2) * 64, part=64),
                             start=True, stop=True)
                s_done = c.inc(i, "pe")
                c.wait("scalar", "pe", s_done)
                pb = L_P + (hh % 2) * 384
                c.a("activation", A(pb, 384), ps[sb][:, 0:384], AF.Exp, scale=SCALE, inc=True)
                exp_done = c.c["ac"]
                c.bank_free(sb, "ac", exp_done)
                c.wait("vector", "ac", exp_done)
                c.v("tensor_tensor", out=A(pb, 128), in0=A(pb, 128), in1=msks[0][:], op=ALU.mult)
                if msks[1] is not None:
                    c.v("tensor_tensor", out=A(pb + 128, 128), in0=A(pb + 128, 128), in1=msks[1][:], op=ALU.mult)
                c.v("tensor_tensor", out=A(pb + 256, 128), in0=A(pb + 256, 128), in1=t["triB"][:], op=ALU.mult)
                c.drain("vector")
                p_done = c.vmark()
                ob = 4 + (hh % 2)
                c.bank_wait_te(ob)
                c.wait("tensor", "ve", p_done)
                for jc in range(3):
                    i = c.mm(ps[ob][:, 0:65], A(pb + jc * 128, 128), A(L_VV + (tt + jc) * 65, 65),
                             start=(jc == 0), stop=(jc == 2))
                o_done = c.inc(i, "pe")
                c.wait("vector", "pe", o_done)
                c.v("reciprocal", out=st[:, 6:7], in_=ps[ob][:, 64:65])
                c.drain("vector")
                c.v("tensor_scalar", out=A(L_AOT2 + hh * 64, 64), in0=ps[ob][:, 0:64],
                    scalar1=st[:, 6:7], scalar2=None, op0=ALU.mult, inc=True)
                c.drain("vector")
                c.bank_free(ob, "ve", c.c["ve"])
            ao_done = c.vmark()
            c.wait("tensor", "ve", ao_done)
            transpose_1024(L_AOT2, L_LOC, j)
        aoT_done = c.vmark()
        ldrow = c.load(t["rowbuf"][0:1, 0:1024], t["bf_row"], sl=4)
        pe_marks = []
        for kc in range(8):
            buf = kc % 2
            wap = A(L_WS + buf * 1024, 1024)
            if kc >= 2:
                c.wait("sync", "pe", pe_marks[kc - 2])
            ld = c.load(wap, t["WfT"][kc * 128:(kc + 1) * 128, :], sl=kc % 2)
            c.waitv("tensor", ld)
            c.wait("tensor", "ve", aoT_done)
            for j in range(4):
                for hf in range(2):
                    bk = j * 2 + hf
                    if kc == 0:
                        c.bank_wait_te(bk)
                    i = c.mm(ps[bk][:, 0:512], A(L_LOC + kc * 512 + j * 128, 128),
                             wap[:, hf * 512:(hf + 1) * 512], start=(kc == 0), stop=False)
            if kc == 7:
                c.waitv("tensor", ldrow)
                for j in range(4):
                    for hf in range(2):
                        bk = j * 2 + hf
                        i = c.mm(ps[bk][:, 0:512], t["onesrow"][0:1, 0:128],
                                 t["rowbuf"][0:1, hf * 512:(hf + 1) * 512], start=False, stop=True)
            pe_marks.append(c.inc(i, "pe"))
        s_done = c.c["pe"]
        for j in range(4):
            for hf in range(2):
                bk = j * 2 + hf
                c.wait("vector", "pe", s_done)
                c.v("tensor_copy", out=A(L_S1 + j * 1024 + hf * 512, 512), in_=ps[bk][:, 0:512], inc=True)
                c.bank_free(bk, "ve", c.c["ve"])
        c.vmark()

    def l1ffn(g):
        for half in range(2):
            hv = [half * 2, half * 2 + 1]
            for jj, j in enumerate(hv):
                ln_tile(A(L_XN1 + j * 1024, D), A(L_LOC + 1024, D), A(L_LOC, D))
                c.v("tensor_tensor", out=A(L_LOC + 1024, D), in0=A(L_LOC + 1024, D),
                    in1=A(L_GBC, D), op=ALU.mult)
                c.drain("vector")
                c.v("tensor_add", out=A(L_LOC + 1024, D), in0=A(L_LOC + 1024, D), in1=A(L_GBC + 1024, D))
                c.drain("vector")
                h2_done = c.vmark()
                c.wait("tensor", "ve", h2_done)
                for hlf in range(2):
                    bk = 6 + hlf
                    c.bank_wait_te(bk)
                    for k in range(4):
                        kk = hlf * 4 + k
                        i = c.tp(ps[bk][:, k * 128:(k + 1) * 128], A(L_LOC + 1024 + kk * 128, 128), 128)
                    pe_v = c.inc(i, "pe")
                    c.wait("vector", "pe", pe_v)
                    for k in range(4):
                        kk = hlf * 4 + k
                        c.v("tensor_copy", out=A(L_H2T + kk * 256 + jj * 128, 128),
                            in_=ps[bk][:, k * 128:(k + 1) * 128], inc=(k == 3))
                    c.bank_free(bk, "ve", c.c["ve"])
            h2T_done = c.vmark()
            pe_marks = []
            pe_half0 = c.c["pe"]
            idx = 0
            for ng in range(4):
                for kc in range(8):
                    buf = idx % 2
                    wap = A(L_WS + buf * 1024, 1024)
                    if idx >= 2:
                        c.wait("sync", "pe", pe_marks[idx - 2])
                    else:
                        c.wait("sync", "pe", pe_half0)
                    ld = c.load(wap, t["w1d"][kc * 128:(kc + 1) * 128, ng * 1024:(ng + 1) * 1024], sl=idx % 2)
                    c.waitv("tensor", ld)
                    c.wait("tensor", "ve", h2T_done)
                    for n in range(8):
                        if kc == 0:
                            c.bank_wait_te(n)
                        i = c.mm(ps[n][:, 0:256], wap[:, n * 128:(n + 1) * 128],
                                 A(L_H2T + kc * 256, 256), start=(kc == 0), stop=(kc == 7))
                    pe_marks.append(c.inc(i, "pe"))
                    idx += 1
                u_done = c.c["pe"]
                for n in range(8):
                    ngl = ng * 8 + n
                    c.wait("scalar", "pe", u_done)
                    c.a("activation", A(L_YT + ngl * 256, 256), ps[n][:, 0:256], AF.Silu,
                        bias=t["fc1bC"][:, ngl:ngl + 1], scale=1.0, inc=True)
                    c.bank_free(n, "ac", c.c["ac"])
            yT_done = c.amark()
            ldrow = c.load(t["rowbuf"][0:1, 0:1024], t["fc2b_row"], sl=4)
            pe_marks2 = []
            for kc in range(32):
                buf = kc % 2
                wap = A(L_WS + buf * 1024, 1024)
                if kc >= 2:
                    c.wait("sync", "pe", pe_marks2[kc - 2])
                else:
                    c.wait("sync", "pe", pe_marks[-1])
                ld = c.load(wap, t["w2d"][kc * 128:(kc + 1) * 128, :], sl=kc % 2)
                c.waitv("tensor", ld)
                c.wait("tensor", "ac", yT_done)
                for jj in range(2):
                    for hf in range(2):
                        bk = jj * 2 + hf
                        if kc == 0:
                            c.bank_wait_te(bk)
                        i = c.mm(ps[bk][:, 0:512], A(L_YT + kc * 256 + jj * 128, 128),
                                 wap[:, hf * 512:(hf + 1) * 512], start=(kc == 0), stop=False)
                if kc == 31:
                    c.waitv("tensor", ldrow)
                    for jj in range(2):
                        for hf in range(2):
                            bk = jj * 2 + hf
                            i = c.mm(ps[bk][:, 0:512], t["onesrow"][0:1, 0:128],
                                     t["rowbuf"][0:1, hf * 512:(hf + 1) * 512], start=False, stop=True)
                pe_marks2.append(c.inc(i, "pe"))
            z_done = c.c["pe"]
            for jj, j in enumerate(hv):
                tt = g * 4 + j
                xin = A(L_XIN + jj * 1024, D)
                row = 256 + tt * 128
                ld = c.load(xin, t["x1s"][row:row + 128, :], sl=2 + jj)
                for hf in range(2):
                    bk = jj * 2 + hf
                    c.wait("vector", "pe", z_done)
                    c.v("scalar_tensor_tensor", out=A(L_LOC + 2048 + hf * 512, 512),
                        in0=ps[bk][:, 0:512], scalar=0.5, in1=A(L_XN1 + j * 1024 + hf * 512, 512),
                        op0=ALU.mult, op1=ALU.add)
                pre_done = c.vmark()
                c.wait("scalar", "ve", pre_done)
                c.a("activation", A(L_LOC + 1024, D), A(L_LOC + 2048, D), AF.Silu, inc=True)
                g2_done = c.c["ac"]
                if _DBG4:
                    c.wait("gpsimd", "ac", g2_done)
                    d1 = c.store(t["dbg_g2"][tt * 128:(tt + 1) * 128, :], A(L_LOC + 1024, D), sl=1)
                    d2 = c.store(t["dbg_s1"][tt * 128:(tt + 1) * 128, :], A(L_S1 + j * 1024, D), sl=1)
                    c.waitv("vector", d2)
                    c.waitv("scalar", d2)
                c.wait("vector", "ac", g2_done)
                c.v("tensor_tensor", out=A(L_LOC + 1024, D), in0=A(L_LOC + 1024, D),
                    in1=A(L_S1 + j * 1024, D), op=ALU.mult)
                c.drain("vector")
                c.waitv("vector", ld)
                c.v("tensor_add", out=A(L_LOC + 1024, D), in0=A(L_LOC + 1024, D), in1=xin, inc=True)
                x2_done = c.c["ve"]
                c.bank_free(jj * 2, "ve", x2_done)
                c.bank_free(jj * 2 + 1, "ve", x2_done)
                c.drain("vector")
                c.vmark()
                c.wait("gpsimd", "ve", c.c["ve"])
                s_ = c.store(t["out"][tt * 128:(tt + 1) * 128, :], A(L_LOC + 1024, D))
                c.waitv("vector", s_)
                c.waitv("scalar", s_)
                c.waitv("sync", s_)

    for g in range(4):
        l1pre(g)
        c.bar()
        l1att(g)
        c.bar()
        l1ffn(g)
        c.bar()

    c.wait_stores("gpsimd")
    c.bar()


def _build():
    nc = bass.Bass("TRN2", target_bir_lowering=False)
    t = {}
    def din(name, shape, dt=F32):
        t[name] = nc.dram_tensor(name, shape, dt, kind="ExternalInput").ap()
    din("x_own", [2048, D])
    din("x_pre2", [512, D])
    din("x_wrap", [512, D])
    din("W_qkv", [D, 3072])
    din("W_out", [D, D])
    din("w_gate", [FF0, D])
    din("w_up", [FF0, D])
    din("w_down", [D, FF0])
    din("Wq", [D, D])
    din("Wk", [HD, D])
    din("Wv", [HD, D])
    din("Wf", [D, D])
    din("fc1_q", [D, FF1], I32)
    din("fc1_s", [8, FF1])
    din("fc2_q", [FF1, D], I32)
    din("fc2_s", [32, D])
    din("bqkv_t", [24, 128])
    din("bqkv_row", [1, 3072])
    din("bout_row", [1, D])
    din("bf_row", [1, D])
    din("fc1b_t", [32, 128])
    din("fc2b_row", [1, D])
    din("gam_row", [1, D])
    din("bet_row", [1, D])
    din("ident_in", [128, 128])
    din("triA_in", [128, 128])
    din("triB_in", [128, 128])
    din("m00_in", [128, 128])
    din("m01_in", [128, 128])
    din("m10_in", [128, 128])
    t["out"] = nc.dram_tensor("out", [2048, D], F32, kind="ExternalOutput").ap()
    if _LIMIT == "prologue":
        for name, shape in (("wgT", [D, FF0]), ("wuT", [D, FF0]), ("wdT", [FF0, D]),
                            ("w1d", [D, FF1]), ("w2d", [FF1, D])):
            t["dbg_" + name] = nc.dram_tensor("dbg_" + name, shape, F32, kind="ExternalOutput").ap()
    if _LIMIT == "L0":
        t["dbg_x1s"] = nc.dram_tensor("dbg_x1s", [2304, D], F32, kind="ExternalOutput").ap()
    if _DBG2:
        t["dbg_s"] = nc.dram_tensor("dbg_s", [2304, D], F32, kind="ExternalOutput").ap()
        t["dbg_g"] = nc.dram_tensor("dbg_g", [2304, D], F32, kind="ExternalOutput").ap()
    if _DBG3:
        t["dbg_xqT"] = nc.dram_tensor("dbg_xqT", [128, 2048], F32, kind="ExternalOutput").ap()
        t["dbg_hqT"] = nc.dram_tensor("dbg_hqT", [128, 5632], F32, kind="ExternalOutput").ap()
    if _DBG4:
        t["dbg_s1"] = nc.dram_tensor("dbg_s1", [2048, D], F32, kind="ExternalOutput").ap()
        t["dbg_g2"] = nc.dram_tensor("dbg_g2", [2048, D], F32, kind="ExternalOutput").ap()
        t["dbg_kT"] = nc.dram_tensor("dbg_kT", [128, 2560], F32, kind="ExternalOutput").ap()
        t["dbg_vv"] = nc.dram_tensor("dbg_vv", [128, 1300], F32, kind="ExternalOutput").ap()
        t["dbg_qT"] = nc.dram_tensor("dbg_qT", [128, 4096], F32, kind="ExternalOutput").ap()
    def dscr(name, shape, dt=F32):
        t[name] = nc.dram_tensor(name, shape, dt, kind="Internal").ap()
    dscr("x1s", [2304, D])
    dscr("wgT", [D, FF0])
    dscr("wuT", [D, FF0])
    dscr("wdT", [FF0, D])
    dscr("w1d", [D, FF1])
    dscr("w2d", [FF1, D])
    dscr("WqT", [D, D])
    dscr("WfT", [D, D])
    dscr("WkT", [D, HD])
    dscr("WvT", [D, HD])

    ctx = contextlib.ExitStack()
    with ctx:
        t["arena"] = ctx.enter_context(nc.sbuf_tensor("arena", [128, ARENA], F32)).ap()
        for name, shape in (
            ("ident", [128, 128]), ("triA", [128, 128]), ("triB", [128, 128]),
            ("m00", [128, 128]), ("m01", [128, 128]), ("m10", [128, 128]),
            ("onesrow", [1, 128]), ("onescol", [128, 1]), ("eps", [128, 1]),
            ("bqkvT_s", [24, 128]), ("fc1bT_s", [32, 128]),
            ("bqkvC", [128, 24]), ("fc1bC", [128, 32]),
            ("stats", [128, 16]), ("qsc", [128, 192]), ("rowbuf", [1, 3072]),
        ):
            t[name] = ctx.enter_context(nc.sbuf_tensor(name, shape, F32))
        t["psum"] = [ctx.enter_context(nc.psum_tensor(f"p{i}", [128, 512], F32)) for i in range(8)]
        sem = {}
        semnames = ("pe", "ve", "ac") + tuple("dl%d" % i for i in range(6)) + ("ds0", "ds1") \
            + tuple("b_" + e for e in ENGINES)
        for name in semnames:
            sem[name] = ctx.enter_context(nc.semaphore(name))
        block = ctx.enter_context(nc.Block())

        def mk(en):
            def f(eng):
                _emit(C(nc, en, eng, sem, t))
            return f
        block.sync(mk("sync"))
        block.tensor(mk("tensor"))
        block.vector(mk("vector"))
        block.scalar(mk("scalar"))
        block.gpsimd(mk("gpsimd"))
    return nc


def _host_inputs(inputs):
    _ = inputs["L1_attn_segment_pos"]
    x = np.ascontiguousarray(np.asarray(inputs["L0_attn_x"], dtype=np.float32))
    i32 = lambda a: np.ascontiguousarray(np.asarray(a, dtype=np.int32))
    f32 = lambda a: np.ascontiguousarray(np.asarray(a, dtype=np.float32))
    kk = np.arange(128)[:, None]
    qq = np.arange(128)[None, :]
    triA = (qq <= kk).astype(np.float32)
    triB = (qq >= kk).astype(np.float32)
    full = np.ones((128, 128), np.float32)
    zero = np.zeros((128, 128), np.float32)
    common = dict(
        W_qkv=f32(inputs["L0_attn_W_qkv"]),
        W_out=f32(inputs["L0_attn_W_out"]),
        w_gate=f32(inputs["L0_ffn_w_gate"]),
        w_up=f32(inputs["L0_ffn_w_up"]),
        w_down=f32(inputs["L0_ffn_w_down"]),
        Wq=f32(inputs["L1_attn_Wq"]),
        Wk=f32(inputs["L1_attn_Wk"]),
        Wv=f32(inputs["L1_attn_Wv"]),
        Wf=f32(inputs["L1_attn_Wf"]),
        fc1_q=i32(inputs["L1_ffn_fc1_q"]),
        fc1_s=f32(inputs["L1_ffn_fc1_scales"]),
        fc2_q=i32(inputs["L1_ffn_fc2_q"]),
        fc2_s=f32(inputs["L1_ffn_fc2_scales"]),
        bqkv_t=f32(inputs["L0_attn_b_qkv"]).reshape(24, 128),
        bqkv_row=f32(inputs["L0_attn_b_qkv"]).reshape(1, 3072),
        bout_row=f32(inputs["L0_attn_b_out"]).reshape(1, D),
        bf_row=f32(inputs["L1_attn_bf"]).reshape(1, D),
        fc1b_t=f32(inputs["L1_ffn_fc1_bias"]).reshape(32, 128),
        fc2b_row=f32(inputs["L1_ffn_fc2_bias"]).reshape(1, D),
        gam_row=f32(inputs["L1_ffn_ln_gamma"]).reshape(1, D),
        bet_row=f32(inputs["L1_ffn_ln_beta"]).reshape(1, D),
        ident_in=np.eye(128, dtype=np.float32),
        triA_in=triA, triB_in=triB,
    )
    in_maps = []
    for cid in range(8):
        b, half = cid // 2, cid % 2
        t0 = half * 2048
        m = dict(common)
        m["x_own"] = np.ascontiguousarray(x[b, t0:t0 + 2048])
        m["x_pre2"] = np.ascontiguousarray(x[b, 1536:2048])
        m["x_wrap"] = np.ascontiguousarray(x[b, 2048:2560] if half == 0 else x[b, 0:512])
        if half == 0:
            m["m00_in"], m["m01_in"], m["m10_in"] = zero, zero, zero
        else:
            m["m00_in"], m["m01_in"], m["m10_in"] = triA, full, triA
        in_maps.append(m)
    return x, in_maps


def kernel(**inputs):
    x, in_maps = _host_inputs(inputs)
    nc = _build()
    res = run_bass_kernel_spmd(nc, in_maps, core_ids=list(range(8)))
    out = np.empty((4, 4096, D), dtype=np.float32)
    for cid in range(8):
        b, half = cid // 2, cid % 2
        out[b, half * 2048:(half + 1) * 2048] = res.results[cid]["out"]
    return out


# version 05
# speedup vs baseline: 1.1836x; 1.0381x over previous
import contextlib
import numpy as np

import concourse.bass as bass
import concourse.mybir as mybir
from concourse.bass_utils import run_bass_kernel_spmd

F32 = mybir.dt.float32
BF16 = mybir.dt.bfloat16
I32 = mybir.dt.int32
AF = mybir.ActivationFunctionType
ALU = mybir.AluOpType
AX = mybir.AxisListType.X

D = 1024
NH, HD = 16, 64
FF0 = 2816
FF1 = 4096
SCALE = 0.125
MID = [0.25, 0.75, 1.25, 1.75, 2.5, 3.5, 5.0]
DG = [0.5, 0.5, 0.5, 0.5, 1.0, 1.0, 2.0]
N0CH = [512, 512, 512, 512, 512, 256]
ENGINES = ("sync", "tensor", "vector", "scalar", "gpsimd")

_DBG = False
_DBG2 = False
_DBG3 = False
_DBG4 = False
_LIMIT = None

ARENA = 43840
A_KT = 0
A_VV = 4096
A_Q = (8256, 12352)
A_AO = 16448
A_S = 20544
A_P = 24640
P_XNT = 26688
P_XSQ = 30784
P_XN = 31808
P_XIN = 32832
P_WS = 34880
W_AOT = 26688
W_WS = 30784
F_HQT = 26688
F_XQT = 32320
F_XIN = 34368
F_WS = 36416
F_WS2 = 36416
F_XN = 38464
F_SCR = 39488
F_H = 42048
F_HQ = 42560
F_H1 = A_P + 1024
F_HQ1 = A_P + 1536
Q_CB = [F_SCR + 1536 + i * 256 for i in range(4)] + [A_P + i * 256 for i in range(3)]
F_G = F_XN
G_WIN = 26688
G_WQ = 30784
G_STG = 33600
G_WS = 34624
G_SROW = 38720
L_XN1 = 0
L_XN1T = 4096
L_QT = 8192
L_KT = 12288
L_VV = 14848
L_S1 = 16192
L_H2T = 20288
L_P = 22336
L_YT = 24384
L_GBC = 32576
L_LOC = 34624
L_WS = 38720
L_WKV = 40768
L_AOT2 = 40768
L_XIN = 41792

assert L_XIN + 2048 <= ARENA


class C:

    def __init__(self, nc, en, eng, sem, t):
        self.nc, self.en, self.eng, self.sem, self.t = nc, en, eng, sem, t
        self.c = {k: 0 for k in sem}
        self.nbar = 0
        self.bankw = [None] * 8

    def _do(self, en, f):
        if self.en == en:
            return f(self.eng)
        return None

    def on(self, en):
        return self.en == en

    def inc(self, instr, s, k=1):
        self.c[s] += k
        if instr is not None:
            instr.then_inc(self.sem[s], k)
        return self.c[s]

    def wait(self, en, s, val=None):
        if val is None:
            val = self.c[s]
        if self.en == en:
            self.eng.wait_ge(self.sem[s], val)

    def drain(self, en):
        if self.en == en:
            self.eng.drain()

    def vmark(self):
        s = self.t["stats"]
        return self.inc(self._do("vector", lambda e: e.tensor_copy(out=s[:, 7:8], in_=s[:, 7:8])), "ve")

    def amark(self):
        s = self.t["stats"]
        return self.inc(self._do("scalar", lambda e: e.copy(s[:, 7:8], s[:, 7:8])), "ac")

    def bar(self):
        self.nbar += 1
        for e in ENGINES:
            if self.en == e:
                self.eng.drain().then_inc(self.sem["b_" + e], 1)
        for e in ENGINES:
            if e != self.en:
                self.wait(self.en, "b_" + e, self.nbar)
        for e in ENGINES:
            self.c["b_" + e] = self.nbar

    def load(self, out_ap, in_ap, sl=0):
        s = "dl%d" % sl
        i = self._do("sync", lambda e: e.dma_start(out=out_ap, in_=in_ap))
        return (s, self.inc(i, s, 16))

    def store(self, out_ap, in_ap, sl=0):
        s = "ds%d" % sl
        i = self._do("gpsimd", lambda e: e.dma_start(out=out_ap, in_=in_ap))
        return (s, self.inc(i, s, 16))

    def waitv(self, en, tok):
        if self.en == en:
            self.eng.wait_ge(self.sem[tok[0]], tok[1])

    def wait_stores(self, en):
        for s in ("ds0", "ds1"):
            self.wait(en, s, self.c[s])

    def mm(self, out, lhsT, rhs, start, stop, tile_position=None):
        return self._do("tensor", lambda e: e.matmul(out, lhsT, rhs, start=start, stop=stop,
                                                     tile_position=tile_position))

    def tp(self, out, in_, np_):
        return self._do("tensor", lambda e: e.transpose(out, in_, self.t["ident"][0:np_, 0:np_]))

    def v(self, meth, *a, inc=False, **kw):
        i = self._do("vector", lambda e: getattr(e, meth)(*a, **kw))
        if inc:
            return self.inc(i, "ve")
        return i

    def a(self, meth, *a, inc=False, **kw):
        i = self._do("scalar", lambda e: getattr(e, meth)(*a, **kw))
        if inc:
            return self.inc(i, "ac")
        return i

    def bank_wait_te(self, b):
        if self.bankw[b] is not None:
            s, v = self.bankw[b]
            self.wait("tensor", s, v)
            self.bankw[b] = None

    def bank_free(self, b, s, v):
        self.bankw[b] = (s, v)


def _emit(c):
    t = c.t
    ps = t["psum"]
    ar = t["arena"]

    def A(off, n, p0=0, part=128):
        return ar[p0:p0 + part, off:off + n]

    st = t["stats"]

    def ln_tile(src, dst, xsq, n=D):
        c.wait("scalar", "ve", c.c["ve"])
        c.a("activation", xsq, src, AF.Square, inc=True)
        sq_done = c.c["ac"]
        c.v("reduce_sum", out=st[:, 0:1], in_=src, axis=AX)
        c.drain("vector")
        c.v("tensor_scalar", out=st[:, 0:1], in0=st[:, 0:1], scalar1=1.0 / n, scalar2=None, op0=ALU.mult)
        c.drain("vector")
        c.wait("vector", "ac", sq_done)
        c.v("reduce_sum", out=st[:, 1:2], in_=xsq, axis=AX)
        c.drain("vector")
        c.v("tensor_scalar", out=st[:, 1:2], in0=st[:, 1:2], scalar1=1.0 / n, scalar2=None, op0=ALU.mult)
        c.v("tensor_scalar", out=st[:, 2:3], in0=st[:, 0:1], scalar1=st[:, 0:1], scalar2=None, op0=ALU.mult)
        c.drain("vector")
        c.v("tensor_sub", out=st[:, 3:4], in0=st[:, 1:2], in1=st[:, 2:3], inc=True)
        var_done = c.c["ve"]
        c.wait("scalar", "ve", var_done)
        c.a("activation", st[:, 4:5], st[:, 3:4], AF.Sqrt, bias=t["eps"][:, 0:1], scale=1.0, inc=True)
        sd_done = c.c["ac"]
        c.wait("vector", "ac", sd_done)
        c.v("reciprocal", out=st[:, 5:6], in_=st[:, 4:5])
        c.drain("vector")
        c.v("tensor_scalar", out=dst, in0=src, scalar1=st[:, 0:1], scalar2=st[:, 5:6],
            op0=ALU.subtract, op1=ALU.mult)
        c.drain("vector")

    def fp4_quant(src, dst, n):
        assert n <= 512
        nb = n // 16
        tt = A(F_SCR, n)
        sg = A(F_SCR + 512, n)
        at = A(F_SCR + 1024, n)
        cb = [ar[0:128, o:o + (n + 1) // 2].bitcast(BF16) for o in Q_CB]
        qsc = t["qsc"]
        bmv, scv, rsv = qsc[:, 0:nb], qsc[:, 64:64 + nb], qsc[:, 128:128 + nb]
        src3 = src.rearrange("p (g k) -> p g k", k=16)
        dst3 = dst.rearrange("p (g k) -> p g k", k=16)
        t3 = tt.rearrange("p (g k) -> p g k", k=16)
        c.v("reduce_max", out=bmv, in_=src3, axis=AX, apply_absolute_value=True)
        c.drain("vector")
        c.v("tensor_scalar", out=scv, in0=bmv, scalar1=1.0 / 6.0, scalar2=1e-12, op0=ALU.mult, op1=ALU.max)
        c.drain("vector")
        c.v("reciprocal", out=rsv, in_=scv)
        c.drain("vector")
        c.v("tensor_tensor", out=t3, in0=src3, in1=rsv.unsqueeze(2).broadcast_to([128, nb, 16]),
            op=ALU.mult, inc=True)
        t_done = c.c["ve"]
        c.wait("scalar", "ve", t_done)
        c.a("activation", at, tt, AF.Abs)
        c.a("activation", sg, tt, AF.Sign, inc=True)
        absgn_done = c.c["ac"]
        c.wait("vector", "ac", absgn_done)
        for j in range(7):
            c.v("tensor_scalar", out=cb[j], in0=at, scalar1=MID[j], scalar2=DG[j],
                op0=ALU.is_gt, op1=ALU.mult)
        c.drain("vector")
        c.v("tensor_add", out=cb[0], in0=cb[0], in1=cb[1])
        c.v("tensor_add", out=cb[2], in0=cb[2], in1=cb[3])
        c.v("tensor_add", out=cb[4], in0=cb[4], in1=cb[5])
        c.drain("vector")
        c.v("tensor_add", out=cb[0], in0=cb[0], in1=cb[2])
        c.v("tensor_add", out=cb[4], in0=cb[4], in1=cb[6])
        c.drain("vector")
        c.v("tensor_add", out=cb[0], in0=cb[0], in1=cb[4])
        c.drain("vector")
        c.v("tensor_tensor", out=tt, in0=sg, in1=cb[0], op=ALU.mult)
        c.drain("vector")
        c.v("tensor_tensor", out=dst3, in0=t3, in1=scv.unsqueeze(2).broadcast_to([128, nb, 16]),
            op=ALU.mult)
        c.drain("vector")

    def transpose_1024(src_off, dst_off, j, banks=(6, 7), src_stride=128):
        for half in range(2):
            bk = banks[half]
            c.bank_wait_te(bk)
            for k in range(4):
                kk = half * 4 + k
                i = c.tp(ps[bk][:, k * 128:(k + 1) * 128], A(src_off + kk * 128, 128), 128)
            pe_v = c.inc(i, "pe")
            c.wait("vector", "pe", pe_v)
            for k in range(4):
                kk = half * 4 + k
                c.v("tensor_copy", out=A(dst_off + kk * 512 + j * 128, 128),
                    in_=ps[bk][:, k * 128:(k + 1) * 128], inc=(k == 3))
            c.bank_free(bk, "ve", c.c["ve"])

    for name in ("ident", "triA", "triB", "m00", "m01", "m10"):
        c.load(t[name][:], t[name + "_in"], sl=4)
    c.load(t["bqkvT_s"][:], t["bqkv_t"], sl=4)
    ld0 = c.load(t["fc1bT_s"][:], t["fc1b_t"], sl=4)
    if c.on("vector"):
        c.eng.memset(t["onesrow"][:], 1.0)
        c.eng.memset(t["onescol"][:], 1.0)
        c.eng.memset(t["eps"][:], 1e-5)
        vc = ar[0:128, A_VV:A_VV + 4 * NH * 65].rearrange("p (cell w) -> p cell w", w=65)
        c.eng.memset(vc[:, :, 64:65], 1.0)
    c.vmark()
    c.waitv("tensor", ld0)
    c.tp(ps[0][:, 0:24], t["bqkvT_s"][:], 24)
    i = c.tp(ps[1][:, 0:32], t["fc1bT_s"][:], 32)
    c.inc(i, "pe")
    c.wait("vector", "pe", c.c["pe"])
    c.v("tensor_copy", out=t["bqkvC"][:], in_=ps[0][:, 0:24])
    c.v("tensor_copy", out=t["fc1bC"][:], in_=ps[1][:, 0:32], inc=True)
    c.bar()

    WPREP = [(G_WIN, G_WQ, G_WS), (8256, 12352, 16448)]
    wrow_free = [0, 0]
    wrow_store = [None, None]
    ri = 0
    for (wsrc, wdst, nrows, ncols) in (
        (t["w_gate"], t["wgT"], FF0, D),
        (t["w_up"], t["wuT"], FF0, D),
        (t["w_down"], t["wdT"], D, FF0),
    ):
        for r in range(nrows // 128):
            bset = ri % 2
            ri += 1
            WIN, WQ, WS = WPREP[bset]
            win = A(WIN, ncols)
            c.wait("sync", "ve", wrow_free[bset])
            if wrow_store[bset] is not None:
                c.waitv("sync", wrow_store[bset])
            ld = c.load(win, wsrc[r * 128:(r + 1) * 128, :], sl=bset)
            c.waitv("vector", ld)
            for hh in range((ncols + 511) // 512):
                n = min(512, ncols - hh * 512)
                fp4_quant(win[:, hh * 512:hh * 512 + n], A(WQ + hh * 512, n), n)
            q_done = c.vmark()
            nct = ncols // 128
            for k in range(nct):
                bk = k % 4
                c.bank_wait_te(bk)
                c.wait("tensor", "ve", q_done)
                i = c.tp(ps[bk][:, 0:128], A(WQ + k * 128, 128), 128)
                pe_v = c.inc(i, "pe")
                c.wait("vector", "pe", pe_v)
                c.v("tensor_copy", out=A(WS + k * 128, 128), in_=ps[bk][:, 0:128], inc=True)
                c.bank_free(bk, "ve", c.c["ve"])
            row_done = c.vmark()
            wrow_free[bset] = row_done
            c.wait("gpsimd", "ve", row_done)
            s_ = c.store(wdst.rearrange("(a p) b -> p a b", p=128)[:, :, r * 128:(r + 1) * 128],
                         A(WS, nct * 128).rearrange("p (a b) -> p a b", b=128), sl=bset)
            wrow_store[bset] = s_
    c.bar()

    for (qsrc, ssrc, wdst, N, ng) in (
        (t["fc1_q"], t["fc1_s"], t["w1d"], FF1, 8),
        (t["fc2_q"], t["fc2_s"], t["w2d"], D, 32),
    ):
        for g in range(ng):
            qin = A(G_WIN, N).bitcast(I32)
            ld = c.load(qin, qsrc[g * 128:(g + 1) * 128, :], sl=0)
            lds = c.load(ar[0:1, G_SROW:G_SROW + N], ssrc[g:g + 1, :], sl=4)
            c.waitv("vector", ld)
            c.v("tensor_scalar", out=A(G_WS, N), in0=qin, scalar1=128.0, scalar2=None,
                op0=ALU.subtract, inc=True)
            qf_done = c.c["ve"]
            for ch in range(N // 512):
                bk = ch % 2
                c.bank_wait_te(bk)
                c.waitv("tensor", lds)
                i = c.mm(ps[bk][:, 0:512], t["onesrow"][0:1, 0:128],
                         ar[0:1, G_SROW + ch * 512:G_SROW + (ch + 1) * 512], start=True, stop=True)
                pe_v = c.inc(i, "pe")
                c.wait("vector", "pe", pe_v)
                c.wait("vector", "ve", qf_done)
                c.v("tensor_tensor", out=A(G_WIN + ch * 512, 512), in0=A(G_WS + ch * 512, 512),
                    in1=ps[bk][:, 0:512], op=ALU.mult, inc=True)
                c.bank_free(bk, "ve", c.c["ve"])
            wrow = c.vmark()
            c.wait("gpsimd", "ve", wrow)
            s_ = c.store(wdst[g * 128:(g + 1) * 128, :], A(G_WIN, N))
            c.waitv("vector", s_)
            c.waitv("sync", s_)
    for (wsrc, wdst) in ((t["Wq"], t["WqT"]), (t["Wf"], t["WfT"])):
        for r in range(8):
            ld = c.load(A(G_WIN, D), wsrc[r * 128:(r + 1) * 128, :], sl=0)
            for k in range(8):
                bk = k % 4
                c.bank_wait_te(bk)
                c.waitv("tensor", ld)
                i = c.tp(ps[bk][:, 0:128], A(G_WIN + k * 128, 128), 128)
                pe_v = c.inc(i, "pe")
                c.wait("vector", "pe", pe_v)
                c.v("tensor_copy", out=A(G_WS + k * 128, 128), in_=ps[bk][:, 0:128], inc=True)
                c.bank_free(bk, "ve", c.c["ve"])
            done = c.vmark()
            c.wait("gpsimd", "ve", done)
            s_ = c.store(wdst.rearrange("(a p) b -> p a b", p=128)[:, :, r * 128:(r + 1) * 128],
                         A(G_WS, 1024).rearrange("p (a b) -> p a b", b=128))
            c.waitv("vector", s_)
            c.waitv("sync", s_)
    if _LIMIT == "prologue":
        c.bar()
        for name in ("wgT", "wuT", "wdT", "w1d", "w2d"):
            c.store(t["dbg_" + name], t[name], sl=1)
        c.wait_stores("gpsimd")
        c.bar()
        return
    for (wsrc, wdst) in ((t["Wk"], t["WkT"]), (t["Wv"], t["WvT"])):
        ld = c.load(A(G_WIN, D, part=64), wsrc, sl=0)
        for k in range(8):
            bk = k % 4
            c.bank_wait_te(bk)
            c.waitv("tensor", ld)
            i = c.tp(ps[bk][:, 0:64], A(G_WIN + k * 128, 128, part=64), 64)
            pe_v = c.inc(i, "pe")
            c.wait("vector", "pe", pe_v)
            c.v("tensor_copy", out=A(G_WS + k * 64, 64), in_=ps[bk][:, 0:64], inc=True)
            c.bank_free(bk, "ve", c.c["ve"])
        done = c.vmark()
        c.wait("gpsimd", "ve", done)
        for a in range(8):
            s_ = c.store(wdst[a * 128:(a + 1) * 128, :], A(G_WS + a * 64, 64))
        c.waitv("vector", s_)
        c.waitv("sync", s_)
    c.bar()

    blocks = [
        dict(name="pre2", xsrc=t["x_pre2"], xoff=0, ffn=[2, 3], qt=[2, 3], x1row=-512, q=True),
        dict(name="own0", xsrc=t["x_own"], xoff=0, ffn=[0, 1, 2, 3], qt=[0, 1, 2, 3], x1row=0, q=True),
        dict(name="own1", xsrc=t["x_own"], xoff=512, ffn=[0, 1, 2, 3], qt=[0, 1, 2, 3], x1row=512, q=True),
        dict(name="own2", xsrc=t["x_own"], xoff=1024, ffn=[0, 1, 2, 3], qt=[0, 1, 2, 3], x1row=1024, q=True),
        dict(name="own3", xsrc=t["x_own"], xoff=1536, ffn=[0, 1, 2, 3], qt=[0, 1, 2, 3], x1row=1536, q=True),
        dict(name="wrap", xsrc=t["x_wrap"], xoff=0, ffn=[], qt=[], x1row=None, q=False),
    ]

    def kT_ap(hh, j0, nj):
        return A(A_KT + (hh // 2) * 512 + j0, nj, p0=(hh % 2) * 64, part=64)

    def qT_ap(slot, hh, q0, nq):
        return A(A_Q[slot] + (hh // 2) * 512 + q0, nq, p0=(hh % 2) * 64, part=64)

    def vv_ap(jc, hh):
        return A(A_VV + (jc * NH + hh) * 65, 65)

    def prekv(b, qslot):
        ln_marks = []
        for j in range(4):
            xin = A(P_XIN + (j % 2) * 1024, D)
            if j >= 2:
                c.wait("sync", "ve", ln_marks[j - 2])
            ld = c.load(xin, b["xsrc"][b["xoff"] + j * 128:b["xoff"] + (j + 1) * 128, :], sl=2 + (j % 2))
            c.waitv("vector", ld)
            c.waitv("scalar", ld)
            ln_tile(xin, A(P_XN, D), A(P_XSQ, D))
            ln_done = c.vmark()
            ln_marks.append(ln_done)
            c.wait("tensor", "ve", ln_done)
            transpose_1024(P_XN, P_XNT, j)
        xnT_done = c.vmark()
        ldrow = c.load(t["rowbuf"][0:1, 0:3072], t["bqkv_row"], sl=4)
        groups = ([0] if b["q"] else []) + [1, 2]
        pe_marks = []
        for gi, grp in enumerate(groups):
            for kc in range(8):
                idx = gi * 8 + kc
                buf = idx % 2
                wap = A(P_WS + buf * 1024, 1024)
                if idx >= 2:
                    c.wait("sync", "pe", pe_marks[idx - 2])
                ld = c.load(wap, t["W_qkv"][kc * 128:(kc + 1) * 128, grp * 1024:(grp + 1) * 1024], sl=idx % 2)
                c.waitv("tensor", ld)
                c.wait("tensor", "ve", xnT_done)
                if grp in (0, 1):
                    for n in range(8):
                        if kc == 0:
                            c.bank_wait_te(n)
                        i = c.mm(ps[n][:, 0:512], wap[:, n * 128:(n + 1) * 128],
                                 A(P_XNT + kc * 512, 512), start=(kc == 0), stop=(kc == 7))
                else:
                    for j in range(4):
                        for hf in range(2):
                            bk = j * 2 + hf
                            if kc == 0:
                                c.bank_wait_te(bk)
                            i = c.mm(ps[bk][:, 0:512], A(P_XNT + kc * 512 + j * 128, 128),
                                     wap[:, hf * 512:(hf + 1) * 512], start=(kc == 0), stop=False)
                    if kc == 7:
                        c.waitv("tensor", ldrow)
                        for j in range(4):
                            for hf in range(2):
                                bk = j * 2 + hf
                                i = c.mm(ps[bk][:, 0:512], t["onesrow"][0:1, 0:128],
                                         t["rowbuf"][0:1, 2048 + hf * 512:2048 + (hf + 1) * 512],
                                         start=False, stop=True)
                pe_marks.append(c.inc(i, "pe"))
            mm_done = c.c["pe"]
            if grp == 0:
                for n in range(8):
                    c.wait("vector", "pe", mm_done)
                    c.v("tensor_scalar", out=A(A_Q[qslot] + n * 512, 512), in0=ps[n][:, 0:512],
                        scalar1=t["bqkvC"][:, n:n + 1], scalar2=None, op0=ALU.add, inc=True)
                    c.bank_free(n, "ve", c.c["ve"])
            elif grp == 1:
                for n in range(8):
                    c.wait("vector", "pe", mm_done)
                    c.v("tensor_scalar", out=A(A_KT + n * 512, 512), in0=ps[n][:, 0:512],
                        scalar1=t["bqkvC"][:, 8 + n:9 + n], scalar2=None, op0=ALU.add, inc=True)
                    c.bank_free(n, "ve", c.c["ve"])
            else:
                for j in range(4):
                    for hf in range(2):
                        bk = j * 2 + hf
                        c.wait("vector", "pe", mm_done)
                        dst = ar[0:128, A_VV + (j * NH + hf * 8) * 65:A_VV + (j * NH + hf * 8) * 65 + 8 * 65]
                        c.v("tensor_copy", out=dst.rearrange("p (h w) -> p h w", w=65)[:, :, 0:64],
                            in_=ps[bk][:, 0:512].rearrange("p (h w) -> p h w", w=64), inc=True)
                        c.bank_free(bk, "ve", c.c["ve"])
        c.vmark()

    def att_phase(b, qslot, phase):
        qtiles = b["qt"]
        for hh in range(NH):
            exp_done = {}
            for jc in range(4):
                bk = jc
                c.bank_wait_te(bk)
                i = c.mm(ps[bk][:, 0:512], kT_ap(hh, jc * 128, 128), qT_ap(qslot, hh, 0, 512),
                         start=True, stop=True)
                s_done = c.inc(i, "pe")
                c.wait("scalar", "pe", s_done)
                c.a("activation", A(A_P + jc * 512, 512), ps[bk][:, 0:512], AF.Exp,
                    scale=SCALE, inc=True)
                exp_done[jc] = c.c["ac"]
                c.bank_free(bk, "ac", exp_done[jc])
            ob = 4 + (hh % 2)
            c.bank_wait_te(ob)
            for qt in qtiles:
                for jc in range(4):
                    c.wait("tensor", "ac", exp_done[jc])
                    i = c.mm(ps[ob][:, qt * 65:qt * 65 + 65],
                             A(A_P + jc * 512 + qt * 128, 128), vv_ap(jc, hh),
                             start=(jc == 0), stop=(jc == 3))
            o_done = c.inc(i, "pe")
            c.wait("vector", "pe", o_done)
            for qt in qtiles:
                c.v("reciprocal", out=st[:, 6:7], in_=ps[ob][:, qt * 65 + 64:qt * 65 + 65])
                c.drain("vector")
                ao = A(A_AO + qt * 1024 + hh * 64, 64)
                if phase == 1:
                    c.v("tensor_scalar", out=ao, in0=ps[ob][:, qt * 65:qt * 65 + 64],
                        scalar1=st[:, 6:7], scalar2=None, op0=ALU.mult)
                else:
                    c.v("scalar_tensor_tensor", out=ao, in0=ps[ob][:, qt * 65:qt * 65 + 64],
                        scalar=st[:, 6:7], in1=ao, op0=ALU.mult, op1=ALU.add)
                c.drain("vector")
            c.vmark()
            c.bank_free(ob, "ve", c.c["ve"])

    def wout(b):
        qtiles = b["qt"]
        for j in qtiles:
            transpose_1024(A_AO + j * 1024, W_AOT, j)
        aoT_done = c.vmark()
        ldrow = c.load(t["rowbuf"][0:1, 0:1024], t["bout_row"], sl=4)
        pe_marks = []
        for kc in range(8):
            wap = A(W_WS + (kc % 2) * 1024, 1024)
            if kc >= 2:
                c.wait("sync", "pe", pe_marks[kc - 2])
            ld = c.load(wap, t["W_out"][kc * 128:(kc + 1) * 128, :], sl=kc % 2)
            c.waitv("tensor", ld)
            c.wait("tensor", "ve", aoT_done)
            for ji, j in enumerate(qtiles):
                for hf in range(2):
                    bk = ji * 2 + hf
                    if kc == 0:
                        c.bank_wait_te(bk)
                    i = c.mm(ps[bk][:, 0:512], A(W_AOT + kc * 512 + j * 128, 128),
                             wap[:, hf * 512:(hf + 1) * 512], start=(kc == 0), stop=False)
            if kc == 7:
                c.waitv("tensor", ldrow)
                for ji, j in enumerate(qtiles):
                    for hf in range(2):
                        bk = ji * 2 + hf
                        i = c.mm(ps[bk][:, 0:512], t["onesrow"][0:1, 0:128],
                                 t["rowbuf"][0:1, hf * 512:(hf + 1) * 512], start=False, stop=True)
            pe_marks.append(c.inc(i, "pe"))
        s_done = c.c["pe"]
        for ji, j in enumerate(qtiles):
            for hf in range(2):
                bk = ji * 2 + hf
                c.wait("vector", "pe", s_done)
                c.v("tensor_copy", out=A(A_S + j * 1024 + hf * 512, 512), in_=ps[bk][:, 0:512], inc=True)
                c.bank_free(bk, "ve", c.c["ve"])
        c.vmark()

    def ffnc(b):
        tiles = b["ffn"]
        for hv_i in range(0, len(tiles), 2):
            hv = tiles[hv_i:hv_i + 2]
            for jj, j in enumerate(hv):
                xin = A(F_XIN + jj * 1024, D)
                ld = c.load(xin, b["xsrc"][b["xoff"] + j * 128:b["xoff"] + (j + 1) * 128, :], sl=2 + jj)
                c.waitv("vector", ld)
                c.waitv("scalar", ld)
                ln_tile(xin, A(F_XN, D), A(F_SCR, D))
                for hf in range(2):
                    fp4_quant(A(F_XN + hf * 512, 512), A(F_HQ, 512), 512)
                    qdone = c.vmark()
                    bk = 6
                    c.bank_wait_te(bk)
                    c.wait("tensor", "ve", qdone)
                    for k in range(4):
                        i = c.tp(ps[bk][:, k * 128:(k + 1) * 128], A(F_HQ + k * 128, 128), 128)
                    pe_v = c.inc(i, "pe")
                    c.wait("vector", "pe", pe_v)
                    for k in range(4):
                        kk = hf * 4 + k
                        c.v("tensor_copy", out=A(F_XQT + (jj * 8 + kk) * 128, 128),
                            in_=ps[bk][:, k * 128:(k + 1) * 128], inc=(k == 3))
                    c.bank_free(bk, "ve", c.c["ve"])
            xq_done = c.vmark()
            if _DBG3 and b["name"] == "own0" and hv_i == 0:
                c.wait("gpsimd", "ve", xq_done)
                dx = c.store(t["dbg_xqT"], A(F_XQT, 2048), sl=1)
                c.waitv("sync", dx)
            pe_marks = []
            pe_half0 = c.c["pe"]
            h_free = {0: c.c["ve"], 1: 0}
            idx = 0
            for nchi, n in enumerate(N0CH):
                ncol0 = nchi * 512
                for kc in range(8):
                    buf = idx % 2
                    wg = A(F_WS + buf * 1024, n)
                    wu = A(F_WS + buf * 1024 + 512, n)
                    if idx >= 2:
                        c.wait("sync", "pe", pe_marks[idx - 2])
                    else:
                        c.wait("sync", "pe", pe_half0)
                    ldg = c.load(wg, t["wgT"][kc * 128:(kc + 1) * 128, ncol0:ncol0 + n], sl=idx % 2)
                    ldu = c.load(wu, t["wuT"][kc * 128:(kc + 1) * 128, ncol0:ncol0 + n], sl=idx % 2)
                    c.waitv("tensor", ldu)
                    c.wait("tensor", "ve", xq_done)
                    for jj in range(len(hv)):
                        bg, by = jj * 2, jj * 2 + 1
                        if kc == 0:
                            c.bank_wait_te(bg)
                            c.bank_wait_te(by)
                        c.mm(ps[bg][:, 0:n], A(F_XQT + (jj * 8 + kc) * 128, 128), wg,
                             start=(kc == 0), stop=(kc == 7))
                        i = c.mm(ps[by][:, 0:n], A(F_XQT + (jj * 8 + kc) * 128, 128), wu,
                                 start=(kc == 0), stop=(kc == 7))
                    pe_marks.append(c.inc(i, "pe"))
                    idx += 1
                gu_done = c.c["pe"]
                for jj in range(len(hv)):
                    bg, by = jj * 2, jj * 2 + 1
                    p2 = (nchi * len(hv) + jj) % 2
                    FH = F_H if p2 == 0 else F_H1
                    FQ = F_HQ if p2 == 0 else F_HQ1
                    c.wait("scalar", "pe", gu_done)
                    c.wait("scalar", "ve", h_free[p2])
                    c.a("activation", A(FH, n), ps[bg][:, 0:n], AF.Silu, inc=True)
                    hs_done = c.c["ac"]
                    c.bank_free(bg, "ac", hs_done)
                    c.wait("vector", "ac", hs_done)
                    c.v("tensor_tensor", out=A(FH, n), in0=A(FH, n), in1=ps[by][:, 0:n],
                        op=ALU.mult, inc=True)
                    c.bank_free(by, "ve", c.c["ve"])
                    c.drain("vector")
                    fp4_quant(A(FH, n), A(FQ, n), n)
                    hq_done = c.vmark()
                    h_free[p2] = hq_done
                    bk = 6 + jj
                    c.bank_wait_te(bk)
                    c.wait("tensor", "ve", hq_done)
                    nk = n // 128
                    for k in range(nk):
                        i = c.tp(ps[bk][:, k * 128:(k + 1) * 128], A(FQ + k * 128, 128), 128)
                    pe_v = c.inc(i, "pe")
                    c.wait("vector", "pe", pe_v)
                    for k in range(nk):
                        fc = nchi * 4 + k
                        c.v("tensor_copy", out=A(F_HQT + fc * 256 + jj * 128, 128),
                            in_=ps[bk][:, k * 128:(k + 1) * 128], inc=(k == nk - 1))
                    c.bank_free(bk, "ve", c.c["ve"])
            hqT_done = c.vmark()
            if _DBG3 and b["name"] == "own0" and hv_i == 0:
                c.wait("gpsimd", "ve", hqT_done)
                dh = c.store(t["dbg_hqT"], A(F_HQT, 5632), sl=1)
                c.waitv("sync", dh)
            pe_marks2 = []
            for fc in range(22):
                buf = fc % 2
                wd = A(F_WS2 + buf * 1024, 1024)
                if fc >= 2:
                    c.wait("sync", "pe", pe_marks2[fc - 2])
                else:
                    c.wait("sync", "pe", pe_marks[-1])
                ld = c.load(wd, t["wdT"][fc * 128:(fc + 1) * 128, :], sl=fc % 2)
                c.waitv("tensor", ld)
                c.wait("tensor", "ve", hqT_done)
                for jj in range(len(hv)):
                    for hf in range(2):
                        bk = jj * 2 + hf
                        if fc == 0:
                            c.bank_wait_te(bk)
                        i = c.mm(ps[bk][:, 0:512], A(F_HQT + fc * 256 + jj * 128, 128),
                                 wd[:, hf * 512:(hf + 1) * 512], start=(fc == 0), stop=(fc == 21))
                pe_marks2.append(c.inc(i, "pe"))
            z_done = c.c["pe"]
            for jj, j in enumerate(hv):
                xin = A(F_XIN + jj * 1024, D)
                for hf in range(2):
                    bk = jj * 2 + hf
                    c.wait("scalar", "pe", z_done)
                    c.a("activation", A(F_G + hf * 512, 512), ps[bk][:, 0:512], AF.Silu, inc=True)
                g_done = c.c["ac"]
                if _DBG2:
                    row = 256 + b["x1row"] + j * 128
                    c.wait("gpsimd", "ac", g_done)
                    sg1 = c.store(t["dbg_g"][row:row + 128, :], A(F_G, D), sl=1)
                    sg2 = c.store(t["dbg_s"][row:row + 128, :], A(A_S + j * 1024, D), sl=1)
                    c.waitv("vector", sg2)
                    c.waitv("scalar", sg2)
                c.wait("vector", "ac", g_done)
                c.v("tensor_tensor", out=A(F_G, D), in0=A(F_G, D), in1=A(A_S + j * 1024, D), op=ALU.mult)
                c.drain("vector")
                c.v("tensor_add", out=A(F_G, D), in0=A(F_G, D), in1=xin, inc=True)
                x1_done = c.c["ve"]
                c.bank_free(jj * 2, "ve", x1_done)
                c.bank_free(jj * 2 + 1, "ve", x1_done)
                c.drain("vector")
                c.vmark()
                c.wait("gpsimd", "ve", c.c["ve"])
                row = 256 + b["x1row"] + j * 128
                s_ = c.store(t["x1s"][row:row + 128, :], A(F_G, D))
                c.waitv("vector", s_)
                c.waitv("scalar", s_)
                c.waitv("sync", s_)

    prekv(blocks[0], 0)
    c.bar()
    att_phase(blocks[0], 0, 1)
    c.bar()
    prekv(blocks[1], 1)
    c.bar()
    att_phase(blocks[0], 0, 2)
    wout(blocks[0])
    c.bar()
    ffnc(blocks[0])
    c.bar()
    qsl = {1: 1, 2: 0, 3: 1, 4: 0}
    for bi in range(1, 5):
        att_phase(blocks[bi], qsl[bi], 1)
        c.bar()
        prekv(blocks[bi + 1], qsl[bi + 1] if bi + 1 < 5 else 0)
        c.bar()
        att_phase(blocks[bi], qsl[bi], 2)
        wout(blocks[bi])
        c.bar()
        ffnc(blocks[bi])
        c.bar()

    if _LIMIT == "L0":
        c.bar()
        c.store(t["dbg_x1s"], t["x1s"], sl=1)
        c.wait_stores("gpsimd")
        c.bar()
        return

    def load_wkv():
        ld1 = c.load(A(L_WKV, 512).rearrange("p (a b) -> p a b", b=64),
                     t["WkT"].rearrange("(a p) b -> p a b", p=128), sl=5)
        ld2 = c.load(A(L_WKV + 512, 512).rearrange("p (a b) -> p a b", b=64),
                     t["WvT"].rearrange("(a p) b -> p a b", p=128), sl=5)
        return ld2

    ldg = c.load(t["rowbuf"][0:1, 0:1024], t["gam_row"], sl=4)
    ldb = c.load(t["rowbuf"][0:1, 1024:2048], t["bet_row"], sl=4)
    for gb in range(2):
        for ch in range(2):
            bk = gb * 2 + ch
            c.bank_wait_te(bk)
            c.waitv("tensor", ldb)
            i = c.mm(ps[bk][:, 0:512], t["onesrow"][0:1, 0:128],
                     t["rowbuf"][0:1, gb * 1024 + ch * 512:gb * 1024 + (ch + 1) * 512],
                     start=True, stop=True)
            pe_v = c.inc(i, "pe")
            c.wait("vector", "pe", pe_v)
            c.v("tensor_copy", out=A(L_GBC + gb * 1024 + ch * 512, 512), in_=ps[bk][:, 0:512], inc=True)
            c.bank_free(bk, "ve", c.c["ve"])
    if c.on("vector"):
        vc = ar[0:128, L_VV:L_VV + 20 * 65].rearrange("p (cell w) -> p cell w", w=65)
        c.eng.memset(vc[:, :, 64:65], 1.0)
    c.vmark()
    c.bar()

    def l1_kv_tiles(ch0, ntile, xn1T_off):
        wkv_ld = load_wkv()
        c.waitv("tensor", wkv_ld)
        c.bank_wait_te(4)
        for kc in range(8):
            i = c.mm(ps[4][0:64, 0:ntile * 128], A(L_WKV + kc * 64, 64),
                     A(xn1T_off + kc * 512, ntile * 128), start=(kc == 0), stop=(kc == 7))
        c.inc(i, "pe")
        for kc in range(8):
            i = c.mm(ps[4][64:128, 0:ntile * 128], A(L_WKV + kc * 64, 64),
                     A(xn1T_off + kc * 512, ntile * 128), start=(kc == 0), stop=(kc == 7),
                     tile_position=(0, 64))
        c.inc(i, "pe")
        c.bank_wait_te(5)
        for kc in range(8):
            for j in range(ntile):
                i = c.mm(ps[5][:, j * 64:(j + 1) * 64], A(xn1T_off + kc * 512 + j * 128, 128),
                         A(L_WKV + 512 + kc * 64, 64), start=(kc == 0 and j == 0), stop=(kc == 7))
        c.inc(i, "pe")
        kv_done = c.c["pe"]
        c.wait("vector", "pe", kv_done)
        c.v("tensor_copy", out=A(L_KT + ch0 * 128, ntile * 128), in_=ps[4][:, 0:ntile * 128])
        for j in range(ntile):
            c.v("tensor_copy", out=A(L_VV + (ch0 + j) * 65, 64), in_=ps[5][:, j * 64:(j + 1) * 64],
                inc=(j == ntile - 1))
        c.bank_free(4, "ve", c.c["ve"])
        c.bank_free(5, "ve", c.c["ve"])
        c.vmark()

    for j in range(2):
        xin = A(L_XIN + j * 1024, D)
        ld = c.load(xin, t["x1s"][j * 128:(j + 1) * 128, :], sl=2 + j)
        c.waitv("vector", ld)
        c.waitv("scalar", ld)
        ln_tile(xin, A(L_LOC, D), A(L_LOC + 1024, D))
        ln_done = c.vmark()
        c.wait("tensor", "ve", ln_done)
        transpose_1024(L_LOC, L_XN1T, j)
    c.vmark()
    l1_kv_tiles(0, 2, L_XN1T)
    c.bar()

    def l1pre(g):
        ln_marks = []
        for j in range(4):
            xin = A(L_XIN + (j % 2) * 1024, D)
            row = 256 + (g * 4 + j) * 128
            if j >= 2:
                c.wait("sync", "ve", ln_marks[j - 2])
            ld = c.load(xin, t["x1s"][row:row + 128, :], sl=2 + (j % 2))
            c.waitv("vector", ld)
            c.waitv("scalar", ld)
            ln_tile(xin, A(L_XN1 + j * 1024, D), A(L_LOC, D))
            ln_done = c.vmark()
            ln_marks.append(ln_done)
            c.wait("tensor", "ve", ln_done)
            transpose_1024(L_XN1 + j * 1024, L_XN1T, j)
        xn1T_done = c.vmark()
        pe_marks = []
        for kc in range(8):
            buf = kc % 2
            wap = A(L_WS + buf * 1024, 1024)
            if kc >= 2:
                c.wait("sync", "pe", pe_marks[kc - 2])
            ld = c.load(wap, t["WqT"][kc * 128:(kc + 1) * 128, :], sl=kc % 2)
            c.waitv("tensor", ld)
            c.wait("tensor", "ve", xn1T_done)
            for n in range(8):
                if kc == 0:
                    c.bank_wait_te(n)
                i = c.mm(ps[n][:, 0:512], wap[:, n * 128:(n + 1) * 128],
                         A(L_XN1T + kc * 512, 512), start=(kc == 0), stop=(kc == 7))
            pe_marks.append(c.inc(i, "pe"))
        q_done = c.c["pe"]
        for n in range(8):
            c.wait("vector", "pe", q_done)
            c.v("tensor_copy", out=A(L_QT + n * 512, 512), in_=ps[n][:, 0:512], inc=True)
            c.bank_free(n, "ve", c.c["ve"])
        c.vmark()
        l1_kv_tiles(2 + g * 4, 4, L_XN1T)

    def l1att(g):
        if _DBG4 and g == 0:
            c.wait("gpsimd", "ve", c.c["ve"])
            d1 = c.store(t["dbg_kT"], A(L_KT, 2560), sl=1)
            d2 = c.store(t["dbg_vv"], A(L_VV, 1300), sl=1)
            d3 = c.store(t["dbg_qT"], A(L_QT, 4096), sl=1)
            c.waitv("sync", d3)
        for j in range(4):
            tt = g * 4 + j
            if tt == 0:
                msks = [t["m00"], t["m01"], None]
            elif tt == 1:
                msks = [t["m10"], None, None]
            else:
                msks = [t["triA"], None, None]
            for hh in range(NH):
                sb = hh % 2
                c.bank_wait_te(sb)
                for jc in range(3):
                    i = c.mm(ps[sb][:, jc * 128:(jc + 1) * 128],
                             A(L_KT + (tt + jc) * 128, 128, p0=(hh % 2) * 64, part=64),
                             A(L_QT + (hh // 2) * 512 + j * 128, 128, p0=(hh % 2) * 64, part=64),
                             start=True, stop=True)
                s_done = c.inc(i, "pe")
                c.wait("scalar", "pe", s_done)
                pb = L_P + (hh % 2) * 384
                c.a("activation", A(pb, 384), ps[sb][:, 0:384], AF.Exp, scale=SCALE, inc=True)
                exp_done = c.c["ac"]
                c.bank_free(sb, "ac", exp_done)
                c.wait("vector", "ac", exp_done)
                c.v("tensor_tensor", out=A(pb, 128), in0=A(pb, 128), in1=msks[0][:], op=ALU.mult)
                if msks[1] is not None:
                    c.v("tensor_tensor", out=A(pb + 128, 128), in0=A(pb + 128, 128), in1=msks[1][:], op=ALU.mult)
                c.v("tensor_tensor", out=A(pb + 256, 128), in0=A(pb + 256, 128), in1=t["triB"][:], op=ALU.mult)
                c.drain("vector")
                p_done = c.vmark()
                ob = 4 + (hh % 2)
                c.bank_wait_te(ob)
                c.wait("tensor", "ve", p_done)
                for jc in range(3):
                    i = c.mm(ps[ob][:, 0:65], A(pb + jc * 128, 128), A(L_VV + (tt + jc) * 65, 65),
                             start=(jc == 0), stop=(jc == 2))
                o_done = c.inc(i, "pe")
                c.wait("vector", "pe", o_done)
                c.v("reciprocal", out=st[:, 6:7], in_=ps[ob][:, 64:65])
                c.drain("vector")
                c.v("tensor_scalar", out=A(L_AOT2 + hh * 64, 64), in0=ps[ob][:, 0:64],
                    scalar1=st[:, 6:7], scalar2=None, op0=ALU.mult, inc=True)
                c.drain("vector")
                c.bank_free(ob, "ve", c.c["ve"])
            ao_done = c.vmark()
            c.wait("tensor", "ve", ao_done)
            transpose_1024(L_AOT2, L_LOC, j)
        aoT_done = c.vmark()
        ldrow = c.load(t["rowbuf"][0:1, 0:1024], t["bf_row"], sl=4)
        pe_marks = []
        for kc in range(8):
            buf = kc % 2
            wap = A(L_WS + buf * 1024, 1024)
            if kc >= 2:
                c.wait("sync", "pe", pe_marks[kc - 2])
            ld = c.load(wap, t["WfT"][kc * 128:(kc + 1) * 128, :], sl=kc % 2)
            c.waitv("tensor", ld)
            c.wait("tensor", "ve", aoT_done)
            for j in range(4):
                for hf in range(2):
                    bk = j * 2 + hf
                    if kc == 0:
                        c.bank_wait_te(bk)
                    i = c.mm(ps[bk][:, 0:512], A(L_LOC + kc * 512 + j * 128, 128),
                             wap[:, hf * 512:(hf + 1) * 512], start=(kc == 0), stop=False)
            if kc == 7:
                c.waitv("tensor", ldrow)
                for j in range(4):
                    for hf in range(2):
                        bk = j * 2 + hf
                        i = c.mm(ps[bk][:, 0:512], t["onesrow"][0:1, 0:128],
                                 t["rowbuf"][0:1, hf * 512:(hf + 1) * 512], start=False, stop=True)
            pe_marks.append(c.inc(i, "pe"))
        s_done = c.c["pe"]
        for j in range(4):
            for hf in range(2):
                bk = j * 2 + hf
                c.wait("vector", "pe", s_done)
                c.v("tensor_copy", out=A(L_S1 + j * 1024 + hf * 512, 512), in_=ps[bk][:, 0:512], inc=True)
                c.bank_free(bk, "ve", c.c["ve"])
        c.vmark()

    def l1ffn(g):
        for half in range(2):
            hv = [half * 2, half * 2 + 1]
            for jj, j in enumerate(hv):
                ln_tile(A(L_XN1 + j * 1024, D), A(L_LOC + 1024, D), A(L_LOC, D))
                c.v("tensor_tensor", out=A(L_LOC + 1024, D), in0=A(L_LOC + 1024, D),
                    in1=A(L_GBC, D), op=ALU.mult)
                c.drain("vector")
                c.v("tensor_add", out=A(L_LOC + 1024, D), in0=A(L_LOC + 1024, D), in1=A(L_GBC + 1024, D))
                c.drain("vector")
                h2_done = c.vmark()
                c.wait("tensor", "ve", h2_done)
                for hlf in range(2):
                    bk = 6 + hlf
                    c.bank_wait_te(bk)
                    for k in range(4):
                        kk = hlf * 4 + k
                        i = c.tp(ps[bk][:, k * 128:(k + 1) * 128], A(L_LOC + 1024 + kk * 128, 128), 128)
                    pe_v = c.inc(i, "pe")
                    c.wait("vector", "pe", pe_v)
                    for k in range(4):
                        kk = hlf * 4 + k
                        c.v("tensor_copy", out=A(L_H2T + kk * 256 + jj * 128, 128),
                            in_=ps[bk][:, k * 128:(k + 1) * 128], inc=(k == 3))
                    c.bank_free(bk, "ve", c.c["ve"])
            h2T_done = c.vmark()
            pe_marks = []
            pe_half0 = c.c["pe"]
            idx = 0
            for ng in range(4):
                for kc in range(8):
                    buf = idx % 2
                    wap = A(L_WS + buf * 1024, 1024)
                    if idx >= 2:
                        c.wait("sync", "pe", pe_marks[idx - 2])
                    else:
                        c.wait("sync", "pe", pe_half0)
                    ld = c.load(wap, t["w1d"][kc * 128:(kc + 1) * 128, ng * 1024:(ng + 1) * 1024], sl=idx % 2)
                    c.waitv("tensor", ld)
                    c.wait("tensor", "ve", h2T_done)
                    for n in range(8):
                        if kc == 0:
                            c.bank_wait_te(n)
                        i = c.mm(ps[n][:, 0:256], wap[:, n * 128:(n + 1) * 128],
                                 A(L_H2T + kc * 256, 256), start=(kc == 0), stop=(kc == 7))
                    pe_marks.append(c.inc(i, "pe"))
                    idx += 1
                u_done = c.c["pe"]
                for n in range(8):
                    ngl = ng * 8 + n
                    c.wait("scalar", "pe", u_done)
                    c.a("activation", A(L_YT + ngl * 256, 256), ps[n][:, 0:256], AF.Silu,
                        bias=t["fc1bC"][:, ngl:ngl + 1], scale=1.0, inc=True)
                    c.bank_free(n, "ac", c.c["ac"])
            yT_done = c.amark()
            ldrow = c.load(t["rowbuf"][0:1, 0:1024], t["fc2b_row"], sl=4)
            pe_marks2 = []
            for kc in range(32):
                buf = kc % 2
                wap = A(L_WS + buf * 1024, 1024)
                if kc >= 2:
                    c.wait("sync", "pe", pe_marks2[kc - 2])
                else:
                    c.wait("sync", "pe", pe_marks[-1])
                ld = c.load(wap, t["w2d"][kc * 128:(kc + 1) * 128, :], sl=kc % 2)
                c.waitv("tensor", ld)
                c.wait("tensor", "ac", yT_done)
                for jj in range(2):
                    for hf in range(2):
                        bk = jj * 2 + hf
                        if kc == 0:
                            c.bank_wait_te(bk)
                        i = c.mm(ps[bk][:, 0:512], A(L_YT + kc * 256 + jj * 128, 128),
                                 wap[:, hf * 512:(hf + 1) * 512], start=(kc == 0), stop=False)
                if kc == 31:
                    c.waitv("tensor", ldrow)
                    for jj in range(2):
                        for hf in range(2):
                            bk = jj * 2 + hf
                            i = c.mm(ps[bk][:, 0:512], t["onesrow"][0:1, 0:128],
                                     t["rowbuf"][0:1, hf * 512:(hf + 1) * 512], start=False, stop=True)
                pe_marks2.append(c.inc(i, "pe"))
            z_done = c.c["pe"]
            for jj, j in enumerate(hv):
                tt = g * 4 + j
                xin = A(L_XIN + jj * 1024, D)
                row = 256 + tt * 128
                ld = c.load(xin, t["x1s"][row:row + 128, :], sl=2 + jj)
                for hf in range(2):
                    bk = jj * 2 + hf
                    c.wait("vector", "pe", z_done)
                    c.v("scalar_tensor_tensor", out=A(L_LOC + 2048 + hf * 512, 512),
                        in0=ps[bk][:, 0:512], scalar=0.5, in1=A(L_XN1 + j * 1024 + hf * 512, 512),
                        op0=ALU.mult, op1=ALU.add)
                pre_done = c.vmark()
                c.wait("scalar", "ve", pre_done)
                c.a("activation", A(L_LOC + 1024, D), A(L_LOC + 2048, D), AF.Silu, inc=True)
                g2_done = c.c["ac"]
                if _DBG4:
                    c.wait("gpsimd", "ac", g2_done)
                    d1 = c.store(t["dbg_g2"][tt * 128:(tt + 1) * 128, :], A(L_LOC + 1024, D), sl=1)
                    d2 = c.store(t["dbg_s1"][tt * 128:(tt + 1) * 128, :], A(L_S1 + j * 1024, D), sl=1)
                    c.waitv("vector", d2)
                    c.waitv("scalar", d2)
                c.wait("vector", "ac", g2_done)
                c.v("tensor_tensor", out=A(L_LOC + 1024, D), in0=A(L_LOC + 1024, D),
                    in1=A(L_S1 + j * 1024, D), op=ALU.mult)
                c.drain("vector")
                c.waitv("vector", ld)
                c.v("tensor_add", out=A(L_LOC + 1024, D), in0=A(L_LOC + 1024, D), in1=xin, inc=True)
                x2_done = c.c["ve"]
                c.bank_free(jj * 2, "ve", x2_done)
                c.bank_free(jj * 2 + 1, "ve", x2_done)
                c.drain("vector")
                c.vmark()
                c.wait("gpsimd", "ve", c.c["ve"])
                s_ = c.store(t["out"][tt * 128:(tt + 1) * 128, :], A(L_LOC + 1024, D))
                c.waitv("vector", s_)
                c.waitv("scalar", s_)
                c.waitv("sync", s_)

    for g in range(4):
        l1pre(g)
        c.bar()
        l1att(g)
        c.bar()
        l1ffn(g)
        c.bar()

    c.wait_stores("gpsimd")
    c.bar()


def _build():
    nc = bass.Bass("TRN2", target_bir_lowering=False)
    t = {}
    def din(name, shape, dt=F32):
        t[name] = nc.dram_tensor(name, shape, dt, kind="ExternalInput").ap()
    din("x_own", [2048, D])
    din("x_pre2", [512, D])
    din("x_wrap", [512, D])
    din("W_qkv", [D, 3072])
    din("W_out", [D, D])
    din("w_gate", [FF0, D])
    din("w_up", [FF0, D])
    din("w_down", [D, FF0])
    din("Wq", [D, D])
    din("Wk", [HD, D])
    din("Wv", [HD, D])
    din("Wf", [D, D])
    din("fc1_q", [D, FF1], I32)
    din("fc1_s", [8, FF1])
    din("fc2_q", [FF1, D], I32)
    din("fc2_s", [32, D])
    din("bqkv_t", [24, 128])
    din("bqkv_row", [1, 3072])
    din("bout_row", [1, D])
    din("bf_row", [1, D])
    din("fc1b_t", [32, 128])
    din("fc2b_row", [1, D])
    din("gam_row", [1, D])
    din("bet_row", [1, D])
    din("ident_in", [128, 128])
    din("triA_in", [128, 128])
    din("triB_in", [128, 128])
    din("m00_in", [128, 128])
    din("m01_in", [128, 128])
    din("m10_in", [128, 128])
    t["out"] = nc.dram_tensor("out", [2048, D], F32, kind="ExternalOutput").ap()
    if _LIMIT == "prologue":
        for name, shape in (("wgT", [D, FF0]), ("wuT", [D, FF0]), ("wdT", [FF0, D]),
                            ("w1d", [D, FF1]), ("w2d", [FF1, D])):
            t["dbg_" + name] = nc.dram_tensor("dbg_" + name, shape, F32, kind="ExternalOutput").ap()
    if _LIMIT == "L0":
        t["dbg_x1s"] = nc.dram_tensor("dbg_x1s", [2304, D], F32, kind="ExternalOutput").ap()
    if _DBG2:
        t["dbg_s"] = nc.dram_tensor("dbg_s", [2304, D], F32, kind="ExternalOutput").ap()
        t["dbg_g"] = nc.dram_tensor("dbg_g", [2304, D], F32, kind="ExternalOutput").ap()
    if _DBG3:
        t["dbg_xqT"] = nc.dram_tensor("dbg_xqT", [128, 2048], F32, kind="ExternalOutput").ap()
        t["dbg_hqT"] = nc.dram_tensor("dbg_hqT", [128, 5632], F32, kind="ExternalOutput").ap()
    if _DBG4:
        t["dbg_s1"] = nc.dram_tensor("dbg_s1", [2048, D], F32, kind="ExternalOutput").ap()
        t["dbg_g2"] = nc.dram_tensor("dbg_g2", [2048, D], F32, kind="ExternalOutput").ap()
        t["dbg_kT"] = nc.dram_tensor("dbg_kT", [128, 2560], F32, kind="ExternalOutput").ap()
        t["dbg_vv"] = nc.dram_tensor("dbg_vv", [128, 1300], F32, kind="ExternalOutput").ap()
        t["dbg_qT"] = nc.dram_tensor("dbg_qT", [128, 4096], F32, kind="ExternalOutput").ap()
    def dscr(name, shape, dt=F32):
        t[name] = nc.dram_tensor(name, shape, dt, kind="Internal").ap()
    dscr("x1s", [2304, D])
    dscr("wgT", [D, FF0])
    dscr("wuT", [D, FF0])
    dscr("wdT", [FF0, D])
    dscr("w1d", [D, FF1])
    dscr("w2d", [FF1, D])
    dscr("WqT", [D, D])
    dscr("WfT", [D, D])
    dscr("WkT", [D, HD])
    dscr("WvT", [D, HD])

    ctx = contextlib.ExitStack()
    with ctx:
        t["arena"] = ctx.enter_context(nc.sbuf_tensor("arena", [128, ARENA], F32)).ap()
        for name, shape in (
            ("ident", [128, 128]), ("triA", [128, 128]), ("triB", [128, 128]),
            ("m00", [128, 128]), ("m01", [128, 128]), ("m10", [128, 128]),
            ("onesrow", [1, 128]), ("onescol", [128, 1]), ("eps", [128, 1]),
            ("bqkvT_s", [24, 128]), ("fc1bT_s", [32, 128]),
            ("bqkvC", [128, 24]), ("fc1bC", [128, 32]),
            ("stats", [128, 16]), ("qsc", [128, 192]), ("rowbuf", [1, 3072]),
        ):
            t[name] = ctx.enter_context(nc.sbuf_tensor(name, shape, F32))
        t["psum"] = [ctx.enter_context(nc.psum_tensor(f"p{i}", [128, 512], F32)) for i in range(8)]
        sem = {}
        semnames = ("pe", "ve", "ac") + tuple("dl%d" % i for i in range(6)) + ("ds0", "ds1") \
            + tuple("b_" + e for e in ENGINES)
        for name in semnames:
            sem[name] = ctx.enter_context(nc.semaphore(name))
        block = ctx.enter_context(nc.Block())

        def mk(en):
            def f(eng):
                _emit(C(nc, en, eng, sem, t))
            return f
        block.sync(mk("sync"))
        block.tensor(mk("tensor"))
        block.vector(mk("vector"))
        block.scalar(mk("scalar"))
        block.gpsimd(mk("gpsimd"))
    return nc


def _host_inputs(inputs):
    _ = inputs["L1_attn_segment_pos"]
    x = np.ascontiguousarray(np.asarray(inputs["L0_attn_x"], dtype=np.float32))
    i32 = lambda a: np.ascontiguousarray(np.asarray(a, dtype=np.int32))
    f32 = lambda a: np.ascontiguousarray(np.asarray(a, dtype=np.float32))
    kk = np.arange(128)[:, None]
    qq = np.arange(128)[None, :]
    triA = (qq <= kk).astype(np.float32)
    triB = (qq >= kk).astype(np.float32)
    full = np.ones((128, 128), np.float32)
    zero = np.zeros((128, 128), np.float32)
    common = dict(
        W_qkv=f32(inputs["L0_attn_W_qkv"]),
        W_out=f32(inputs["L0_attn_W_out"]),
        w_gate=f32(inputs["L0_ffn_w_gate"]),
        w_up=f32(inputs["L0_ffn_w_up"]),
        w_down=f32(inputs["L0_ffn_w_down"]),
        Wq=f32(inputs["L1_attn_Wq"]),
        Wk=f32(inputs["L1_attn_Wk"]),
        Wv=f32(inputs["L1_attn_Wv"]),
        Wf=f32(inputs["L1_attn_Wf"]),
        fc1_q=i32(inputs["L1_ffn_fc1_q"]),
        fc1_s=f32(inputs["L1_ffn_fc1_scales"]),
        fc2_q=i32(inputs["L1_ffn_fc2_q"]),
        fc2_s=f32(inputs["L1_ffn_fc2_scales"]),
        bqkv_t=f32(inputs["L0_attn_b_qkv"]).reshape(24, 128),
        bqkv_row=f32(inputs["L0_attn_b_qkv"]).reshape(1, 3072),
        bout_row=f32(inputs["L0_attn_b_out"]).reshape(1, D),
        bf_row=f32(inputs["L1_attn_bf"]).reshape(1, D),
        fc1b_t=f32(inputs["L1_ffn_fc1_bias"]).reshape(32, 128),
        fc2b_row=f32(inputs["L1_ffn_fc2_bias"]).reshape(1, D),
        gam_row=f32(inputs["L1_ffn_ln_gamma"]).reshape(1, D),
        bet_row=f32(inputs["L1_ffn_ln_beta"]).reshape(1, D),
        ident_in=np.eye(128, dtype=np.float32),
        triA_in=triA, triB_in=triB,
    )
    in_maps = []
    for cid in range(8):
        b, half = cid // 2, cid % 2
        t0 = half * 2048
        m = dict(common)
        m["x_own"] = np.ascontiguousarray(x[b, t0:t0 + 2048])
        m["x_pre2"] = np.ascontiguousarray(x[b, 1536:2048])
        m["x_wrap"] = np.ascontiguousarray(x[b, 2048:2560] if half == 0 else x[b, 0:512])
        if half == 0:
            m["m00_in"], m["m01_in"], m["m10_in"] = zero, zero, zero
        else:
            m["m00_in"], m["m01_in"], m["m10_in"] = triA, full, triA
        in_maps.append(m)
    return x, in_maps


def kernel(**inputs):
    x, in_maps = _host_inputs(inputs)
    nc = _build()
    res = run_bass_kernel_spmd(nc, in_maps, core_ids=list(range(8)))
    out = np.empty((4, 4096, D), dtype=np.float32)
    for cid in range(8):
        b, half = cid // 2, cid % 2
        out[b, half * 2048:(half + 1) * 2048] = res.results[cid]["out"]
    return out


# version 06
# speedup vs baseline: 1.2022x; 1.0157x over previous
import contextlib
import numpy as np

import concourse.bass as bass
import concourse.mybir as mybir
from concourse.bass_utils import run_bass_kernel_spmd

F32 = mybir.dt.float32
BF16 = mybir.dt.bfloat16
I32 = mybir.dt.int32
AF = mybir.ActivationFunctionType
ALU = mybir.AluOpType
AX = mybir.AxisListType.X

D = 1024
NH, HD = 16, 64
FF0 = 2816
FF1 = 4096
SCALE = 0.125
MID = [0.25, 0.75, 1.25, 1.75, 2.5, 3.5, 5.0]
DG = [0.5, 0.5, 0.5, 0.5, 1.0, 1.0, 2.0]
N0CH = [512, 512, 512, 512, 512, 256]
ENGINES = ("sync", "tensor", "vector", "scalar", "gpsimd")

_DBG = False
_DBG2 = False
_DBG3 = False
_DBG4 = False
_LIMIT = None

ARENA = 43840
A_KT = 0
A_VV = 4096
A_Q = (8256, 12352)
A_AO = 16448
A_S = 20544
A_P = 24640
P_XNT = 26688
P_XSQ = 30784
P_XN = 31808
P_XIN = 32832
P_WS = 34880
W_AOT = 26688
W_WS = 36416
F_HQT = 26688
F_XQT = 32320
F_XIN = 34368
F_WS = 36416
F_WS2 = 36416
F_XN = 38464
F_SCR = 39488
F_H = 42048
F_HQ = 42560
F_H1 = A_P + 1024
F_HQ1 = A_P + 1536
Q_CB = [F_SCR + 1536 + i * 256 for i in range(4)] + [A_P + i * 256 for i in range(3)]
F_XQT_B = 16448
F_XIN_B = 18496
F_G = F_XN
G_WIN = 26688
G_WQ = 30784
G_STG = 33600
G_WS = 34624
G_SROW = 38720
L_XN1 = 0
L_XN1T = 4096
L_QT = 8192
L_KT = 12288
L_VV = 14848
L_S1 = 16192
L_H2T = 20288
L_P = 22336
L_YT = 24384
L_GBC = 32576
L_LOC = 34624
L_WS = 38720
L_WKV = 40768
L_AOT2 = 40768
L_XIN = 41792

assert L_XIN + 2048 <= ARENA


class C:

    def __init__(self, nc, en, eng, sem, t):
        self.nc, self.en, self.eng, self.sem, self.t = nc, en, eng, sem, t
        self.c = {k: 0 for k in sem}
        self.nbar = 0
        self.bankw = [None] * 8

    def _do(self, en, f):
        if self.en == en:
            return f(self.eng)
        return None

    def on(self, en):
        return self.en == en

    def inc(self, instr, s, k=1):
        self.c[s] += k
        if instr is not None:
            instr.then_inc(self.sem[s], k)
        return self.c[s]

    def wait(self, en, s, val=None):
        if val is None:
            val = self.c[s]
        if self.en == en:
            self.eng.wait_ge(self.sem[s], val)

    def drain(self, en):
        if self.en == en:
            self.eng.drain()

    def vmark(self):
        s = self.t["stats"]
        return self.inc(self._do("vector", lambda e: e.tensor_copy(out=s[:, 7:8], in_=s[:, 7:8])), "ve")

    def amark(self):
        s = self.t["stats"]
        return self.inc(self._do("scalar", lambda e: e.copy(s[:, 7:8], s[:, 7:8])), "ac")

    def bar(self):
        self.nbar += 1
        for e in ENGINES:
            if self.en == e:
                self.eng.drain().then_inc(self.sem["b_" + e], 1)
        for e in ENGINES:
            if e != self.en:
                self.wait(self.en, "b_" + e, self.nbar)
        for e in ENGINES:
            self.c["b_" + e] = self.nbar

    def load(self, out_ap, in_ap, sl=0):
        s = "dl%d" % sl
        i = self._do("sync", lambda e: e.dma_start(out=out_ap, in_=in_ap))
        return (s, self.inc(i, s, 16))

    def store(self, out_ap, in_ap, sl=0):
        s = "ds%d" % sl
        i = self._do("gpsimd", lambda e: e.dma_start(out=out_ap, in_=in_ap))
        return (s, self.inc(i, s, 16))

    def waitv(self, en, tok):
        if self.en == en:
            self.eng.wait_ge(self.sem[tok[0]], tok[1])

    def wait_stores(self, en):
        for s in ("ds0", "ds1"):
            self.wait(en, s, self.c[s])

    def mm(self, out, lhsT, rhs, start, stop, tile_position=None):
        return self._do("tensor", lambda e: e.matmul(out, lhsT, rhs, start=start, stop=stop,
                                                     tile_position=tile_position))

    def tp(self, out, in_, np_):
        return self._do("tensor", lambda e: e.transpose(out, in_, self.t["ident"][0:np_, 0:np_]))

    def v(self, meth, *a, inc=False, **kw):
        i = self._do("vector", lambda e: getattr(e, meth)(*a, **kw))
        if inc:
            return self.inc(i, "ve")
        return i

    def a(self, meth, *a, inc=False, **kw):
        i = self._do("scalar", lambda e: getattr(e, meth)(*a, **kw))
        if inc:
            return self.inc(i, "ac")
        return i

    def bank_wait_te(self, b):
        if self.bankw[b] is not None:
            s, v = self.bankw[b]
            self.wait("tensor", s, v)
            self.bankw[b] = None

    def bank_free(self, b, s, v):
        self.bankw[b] = (s, v)


def _emit(c):
    t = c.t
    ps = t["psum"]
    ar = t["arena"]

    def A(off, n, p0=0, part=128):
        return ar[p0:p0 + part, off:off + n]

    st = t["stats"]

    def ln_tile(src, dst, xsq, n=D):
        c.wait("scalar", "ve", c.c["ve"])
        c.a("activation", xsq, src, AF.Square, inc=True)
        sq_done = c.c["ac"]
        c.v("reduce_sum", out=st[:, 0:1], in_=src, axis=AX)
        c.drain("vector")
        c.v("tensor_scalar", out=st[:, 0:1], in0=st[:, 0:1], scalar1=1.0 / n, scalar2=None, op0=ALU.mult)
        c.drain("vector")
        c.wait("vector", "ac", sq_done)
        c.v("reduce_sum", out=st[:, 1:2], in_=xsq, axis=AX)
        c.drain("vector")
        c.v("tensor_scalar", out=st[:, 1:2], in0=st[:, 1:2], scalar1=1.0 / n, scalar2=None, op0=ALU.mult)
        c.v("tensor_scalar", out=st[:, 2:3], in0=st[:, 0:1], scalar1=st[:, 0:1], scalar2=None, op0=ALU.mult)
        c.drain("vector")
        c.v("tensor_sub", out=st[:, 3:4], in0=st[:, 1:2], in1=st[:, 2:3], inc=True)
        var_done = c.c["ve"]
        c.wait("scalar", "ve", var_done)
        c.a("activation", st[:, 4:5], st[:, 3:4], AF.Sqrt, bias=t["eps"][:, 0:1], scale=1.0, inc=True)
        sd_done = c.c["ac"]
        c.wait("vector", "ac", sd_done)
        c.v("reciprocal", out=st[:, 5:6], in_=st[:, 4:5])
        c.drain("vector")
        c.v("tensor_scalar", out=dst, in0=src, scalar1=st[:, 0:1], scalar2=st[:, 5:6],
            op0=ALU.subtract, op1=ALU.mult)
        c.drain("vector")

    def fp4_quant(src, dst, n):
        assert n <= 512
        nb = n // 16
        tt = A(F_SCR, n)
        sg = A(F_SCR + 512, n)
        at = A(F_SCR + 1024, n)
        cb = [ar[0:128, o:o + (n + 1) // 2].bitcast(BF16) for o in Q_CB]
        qsc = t["qsc"]
        bmv, scv, rsv = qsc[:, 0:nb], qsc[:, 64:64 + nb], qsc[:, 128:128 + nb]
        src3 = src.rearrange("p (g k) -> p g k", k=16)
        dst3 = dst.rearrange("p (g k) -> p g k", k=16)
        t3 = tt.rearrange("p (g k) -> p g k", k=16)
        c.v("reduce_max", out=bmv, in_=src3, axis=AX, apply_absolute_value=True)
        c.drain("vector")
        c.v("tensor_scalar", out=scv, in0=bmv, scalar1=1.0 / 6.0, scalar2=1e-12, op0=ALU.mult, op1=ALU.max)
        c.drain("vector")
        c.v("reciprocal", out=rsv, in_=scv)
        c.drain("vector")
        c.v("tensor_tensor", out=t3, in0=src3, in1=rsv.unsqueeze(2).broadcast_to([128, nb, 16]),
            op=ALU.mult, inc=True)
        t_done = c.c["ve"]
        c.wait("scalar", "ve", t_done)
        c.a("activation", at, tt, AF.Abs)
        c.a("activation", sg, tt, AF.Sign, inc=True)
        absgn_done = c.c["ac"]
        c.wait("vector", "ac", absgn_done)
        for j in range(7):
            c.v("tensor_scalar", out=cb[j], in0=at, scalar1=MID[j], scalar2=DG[j],
                op0=ALU.is_gt, op1=ALU.mult)
        c.drain("vector")
        c.v("tensor_add", out=cb[0], in0=cb[0], in1=cb[1])
        c.v("tensor_add", out=cb[2], in0=cb[2], in1=cb[3])
        c.v("tensor_add", out=cb[4], in0=cb[4], in1=cb[5])
        c.drain("vector")
        c.v("tensor_add", out=cb[0], in0=cb[0], in1=cb[2])
        c.v("tensor_add", out=cb[4], in0=cb[4], in1=cb[6])
        c.drain("vector")
        c.v("tensor_add", out=cb[0], in0=cb[0], in1=cb[4])
        c.drain("vector")
        c.v("tensor_tensor", out=tt, in0=sg, in1=cb[0], op=ALU.mult)
        c.drain("vector")
        c.v("tensor_tensor", out=dst3, in0=t3, in1=scv.unsqueeze(2).broadcast_to([128, nb, 16]),
            op=ALU.mult)
        c.drain("vector")

    def transpose_1024(src_off, dst_off, j, banks=(6, 7), src_stride=128):
        for half in range(2):
            bk = banks[half]
            c.bank_wait_te(bk)
            for k in range(4):
                kk = half * 4 + k
                i = c.tp(ps[bk][:, k * 128:(k + 1) * 128], A(src_off + kk * 128, 128), 128)
            pe_v = c.inc(i, "pe")
            c.wait("vector", "pe", pe_v)
            for k in range(4):
                kk = half * 4 + k
                c.v("tensor_copy", out=A(dst_off + kk * 512 + j * 128, 128),
                    in_=ps[bk][:, k * 128:(k + 1) * 128], inc=(k == 3))
            c.bank_free(bk, "ve", c.c["ve"])

    for name in ("ident", "triA", "triB", "m00", "m01", "m10"):
        c.load(t[name][:], t[name + "_in"], sl=4)
    c.load(t["bqkvT_s"][:], t["bqkv_t"], sl=4)
    ld0 = c.load(t["fc1bT_s"][:], t["fc1b_t"], sl=4)
    if c.on("vector"):
        c.eng.memset(t["onesrow"][:], 1.0)
        c.eng.memset(t["onescol"][:], 1.0)
        c.eng.memset(t["eps"][:], 1e-5)
        vc = ar[0:128, A_VV:A_VV + 4 * NH * 65].rearrange("p (cell w) -> p cell w", w=65)
        c.eng.memset(vc[:, :, 64:65], 1.0)
    c.vmark()
    c.waitv("tensor", ld0)
    c.tp(ps[0][:, 0:24], t["bqkvT_s"][:], 24)
    i = c.tp(ps[1][:, 0:32], t["fc1bT_s"][:], 32)
    c.inc(i, "pe")
    c.wait("vector", "pe", c.c["pe"])
    c.v("tensor_copy", out=t["bqkvC"][:], in_=ps[0][:, 0:24])
    c.v("tensor_copy", out=t["fc1bC"][:], in_=ps[1][:, 0:32], inc=True)
    c.bar()

    WPREP = [(G_WIN, G_WQ, G_WS), (8256, 12352, 16448)]
    wrow_free = [0, 0]
    wrow_store = [None, None]
    ri = 0
    for (wsrc, wdst, nrows, ncols) in (
        (t["w_gate"], t["wgT"], FF0, D),
        (t["w_up"], t["wuT"], FF0, D),
        (t["w_down"], t["wdT"], D, FF0),
    ):
        for r in range(nrows // 128):
            bset = ri % 2
            ri += 1
            WIN, WQ, WS = WPREP[bset]
            win = A(WIN, ncols)
            c.wait("sync", "ve", wrow_free[bset])
            if wrow_store[bset] is not None:
                c.waitv("sync", wrow_store[bset])
            ld = c.load(win, wsrc[r * 128:(r + 1) * 128, :], sl=bset)
            c.waitv("vector", ld)
            for hh in range((ncols + 511) // 512):
                n = min(512, ncols - hh * 512)
                fp4_quant(win[:, hh * 512:hh * 512 + n], A(WQ + hh * 512, n), n)
            q_done = c.vmark()
            nct = ncols // 128
            for k in range(nct):
                bk = k % 4
                c.bank_wait_te(bk)
                c.wait("tensor", "ve", q_done)
                i = c.tp(ps[bk][:, 0:128], A(WQ + k * 128, 128), 128)
                pe_v = c.inc(i, "pe")
                c.wait("vector", "pe", pe_v)
                c.v("tensor_copy", out=A(WS + k * 128, 128), in_=ps[bk][:, 0:128], inc=True)
                c.bank_free(bk, "ve", c.c["ve"])
            row_done = c.vmark()
            wrow_free[bset] = row_done
            c.wait("gpsimd", "ve", row_done)
            s_ = c.store(wdst.rearrange("(a p) b -> p a b", p=128)[:, :, r * 128:(r + 1) * 128],
                         A(WS, nct * 128).rearrange("p (a b) -> p a b", b=128), sl=bset)
            wrow_store[bset] = s_
    c.bar()

    for (qsrc, ssrc, wdst, N, ng) in (
        (t["fc1_q"], t["fc1_s"], t["w1d"], FF1, 8),
        (t["fc2_q"], t["fc2_s"], t["w2d"], D, 32),
    ):
        for g in range(ng):
            qin = A(G_WIN, N).bitcast(I32)
            ld = c.load(qin, qsrc[g * 128:(g + 1) * 128, :], sl=0)
            lds = c.load(ar[0:1, G_SROW:G_SROW + N], ssrc[g:g + 1, :], sl=4)
            c.waitv("vector", ld)
            c.v("tensor_scalar", out=A(G_WS, N), in0=qin, scalar1=128.0, scalar2=None,
                op0=ALU.subtract, inc=True)
            qf_done = c.c["ve"]
            for ch in range(N // 512):
                bk = ch % 2
                c.bank_wait_te(bk)
                c.waitv("tensor", lds)
                i = c.mm(ps[bk][:, 0:512], t["onesrow"][0:1, 0:128],
                         ar[0:1, G_SROW + ch * 512:G_SROW + (ch + 1) * 512], start=True, stop=True)
                pe_v = c.inc(i, "pe")
                c.wait("vector", "pe", pe_v)
                c.wait("vector", "ve", qf_done)
                c.v("tensor_tensor", out=A(G_WIN + ch * 512, 512), in0=A(G_WS + ch * 512, 512),
                    in1=ps[bk][:, 0:512], op=ALU.mult, inc=True)
                c.bank_free(bk, "ve", c.c["ve"])
            wrow = c.vmark()
            c.wait("gpsimd", "ve", wrow)
            s_ = c.store(wdst[g * 128:(g + 1) * 128, :], A(G_WIN, N))
            c.waitv("vector", s_)
            c.waitv("sync", s_)
    for (wsrc, wdst) in ((t["Wq"], t["WqT"]), (t["Wf"], t["WfT"])):
        for r in range(8):
            ld = c.load(A(G_WIN, D), wsrc[r * 128:(r + 1) * 128, :], sl=0)
            for k in range(8):
                bk = k % 4
                c.bank_wait_te(bk)
                c.waitv("tensor", ld)
                i = c.tp(ps[bk][:, 0:128], A(G_WIN + k * 128, 128), 128)
                pe_v = c.inc(i, "pe")
                c.wait("vector", "pe", pe_v)
                c.v("tensor_copy", out=A(G_WS + k * 128, 128), in_=ps[bk][:, 0:128], inc=True)
                c.bank_free(bk, "ve", c.c["ve"])
            done = c.vmark()
            c.wait("gpsimd", "ve", done)
            s_ = c.store(wdst.rearrange("(a p) b -> p a b", p=128)[:, :, r * 128:(r + 1) * 128],
                         A(G_WS, 1024).rearrange("p (a b) -> p a b", b=128))
            c.waitv("vector", s_)
            c.waitv("sync", s_)
    if _LIMIT == "prologue":
        c.bar()
        for name in ("wgT", "wuT", "wdT", "w1d", "w2d"):
            c.store(t["dbg_" + name], t[name], sl=1)
        c.wait_stores("gpsimd")
        c.bar()
        return
    for (wsrc, wdst) in ((t["Wk"], t["WkT"]), (t["Wv"], t["WvT"])):
        ld = c.load(A(G_WIN, D, part=64), wsrc, sl=0)
        for k in range(8):
            bk = k % 4
            c.bank_wait_te(bk)
            c.waitv("tensor", ld)
            i = c.tp(ps[bk][:, 0:64], A(G_WIN + k * 128, 128, part=64), 64)
            pe_v = c.inc(i, "pe")
            c.wait("vector", "pe", pe_v)
            c.v("tensor_copy", out=A(G_WS + k * 64, 64), in_=ps[bk][:, 0:64], inc=True)
            c.bank_free(bk, "ve", c.c["ve"])
        done = c.vmark()
        c.wait("gpsimd", "ve", done)
        for a in range(8):
            s_ = c.store(wdst[a * 128:(a + 1) * 128, :], A(G_WS + a * 64, 64))
        c.waitv("vector", s_)
        c.waitv("sync", s_)
    c.bar()

    blocks = [
        dict(name="pre2", xsrc=t["x_pre2"], xoff=0, ffn=[2, 3], qt=[2, 3], x1row=-512, q=True),
        dict(name="own0", xsrc=t["x_own"], xoff=0, ffn=[0, 1, 2, 3], qt=[0, 1, 2, 3], x1row=0, q=True),
        dict(name="own1", xsrc=t["x_own"], xoff=512, ffn=[0, 1, 2, 3], qt=[0, 1, 2, 3], x1row=512, q=True),
        dict(name="own2", xsrc=t["x_own"], xoff=1024, ffn=[0, 1, 2, 3], qt=[0, 1, 2, 3], x1row=1024, q=True),
        dict(name="own3", xsrc=t["x_own"], xoff=1536, ffn=[0, 1, 2, 3], qt=[0, 1, 2, 3], x1row=1536, q=True),
        dict(name="wrap", xsrc=t["x_wrap"], xoff=0, ffn=[], qt=[], x1row=None, q=False),
    ]

    def kT_ap(hh, j0, nj):
        return A(A_KT + (hh // 2) * 512 + j0, nj, p0=(hh % 2) * 64, part=64)

    def qT_ap(slot, hh, q0, nq):
        return A(A_Q[slot] + (hh // 2) * 512 + q0, nq, p0=(hh % 2) * 64, part=64)

    def vv_ap(jc, hh):
        return A(A_VV + (jc * NH + hh) * 65, 65)

    def prekv(b, qslot):
        ln_marks = []
        for j in range(4):
            xin = A(P_XIN + (j % 2) * 1024, D)
            if j >= 2:
                c.wait("sync", "ve", ln_marks[j - 2])
            ld = c.load(xin, b["xsrc"][b["xoff"] + j * 128:b["xoff"] + (j + 1) * 128, :], sl=2 + (j % 2))
            c.waitv("vector", ld)
            c.waitv("scalar", ld)
            ln_tile(xin, A(P_XN, D), A(P_XSQ, D))
            ln_done = c.vmark()
            ln_marks.append(ln_done)
            c.wait("tensor", "ve", ln_done)
            transpose_1024(P_XN, P_XNT, j)
        xnT_done = c.vmark()
        ldrow = c.load(t["rowbuf"][0:1, 0:3072], t["bqkv_row"], sl=4)
        groups = ([0] if b["q"] else []) + [1, 2]
        pe_marks = []
        for gi, grp in enumerate(groups):
            for kc in range(8):
                idx = gi * 8 + kc
                buf = idx % 2
                wap = A(P_WS + buf * 1024, 1024)
                if idx >= 2:
                    c.wait("sync", "pe", pe_marks[idx - 2])
                ld = c.load(wap, t["W_qkv"][kc * 128:(kc + 1) * 128, grp * 1024:(grp + 1) * 1024], sl=idx % 2)
                c.waitv("tensor", ld)
                c.wait("tensor", "ve", xnT_done)
                if grp in (0, 1):
                    for n in range(8):
                        if kc == 0:
                            c.bank_wait_te(n)
                        i = c.mm(ps[n][:, 0:512], wap[:, n * 128:(n + 1) * 128],
                                 A(P_XNT + kc * 512, 512), start=(kc == 0), stop=(kc == 7))
                else:
                    for j in range(4):
                        for hf in range(2):
                            bk = j * 2 + hf
                            if kc == 0:
                                c.bank_wait_te(bk)
                            i = c.mm(ps[bk][:, 0:512], A(P_XNT + kc * 512 + j * 128, 128),
                                     wap[:, hf * 512:(hf + 1) * 512], start=(kc == 0), stop=False)
                    if kc == 7:
                        c.waitv("tensor", ldrow)
                        for j in range(4):
                            for hf in range(2):
                                bk = j * 2 + hf
                                i = c.mm(ps[bk][:, 0:512], t["onesrow"][0:1, 0:128],
                                         t["rowbuf"][0:1, 2048 + hf * 512:2048 + (hf + 1) * 512],
                                         start=False, stop=True)
                pe_marks.append(c.inc(i, "pe"))
            mm_done = c.c["pe"]
            if grp == 0:
                for n in range(8):
                    c.wait("vector", "pe", mm_done)
                    c.v("tensor_scalar", out=A(A_Q[qslot] + n * 512, 512), in0=ps[n][:, 0:512],
                        scalar1=t["bqkvC"][:, n:n + 1], scalar2=None, op0=ALU.add, inc=True)
                    c.bank_free(n, "ve", c.c["ve"])
            elif grp == 1:
                for n in range(8):
                    c.wait("vector", "pe", mm_done)
                    c.v("tensor_scalar", out=A(A_KT + n * 512, 512), in0=ps[n][:, 0:512],
                        scalar1=t["bqkvC"][:, 8 + n:9 + n], scalar2=None, op0=ALU.add, inc=True)
                    c.bank_free(n, "ve", c.c["ve"])
            else:
                for j in range(4):
                    for hf in range(2):
                        bk = j * 2 + hf
                        c.wait("vector", "pe", mm_done)
                        dst = ar[0:128, A_VV + (j * NH + hf * 8) * 65:A_VV + (j * NH + hf * 8) * 65 + 8 * 65]
                        c.v("tensor_copy", out=dst.rearrange("p (h w) -> p h w", w=65)[:, :, 0:64],
                            in_=ps[bk][:, 0:512].rearrange("p (h w) -> p h w", w=64), inc=True)
                        c.bank_free(bk, "ve", c.c["ve"])
        c.vmark()

    def att_phase(b, qslot, phase):
        qtiles = b["qt"]
        for hh in range(NH):
            exp_done = {}
            for jc in range(4):
                bk = jc
                c.bank_wait_te(bk)
                i = c.mm(ps[bk][:, 0:512], kT_ap(hh, jc * 128, 128), qT_ap(qslot, hh, 0, 512),
                         start=True, stop=True)
                s_done = c.inc(i, "pe")
                c.wait("scalar", "pe", s_done)
                c.a("activation", A(A_P + jc * 512, 512), ps[bk][:, 0:512], AF.Exp,
                    scale=SCALE, inc=True)
                exp_done[jc] = c.c["ac"]
                c.bank_free(bk, "ac", exp_done[jc])
            ob = 4 + (hh % 2)
            c.bank_wait_te(ob)
            for qt in qtiles:
                for jc in range(4):
                    c.wait("tensor", "ac", exp_done[jc])
                    i = c.mm(ps[ob][:, qt * 65:qt * 65 + 65],
                             A(A_P + jc * 512 + qt * 128, 128), vv_ap(jc, hh),
                             start=(jc == 0), stop=(jc == 3))
            o_done = c.inc(i, "pe")
            c.wait("vector", "pe", o_done)
            for qt in qtiles:
                c.v("reciprocal", out=st[:, 6:7], in_=ps[ob][:, qt * 65 + 64:qt * 65 + 65])
                c.drain("vector")
                ao = A(A_AO + qt * 1024 + hh * 64, 64)
                if phase == 1:
                    c.v("tensor_scalar", out=ao, in0=ps[ob][:, qt * 65:qt * 65 + 64],
                        scalar1=st[:, 6:7], scalar2=None, op0=ALU.mult)
                else:
                    c.v("scalar_tensor_tensor", out=ao, in0=ps[ob][:, qt * 65:qt * 65 + 64],
                        scalar=st[:, 6:7], in1=ao, op0=ALU.mult, op1=ALU.add)
                c.drain("vector")
            c.vmark()
            c.bank_free(ob, "ve", c.c["ve"])

    def wout(b):
        qtiles = b["qt"]
        for j in qtiles:
            transpose_1024(A_AO + j * 1024, W_AOT, j)
        aoT_done = c.vmark()
        ldrow = c.load(t["rowbuf"][0:1, 0:1024], t["bout_row"], sl=4)
        pe_marks = []
        for kc in range(8):
            wap = A(W_WS + (kc % 2) * 1024, 1024)
            if kc >= 2:
                c.wait("sync", "pe", pe_marks[kc - 2])
            ld = c.load(wap, t["W_out"][kc * 128:(kc + 1) * 128, :], sl=kc % 2)
            c.waitv("tensor", ld)
            c.wait("tensor", "ve", aoT_done)
            for ji, j in enumerate(qtiles):
                for hf in range(2):
                    bk = ji * 2 + hf
                    if kc == 0:
                        c.bank_wait_te(bk)
                    i = c.mm(ps[bk][:, 0:512], A(W_AOT + kc * 512 + j * 128, 128),
                             wap[:, hf * 512:(hf + 1) * 512], start=(kc == 0), stop=False)
            if kc == 7:
                c.waitv("tensor", ldrow)
                for ji, j in enumerate(qtiles):
                    for hf in range(2):
                        bk = ji * 2 + hf
                        i = c.mm(ps[bk][:, 0:512], t["onesrow"][0:1, 0:128],
                                 t["rowbuf"][0:1, hf * 512:(hf + 1) * 512], start=False, stop=True)
            pe_marks.append(c.inc(i, "pe"))
        s_done = c.c["pe"]
        for ji, j in enumerate(qtiles):
            for hf in range(2):
                bk = ji * 2 + hf
                c.wait("vector", "pe", s_done)
                c.v("tensor_copy", out=A(A_S + j * 1024 + hf * 512, 512), in_=ps[bk][:, 0:512], inc=True)
                c.bank_free(bk, "ve", c.c["ve"])
        c.vmark()

    def ffnc_pre(b, hv, XQT, XIN):
        c.wait("vector", "pe", c.c["pe"])
        lds = []
        for jj, j in enumerate(hv):
            xin = A(XIN + jj * 1024, D)
            ld = c.load(xin, b["xsrc"][b["xoff"] + j * 128:b["xoff"] + (j + 1) * 128, :], sl=2 + jj)
            lds.append(ld)
            c.waitv("vector", ld)
            c.waitv("scalar", ld)
            ln_tile(xin, A(F_XN, D), A(F_SCR, D))
            for hf in range(2):
                fp4_quant(A(F_XN + hf * 512, 512), A(F_HQ, 512), 512)
                qdone = c.vmark()
                bk = 6
                c.bank_wait_te(bk)
                c.wait("tensor", "ve", qdone)
                for k in range(4):
                    i = c.tp(ps[bk][:, k * 128:(k + 1) * 128], A(F_HQ + k * 128, 128), 128)
                pe_v = c.inc(i, "pe")
                c.wait("vector", "pe", pe_v)
                for k in range(4):
                    kk = hf * 4 + k
                    c.v("tensor_copy", out=A(XQT + (jj * 8 + kk) * 128, 128),
                        in_=ps[bk][:, k * 128:(k + 1) * 128], inc=(k == 3))
                c.bank_free(bk, "ve", c.c["ve"])
        return c.vmark()

    def ffnc_main(b, pre0_done):
        tiles = b["ffn"]
        halves = [tiles[i:i + 2] for i in range(0, len(tiles), 2)]
        bufs = [(F_XQT, F_XIN), (F_XQT_B, F_XIN_B)]
        xq_marks = {0: pre0_done}
        for hi, hv in enumerate(halves):
            XQT, XIN = bufs[hi % 2]
            xq_done = xq_marks[hi]
            pe_marks = []
            pe_half0 = c.c["pe"]
            h_free = {0: c.c["ve"], 1: c.c["ve"]}
            idx = 0
            for nchi, n in enumerate(N0CH):
                ncol0 = nchi * 512
                for kc in range(8):
                    buf = idx % 2
                    wg = A(F_WS + buf * 1024, n)
                    wu = A(F_WS + buf * 1024 + 512, n)
                    if idx >= 2:
                        c.wait("sync", "pe", pe_marks[idx - 2])
                    else:
                        c.wait("sync", "pe", pe_half0)
                    ldg = c.load(wg, t["wgT"][kc * 128:(kc + 1) * 128, ncol0:ncol0 + n], sl=idx % 2)
                    ldu = c.load(wu, t["wuT"][kc * 128:(kc + 1) * 128, ncol0:ncol0 + n], sl=idx % 2)
                    c.waitv("tensor", ldu)
                    c.wait("tensor", "ve", xq_done)
                    for jj in range(len(hv)):
                        bg, by = jj * 2, jj * 2 + 1
                        if kc == 0:
                            c.bank_wait_te(bg)
                            c.bank_wait_te(by)
                        c.mm(ps[bg][:, 0:n], A(XQT + (jj * 8 + kc) * 128, 128), wg,
                             start=(kc == 0), stop=(kc == 7))
                        i = c.mm(ps[by][:, 0:n], A(XQT + (jj * 8 + kc) * 128, 128), wu,
                                 start=(kc == 0), stop=(kc == 7))
                    pe_marks.append(c.inc(i, "pe"))
                    idx += 1
                gu_done = c.c["pe"]
                for jj in range(len(hv)):
                    bg, by = jj * 2, jj * 2 + 1
                    p2 = (nchi * len(hv) + jj) % 2
                    FH = F_H if p2 == 0 else F_H1
                    FQ = F_HQ if p2 == 0 else F_HQ1
                    c.wait("scalar", "pe", gu_done)
                    c.wait("scalar", "ve", h_free[p2])
                    c.a("activation", A(FH, n), ps[bg][:, 0:n], AF.Silu, inc=True)
                    hs_done = c.c["ac"]
                    c.bank_free(bg, "ac", hs_done)
                    c.wait("vector", "ac", hs_done)
                    c.v("tensor_tensor", out=A(FH, n), in0=A(FH, n), in1=ps[by][:, 0:n],
                        op=ALU.mult, inc=True)
                    c.bank_free(by, "ve", c.c["ve"])
                    c.drain("vector")
                    fp4_quant(A(FH, n), A(FQ, n), n)
                    hq_done = c.vmark()
                    h_free[p2] = hq_done
                    bk = 6 + jj
                    c.bank_wait_te(bk)
                    c.wait("tensor", "ve", hq_done)
                    nk = n // 128
                    for k in range(nk):
                        i = c.tp(ps[bk][:, k * 128:(k + 1) * 128], A(FQ + k * 128, 128), 128)
                    pe_v = c.inc(i, "pe")
                    c.wait("vector", "pe", pe_v)
                    for k in range(nk):
                        fc = nchi * 4 + k
                        c.v("tensor_copy", out=A(F_HQT + fc * 256 + jj * 128, 128),
                            in_=ps[bk][:, k * 128:(k + 1) * 128], inc=(k == nk - 1))
                    c.bank_free(bk, "ve", c.c["ve"])
            hqT_done = c.vmark()
            if hi + 1 < len(halves):
                xq_marks[hi + 1] = ffnc_pre(b, halves[hi + 1], bufs[(hi + 1) % 2][0],
                                            bufs[(hi + 1) % 2][1])
            pe_marks2 = []
            for fc in range(22):
                buf = fc % 2
                wd = A(F_WS2 + buf * 1024, 1024)
                if fc >= 2:
                    c.wait("sync", "pe", pe_marks2[fc - 2])
                else:
                    c.wait("sync", "pe", pe_marks[-1])
                ld = c.load(wd, t["wdT"][fc * 128:(fc + 1) * 128, :], sl=fc % 2)
                c.waitv("tensor", ld)
                c.wait("tensor", "ve", hqT_done)
                for jj in range(len(hv)):
                    for hf in range(2):
                        bk = jj * 2 + hf
                        if fc == 0:
                            c.bank_wait_te(bk)
                        i = c.mm(ps[bk][:, 0:512], A(F_HQT + fc * 256 + jj * 128, 128),
                                 wd[:, hf * 512:(hf + 1) * 512], start=(fc == 0), stop=(fc == 21))
                pe_marks2.append(c.inc(i, "pe"))
            z_done = c.c["pe"]
            for jj, j in enumerate(hv):
                xin = A(XIN + jj * 1024, D)
                for hf in range(2):
                    bk = jj * 2 + hf
                    c.wait("scalar", "pe", z_done)
                    c.wait("scalar", "ve", c.c["ve"])
                    c.a("activation", A(F_G + hf * 512, 512), ps[bk][:, 0:512], AF.Silu, inc=True)
                g_done = c.c["ac"]
                c.wait("vector", "ac", g_done)
                c.v("tensor_tensor", out=A(F_G, D), in0=A(F_G, D), in1=A(A_S + j * 1024, D), op=ALU.mult)
                c.drain("vector")
                c.v("tensor_add", out=A(F_G, D), in0=A(F_G, D), in1=xin, inc=True)
                x1_done = c.c["ve"]
                c.bank_free(jj * 2, "ve", x1_done)
                c.bank_free(jj * 2 + 1, "ve", x1_done)
                c.drain("vector")
                c.vmark()
                c.wait("gpsimd", "ve", c.c["ve"])
                row = 256 + b["x1row"] + j * 128
                s_ = c.store(t["x1s"][row:row + 128, :], A(F_G, D))
                c.waitv("vector", s_)
                c.waitv("scalar", s_)
                c.waitv("sync", s_)

    prekv(blocks[0], 0)
    c.bar()
    att_phase(blocks[0], 0, 1)
    c.bar()
    prekv(blocks[1], 1)
    c.bar()
    att_phase(blocks[0], 0, 2)
    wout(blocks[0])
    pre0 = ffnc_pre(blocks[0], blocks[0]["ffn"][0:2], F_XQT, F_XIN)
    c.bar()
    ffnc_main(blocks[0], pre0)
    c.bar()
    qsl = {1: 1, 2: 0, 3: 1, 4: 0}
    for bi in range(1, 5):
        att_phase(blocks[bi], qsl[bi], 1)
        c.bar()
        prekv(blocks[bi + 1], qsl[bi + 1] if bi + 1 < 5 else 0)
        c.bar()
        att_phase(blocks[bi], qsl[bi], 2)
        wout(blocks[bi])
        pre0 = ffnc_pre(blocks[bi], blocks[bi]["ffn"][0:2], F_XQT, F_XIN)
        c.bar()
        ffnc_main(blocks[bi], pre0)
        c.bar()

    if _LIMIT == "L0":
        c.bar()
        c.store(t["dbg_x1s"], t["x1s"], sl=1)
        c.wait_stores("gpsimd")
        c.bar()
        return

    def load_wkv():
        ld1 = c.load(A(L_WKV, 512).rearrange("p (a b) -> p a b", b=64),
                     t["WkT"].rearrange("(a p) b -> p a b", p=128), sl=5)
        ld2 = c.load(A(L_WKV + 512, 512).rearrange("p (a b) -> p a b", b=64),
                     t["WvT"].rearrange("(a p) b -> p a b", p=128), sl=5)
        return ld2

    ldg = c.load(t["rowbuf"][0:1, 0:1024], t["gam_row"], sl=4)
    ldb = c.load(t["rowbuf"][0:1, 1024:2048], t["bet_row"], sl=4)
    for gb in range(2):
        for ch in range(2):
            bk = gb * 2 + ch
            c.bank_wait_te(bk)
            c.waitv("tensor", ldb)
            i = c.mm(ps[bk][:, 0:512], t["onesrow"][0:1, 0:128],
                     t["rowbuf"][0:1, gb * 1024 + ch * 512:gb * 1024 + (ch + 1) * 512],
                     start=True, stop=True)
            pe_v = c.inc(i, "pe")
            c.wait("vector", "pe", pe_v)
            c.v("tensor_copy", out=A(L_GBC + gb * 1024 + ch * 512, 512), in_=ps[bk][:, 0:512], inc=True)
            c.bank_free(bk, "ve", c.c["ve"])
    if c.on("vector"):
        vc = ar[0:128, L_VV:L_VV + 20 * 65].rearrange("p (cell w) -> p cell w", w=65)
        c.eng.memset(vc[:, :, 64:65], 1.0)
    c.vmark()
    c.bar()

    def l1_kv_tiles(ch0, ntile, xn1T_off):
        wkv_ld = load_wkv()
        c.waitv("tensor", wkv_ld)
        c.bank_wait_te(4)
        for kc in range(8):
            i = c.mm(ps[4][0:64, 0:ntile * 128], A(L_WKV + kc * 64, 64),
                     A(xn1T_off + kc * 512, ntile * 128), start=(kc == 0), stop=(kc == 7))
        c.inc(i, "pe")
        for kc in range(8):
            i = c.mm(ps[4][64:128, 0:ntile * 128], A(L_WKV + kc * 64, 64),
                     A(xn1T_off + kc * 512, ntile * 128), start=(kc == 0), stop=(kc == 7),
                     tile_position=(0, 64))
        c.inc(i, "pe")
        c.bank_wait_te(5)
        for kc in range(8):
            for j in range(ntile):
                i = c.mm(ps[5][:, j * 64:(j + 1) * 64], A(xn1T_off + kc * 512 + j * 128, 128),
                         A(L_WKV + 512 + kc * 64, 64), start=(kc == 0 and j == 0), stop=(kc == 7))
        c.inc(i, "pe")
        kv_done = c.c["pe"]
        c.wait("vector", "pe", kv_done)
        c.v("tensor_copy", out=A(L_KT + ch0 * 128, ntile * 128), in_=ps[4][:, 0:ntile * 128])
        for j in range(ntile):
            c.v("tensor_copy", out=A(L_VV + (ch0 + j) * 65, 64), in_=ps[5][:, j * 64:(j + 1) * 64],
                inc=(j == ntile - 1))
        c.bank_free(4, "ve", c.c["ve"])
        c.bank_free(5, "ve", c.c["ve"])
        c.vmark()

    for j in range(2):
        xin = A(L_XIN + j * 1024, D)
        ld = c.load(xin, t["x1s"][j * 128:(j + 1) * 128, :], sl=2 + j)
        c.waitv("vector", ld)
        c.waitv("scalar", ld)
        ln_tile(xin, A(L_LOC, D), A(L_LOC + 1024, D))
        ln_done = c.vmark()
        c.wait("tensor", "ve", ln_done)
        transpose_1024(L_LOC, L_XN1T, j)
    c.vmark()
    l1_kv_tiles(0, 2, L_XN1T)
    c.bar()

    def l1pre(g):
        ln_marks = []
        for j in range(4):
            xin = A(L_XIN + (j % 2) * 1024, D)
            row = 256 + (g * 4 + j) * 128
            if j >= 2:
                c.wait("sync", "ve", ln_marks[j - 2])
            ld = c.load(xin, t["x1s"][row:row + 128, :], sl=2 + (j % 2))
            c.waitv("vector", ld)
            c.waitv("scalar", ld)
            ln_tile(xin, A(L_XN1 + j * 1024, D), A(L_LOC, D))
            ln_done = c.vmark()
            ln_marks.append(ln_done)
            c.wait("tensor", "ve", ln_done)
            transpose_1024(L_XN1 + j * 1024, L_XN1T, j)
        xn1T_done = c.vmark()
        pe_marks = []
        for kc in range(8):
            buf = kc % 2
            wap = A(L_WS + buf * 1024, 1024)
            if kc >= 2:
                c.wait("sync", "pe", pe_marks[kc - 2])
            ld = c.load(wap, t["WqT"][kc * 128:(kc + 1) * 128, :], sl=kc % 2)
            c.waitv("tensor", ld)
            c.wait("tensor", "ve", xn1T_done)
            for n in range(8):
                if kc == 0:
                    c.bank_wait_te(n)
                i = c.mm(ps[n][:, 0:512], wap[:, n * 128:(n + 1) * 128],
                         A(L_XN1T + kc * 512, 512), start=(kc == 0), stop=(kc == 7))
            pe_marks.append(c.inc(i, "pe"))
        q_done = c.c["pe"]
        for n in range(8):
            c.wait("vector", "pe", q_done)
            c.v("tensor_copy", out=A(L_QT + n * 512, 512), in_=ps[n][:, 0:512], inc=True)
            c.bank_free(n, "ve", c.c["ve"])
        c.vmark()
        l1_kv_tiles(2 + g * 4, 4, L_XN1T)

    def l1att(g):
        if _DBG4 and g == 0:
            c.wait("gpsimd", "ve", c.c["ve"])
            d1 = c.store(t["dbg_kT"], A(L_KT, 2560), sl=1)
            d2 = c.store(t["dbg_vv"], A(L_VV, 1300), sl=1)
            d3 = c.store(t["dbg_qT"], A(L_QT, 4096), sl=1)
            c.waitv("sync", d3)
        for j in range(4):
            tt = g * 4 + j
            if tt == 0:
                msks = [t["m00"], t["m01"], None]
            elif tt == 1:
                msks = [t["m10"], None, None]
            else:
                msks = [t["triA"], None, None]
            for hh in range(NH):
                sb = hh % 2
                c.bank_wait_te(sb)
                for jc in range(3):
                    i = c.mm(ps[sb][:, jc * 128:(jc + 1) * 128],
                             A(L_KT + (tt + jc) * 128, 128, p0=(hh % 2) * 64, part=64),
                             A(L_QT + (hh // 2) * 512 + j * 128, 128, p0=(hh % 2) * 64, part=64),
                             start=True, stop=True)
                s_done = c.inc(i, "pe")
                c.wait("scalar", "pe", s_done)
                pb = L_P + (hh % 2) * 384
                c.a("activation", A(pb, 384), ps[sb][:, 0:384], AF.Exp, scale=SCALE, inc=True)
                exp_done = c.c["ac"]
                c.bank_free(sb, "ac", exp_done)
                c.wait("vector", "ac", exp_done)
                c.v("tensor_tensor", out=A(pb, 128), in0=A(pb, 128), in1=msks[0][:], op=ALU.mult)
                if msks[1] is not None:
                    c.v("tensor_tensor", out=A(pb + 128, 128), in0=A(pb + 128, 128), in1=msks[1][:], op=ALU.mult)
                c.v("tensor_tensor", out=A(pb + 256, 128), in0=A(pb + 256, 128), in1=t["triB"][:], op=ALU.mult)
                c.drain("vector")
                p_done = c.vmark()
                ob = 4 + (hh % 2)
                c.bank_wait_te(ob)
                c.wait("tensor", "ve", p_done)
                for jc in range(3):
                    i = c.mm(ps[ob][:, 0:65], A(pb + jc * 128, 128), A(L_VV + (tt + jc) * 65, 65),
                             start=(jc == 0), stop=(jc == 2))
                o_done = c.inc(i, "pe")
                c.wait("vector", "pe", o_done)
                c.v("reciprocal", out=st[:, 6:7], in_=ps[ob][:, 64:65])
                c.drain("vector")
                c.v("tensor_scalar", out=A(L_AOT2 + hh * 64, 64), in0=ps[ob][:, 0:64],
                    scalar1=st[:, 6:7], scalar2=None, op0=ALU.mult, inc=True)
                c.drain("vector")
                c.bank_free(ob, "ve", c.c["ve"])
            ao_done = c.vmark()
            c.wait("tensor", "ve", ao_done)
            transpose_1024(L_AOT2, L_LOC, j)
        aoT_done = c.vmark()
        ldrow = c.load(t["rowbuf"][0:1, 0:1024], t["bf_row"], sl=4)
        pe_marks = []
        for kc in range(8):
            buf = kc % 2
            wap = A(L_WS + buf * 1024, 1024)
            if kc >= 2:
                c.wait("sync", "pe", pe_marks[kc - 2])
            ld = c.load(wap, t["WfT"][kc * 128:(kc + 1) * 128, :], sl=kc % 2)
            c.waitv("tensor", ld)
            c.wait("tensor", "ve", aoT_done)
            for j in range(4):
                for hf in range(2):
                    bk = j * 2 + hf
                    if kc == 0:
                        c.bank_wait_te(bk)
                    i = c.mm(ps[bk][:, 0:512], A(L_LOC + kc * 512 + j * 128, 128),
                             wap[:, hf * 512:(hf + 1) * 512], start=(kc == 0), stop=False)
            if kc == 7:
                c.waitv("tensor", ldrow)
                for j in range(4):
                    for hf in range(2):
                        bk = j * 2 + hf
                        i = c.mm(ps[bk][:, 0:512], t["onesrow"][0:1, 0:128],
                                 t["rowbuf"][0:1, hf * 512:(hf + 1) * 512], start=False, stop=True)
            pe_marks.append(c.inc(i, "pe"))
        s_done = c.c["pe"]
        for j in range(4):
            for hf in range(2):
                bk = j * 2 + hf
                c.wait("vector", "pe", s_done)
                c.v("tensor_copy", out=A(L_S1 + j * 1024 + hf * 512, 512), in_=ps[bk][:, 0:512], inc=True)
                c.bank_free(bk, "ve", c.c["ve"])
        c.vmark()

    def l1ffn(g):
        for half in range(2):
            hv = [half * 2, half * 2 + 1]
            for jj, j in enumerate(hv):
                ln_tile(A(L_XN1 + j * 1024, D), A(L_LOC + 1024, D), A(L_LOC, D))
                c.v("tensor_tensor", out=A(L_LOC + 1024, D), in0=A(L_LOC + 1024, D),
                    in1=A(L_GBC, D), op=ALU.mult)
                c.drain("vector")
                c.v("tensor_add", out=A(L_LOC + 1024, D), in0=A(L_LOC + 1024, D), in1=A(L_GBC + 1024, D))
                c.drain("vector")
                h2_done = c.vmark()
                c.wait("tensor", "ve", h2_done)
                for hlf in range(2):
                    bk = 6 + hlf
                    c.bank_wait_te(bk)
                    for k in range(4):
                        kk = hlf * 4 + k
                        i = c.tp(ps[bk][:, k * 128:(k + 1) * 128], A(L_LOC + 1024 + kk * 128, 128), 128)
                    pe_v = c.inc(i, "pe")
                    c.wait("vector", "pe", pe_v)
                    for k in range(4):
                        kk = hlf * 4 + k
                        c.v("tensor_copy", out=A(L_H2T + kk * 256 + jj * 128, 128),
                            in_=ps[bk][:, k * 128:(k + 1) * 128], inc=(k == 3))
                    c.bank_free(bk, "ve", c.c["ve"])
            h2T_done = c.vmark()
            pe_marks = []
            pe_half0 = c.c["pe"]
            idx = 0
            for ng in range(4):
                for kc in range(8):
                    buf = idx % 2
                    wap = A(L_WS + buf * 1024, 1024)
                    if idx >= 2:
                        c.wait("sync", "pe", pe_marks[idx - 2])
                    else:
                        c.wait("sync", "pe", pe_half0)
                    ld = c.load(wap, t["w1d"][kc * 128:(kc + 1) * 128, ng * 1024:(ng + 1) * 1024], sl=idx % 2)
                    c.waitv("tensor", ld)
                    c.wait("tensor", "ve", h2T_done)
                    for n in range(8):
                        if kc == 0:
                            c.bank_wait_te(n)
                        i = c.mm(ps[n][:, 0:256], wap[:, n * 128:(n + 1) * 128],
                                 A(L_H2T + kc * 256, 256), start=(kc == 0), stop=(kc == 7))
                    pe_marks.append(c.inc(i, "pe"))
                    idx += 1
                u_done = c.c["pe"]
                for n in range(8):
                    ngl = ng * 8 + n
                    c.wait("scalar", "pe", u_done)
                    c.a("activation", A(L_YT + ngl * 256, 256), ps[n][:, 0:256], AF.Silu,
                        bias=t["fc1bC"][:, ngl:ngl + 1], scale=1.0, inc=True)
                    c.bank_free(n, "ac", c.c["ac"])
            yT_done = c.amark()
            ldrow = c.load(t["rowbuf"][0:1, 0:1024], t["fc2b_row"], sl=4)
            pe_marks2 = []
            for kc in range(32):
                buf = kc % 2
                wap = A(L_WS + buf * 1024, 1024)
                if kc >= 2:
                    c.wait("sync", "pe", pe_marks2[kc - 2])
                else:
                    c.wait("sync", "pe", pe_marks[-1])
                ld = c.load(wap, t["w2d"][kc * 128:(kc + 1) * 128, :], sl=kc % 2)
                c.waitv("tensor", ld)
                c.wait("tensor", "ac", yT_done)
                for jj in range(2):
                    for hf in range(2):
                        bk = jj * 2 + hf
                        if kc == 0:
                            c.bank_wait_te(bk)
                        i = c.mm(ps[bk][:, 0:512], A(L_YT + kc * 256 + jj * 128, 128),
                                 wap[:, hf * 512:(hf + 1) * 512], start=(kc == 0), stop=False)
                if kc == 31:
                    c.waitv("tensor", ldrow)
                    for jj in range(2):
                        for hf in range(2):
                            bk = jj * 2 + hf
                            i = c.mm(ps[bk][:, 0:512], t["onesrow"][0:1, 0:128],
                                     t["rowbuf"][0:1, hf * 512:(hf + 1) * 512], start=False, stop=True)
                pe_marks2.append(c.inc(i, "pe"))
            z_done = c.c["pe"]
            for jj, j in enumerate(hv):
                tt = g * 4 + j
                xin = A(L_XIN + jj * 1024, D)
                row = 256 + tt * 128
                ld = c.load(xin, t["x1s"][row:row + 128, :], sl=2 + jj)
                for hf in range(2):
                    bk = jj * 2 + hf
                    c.wait("vector", "pe", z_done)
                    c.v("scalar_tensor_tensor", out=A(L_LOC + 2048 + hf * 512, 512),
                        in0=ps[bk][:, 0:512], scalar=0.5, in1=A(L_XN1 + j * 1024 + hf * 512, 512),
                        op0=ALU.mult, op1=ALU.add)
                pre_done = c.vmark()
                c.wait("scalar", "ve", pre_done)
                c.a("activation", A(L_LOC + 1024, D), A(L_LOC + 2048, D), AF.Silu, inc=True)
                g2_done = c.c["ac"]
                if _DBG4:
                    c.wait("gpsimd", "ac", g2_done)
                    d1 = c.store(t["dbg_g2"][tt * 128:(tt + 1) * 128, :], A(L_LOC + 1024, D), sl=1)
                    d2 = c.store(t["dbg_s1"][tt * 128:(tt + 1) * 128, :], A(L_S1 + j * 1024, D), sl=1)
                    c.waitv("vector", d2)
                    c.waitv("scalar", d2)
                c.wait("vector", "ac", g2_done)
                c.v("tensor_tensor", out=A(L_LOC + 1024, D), in0=A(L_LOC + 1024, D),
                    in1=A(L_S1 + j * 1024, D), op=ALU.mult)
                c.drain("vector")
                c.waitv("vector", ld)
                c.v("tensor_add", out=A(L_LOC + 1024, D), in0=A(L_LOC + 1024, D), in1=xin, inc=True)
                x2_done = c.c["ve"]
                c.bank_free(jj * 2, "ve", x2_done)
                c.bank_free(jj * 2 + 1, "ve", x2_done)
                c.drain("vector")
                c.vmark()
                c.wait("gpsimd", "ve", c.c["ve"])
                s_ = c.store(t["out"][tt * 128:(tt + 1) * 128, :], A(L_LOC + 1024, D))
                c.waitv("vector", s_)
                c.waitv("scalar", s_)
                c.waitv("sync", s_)

    for g in range(4):
        l1pre(g)
        c.bar()
        l1att(g)
        c.bar()
        l1ffn(g)
        c.bar()

    c.wait_stores("gpsimd")
    c.bar()


def _build():
    nc = bass.Bass("TRN2", target_bir_lowering=False)
    t = {}
    def din(name, shape, dt=F32):
        t[name] = nc.dram_tensor(name, shape, dt, kind="ExternalInput").ap()
    din("x_own", [2048, D])
    din("x_pre2", [512, D])
    din("x_wrap", [512, D])
    din("W_qkv", [D, 3072])
    din("W_out", [D, D])
    din("w_gate", [FF0, D])
    din("w_up", [FF0, D])
    din("w_down", [D, FF0])
    din("Wq", [D, D])
    din("Wk", [HD, D])
    din("Wv", [HD, D])
    din("Wf", [D, D])
    din("fc1_q", [D, FF1], I32)
    din("fc1_s", [8, FF1])
    din("fc2_q", [FF1, D], I32)
    din("fc2_s", [32, D])
    din("bqkv_t", [24, 128])
    din("bqkv_row", [1, 3072])
    din("bout_row", [1, D])
    din("bf_row", [1, D])
    din("fc1b_t", [32, 128])
    din("fc2b_row", [1, D])
    din("gam_row", [1, D])
    din("bet_row", [1, D])
    din("ident_in", [128, 128])
    din("triA_in", [128, 128])
    din("triB_in", [128, 128])
    din("m00_in", [128, 128])
    din("m01_in", [128, 128])
    din("m10_in", [128, 128])
    t["out"] = nc.dram_tensor("out", [2048, D], F32, kind="ExternalOutput").ap()
    if _LIMIT == "prologue":
        for name, shape in (("wgT", [D, FF0]), ("wuT", [D, FF0]), ("wdT", [FF0, D]),
                            ("w1d", [D, FF1]), ("w2d", [FF1, D])):
            t["dbg_" + name] = nc.dram_tensor("dbg_" + name, shape, F32, kind="ExternalOutput").ap()
    if _LIMIT == "L0":
        t["dbg_x1s"] = nc.dram_tensor("dbg_x1s", [2304, D], F32, kind="ExternalOutput").ap()
    if _DBG2:
        t["dbg_s"] = nc.dram_tensor("dbg_s", [2304, D], F32, kind="ExternalOutput").ap()
        t["dbg_g"] = nc.dram_tensor("dbg_g", [2304, D], F32, kind="ExternalOutput").ap()
    if _DBG3:
        t["dbg_xqT"] = nc.dram_tensor("dbg_xqT", [128, 2048], F32, kind="ExternalOutput").ap()
        t["dbg_hqT"] = nc.dram_tensor("dbg_hqT", [128, 5632], F32, kind="ExternalOutput").ap()
    if _DBG4:
        t["dbg_s1"] = nc.dram_tensor("dbg_s1", [2048, D], F32, kind="ExternalOutput").ap()
        t["dbg_g2"] = nc.dram_tensor("dbg_g2", [2048, D], F32, kind="ExternalOutput").ap()
        t["dbg_kT"] = nc.dram_tensor("dbg_kT", [128, 2560], F32, kind="ExternalOutput").ap()
        t["dbg_vv"] = nc.dram_tensor("dbg_vv", [128, 1300], F32, kind="ExternalOutput").ap()
        t["dbg_qT"] = nc.dram_tensor("dbg_qT", [128, 4096], F32, kind="ExternalOutput").ap()
    def dscr(name, shape, dt=F32):
        t[name] = nc.dram_tensor(name, shape, dt, kind="Internal").ap()
    dscr("x1s", [2304, D])
    dscr("wgT", [D, FF0])
    dscr("wuT", [D, FF0])
    dscr("wdT", [FF0, D])
    dscr("w1d", [D, FF1])
    dscr("w2d", [FF1, D])
    dscr("WqT", [D, D])
    dscr("WfT", [D, D])
    dscr("WkT", [D, HD])
    dscr("WvT", [D, HD])

    ctx = contextlib.ExitStack()
    with ctx:
        t["arena"] = ctx.enter_context(nc.sbuf_tensor("arena", [128, ARENA], F32)).ap()
        for name, shape in (
            ("ident", [128, 128]), ("triA", [128, 128]), ("triB", [128, 128]),
            ("m00", [128, 128]), ("m01", [128, 128]), ("m10", [128, 128]),
            ("onesrow", [1, 128]), ("onescol", [128, 1]), ("eps", [128, 1]),
            ("bqkvT_s", [24, 128]), ("fc1bT_s", [32, 128]),
            ("bqkvC", [128, 24]), ("fc1bC", [128, 32]),
            ("stats", [128, 16]), ("qsc", [128, 192]), ("rowbuf", [1, 3072]),
        ):
            t[name] = ctx.enter_context(nc.sbuf_tensor(name, shape, F32))
        t["psum"] = [ctx.enter_context(nc.psum_tensor(f"p{i}", [128, 512], F32)) for i in range(8)]
        sem = {}
        semnames = ("pe", "ve", "ac") + tuple("dl%d" % i for i in range(6)) + ("ds0", "ds1") \
            + tuple("b_" + e for e in ENGINES)
        for name in semnames:
            sem[name] = ctx.enter_context(nc.semaphore(name))
        block = ctx.enter_context(nc.Block())

        def mk(en):
            def f(eng):
                _emit(C(nc, en, eng, sem, t))
            return f
        block.sync(mk("sync"))
        block.tensor(mk("tensor"))
        block.vector(mk("vector"))
        block.scalar(mk("scalar"))
        block.gpsimd(mk("gpsimd"))
    return nc


def _host_inputs(inputs):
    _ = inputs["L1_attn_segment_pos"]
    x = np.ascontiguousarray(np.asarray(inputs["L0_attn_x"], dtype=np.float32))
    i32 = lambda a: np.ascontiguousarray(np.asarray(a, dtype=np.int32))
    f32 = lambda a: np.ascontiguousarray(np.asarray(a, dtype=np.float32))
    kk = np.arange(128)[:, None]
    qq = np.arange(128)[None, :]
    triA = (qq <= kk).astype(np.float32)
    triB = (qq >= kk).astype(np.float32)
    full = np.ones((128, 128), np.float32)
    zero = np.zeros((128, 128), np.float32)
    common = dict(
        W_qkv=f32(inputs["L0_attn_W_qkv"]),
        W_out=f32(inputs["L0_attn_W_out"]),
        w_gate=f32(inputs["L0_ffn_w_gate"]),
        w_up=f32(inputs["L0_ffn_w_up"]),
        w_down=f32(inputs["L0_ffn_w_down"]),
        Wq=f32(inputs["L1_attn_Wq"]),
        Wk=f32(inputs["L1_attn_Wk"]),
        Wv=f32(inputs["L1_attn_Wv"]),
        Wf=f32(inputs["L1_attn_Wf"]),
        fc1_q=i32(inputs["L1_ffn_fc1_q"]),
        fc1_s=f32(inputs["L1_ffn_fc1_scales"]),
        fc2_q=i32(inputs["L1_ffn_fc2_q"]),
        fc2_s=f32(inputs["L1_ffn_fc2_scales"]),
        bqkv_t=f32(inputs["L0_attn_b_qkv"]).reshape(24, 128),
        bqkv_row=f32(inputs["L0_attn_b_qkv"]).reshape(1, 3072),
        bout_row=f32(inputs["L0_attn_b_out"]).reshape(1, D),
        bf_row=f32(inputs["L1_attn_bf"]).reshape(1, D),
        fc1b_t=f32(inputs["L1_ffn_fc1_bias"]).reshape(32, 128),
        fc2b_row=f32(inputs["L1_ffn_fc2_bias"]).reshape(1, D),
        gam_row=f32(inputs["L1_ffn_ln_gamma"]).reshape(1, D),
        bet_row=f32(inputs["L1_ffn_ln_beta"]).reshape(1, D),
        ident_in=np.eye(128, dtype=np.float32),
        triA_in=triA, triB_in=triB,
    )
    in_maps = []
    for cid in range(8):
        b, half = cid // 2, cid % 2
        t0 = half * 2048
        m = dict(common)
        m["x_own"] = np.ascontiguousarray(x[b, t0:t0 + 2048])
        m["x_pre2"] = np.ascontiguousarray(x[b, 1536:2048])
        m["x_wrap"] = np.ascontiguousarray(x[b, 2048:2560] if half == 0 else x[b, 0:512])
        if half == 0:
            m["m00_in"], m["m01_in"], m["m10_in"] = zero, zero, zero
        else:
            m["m00_in"], m["m01_in"], m["m10_in"] = triA, full, triA
        in_maps.append(m)
    return x, in_maps


def kernel(**inputs):
    x, in_maps = _host_inputs(inputs)
    nc = _build()
    res = run_bass_kernel_spmd(nc, in_maps, core_ids=list(range(8)))
    out = np.empty((4, 4096, D), dtype=np.float32)
    for cid in range(8):
        b, half = cid // 2, cid % 2
        out[b, half * 2048:(half + 1) * 2048] = res.results[cid]["out"]
    return out


# version 09
# speedup vs baseline: 1.2239x; 1.0181x over previous
import contextlib
import numpy as np

import concourse.bass as bass
import concourse.mybir as mybir
from concourse.bass_utils import run_bass_kernel_spmd

F32 = mybir.dt.float32
BF16 = mybir.dt.bfloat16
I32 = mybir.dt.int32
AF = mybir.ActivationFunctionType
ALU = mybir.AluOpType
AX = mybir.AxisListType.X

D = 1024
NH, HD = 16, 64
FF0 = 2816
FF1 = 4096
SCALE = 0.125
MID = [0.25, 0.75, 1.25, 1.75, 2.5, 3.5, 5.0]
DG = [0.5, 0.5, 0.5, 0.5, 1.0, 1.0, 2.0]
N0CH = [512, 512, 512, 512, 512, 256]
ENGINES = ("sync", "tensor", "vector", "scalar", "gpsimd")

_DBG = False
_DBG2 = False
_DBG3 = False
_DBG4 = False
_LIMIT = None

ARENA = 43840
A_KT = 0
A_VV = 4096
A_Q = (8256, 12352)
A_AO = 16448
A_S = 20544
A_P = 24640
P_XNT = 26688
P_XSQ = 30784
P_XN = 31808
P_XIN = 32832
P_WS = 34880
W_AOT = 26688
W_WS = 36416
F_HQT = 26688
F_XQT = 32320
F_XIN = 34368
F_WS = 36416
F_WS2 = 36416
F_XN = 38464
F_SCR = 39488
F_H = 42048
F_HQ = 42560
F_H1 = A_P + 1024
F_HQ1 = A_P + 1536
Q_CB = [F_SCR + 1536 + i * 256 for i in range(4)] + [A_P + i * 256 for i in range(3)]
F_XQT_B = 16448
F_XIN_B = 18496
F_G = F_XN
G_WIN = 26688
G_WQ = 30784
G_STG = 33600
G_WS = 34624
G_SROW = 38720
L_XN1 = 0
L_XN1T = 4096
L_QT = 8192
L_KT = 12288
L_VV = 14848
L_S1 = 16192
L_H2T = 20288
L_P = 22336
L_YT = 24384
L_GBC = 32576
L_LOC = 34624
L_WS = 38720
L_WKV = 40768
L_AOT2 = 40768
L_XIN = 41792

assert L_XIN + 2048 <= ARENA


class C:

    def __init__(self, nc, en, eng, sem, t):
        self.nc, self.en, self.eng, self.sem, self.t = nc, en, eng, sem, t
        self.c = {k: 0 for k in sem}
        self.nbar = 0
        self.bankw = [None] * 8

    def _do(self, en, f):
        if self.en == en:
            return f(self.eng)
        return None

    def on(self, en):
        return self.en == en

    def inc(self, instr, s, k=1):
        self.c[s] += k
        if instr is not None:
            instr.then_inc(self.sem[s], k)
        return self.c[s]

    def wait(self, en, s, val=None):
        if val is None:
            val = self.c[s]
        if self.en == en:
            self.eng.wait_ge(self.sem[s], val)

    def drain(self, en):
        if self.en == en:
            self.eng.drain()

    def vmark(self):
        s = self.t["stats"]
        return self.inc(self._do("vector", lambda e: e.tensor_copy(out=s[:, 7:8], in_=s[:, 7:8])), "ve")

    def amark(self):
        s = self.t["stats"]
        return self.inc(self._do("scalar", lambda e: e.copy(s[:, 7:8], s[:, 7:8])), "ac")

    def bar(self):
        self.nbar += 1
        for e in ENGINES:
            if self.en == e:
                self.eng.drain().then_inc(self.sem["b_" + e], 1)
        for e in ENGINES:
            if e != self.en:
                self.wait(self.en, "b_" + e, self.nbar)
        for e in ENGINES:
            self.c["b_" + e] = self.nbar

    def load(self, out_ap, in_ap, sl=0):
        s = "dl%d" % sl
        i = self._do("sync", lambda e: e.dma_start(out=out_ap, in_=in_ap))
        return (s, self.inc(i, s, 16))

    def store(self, out_ap, in_ap, sl=0):
        s = "ds%d" % sl
        i = self._do("gpsimd", lambda e: e.dma_start(out=out_ap, in_=in_ap))
        return (s, self.inc(i, s, 16))

    def waitv(self, en, tok):
        if self.en == en:
            self.eng.wait_ge(self.sem[tok[0]], tok[1])

    def wait_stores(self, en):
        for s in ("ds0", "ds1", "ds2"):
            self.wait(en, s, self.c[s])

    def mm(self, out, lhsT, rhs, start, stop, tile_position=None):
        return self._do("tensor", lambda e: e.matmul(out, lhsT, rhs, start=start, stop=stop,
                                                     tile_position=tile_position))

    def tp(self, out, in_, np_):
        return self._do("tensor", lambda e: e.transpose(out, in_, self.t["ident"][0:np_, 0:np_]))

    def v(self, meth, *a, inc=False, **kw):
        i = self._do("vector", lambda e: getattr(e, meth)(*a, **kw))
        if inc:
            return self.inc(i, "ve")
        return i

    def a(self, meth, *a, inc=False, **kw):
        i = self._do("scalar", lambda e: getattr(e, meth)(*a, **kw))
        if inc:
            return self.inc(i, "ac")
        return i

    def bank_wait_te(self, b):
        if self.bankw[b] is not None:
            s, v = self.bankw[b]
            self.wait("tensor", s, v)
            self.bankw[b] = None

    def bank_free(self, b, s, v):
        self.bankw[b] = (s, v)


def _emit(c):
    t = c.t
    ps = t["psum"]
    ar = t["arena"]

    def A(off, n, p0=0, part=128):
        return ar[p0:p0 + part, off:off + n]

    st = t["stats"]

    def ln_tile(src, dst, xsq, n=D):
        c.wait("scalar", "ve", c.c["ve"])
        c.a("activation", xsq, src, AF.Square, inc=True)
        sq_done = c.c["ac"]
        c.v("reduce_sum", out=st[:, 0:1], in_=src, axis=AX)
        c.drain("vector")
        c.v("tensor_scalar", out=st[:, 0:1], in0=st[:, 0:1], scalar1=1.0 / n, scalar2=None, op0=ALU.mult)
        c.drain("vector")
        c.wait("vector", "ac", sq_done)
        c.v("reduce_sum", out=st[:, 1:2], in_=xsq, axis=AX)
        c.drain("vector")
        c.v("tensor_scalar", out=st[:, 1:2], in0=st[:, 1:2], scalar1=1.0 / n, scalar2=None, op0=ALU.mult)
        c.v("tensor_scalar", out=st[:, 2:3], in0=st[:, 0:1], scalar1=st[:, 0:1], scalar2=None, op0=ALU.mult)
        c.drain("vector")
        c.v("tensor_sub", out=st[:, 3:4], in0=st[:, 1:2], in1=st[:, 2:3], inc=True)
        var_done = c.c["ve"]
        c.wait("scalar", "ve", var_done)
        c.a("activation", st[:, 4:5], st[:, 3:4], AF.Sqrt, bias=t["eps"][:, 0:1], scale=1.0, inc=True)
        sd_done = c.c["ac"]
        c.wait("vector", "ac", sd_done)
        c.v("reciprocal", out=st[:, 5:6], in_=st[:, 4:5])
        c.drain("vector")
        c.v("tensor_scalar", out=dst, in0=src, scalar1=st[:, 0:1], scalar2=st[:, 5:6],
            op0=ALU.subtract, op1=ALU.mult)
        c.drain("vector")

    def fp4_quant(src, dst, n):
        assert n <= 512
        nb = n // 16
        tt = A(F_SCR, n)
        sg = A(F_SCR + 512, n)
        at = A(F_SCR + 1024, n)
        cb = [ar[0:128, o:o + (n + 1) // 2].bitcast(BF16) for o in Q_CB]
        qsc = t["qsc"]
        bmv, scv, rsv = qsc[:, 0:nb], qsc[:, 64:64 + nb], qsc[:, 128:128 + nb]
        src3 = src.rearrange("p (g k) -> p g k", k=16)
        dst3 = dst.rearrange("p (g k) -> p g k", k=16)
        t3 = tt.rearrange("p (g k) -> p g k", k=16)
        c.v("reduce_max", out=bmv, in_=src3, axis=AX, apply_absolute_value=True)
        c.drain("vector")
        c.v("tensor_scalar", out=scv, in0=bmv, scalar1=1.0 / 6.0, scalar2=1e-12, op0=ALU.mult, op1=ALU.max)
        c.drain("vector")
        c.v("reciprocal", out=rsv, in_=scv)
        c.drain("vector")
        c.v("tensor_tensor", out=t3, in0=src3, in1=rsv.unsqueeze(2).broadcast_to([128, nb, 16]),
            op=ALU.mult, inc=True)
        t_done = c.c["ve"]
        c.wait("scalar", "ve", t_done)
        c.a("activation", at, tt, AF.Abs)
        c.a("activation", sg, tt, AF.Sign, inc=True)
        absgn_done = c.c["ac"]
        c.wait("vector", "ac", absgn_done)
        for j in range(7):
            c.v("tensor_scalar", out=cb[j], in0=at, scalar1=MID[j], scalar2=DG[j],
                op0=ALU.is_gt, op1=ALU.mult)
        c.drain("vector")
        c.v("tensor_add", out=cb[0], in0=cb[0], in1=cb[1])
        c.v("tensor_add", out=cb[2], in0=cb[2], in1=cb[3])
        c.v("tensor_add", out=cb[4], in0=cb[4], in1=cb[5])
        c.drain("vector")
        c.v("tensor_add", out=cb[0], in0=cb[0], in1=cb[2])
        c.v("tensor_add", out=cb[4], in0=cb[4], in1=cb[6])
        c.drain("vector")
        c.v("tensor_add", out=cb[0], in0=cb[0], in1=cb[4])
        c.drain("vector")
        c.v("tensor_tensor", out=tt, in0=sg, in1=cb[0], op=ALU.mult)
        c.drain("vector")
        c.v("tensor_tensor", out=dst3, in0=t3, in1=scv.unsqueeze(2).broadcast_to([128, nb, 16]),
            op=ALU.mult)
        c.drain("vector")

    def transpose_1024(src_off, dst_off, j, banks=(6, 7), src_stride=128):
        for half in range(2):
            bk = banks[half]
            c.bank_wait_te(bk)
            for k in range(4):
                kk = half * 4 + k
                i = c.tp(ps[bk][:, k * 128:(k + 1) * 128], A(src_off + kk * 128, 128), 128)
            pe_v = c.inc(i, "pe")
            c.wait("vector", "pe", pe_v)
            for k in range(4):
                kk = half * 4 + k
                c.v("tensor_copy", out=A(dst_off + kk * 512 + j * 128, 128),
                    in_=ps[bk][:, k * 128:(k + 1) * 128], inc=(k == 3))
            c.bank_free(bk, "ve", c.c["ve"])

    for name in ("ident", "triA", "triB", "m00", "m01", "m10"):
        c.load(t[name][:], t[name + "_in"], sl=4)
    c.load(t["bqkvT_s"][:], t["bqkv_t"], sl=4)
    ld0 = c.load(t["fc1bT_s"][:], t["fc1b_t"], sl=4)
    if c.on("vector"):
        c.eng.memset(t["onesrow"][:], 1.0)
        c.eng.memset(t["onescol"][:], 1.0)
        c.eng.memset(t["eps"][:], 1e-5)
        vc = ar[0:128, A_VV:A_VV + 4 * NH * 65].rearrange("p (cell w) -> p cell w", w=65)
        c.eng.memset(vc[:, :, 64:65], 1.0)
    c.vmark()
    c.waitv("tensor", ld0)
    c.tp(ps[0][:, 0:24], t["bqkvT_s"][:], 24)
    i = c.tp(ps[1][:, 0:32], t["fc1bT_s"][:], 32)
    c.inc(i, "pe")
    c.wait("vector", "pe", c.c["pe"])
    c.v("tensor_copy", out=t["bqkvC"][:], in_=ps[0][:, 0:24])
    c.v("tensor_copy", out=t["fc1bC"][:], in_=ps[1][:, 0:32], inc=True)
    c.bar()

    WPREP = [(G_WIN, G_WQ, G_WS), (8256, 12352, 16448)]
    S2Q, S2S = 0, 4160
    S2ROW = 20544
    wrow_free = [0, 0]
    wrow_store = [None, None]
    s2_store = None

    jobs1 = []
    for (wsrc, wdst, nrows, ncols) in (
        (t["w_gate"], t["wgT"], FF0, D),
        (t["w_up"], t["wuT"], FF0, D),
        (t["w_down"], t["wdT"], D, FF0),
    ):
        for r in range(nrows // 128):
            jobs1.append((wsrc, wdst, ncols, r))
    jobs2 = []
    for (qsrc, ssrc, wdst, N, ng) in (
        (t["fc1_q"], t["fc1_s"], t["w1d"], FF1, 8),
        (t["fc2_q"], t["fc2_s"], t["w2d"], D, 32),
    ):
        for g in range(ng):
            jobs2.append((qsrc, ssrc, wdst, N, g))

    def s1_job(ji):
        nonlocal_marker = None
        wsrc, wdst, ncols, r = jobs1[ji]
        bset = ji % 2
        WIN, WQ, WS = WPREP[bset]
        win = A(WIN, ncols)
        c.wait("sync", "ve", wrow_free[bset])
        if wrow_store[bset] is not None:
            c.waitv("sync", wrow_store[bset])
        ld = c.load(win, wsrc[r * 128:(r + 1) * 128, :], sl=bset)
        c.waitv("vector", ld)
        for hh in range((ncols + 511) // 512):
            n = min(512, ncols - hh * 512)
            fp4_quant(win[:, hh * 512:hh * 512 + n], A(WQ + hh * 512, n), n)
        q_done = c.vmark()
        nct = ncols // 128
        for k in range(nct):
            bk = k % 4
            c.bank_wait_te(bk)
            c.wait("tensor", "ve", q_done)
            i = c.tp(ps[bk][:, 0:128], A(WQ + k * 128, 128), 128)
            pe_v = c.inc(i, "pe")
            c.wait("vector", "pe", pe_v)
            c.v("tensor_copy", out=A(WS + k * 128, 128), in_=ps[bk][:, 0:128], inc=True)
            c.bank_free(bk, "ve", c.c["ve"])
        row_done = c.vmark()
        wrow_free[bset] = row_done
        c.wait("gpsimd", "ve", row_done)
        s_ = c.store(wdst.rearrange("(a p) b -> p a b", p=128)[:, :, r * 128:(r + 1) * 128],
                     A(WS, nct * 128).rearrange("p (a b) -> p a b", b=128), sl=bset)
        wrow_store[bset] = s_
        return None

    def s2_job(ji):
        qsrc, ssrc, wdst, N, g = jobs2[ji]
        if s2_state["store"] is not None:
            c.waitv("sync", s2_state["store"])
        qin = A(S2Q, N).bitcast(I32)
        ld = c.load(qin, qsrc[g * 128:(g + 1) * 128, :], sl=2)
        lds = c.load(ar[0:1, S2ROW:S2ROW + N], ssrc[g:g + 1, :], sl=4)
        c.waitv("vector", ld)
        c.v("tensor_scalar", out=A(S2S, N), in0=qin, scalar1=128.0, scalar2=None,
            op0=ALU.subtract, inc=True)
        qf_done = c.c["ve"]
        for ch in range(N // 512):
            bk = 4 + ch % 2
            c.bank_wait_te(bk)
            c.waitv("tensor", lds)
            i = c.mm(ps[bk][:, 0:512], t["onesrow"][0:1, 0:128],
                     ar[0:1, S2ROW + ch * 512:S2ROW + (ch + 1) * 512], start=True, stop=True)
            pe_v = c.inc(i, "pe")
            c.wait("vector", "pe", pe_v)
            c.wait("vector", "ve", qf_done)
            c.v("tensor_tensor", out=A(S2Q + ch * 512, 512), in0=A(S2S + ch * 512, 512),
                in1=ps[bk][:, 0:512], op=ALU.mult, inc=True)
            c.bank_free(bk, "ve", c.c["ve"])
        wrow = c.vmark()
        c.wait("gpsimd", "ve", wrow)
        s_ = c.store(wdst[g * 128:(g + 1) * 128, :], A(S2Q, N), sl=2)
        s2_state["store"] = s_

    s2_state = {"store": None}
    nj = max(len(jobs1), len(jobs2))
    for ji in range(nj):
        if ji < len(jobs1):
            s1_job(ji)
        if ji < len(jobs2):
            s2_job(ji)
    c.bar()
    for (wsrc, wdst) in ((t["Wq"], t["WqT"]), (t["Wf"], t["WfT"])):
        for r in range(8):
            ld = c.load(A(G_WIN, D), wsrc[r * 128:(r + 1) * 128, :], sl=0)
            for k in range(8):
                bk = k % 4
                c.bank_wait_te(bk)
                c.waitv("tensor", ld)
                i = c.tp(ps[bk][:, 0:128], A(G_WIN + k * 128, 128), 128)
                pe_v = c.inc(i, "pe")
                c.wait("vector", "pe", pe_v)
                c.v("tensor_copy", out=A(G_WS + k * 128, 128), in_=ps[bk][:, 0:128], inc=True)
                c.bank_free(bk, "ve", c.c["ve"])
            done = c.vmark()
            c.wait("gpsimd", "ve", done)
            s_ = c.store(wdst.rearrange("(a p) b -> p a b", p=128)[:, :, r * 128:(r + 1) * 128],
                         A(G_WS, 1024).rearrange("p (a b) -> p a b", b=128))
            c.waitv("vector", s_)
            c.waitv("sync", s_)
    if _LIMIT == "prologue":
        c.bar()
        for name in ("wgT", "wuT", "wdT", "w1d", "w2d"):
            c.store(t["dbg_" + name], t[name], sl=1)
        c.wait_stores("gpsimd")
        c.bar()
        return
    for (wsrc, wdst) in ((t["Wk"], t["WkT"]), (t["Wv"], t["WvT"])):
        ld = c.load(A(G_WIN, D, part=64), wsrc, sl=0)
        for k in range(8):
            bk = k % 4
            c.bank_wait_te(bk)
            c.waitv("tensor", ld)
            i = c.tp(ps[bk][:, 0:64], A(G_WIN + k * 128, 128, part=64), 64)
            pe_v = c.inc(i, "pe")
            c.wait("vector", "pe", pe_v)
            c.v("tensor_copy", out=A(G_WS + k * 64, 64), in_=ps[bk][:, 0:64], inc=True)
            c.bank_free(bk, "ve", c.c["ve"])
        done = c.vmark()
        c.wait("gpsimd", "ve", done)
        for a in range(8):
            s_ = c.store(wdst[a * 128:(a + 1) * 128, :], A(G_WS + a * 64, 64))
        c.waitv("vector", s_)
        c.waitv("sync", s_)
    c.bar()

    blocks = [
        dict(name="pre2", xsrc=t["x_pre2"], xoff=0, ffn=[2, 3], qt=[2, 3], x1row=-512, q=True),
        dict(name="own0", xsrc=t["x_own"], xoff=0, ffn=[0, 1, 2, 3], qt=[0, 1, 2, 3], x1row=0, q=True),
        dict(name="own1", xsrc=t["x_own"], xoff=512, ffn=[0, 1, 2, 3], qt=[0, 1, 2, 3], x1row=512, q=True),
        dict(name="own2", xsrc=t["x_own"], xoff=1024, ffn=[0, 1, 2, 3], qt=[0, 1, 2, 3], x1row=1024, q=True),
        dict(name="own3", xsrc=t["x_own"], xoff=1536, ffn=[0, 1, 2, 3], qt=[0, 1, 2, 3], x1row=1536, q=True),
        dict(name="wrap", xsrc=t["x_wrap"], xoff=0, ffn=[], qt=[], x1row=None, q=False),
    ]

    def kT_ap(hh, j0, nj):
        return A(A_KT + (hh // 2) * 512 + j0, nj, p0=(hh % 2) * 64, part=64)

    def qT_ap(slot, hh, q0, nq):
        return A(A_Q[slot] + (hh // 2) * 512 + q0, nq, p0=(hh % 2) * 64, part=64)

    def vv_ap(jc, hh):
        return A(A_VV + (jc * NH + hh) * 65, 65)

    def prekv(b, qslot):
        ln_marks = []
        for j in range(4):
            xin = A(P_XIN + (j % 2) * 1024, D)
            if j >= 2:
                c.wait("sync", "ve", ln_marks[j - 2])
            ld = c.load(xin, b["xsrc"][b["xoff"] + j * 128:b["xoff"] + (j + 1) * 128, :], sl=2 + (j % 2))
            c.waitv("vector", ld)
            c.waitv("scalar", ld)
            ln_tile(xin, A(P_XN, D), A(P_XSQ, D))
            ln_done = c.vmark()
            ln_marks.append(ln_done)
            c.wait("tensor", "ve", ln_done)
            transpose_1024(P_XN, P_XNT, j)
        xnT_done = c.vmark()
        ldrow = c.load(t["rowbuf"][0:1, 0:3072], t["bqkv_row"], sl=4)
        groups = ([0] if b["q"] else []) + [1, 2]
        pe_marks = []
        for gi, grp in enumerate(groups):
            for kc in range(8):
                idx = gi * 8 + kc
                buf = idx % 2
                wap = A(P_WS + buf * 1024, 1024)
                if idx >= 2:
                    c.wait("sync", "pe", pe_marks[idx - 2])
                ld = c.load(wap, t["W_qkv"][kc * 128:(kc + 1) * 128, grp * 1024:(grp + 1) * 1024], sl=idx % 2)
                c.waitv("tensor", ld)
                c.wait("tensor", "ve", xnT_done)
                if grp in (0, 1):
                    for n in range(8):
                        if kc == 0:
                            c.bank_wait_te(n)
                        i = c.mm(ps[n][:, 0:512], wap[:, n * 128:(n + 1) * 128],
                                 A(P_XNT + kc * 512, 512), start=(kc == 0), stop=(kc == 7))
                else:
                    for j in range(4):
                        for hf in range(2):
                            bk = j * 2 + hf
                            if kc == 0:
                                c.bank_wait_te(bk)
                            i = c.mm(ps[bk][:, 0:512], A(P_XNT + kc * 512 + j * 128, 128),
                                     wap[:, hf * 512:(hf + 1) * 512], start=(kc == 0), stop=False)
                    if kc == 7:
                        c.waitv("tensor", ldrow)
                        for j in range(4):
                            for hf in range(2):
                                bk = j * 2 + hf
                                i = c.mm(ps[bk][:, 0:512], t["onesrow"][0:1, 0:128],
                                         t["rowbuf"][0:1, 2048 + hf * 512:2048 + (hf + 1) * 512],
                                         start=False, stop=True)
                pe_marks.append(c.inc(i, "pe"))
            mm_done = c.c["pe"]
            if grp == 0:
                for n in range(8):
                    c.wait("vector", "pe", mm_done)
                    c.v("tensor_scalar", out=A(A_Q[qslot] + n * 512, 512), in0=ps[n][:, 0:512],
                        scalar1=t["bqkvC"][:, n:n + 1], scalar2=None, op0=ALU.add, inc=True)
                    c.bank_free(n, "ve", c.c["ve"])
            elif grp == 1:
                for n in range(8):
                    c.wait("vector", "pe", mm_done)
                    c.v("tensor_scalar", out=A(A_KT + n * 512, 512), in0=ps[n][:, 0:512],
                        scalar1=t["bqkvC"][:, 8 + n:9 + n], scalar2=None, op0=ALU.add, inc=True)
                    c.bank_free(n, "ve", c.c["ve"])
            else:
                for j in range(4):
                    for hf in range(2):
                        bk = j * 2 + hf
                        c.wait("vector", "pe", mm_done)
                        dst = ar[0:128, A_VV + (j * NH + hf * 8) * 65:A_VV + (j * NH + hf * 8) * 65 + 8 * 65]
                        c.v("tensor_copy", out=dst.rearrange("p (h w) -> p h w", w=65)[:, :, 0:64],
                            in_=ps[bk][:, 0:512].rearrange("p (h w) -> p h w", w=64), inc=True)
                        c.bank_free(bk, "ve", c.c["ve"])
        c.vmark()

    def att_phase(b, qslot, phase):
        qtiles = b["qt"]
        for hh in range(NH):
            exp_done = {}
            for jc in range(4):
                bk = jc
                c.bank_wait_te(bk)
                i = c.mm(ps[bk][:, 0:512], kT_ap(hh, jc * 128, 128), qT_ap(qslot, hh, 0, 512),
                         start=True, stop=True)
                s_done = c.inc(i, "pe")
                c.wait("scalar", "pe", s_done)
                c.a("activation", A(A_P + jc * 512, 512), ps[bk][:, 0:512], AF.Exp,
                    scale=SCALE, inc=True)
                exp_done[jc] = c.c["ac"]
                c.bank_free(bk, "ac", exp_done[jc])
            ob = 4 + (hh % 2)
            c.bank_wait_te(ob)
            for qt in qtiles:
                for jc in range(4):
                    c.wait("tensor", "ac", exp_done[jc])
                    i = c.mm(ps[ob][:, qt * 65:qt * 65 + 65],
                             A(A_P + jc * 512 + qt * 128, 128), vv_ap(jc, hh),
                             start=(jc == 0), stop=(jc == 3))
            o_done = c.inc(i, "pe")
            c.wait("vector", "pe", o_done)
            for qt in qtiles:
                c.v("reciprocal", out=st[:, 6:7], in_=ps[ob][:, qt * 65 + 64:qt * 65 + 65])
                c.drain("vector")
                ao = A(A_AO + qt * 1024 + hh * 64, 64)
                if phase == 1:
                    c.v("tensor_scalar", out=ao, in0=ps[ob][:, qt * 65:qt * 65 + 64],
                        scalar1=st[:, 6:7], scalar2=None, op0=ALU.mult)
                else:
                    c.v("scalar_tensor_tensor", out=ao, in0=ps[ob][:, qt * 65:qt * 65 + 64],
                        scalar=st[:, 6:7], in1=ao, op0=ALU.mult, op1=ALU.add)
                c.drain("vector")
            c.vmark()
            c.bank_free(ob, "ve", c.c["ve"])

    def wout(b):
        qtiles = b["qt"]
        for j in qtiles:
            transpose_1024(A_AO + j * 1024, W_AOT, j)
        aoT_done = c.vmark()
        ldrow = c.load(t["rowbuf"][0:1, 0:1024], t["bout_row"], sl=4)
        pe_marks = []
        for kc in range(8):
            wap = A(W_WS + (kc % 2) * 1024, 1024)
            if kc >= 2:
                c.wait("sync", "pe", pe_marks[kc - 2])
            ld = c.load(wap, t["W_out"][kc * 128:(kc + 1) * 128, :], sl=kc % 2)
            c.waitv("tensor", ld)
            c.wait("tensor", "ve", aoT_done)
            for ji, j in enumerate(qtiles):
                for hf in range(2):
                    bk = ji * 2 + hf
                    if kc == 0:
                        c.bank_wait_te(bk)
                    i = c.mm(ps[bk][:, 0:512], A(W_AOT + kc * 512 + j * 128, 128),
                             wap[:, hf * 512:(hf + 1) * 512], start=(kc == 0), stop=False)
            if kc == 7:
                c.waitv("tensor", ldrow)
                for ji, j in enumerate(qtiles):
                    for hf in range(2):
                        bk = ji * 2 + hf
                        i = c.mm(ps[bk][:, 0:512], t["onesrow"][0:1, 0:128],
                                 t["rowbuf"][0:1, hf * 512:(hf + 1) * 512], start=False, stop=True)
            pe_marks.append(c.inc(i, "pe"))
        s_done = c.c["pe"]
        for ji, j in enumerate(qtiles):
            for hf in range(2):
                bk = ji * 2 + hf
                c.wait("vector", "pe", s_done)
                c.v("tensor_copy", out=A(A_S + j * 1024 + hf * 512, 512), in_=ps[bk][:, 0:512], inc=True)
                c.bank_free(bk, "ve", c.c["ve"])
        c.vmark()

    def ffnc_pre(b, hv, XQT, XIN):
        c.wait("vector", "pe", c.c["pe"])
        lds = []
        for jj, j in enumerate(hv):
            xin = A(XIN + jj * 1024, D)
            ld = c.load(xin, b["xsrc"][b["xoff"] + j * 128:b["xoff"] + (j + 1) * 128, :], sl=2 + jj)
            lds.append(ld)
            c.waitv("vector", ld)
            c.waitv("scalar", ld)
            ln_tile(xin, A(F_XN, D), A(F_SCR, D))
            for hf in range(2):
                fp4_quant(A(F_XN + hf * 512, 512), A(F_HQ, 512), 512)
                qdone = c.vmark()
                bk = 6
                c.bank_wait_te(bk)
                c.wait("tensor", "ve", qdone)
                for k in range(4):
                    i = c.tp(ps[bk][:, k * 128:(k + 1) * 128], A(F_HQ + k * 128, 128), 128)
                pe_v = c.inc(i, "pe")
                c.wait("vector", "pe", pe_v)
                for k in range(4):
                    kk = hf * 4 + k
                    c.v("tensor_copy", out=A(XQT + (jj * 8 + kk) * 128, 128),
                        in_=ps[bk][:, k * 128:(k + 1) * 128], inc=(k == 3))
                c.bank_free(bk, "ve", c.c["ve"])
        return c.vmark()

    def ffnc_main(b, pre0_done):
        tiles = b["ffn"]
        halves = [tiles[i:i + 2] for i in range(0, len(tiles), 2)]
        bufs = [(F_XQT, F_XIN), (F_XQT_B, F_XIN_B)]
        xq_marks = {0: pre0_done}
        for hi, hv in enumerate(halves):
            XQT, XIN = bufs[hi % 2]
            xq_done = xq_marks[hi]
            pe_marks = []
            pe_half0 = c.c["pe"]
            h_free = {0: c.c["ve"], 1: c.c["ve"]}
            idx = 0
            for nchi, n in enumerate(N0CH):
                ncol0 = nchi * 512
                for kc in range(8):
                    buf = idx % 2
                    wg = A(F_WS + buf * 1024, n)
                    wu = A(F_WS + buf * 1024 + 512, n)
                    if idx >= 2:
                        c.wait("sync", "pe", pe_marks[idx - 2])
                    else:
                        c.wait("sync", "pe", pe_half0)
                    ldg = c.load(wg, t["wgT"][kc * 128:(kc + 1) * 128, ncol0:ncol0 + n], sl=idx % 2)
                    ldu = c.load(wu, t["wuT"][kc * 128:(kc + 1) * 128, ncol0:ncol0 + n], sl=idx % 2)
                    c.waitv("tensor", ldu)
                    c.wait("tensor", "ve", xq_done)
                    for jj in range(len(hv)):
                        bg, by = jj * 2, jj * 2 + 1
                        if kc == 0:
                            c.bank_wait_te(bg)
                            c.bank_wait_te(by)
                        c.mm(ps[bg][:, 0:n], A(XQT + (jj * 8 + kc) * 128, 128), wg,
                             start=(kc == 0), stop=(kc == 7))
                        i = c.mm(ps[by][:, 0:n], A(XQT + (jj * 8 + kc) * 128, 128), wu,
                                 start=(kc == 0), stop=(kc == 7))
                    pe_marks.append(c.inc(i, "pe"))
                    idx += 1
                gu_done = c.c["pe"]
                for jj in range(len(hv)):
                    bg, by = jj * 2, jj * 2 + 1
                    p2 = (nchi * len(hv) + jj) % 2
                    FH = F_H if p2 == 0 else F_H1
                    FQ = F_HQ if p2 == 0 else F_HQ1
                    c.wait("scalar", "pe", gu_done)
                    c.wait("scalar", "ve", h_free[p2])
                    c.a("activation", A(FH, n), ps[bg][:, 0:n], AF.Silu, inc=True)
                    hs_done = c.c["ac"]
                    c.bank_free(bg, "ac", hs_done)
                    c.wait("vector", "ac", hs_done)
                    c.v("tensor_tensor", out=A(FH, n), in0=A(FH, n), in1=ps[by][:, 0:n],
                        op=ALU.mult, inc=True)
                    c.bank_free(by, "ve", c.c["ve"])
                    c.drain("vector")
                    fp4_quant(A(FH, n), A(FQ, n), n)
                    hq_done = c.vmark()
                    h_free[p2] = hq_done
                    bk = 6 + jj
                    c.bank_wait_te(bk)
                    c.wait("tensor", "ve", hq_done)
                    nk = n // 128
                    for k in range(nk):
                        i = c.tp(ps[bk][:, k * 128:(k + 1) * 128], A(FQ + k * 128, 128), 128)
                    pe_v = c.inc(i, "pe")
                    c.wait("vector", "pe", pe_v)
                    for k in range(nk):
                        fc = nchi * 4 + k
                        c.v("tensor_copy", out=A(F_HQT + fc * 256 + jj * 128, 128),
                            in_=ps[bk][:, k * 128:(k + 1) * 128], inc=(k == nk - 1))
                    c.bank_free(bk, "ve", c.c["ve"])
            hqT_done = c.vmark()
            if hi + 1 < len(halves):
                xq_marks[hi + 1] = ffnc_pre(b, halves[hi + 1], bufs[(hi + 1) % 2][0],
                                            bufs[(hi + 1) % 2][1])
            pe_marks2 = []
            for fc in range(22):
                buf = fc % 2
                wd = A(F_WS2 + buf * 1024, 1024)
                if fc >= 2:
                    c.wait("sync", "pe", pe_marks2[fc - 2])
                else:
                    c.wait("sync", "pe", pe_marks[-1])
                ld = c.load(wd, t["wdT"][fc * 128:(fc + 1) * 128, :], sl=fc % 2)
                c.waitv("tensor", ld)
                c.wait("tensor", "ve", hqT_done)
                for jj in range(len(hv)):
                    for hf in range(2):
                        bk = jj * 2 + hf
                        if fc == 0:
                            c.bank_wait_te(bk)
                        i = c.mm(ps[bk][:, 0:512], A(F_HQT + fc * 256 + jj * 128, 128),
                                 wd[:, hf * 512:(hf + 1) * 512], start=(fc == 0), stop=(fc == 21))
                pe_marks2.append(c.inc(i, "pe"))
            z_done = c.c["pe"]
            for jj, j in enumerate(hv):
                xin = A(XIN + jj * 1024, D)
                for hf in range(2):
                    bk = jj * 2 + hf
                    c.wait("scalar", "pe", z_done)
                    c.wait("scalar", "ve", c.c["ve"])
                    c.a("activation", A(F_G + hf * 512, 512), ps[bk][:, 0:512], AF.Silu, inc=True)
                g_done = c.c["ac"]
                c.wait("vector", "ac", g_done)
                c.v("tensor_tensor", out=A(F_G, D), in0=A(F_G, D), in1=A(A_S + j * 1024, D), op=ALU.mult)
                c.drain("vector")
                c.v("tensor_add", out=A(F_G, D), in0=A(F_G, D), in1=xin, inc=True)
                x1_done = c.c["ve"]
                c.bank_free(jj * 2, "ve", x1_done)
                c.bank_free(jj * 2 + 1, "ve", x1_done)
                c.drain("vector")
                c.vmark()
                c.wait("gpsimd", "ve", c.c["ve"])
                row = 256 + b["x1row"] + j * 128
                s_ = c.store(t["x1s"][row:row + 128, :], A(F_G, D))
                c.waitv("vector", s_)
                c.waitv("scalar", s_)
                c.waitv("sync", s_)

    if c.on("vector"):
        vc = ar[0:128, A_VV:A_VV + 4 * NH * 65].rearrange("p (cell w) -> p cell w", w=65)
        c.eng.memset(vc[:, :, 64:65], 1.0)
    c.vmark()
    prekv(blocks[0], 0)
    c.bar()
    att_phase(blocks[0], 0, 1)
    c.bar()
    prekv(blocks[1], 1)
    c.bar()
    att_phase(blocks[0], 0, 2)
    wout(blocks[0])
    pre0 = ffnc_pre(blocks[0], blocks[0]["ffn"][0:2], F_XQT, F_XIN)
    c.bar()
    ffnc_main(blocks[0], pre0)
    c.bar()
    qsl = {1: 1, 2: 0, 3: 1, 4: 0}
    for bi in range(1, 5):
        att_phase(blocks[bi], qsl[bi], 1)
        c.bar()
        prekv(blocks[bi + 1], qsl[bi + 1] if bi + 1 < 5 else 0)
        c.bar()
        att_phase(blocks[bi], qsl[bi], 2)
        wout(blocks[bi])
        pre0 = ffnc_pre(blocks[bi], blocks[bi]["ffn"][0:2], F_XQT, F_XIN)
        c.bar()
        ffnc_main(blocks[bi], pre0)
        c.bar()

    if _LIMIT == "L0":
        c.bar()
        c.store(t["dbg_x1s"], t["x1s"], sl=1)
        c.wait_stores("gpsimd")
        c.bar()
        return

    def load_wkv():
        ld1 = c.load(A(L_WKV, 512).rearrange("p (a b) -> p a b", b=64),
                     t["WkT"].rearrange("(a p) b -> p a b", p=128), sl=5)
        ld2 = c.load(A(L_WKV + 512, 512).rearrange("p (a b) -> p a b", b=64),
                     t["WvT"].rearrange("(a p) b -> p a b", p=128), sl=5)
        return ld2

    ldg = c.load(t["rowbuf"][0:1, 0:1024], t["gam_row"], sl=4)
    ldb = c.load(t["rowbuf"][0:1, 1024:2048], t["bet_row"], sl=4)
    for gb in range(2):
        for ch in range(2):
            bk = gb * 2 + ch
            c.bank_wait_te(bk)
            c.waitv("tensor", ldb)
            i = c.mm(ps[bk][:, 0:512], t["onesrow"][0:1, 0:128],
                     t["rowbuf"][0:1, gb * 1024 + ch * 512:gb * 1024 + (ch + 1) * 512],
                     start=True, stop=True)
            pe_v = c.inc(i, "pe")
            c.wait("vector", "pe", pe_v)
            c.v("tensor_copy", out=A(L_GBC + gb * 1024 + ch * 512, 512), in_=ps[bk][:, 0:512], inc=True)
            c.bank_free(bk, "ve", c.c["ve"])
    if c.on("vector"):
        vc = ar[0:128, L_VV:L_VV + 20 * 65].rearrange("p (cell w) -> p cell w", w=65)
        c.eng.memset(vc[:, :, 64:65], 1.0)
    c.vmark()
    c.bar()

    def l1_kv_tiles(ch0, ntile, xn1T_off):
        wkv_ld = load_wkv()
        c.waitv("tensor", wkv_ld)
        c.bank_wait_te(4)
        for kc in range(8):
            i = c.mm(ps[4][0:64, 0:ntile * 128], A(L_WKV + kc * 64, 64),
                     A(xn1T_off + kc * 512, ntile * 128), start=(kc == 0), stop=(kc == 7))
        c.inc(i, "pe")
        for kc in range(8):
            i = c.mm(ps[4][64:128, 0:ntile * 128], A(L_WKV + kc * 64, 64),
                     A(xn1T_off + kc * 512, ntile * 128), start=(kc == 0), stop=(kc == 7),
                     tile_position=(0, 64))
        c.inc(i, "pe")
        c.bank_wait_te(5)
        for kc in range(8):
            for j in range(ntile):
                i = c.mm(ps[5][:, j * 64:(j + 1) * 64], A(xn1T_off + kc * 512 + j * 128, 128),
                         A(L_WKV + 512 + kc * 64, 64), start=(kc == 0 and j == 0), stop=(kc == 7))
        c.inc(i, "pe")
        kv_done = c.c["pe"]
        c.wait("vector", "pe", kv_done)
        c.v("tensor_copy", out=A(L_KT + ch0 * 128, ntile * 128), in_=ps[4][:, 0:ntile * 128])
        for j in range(ntile):
            c.v("tensor_copy", out=A(L_VV + (ch0 + j) * 65, 64), in_=ps[5][:, j * 64:(j + 1) * 64],
                inc=(j == ntile - 1))
        c.bank_free(4, "ve", c.c["ve"])
        c.bank_free(5, "ve", c.c["ve"])
        c.vmark()

    for j in range(2):
        xin = A(L_XIN + j * 1024, D)
        ld = c.load(xin, t["x1s"][j * 128:(j + 1) * 128, :], sl=2 + j)
        c.waitv("vector", ld)
        c.waitv("scalar", ld)
        ln_tile(xin, A(L_LOC, D), A(L_LOC + 1024, D))
        ln_done = c.vmark()
        c.wait("tensor", "ve", ln_done)
        transpose_1024(L_LOC, L_XN1T, j)
    c.vmark()
    l1_kv_tiles(0, 2, L_XN1T)
    c.bar()

    def l1pre(g):
        ln_marks = []
        for j in range(4):
            xin = A(L_XIN + (j % 2) * 1024, D)
            row = 256 + (g * 4 + j) * 128
            if j >= 2:
                c.wait("sync", "ve", ln_marks[j - 2])
            ld = c.load(xin, t["x1s"][row:row + 128, :], sl=2 + (j % 2))
            c.waitv("vector", ld)
            c.waitv("scalar", ld)
            ln_tile(xin, A(L_XN1 + j * 1024, D), A(L_LOC, D))
            ln_done = c.vmark()
            ln_marks.append(ln_done)
            c.wait("tensor", "ve", ln_done)
            transpose_1024(L_XN1 + j * 1024, L_XN1T, j)
        xn1T_done = c.vmark()
        pe_marks = []
        for kc in range(8):
            buf = kc % 2
            wap = A(L_WS + buf * 1024, 1024)
            if kc >= 2:
                c.wait("sync", "pe", pe_marks[kc - 2])
            ld = c.load(wap, t["WqT"][kc * 128:(kc + 1) * 128, :], sl=kc % 2)
            c.waitv("tensor", ld)
            c.wait("tensor", "ve", xn1T_done)
            for n in range(8):
                if kc == 0:
                    c.bank_wait_te(n)
                i = c.mm(ps[n][:, 0:512], wap[:, n * 128:(n + 1) * 128],
                         A(L_XN1T + kc * 512, 512), start=(kc == 0), stop=(kc == 7))
            pe_marks.append(c.inc(i, "pe"))
        q_done = c.c["pe"]
        for n in range(8):
            c.wait("vector", "pe", q_done)
            c.v("tensor_copy", out=A(L_QT + n * 512, 512), in_=ps[n][:, 0:512], inc=True)
            c.bank_free(n, "ve", c.c["ve"])
        c.vmark()
        l1_kv_tiles(2 + g * 4, 4, L_XN1T)

    def l1att(g):
        if _DBG4 and g == 0:
            c.wait("gpsimd", "ve", c.c["ve"])
            d1 = c.store(t["dbg_kT"], A(L_KT, 2560), sl=1)
            d2 = c.store(t["dbg_vv"], A(L_VV, 1300), sl=1)
            d3 = c.store(t["dbg_qT"], A(L_QT, 4096), sl=1)
            c.waitv("sync", d3)
        for j in range(4):
            tt = g * 4 + j
            if tt == 0:
                msks = [t["m00"], t["m01"], None]
            elif tt == 1:
                msks = [t["m10"], None, None]
            else:
                msks = [t["triA"], None, None]
            for hh in range(NH):
                sb = hh % 2
                c.bank_wait_te(sb)
                for jc in range(3):
                    i = c.mm(ps[sb][:, jc * 128:(jc + 1) * 128],
                             A(L_KT + (tt + jc) * 128, 128, p0=(hh % 2) * 64, part=64),
                             A(L_QT + (hh // 2) * 512 + j * 128, 128, p0=(hh % 2) * 64, part=64),
                             start=True, stop=True)
                s_done = c.inc(i, "pe")
                c.wait("scalar", "pe", s_done)
                pb = L_P + (hh % 2) * 384
                c.a("activation", A(pb, 384), ps[sb][:, 0:384], AF.Exp, scale=SCALE, inc=True)
                exp_done = c.c["ac"]
                c.bank_free(sb, "ac", exp_done)
                c.wait("vector", "ac", exp_done)
                c.v("tensor_tensor", out=A(pb, 128), in0=A(pb, 128), in1=msks[0][:], op=ALU.mult)
                if msks[1] is not None:
                    c.v("tensor_tensor", out=A(pb + 128, 128), in0=A(pb + 128, 128), in1=msks[1][:], op=ALU.mult)
                c.v("tensor_tensor", out=A(pb + 256, 128), in0=A(pb + 256, 128), in1=t["triB"][:], op=ALU.mult)
                c.drain("vector")
                p_done = c.vmark()
                ob = 4 + (hh % 2)
                c.bank_wait_te(ob)
                c.wait("tensor", "ve", p_done)
                for jc in range(3):
                    i = c.mm(ps[ob][:, 0:65], A(pb + jc * 128, 128), A(L_VV + (tt + jc) * 65, 65),
                             start=(jc == 0), stop=(jc == 2))
                o_done = c.inc(i, "pe")
                c.wait("vector", "pe", o_done)
                c.v("reciprocal", out=st[:, 6:7], in_=ps[ob][:, 64:65])
                c.drain("vector")
                c.v("tensor_scalar", out=A(L_AOT2 + hh * 64, 64), in0=ps[ob][:, 0:64],
                    scalar1=st[:, 6:7], scalar2=None, op0=ALU.mult, inc=True)
                c.drain("vector")
                c.bank_free(ob, "ve", c.c["ve"])
            ao_done = c.vmark()
            c.wait("tensor", "ve", ao_done)
            transpose_1024(L_AOT2, L_LOC, j)
        aoT_done = c.vmark()
        ldrow = c.load(t["rowbuf"][0:1, 0:1024], t["bf_row"], sl=4)
        pe_marks = []
        for kc in range(8):
            buf = kc % 2
            wap = A(L_WS + buf * 1024, 1024)
            if kc >= 2:
                c.wait("sync", "pe", pe_marks[kc - 2])
            ld = c.load(wap, t["WfT"][kc * 128:(kc + 1) * 128, :], sl=kc % 2)
            c.waitv("tensor", ld)
            c.wait("tensor", "ve", aoT_done)
            for j in range(4):
                for hf in range(2):
                    bk = j * 2 + hf
                    if kc == 0:
                        c.bank_wait_te(bk)
                    i = c.mm(ps[bk][:, 0:512], A(L_LOC + kc * 512 + j * 128, 128),
                             wap[:, hf * 512:(hf + 1) * 512], start=(kc == 0), stop=False)
            if kc == 7:
                c.waitv("tensor", ldrow)
                for j in range(4):
                    for hf in range(2):
                        bk = j * 2 + hf
                        i = c.mm(ps[bk][:, 0:512], t["onesrow"][0:1, 0:128],
                                 t["rowbuf"][0:1, hf * 512:(hf + 1) * 512], start=False, stop=True)
            pe_marks.append(c.inc(i, "pe"))
        s_done = c.c["pe"]
        for j in range(4):
            for hf in range(2):
                bk = j * 2 + hf
                c.wait("vector", "pe", s_done)
                c.v("tensor_copy", out=A(L_S1 + j * 1024 + hf * 512, 512), in_=ps[bk][:, 0:512], inc=True)
                c.bank_free(bk, "ve", c.c["ve"])
        c.vmark()

    def l1ffn(g):
        for half in range(2):
            hv = [half * 2, half * 2 + 1]
            for jj, j in enumerate(hv):
                ln_tile(A(L_XN1 + j * 1024, D), A(L_LOC + 1024, D), A(L_LOC, D))
                c.v("tensor_tensor", out=A(L_LOC + 1024, D), in0=A(L_LOC + 1024, D),
                    in1=A(L_GBC, D), op=ALU.mult)
                c.drain("vector")
                c.v("tensor_add", out=A(L_LOC + 1024, D), in0=A(L_LOC + 1024, D), in1=A(L_GBC + 1024, D))
                c.drain("vector")
                h2_done = c.vmark()
                c.wait("tensor", "ve", h2_done)
                for hlf in range(2):
                    bk = 6 + hlf
                    c.bank_wait_te(bk)
                    for k in range(4):
                        kk = hlf * 4 + k
                        i = c.tp(ps[bk][:, k * 128:(k + 1) * 128], A(L_LOC + 1024 + kk * 128, 128), 128)
                    pe_v = c.inc(i, "pe")
                    c.wait("vector", "pe", pe_v)
                    for k in range(4):
                        kk = hlf * 4 + k
                        c.v("tensor_copy", out=A(L_H2T + kk * 256 + jj * 128, 128),
                            in_=ps[bk][:, k * 128:(k + 1) * 128], inc=(k == 3))
                    c.bank_free(bk, "ve", c.c["ve"])
            h2T_done = c.vmark()
            pe_marks = []
            pe_half0 = c.c["pe"]
            idx = 0
            for ng in range(4):
                for kc in range(8):
                    buf = idx % 2
                    wap = A(L_WS + buf * 1024, 1024)
                    if idx >= 2:
                        c.wait("sync", "pe", pe_marks[idx - 2])
                    else:
                        c.wait("sync", "pe", pe_half0)
                    ld = c.load(wap, t["w1d"][kc * 128:(kc + 1) * 128, ng * 1024:(ng + 1) * 1024], sl=idx % 2)
                    c.waitv("tensor", ld)
                    c.wait("tensor", "ve", h2T_done)
                    for n in range(8):
                        if kc == 0:
                            c.bank_wait_te(n)
                        i = c.mm(ps[n][:, 0:256], wap[:, n * 128:(n + 1) * 128],
                                 A(L_H2T + kc * 256, 256), start=(kc == 0), stop=(kc == 7))
                    pe_marks.append(c.inc(i, "pe"))
                    idx += 1
                u_done = c.c["pe"]
                for n in range(8):
                    ngl = ng * 8 + n
                    c.wait("scalar", "pe", u_done)
                    c.a("activation", A(L_YT + ngl * 256, 256), ps[n][:, 0:256], AF.Silu,
                        bias=t["fc1bC"][:, ngl:ngl + 1], scale=1.0, inc=True)
                    c.bank_free(n, "ac", c.c["ac"])
            yT_done = c.amark()
            ldrow = c.load(t["rowbuf"][0:1, 0:1024], t["fc2b_row"], sl=4)
            pe_marks2 = []
            for kc in range(32):
                buf = kc % 2
                wap = A(L_WS + buf * 1024, 1024)
                if kc >= 2:
                    c.wait("sync", "pe", pe_marks2[kc - 2])
                else:
                    c.wait("sync", "pe", pe_marks[-1])
                ld = c.load(wap, t["w2d"][kc * 128:(kc + 1) * 128, :], sl=kc % 2)
                c.waitv("tensor", ld)
                c.wait("tensor", "ac", yT_done)
                for jj in range(2):
                    for hf in range(2):
                        bk = jj * 2 + hf
                        if kc == 0:
                            c.bank_wait_te(bk)
                        i = c.mm(ps[bk][:, 0:512], A(L_YT + kc * 256 + jj * 128, 128),
                                 wap[:, hf * 512:(hf + 1) * 512], start=(kc == 0), stop=False)
                if kc == 31:
                    c.waitv("tensor", ldrow)
                    for jj in range(2):
                        for hf in range(2):
                            bk = jj * 2 + hf
                            i = c.mm(ps[bk][:, 0:512], t["onesrow"][0:1, 0:128],
                                     t["rowbuf"][0:1, hf * 512:(hf + 1) * 512], start=False, stop=True)
                pe_marks2.append(c.inc(i, "pe"))
            z_done = c.c["pe"]
            for jj, j in enumerate(hv):
                tt = g * 4 + j
                xin = A(L_XIN + jj * 1024, D)
                row = 256 + tt * 128
                ld = c.load(xin, t["x1s"][row:row + 128, :], sl=2 + jj)
                for hf in range(2):
                    bk = jj * 2 + hf
                    c.wait("vector", "pe", z_done)
                    c.v("scalar_tensor_tensor", out=A(L_LOC + 2048 + hf * 512, 512),
                        in0=ps[bk][:, 0:512], scalar=0.5, in1=A(L_XN1 + j * 1024 + hf * 512, 512),
                        op0=ALU.mult, op1=ALU.add)
                pre_done = c.vmark()
                c.wait("scalar", "ve", pre_done)
                c.a("activation", A(L_LOC + 1024, D), A(L_LOC + 2048, D), AF.Silu, inc=True)
                g2_done = c.c["ac"]
                if _DBG4:
                    c.wait("gpsimd", "ac", g2_done)
                    d1 = c.store(t["dbg_g2"][tt * 128:(tt + 1) * 128, :], A(L_LOC + 1024, D), sl=1)
                    d2 = c.store(t["dbg_s1"][tt * 128:(tt + 1) * 128, :], A(L_S1 + j * 1024, D), sl=1)
                    c.waitv("vector", d2)
                    c.waitv("scalar", d2)
                c.wait("vector", "ac", g2_done)
                c.v("tensor_tensor", out=A(L_LOC + 1024, D), in0=A(L_LOC + 1024, D),
                    in1=A(L_S1 + j * 1024, D), op=ALU.mult)
                c.drain("vector")
                c.waitv("vector", ld)
                c.v("tensor_add", out=A(L_LOC + 1024, D), in0=A(L_LOC + 1024, D), in1=xin, inc=True)
                x2_done = c.c["ve"]
                c.bank_free(jj * 2, "ve", x2_done)
                c.bank_free(jj * 2 + 1, "ve", x2_done)
                c.drain("vector")
                c.vmark()
                c.wait("gpsimd", "ve", c.c["ve"])
                s_ = c.store(t["out"][tt * 128:(tt + 1) * 128, :], A(L_LOC + 1024, D))
                c.waitv("vector", s_)
                c.waitv("scalar", s_)
                c.waitv("sync", s_)

    for g in range(4):
        l1pre(g)
        c.bar()
        l1att(g)
        c.bar()
        l1ffn(g)
        c.bar()

    c.wait_stores("gpsimd")
    c.bar()


def _build():
    nc = bass.Bass("TRN2", target_bir_lowering=False)
    t = {}
    def din(name, shape, dt=F32):
        t[name] = nc.dram_tensor(name, shape, dt, kind="ExternalInput").ap()
    din("x_own", [2048, D])
    din("x_pre2", [512, D])
    din("x_wrap", [512, D])
    din("W_qkv", [D, 3072])
    din("W_out", [D, D])
    din("w_gate", [FF0, D])
    din("w_up", [FF0, D])
    din("w_down", [D, FF0])
    din("Wq", [D, D])
    din("Wk", [HD, D])
    din("Wv", [HD, D])
    din("Wf", [D, D])
    din("fc1_q", [D, FF1], I32)
    din("fc1_s", [8, FF1])
    din("fc2_q", [FF1, D], I32)
    din("fc2_s", [32, D])
    din("bqkv_t", [24, 128])
    din("bqkv_row", [1, 3072])
    din("bout_row", [1, D])
    din("bf_row", [1, D])
    din("fc1b_t", [32, 128])
    din("fc2b_row", [1, D])
    din("gam_row", [1, D])
    din("bet_row", [1, D])
    din("ident_in", [128, 128])
    din("triA_in", [128, 128])
    din("triB_in", [128, 128])
    din("m00_in", [128, 128])
    din("m01_in", [128, 128])
    din("m10_in", [128, 128])
    t["out"] = nc.dram_tensor("out", [2048, D], F32, kind="ExternalOutput").ap()
    if _LIMIT == "prologue":
        for name, shape in (("wgT", [D, FF0]), ("wuT", [D, FF0]), ("wdT", [FF0, D]),
                            ("w1d", [D, FF1]), ("w2d", [FF1, D])):
            t["dbg_" + name] = nc.dram_tensor("dbg_" + name, shape, F32, kind="ExternalOutput").ap()
    if _LIMIT == "L0":
        t["dbg_x1s"] = nc.dram_tensor("dbg_x1s", [2304, D], F32, kind="ExternalOutput").ap()
    if _DBG2:
        t["dbg_s"] = nc.dram_tensor("dbg_s", [2304, D], F32, kind="ExternalOutput").ap()
        t["dbg_g"] = nc.dram_tensor("dbg_g", [2304, D], F32, kind="ExternalOutput").ap()
    if _DBG3:
        t["dbg_xqT"] = nc.dram_tensor("dbg_xqT", [128, 2048], F32, kind="ExternalOutput").ap()
        t["dbg_hqT"] = nc.dram_tensor("dbg_hqT", [128, 5632], F32, kind="ExternalOutput").ap()
    if _DBG4:
        t["dbg_s1"] = nc.dram_tensor("dbg_s1", [2048, D], F32, kind="ExternalOutput").ap()
        t["dbg_g2"] = nc.dram_tensor("dbg_g2", [2048, D], F32, kind="ExternalOutput").ap()
        t["dbg_kT"] = nc.dram_tensor("dbg_kT", [128, 2560], F32, kind="ExternalOutput").ap()
        t["dbg_vv"] = nc.dram_tensor("dbg_vv", [128, 1300], F32, kind="ExternalOutput").ap()
        t["dbg_qT"] = nc.dram_tensor("dbg_qT", [128, 4096], F32, kind="ExternalOutput").ap()
    def dscr(name, shape, dt=F32):
        t[name] = nc.dram_tensor(name, shape, dt, kind="Internal").ap()
    dscr("x1s", [2304, D])
    dscr("wgT", [D, FF0])
    dscr("wuT", [D, FF0])
    dscr("wdT", [FF0, D])
    dscr("w1d", [D, FF1])
    dscr("w2d", [FF1, D])
    dscr("WqT", [D, D])
    dscr("WfT", [D, D])
    dscr("WkT", [D, HD])
    dscr("WvT", [D, HD])

    ctx = contextlib.ExitStack()
    with ctx:
        t["arena"] = ctx.enter_context(nc.sbuf_tensor("arena", [128, ARENA], F32)).ap()
        for name, shape in (
            ("ident", [128, 128]), ("triA", [128, 128]), ("triB", [128, 128]),
            ("m00", [128, 128]), ("m01", [128, 128]), ("m10", [128, 128]),
            ("onesrow", [1, 128]), ("onescol", [128, 1]), ("eps", [128, 1]),
            ("bqkvT_s", [24, 128]), ("fc1bT_s", [32, 128]),
            ("bqkvC", [128, 24]), ("fc1bC", [128, 32]),
            ("stats", [128, 16]), ("qsc", [128, 192]), ("rowbuf", [1, 3072]),
        ):
            t[name] = ctx.enter_context(nc.sbuf_tensor(name, shape, F32))
        t["psum"] = [ctx.enter_context(nc.psum_tensor(f"p{i}", [128, 512], F32)) for i in range(8)]
        sem = {}
        semnames = ("pe", "ve", "ac") + tuple("dl%d" % i for i in range(6)) + ("ds0", "ds1", "ds2") \
            + tuple("b_" + e for e in ENGINES)
        for name in semnames:
            sem[name] = ctx.enter_context(nc.semaphore(name))
        block = ctx.enter_context(nc.Block())

        def mk(en):
            def f(eng):
                _emit(C(nc, en, eng, sem, t))
            return f
        block.sync(mk("sync"))
        block.tensor(mk("tensor"))
        block.vector(mk("vector"))
        block.scalar(mk("scalar"))
        block.gpsimd(mk("gpsimd"))
    return nc


def _host_inputs(inputs):
    _ = inputs["L1_attn_segment_pos"]
    x = np.ascontiguousarray(np.asarray(inputs["L0_attn_x"], dtype=np.float32))
    i32 = lambda a: np.ascontiguousarray(np.asarray(a, dtype=np.int32))
    f32 = lambda a: np.ascontiguousarray(np.asarray(a, dtype=np.float32))
    kk = np.arange(128)[:, None]
    qq = np.arange(128)[None, :]
    triA = (qq <= kk).astype(np.float32)
    triB = (qq >= kk).astype(np.float32)
    full = np.ones((128, 128), np.float32)
    zero = np.zeros((128, 128), np.float32)
    common = dict(
        W_qkv=f32(inputs["L0_attn_W_qkv"]),
        W_out=f32(inputs["L0_attn_W_out"]),
        w_gate=f32(inputs["L0_ffn_w_gate"]),
        w_up=f32(inputs["L0_ffn_w_up"]),
        w_down=f32(inputs["L0_ffn_w_down"]),
        Wq=f32(inputs["L1_attn_Wq"]),
        Wk=f32(inputs["L1_attn_Wk"]),
        Wv=f32(inputs["L1_attn_Wv"]),
        Wf=f32(inputs["L1_attn_Wf"]),
        fc1_q=i32(inputs["L1_ffn_fc1_q"]),
        fc1_s=f32(inputs["L1_ffn_fc1_scales"]),
        fc2_q=i32(inputs["L1_ffn_fc2_q"]),
        fc2_s=f32(inputs["L1_ffn_fc2_scales"]),
        bqkv_t=f32(inputs["L0_attn_b_qkv"]).reshape(24, 128),
        bqkv_row=f32(inputs["L0_attn_b_qkv"]).reshape(1, 3072),
        bout_row=f32(inputs["L0_attn_b_out"]).reshape(1, D),
        bf_row=f32(inputs["L1_attn_bf"]).reshape(1, D),
        fc1b_t=f32(inputs["L1_ffn_fc1_bias"]).reshape(32, 128),
        fc2b_row=f32(inputs["L1_ffn_fc2_bias"]).reshape(1, D),
        gam_row=f32(inputs["L1_ffn_ln_gamma"]).reshape(1, D),
        bet_row=f32(inputs["L1_ffn_ln_beta"]).reshape(1, D),
        ident_in=np.eye(128, dtype=np.float32),
        triA_in=triA, triB_in=triB,
    )
    in_maps = []
    for cid in range(8):
        b, half = cid // 2, cid % 2
        t0 = half * 2048
        m = dict(common)
        m["x_own"] = np.ascontiguousarray(x[b, t0:t0 + 2048])
        m["x_pre2"] = np.ascontiguousarray(x[b, 1536:2048])
        m["x_wrap"] = np.ascontiguousarray(x[b, 2048:2560] if half == 0 else x[b, 0:512])
        if half == 0:
            m["m00_in"], m["m01_in"], m["m10_in"] = zero, zero, zero
        else:
            m["m00_in"], m["m01_in"], m["m10_in"] = triA, full, triA
        in_maps.append(m)
    return x, in_maps


def kernel(**inputs):
    x, in_maps = _host_inputs(inputs)
    nc = _build()
    res = run_bass_kernel_spmd(nc, in_maps, core_ids=list(range(8)))
    out = np.empty((4, 4096, D), dtype=np.float32)
    for cid in range(8):
        b, half = cid // 2, cid % 2
        out[b, half * 2048:(half + 1) * 2048] = res.results[cid]["out"]
    return out
